# version 05
# baseline (speedup 1.0000x reference)
import numpy as np
import concourse.bass as bass
import concourse.mybir as mybir
from concourse.alu_op_type import AluOpType as OP
from concourse.bass_utils import run_bass_kernel_spmd

F32 = mybir.dt.float32
BF16 = mybir.dt.bfloat16
I8 = mybir.dt.int8
AF = mybir.ActivationFunctionType
AX = mybir.AxisListType

D = 1024
DI = 2048
DH = 4096
NST = 16
HS4 = 1024
NS4 = 32
SPAN = 2048
NSPANS = 8
NB = 2
L = 16384
NCORES = 8
TCH = 512

LN_EPS = 1e-5
RMS_EPS = 1e-6

_SC_DT_C = (0.1 + 0.001) / 2.0
_SC_DT_R = (0.1 - 0.001) / 2.0
_SC_DT_S1 = 0.5 / (_SC_DT_R + 1e-8)
_SC_DT_S2 = _SC_DT_R / 0.5
_SC_A_S1 = 0.5 / (5.0 + 1e-8)
_SC_A_S2 = 5.0 / 0.5



class Prog:
    ENGINES = ("V", "S", "P", "T", "Y")

    def __init__(self, nc):
        self.nc = nc
        self.eng = {"V": nc.vector, "S": nc.scalar, "P": nc.gpsimd,
                    "T": nc.tensor, "Y": nc.sync}
        self._ctx = []
        self.prog, self.progv = {}, {}
        for e in self.ENGINES:
            self.prog[e] = self._sem(f"prog{e}")
            self.progv[e] = 0
        self.dsem, self.dsemv = {}, {}
        for e in ("P", "Y", "S", "V"):
            self.dsem[e] = self._sem(f"dma{e}")
            self.dsemv[e] = 0
        self.ccsem = self._sem("ccsem")
        self.ccv = 0
        self.writer = {}
        self.readers = {}
        self.waited = {e: {} for e in self.ENGINES}
        self._aliases = {}
        self._prefix = ""
        self._phase_psums = []

    def _sem(self, name):
        g = self.nc.semaphore(name)
        s = g.__enter__()
        self._ctx.append(g)
        return s

    def set_prefix(self, p):
        self._prefix = p

    def phase(self):
        self.barrier()
        if hasattr(self, "_tiles"):
            self._sb_off = self._sb_base
            self._tiles = {}
            self._tile_off = {}
        for g in reversed(self._phase_psums):
            g.__exit__(None, None, None)
        self._phase_psums = []
        self._psum_tiles = {}

    def sbuf(self, name, shape, dtype, alias=None):
        name = self._prefix + name
        if alias is not None:
            alias = self._prefix + alias
        if not hasattr(self, "_tiles"):
            self._tiles = {}
            self._tile_off = {}
            self._sb_off = (int(self.nc.sbuf_base) + 31) // 32 * 32
            self._sb_base = self._sb_off
            self._sb_top = int(self.nc.sbuf_top)
        if name in self._tiles:
            return self._tiles[name]
        import math
        nb = int(np.prod(shape[1:])) * mybir.dt.size(dtype)
        if alias is not None:
            off = self._tile_off[alias]
            grp = self._aliases.get(alias)
            if grp is None:
                grp = (alias,)
            grp = tuple(set(grp + (name, alias)))
            for m in grp:
                self._aliases[m] = grp
        else:
            off = self._sb_off
            self._sb_off = (off + nb + 31) // 32 * 32
            assert self._sb_off <= self._sb_top, \
                f"SBUF overflow at {name}: {self._sb_off} > {self._sb_top}"
        t = self.nc.alloc_sbuf_tensor_at(name, shape, dtype, offset=off)
        self._tile_off[name] = off
        self._tiles[name] = t
        return t

    def psum(self, name, shape, dtype):
        name = self._prefix + name
        if not hasattr(self, "_psum_tiles"):
            self._psum_tiles = {}
        if name in self._psum_tiles:
            return self._psum_tiles[name]
        g = self.nc.psum_tensor(name, shape, dtype)
        t = g.__enter__()
        self._phase_psums.append(g)
        self._psum_tiles[name] = t
        return t

    def close(self):
        for g in reversed(self._phase_psums):
            g.__exit__(None, None, None)
        self._phase_psums = []
        for g in reversed(self._ctx):
            g.__exit__(None, None, None)

    @staticmethod
    def _key(x):
        if isinstance(x, bass.AP):
            return x.tensor.name
        return x.name

    def _alias_group(self, k):
        return self._aliases.get(k, (k,))

    def _sem_of(self, tag):
        if tag.startswith("d"):
            return self.dsem[tag[1]], tag
        if tag == "cc":
            return self.ccsem, "cc"
        return self.prog[tag], tag

    def _wait(self, e, tag, val):
        if val <= 0:
            return
        w = self.waited[e]
        if w.get(tag, 0) >= val:
            return
        sem, _ = self._sem_of(tag)
        self.eng[e].wait_ge(sem, val)
        w[tag] = val

    def _pre(self, e, reads, writes):
        for x in reads:
            for k in self._alias_group(self._key(x)):
                wr = self.writer.get(k)
                if wr:
                    self._wait(e, wr[0], wr[1])
        for x in writes:
            for k in self._alias_group(self._key(x)):
                wr = self.writer.get(k)
                if wr:
                    self._wait(e, wr[0], wr[1])
                for rt, rv in self.readers.get(k, []):
                    self._wait(e, rt, rv)

    def _post(self, tag, val, reads, writes):
        for x in writes:
            k = self._key(x)
            self.writer[k] = (tag, val)
            self.readers[k] = []
        for x in reads:
            self.readers.setdefault(self._key(x), []).append((tag, val))

    def _emit(self, e, reads, writes, fn):
        self._pre(e, reads, writes)
        inst = fn(self.eng[e])
        self.progv[e] += 1
        inst.then_inc(self.prog[e], 1)
        self._post(e, self.progv[e], reads, writes)
        return inst

    def copy(self, e, out, in_):
        self._emit(e, [in_], [out], lambda g: g.tensor_copy(out, in_))

    def memset(self, e, out, v):
        self._emit(e, [], [out], lambda g: g.memset(out, v))

    def tt(self, e, out, a, b, op):
        self._emit(e, [a, b], [out], lambda g: g.tensor_tensor(out, a, b, op))

    def ts(self, e, out, a, s1, s2, op0, op1=None, extra_reads=()):
        rd = [a] + [s for s in (s1, s2) if isinstance(s, bass.AP)] + list(extra_reads)
        if op1 is None:
            self._emit(e, rd, [out], lambda g: g.tensor_scalar(out, a, s1, None, op0))
        else:
            self._emit(e, rd, [out], lambda g: g.tensor_scalar(out, a, s1, s2, op0, op1))

    def stt(self, e, out, in0, scalar, in1, op0, op1):
        rd = [in0, in1] + ([scalar] if isinstance(scalar, bass.AP) else [])
        self._emit(e, rd, [out], lambda g: g.scalar_tensor_tensor(out, in0, scalar, in1, op0, op1))

    def act(self, e, out, in_, func, bias=None, scale=None, accum_out=None):
        kw = {}
        rd = [in_]
        if bias is not None:
            if isinstance(bias, bass.AP):
                rd.append(bias)
            elif isinstance(bias, float):
                bias = np.float32(bias)
            kw["bias"] = bias
        if scale is not None:
            if isinstance(scale, bass.AP):
                rd.append(scale)
            elif isinstance(scale, float):
                scale = np.float32(scale)
            kw["scale"] = scale
        wr = [out]
        if accum_out is not None:
            kw["accum_out"] = accum_out
            wr.append(accum_out)
        self._emit(e, rd, wr, lambda g: g.activation(out=out, in_=in_, func=func, **kw))

    def red(self, e, out, in_, op, absval=False):
        kw = {}
        if absval:
            kw["apply_absolute_value"] = True
        self._emit(e, [in_], [out], lambda g: g.tensor_reduce(out, in_, axis=AX.X, op=op, **kw))

    def recip(self, e, out, in_):
        self._emit(e, [in_], [out], lambda g: g.reciprocal(out, in_))

    def scan(self, e, out, d0, d1, init, op0, op1):
        rd = [d0, d1] + ([init] if isinstance(init, bass.AP) else [])
        self._emit(e, rd, [out], lambda g: g.tensor_tensor_scan(out, d0, d1, init, op0, op1))

    def select(self, e, out, mask, a, b):
        self._emit(e, [mask, a, b], [out], lambda g: g.select(out, mask, a, b))

    def mm(self, psum, lhsT, rhs, start, stop):
        self._emit("T", [lhsT, rhs], [psum],
                   lambda g: g.matmul(psum, lhsT, rhs, start=start, stop=stop))

    def transpose(self, psum, in_, iden):
        self._emit("T", [in_, iden], [psum], lambda g: g.transpose(psum, in_, iden))

    def dma(self, e, out, in_, **kw):
        self._pre(e, [in_], [out])
        inst = self.eng[e].dma_start(out=out, in_=in_, **kw)
        self.dsemv[e] += 16
        inst.then_inc(self.dsem[e], 16)
        self._post("d" + e, self.dsemv[e], [in_], [out])
        return inst

    def dma_arms(self, e, fn, nranks=NCORES, **kw):
        pairs = [fn(r) for r in range(nranks)]
        reads, writes, seen_r, seen_w = [], [], set(), set()
        for o, i in pairs:
            kr, kw2 = self._key(i), self._key(o)
            if kr not in seen_r:
                seen_r.add(kr)
                reads.append(i)
            if kw2 not in seen_w:
                seen_w.add(kw2)
                writes.append(o)
        self._pre(e, reads, writes)
        eng = self.eng[e]
        for arm in self.nc.switch_core_id(engines=[eng], n=128):
            o, i = pairs[arm.logical % nranks]
            eng.dma_start(out=o, in_=i, **kw).then_inc(self.dsem[e], 16)
        self.dsemv[e] += 16
        self._post("d" + e, self.dsemv[e], reads, writes)

    def dma_t(self, e, out, in_):
        self._pre(e, [in_], [out])
        inst = self.eng[e].dma_start_transpose(out, in_)
        self.dsemv[e] += 16
        inst.then_inc(self.dsem[e], 16)
        self._post("d" + e, self.dsemv[e], [in_], [out])
        return inst

    def collective(self, kind, op, groups, ins, outs):
        self._pre("P", ins, outs)
        inst = self.eng["P"].collective_compute(
            kind, op, replica_groups=groups,
            ins=[a.opt() for a in ins], outs=[a.opt() for a in outs])
        self.ccv += 1
        inst.then_inc(self.ccsem)
        self._post("cc", self.ccv, ins, outs)
        return inst

    def fence_read(self, e, aps):
        self._pre(e, aps, [])

    def fence_all(self, e):
        for k, (tag, val) in list(self.writer.items()):
            self._wait(e, tag, val)

    def barrier(self):
        for e in self.ENGINES:
            for t, v in [("V", self.progv["V"]), ("S", self.progv["S"]),
                         ("P", self.progv["P"]), ("T", self.progv["T"]),
                         ("Y", self.progv["Y"])] +                         [("d" + q, self.dsemv[q]) for q in self.dsemv] +                         [("cc", self.ccv)]:
                self._wait(e, t, v)


def strided_cols(t, start, step, count):
    ap = t[:]
    part = ap.ap[0]
    return bass.AP(ap.tensor, ap.offset + start, [list(part), [step, count]])


def bc8(ap2d, G):
    return ap2d.unsqueeze(1).broadcast_to([ap2d.shape[0], G, ap2d.shape[1]])



def _span_of_core(c):
    return [(0, c), (1, NSPANS - 1 - c)]


def _fm(a):
    return np.ascontiguousarray(np.asarray(a).T)


def host_prepare(inputs):
    x = np.asarray(inputs["L0_attn_a_x"])
    base = {}
    base["WinT"] = _fm(inputs["L0_attn_a_W_in"])
    base["WdtT"] = _fm(inputs["L0_attn_a_W_dt"])
    base["WBT"] = _fm(inputs["L0_attn_a_W_B"])
    base["WCT"] = _fm(inputs["L0_attn_a_W_C"])
    base["WoutT"] = _fm(inputs["L0_attn_a_W_out"])
    base["w1T"] = _fm(inputs["L0_ffn_w1"])
    base["w2T"] = _fm(inputs["L0_ffn_w2"])
    base["normw"] = np.ascontiguousarray(
        np.asarray(inputs["L0_attn_a_norm_w"]).reshape(8, 128).T)
    base["cwa"] = np.ascontiguousarray(
        np.asarray(inputs["L0_attn_a_conv_w"]).reshape(16, 128, 4).transpose(1, 0, 2).reshape(128, 64))
    base["cba"] = np.ascontiguousarray(
        np.asarray(inputs["L0_attn_a_conv_b"]).reshape(16, 128).T)
    base["bdt"] = np.ascontiguousarray(
        np.asarray(inputs["L0_attn_a_b_dt"]).reshape(16, 128).T)
    base["Alog"] = np.ascontiguousarray(
        np.asarray(inputs["L0_attn_a_A_log"]).reshape(16, 128, 16).transpose(1, 0, 2).reshape(128, 256))
    base["Dpar"] = np.ascontiguousarray(
        np.asarray(inputs["L0_attn_a_D_param"]).reshape(16, 128).T)
    base["rscal"] = np.asarray(inputs["L0_attn_a_residual_scale"]).reshape(1, 1)
    base["cwf"] = np.ascontiguousarray(
        np.asarray(inputs["L0_ffn_conv_w"]).reshape(32, 128, 3).transpose(1, 0, 2).reshape(128, 96))
    base["cbf"] = np.ascontiguousarray(
        np.asarray(inputs["L0_ffn_conv_b"]).reshape(32, 128).T)
    base["s4_logdt"] = np.asarray(inputs["L0_attn_b_log_dt"]).reshape(8, 128).T.copy()
    base["s4_Ar"] = np.ascontiguousarray(
        np.asarray(inputs["L0_attn_b_A_real"]).reshape(8, 128, 32).transpose(1, 0, 2).reshape(128, 256))
    base["s4_Ai"] = np.ascontiguousarray(
        np.asarray(inputs["L0_attn_b_A_imag"]).reshape(8, 128, 32).transpose(1, 0, 2).reshape(128, 256))
    base["s4_Cr"] = np.ascontiguousarray(
        np.asarray(inputs["L0_attn_b_C_real"]).reshape(8, 128, 32).transpose(1, 0, 2).reshape(128, 256))
    base["s4_Ci"] = np.ascontiguousarray(
        np.asarray(inputs["L0_attn_b_C_imag"]).reshape(8, 128, 32).transpose(1, 0, 2).reshape(128, 256))
    base["s4_D"] = np.asarray(inputs["L0_attn_b_D"]).reshape(1, 1)
    base["ones1x128f"] = np.ones((1, 128), np.float32)
    base["idenf"] = np.eye(128, dtype=np.float32)
    base["antif"] = np.ascontiguousarray(np.eye(128, dtype=np.float32)[::-1])

    maps = [dict(base) for _ in range(NCORES)]
    for c in range(NCORES):
        for si, (b, s) in enumerate(_span_of_core(c)):
            t0 = s * SPAN
            maps[c][f"x_{si}"] = np.ascontiguousarray(x[b, t0:t0 + SPAN].T)
            halo = np.zeros((D, 128), np.float32)
            if t0 > 0:
                halo[:, :] = x[b, t0 - 128:t0].T
            maps[c][f"xhalo_{si}"] = halo
    return maps


def ml_bf16():
    import ml_dtypes
    return ml_dtypes.bfloat16


def host_assemble(results):
    out = np.empty((NB, L, D), np.float32)
    for c in range(NCORES):
        for si, (b, s) in enumerate(_span_of_core(c)):
            t0 = s * SPAN
            out[b, t0:t0 + SPAN] = np.asarray(results[c][f"out_{si}"]).T
    return out



def declare_in(nc, name, shape, dtype=F32):
    return nc.dram_tensor(name, shape, dtype, kind="ExternalInput").ap()


def declare_out(nc, name, shape, dtype=F32):
    return nc.dram_tensor(name, shape, dtype, kind="ExternalOutput").ap()



def emit_weights_1a(pr, io):
    w = {}
    w["normw"] = pr.sbuf("w_normw", [128, 8], F32)
    w["cwa"] = pr.sbuf("w_cwa", [128, 64], F32)
    w["cba"] = pr.sbuf("w_cba", [128, 16], F32)
    w["onesb"] = pr.sbuf("w_onesb", [128, 1], BF16)
    w["ones1x128f"] = pr.sbuf("w_ones1f", [1, 128], F32)
    pr.memset("V", w["onesb"][:], 1.0)
    w["Winb"] = pr.sbuf("w_Winb", [128, 8 * DH], BF16)
    wtmp = pr.sbuf("s1_xch", [128, 8 * TCH], F32)
    for k in ("normw", "cwa", "cba", "ones1x128f"):
        pr.dma("Y", w[k][:], io[k])
    WinT = io["WinT"].rearrange("(g p) c -> p g c", p=128)
    Winb3 = w["Winb"][:].rearrange("p (g c) -> p g c", g=8)
    for g in range(8):
        pr.dma("Y", wtmp[:, :DH], WinT[:, g, :])
        pr.ts("V", Winb3[:, g, :], wtmp[:, :DH], w["normw"][:, g:g + 1], None, OP.mult)
    return w


def emit_weights_1b(pr, io):
    w = {}
    w["bdt"] = pr.sbuf("w_bdt", [128, 16], F32)
    w["Wdtb"] = pr.sbuf("w_Wdtb", [128, 16 * DI], BF16)
    w["WBb"] = pr.sbuf("w_WBb", [128, 16 * NST], BF16)
    w["WCb"] = pr.sbuf("w_WCb", [128, 16 * NST], BF16)
    wtmp = pr.sbuf("s1_wtmp", [128, DI + 2 * NST], F32)
    pr.dma("Y", w["bdt"][:], io["bdt"])
    WdtT = io["WdtT"].rearrange("(g p) c -> p g c", p=128)
    Wdtb3 = w["Wdtb"][:].rearrange("p (g c) -> p g c", g=16)
    WBT = io["WBT"].rearrange("(g p) c -> p g c", p=128)
    WCT = io["WCT"].rearrange("(g p) c -> p g c", p=128)
    WBb3 = w["WBb"][:].rearrange("p (g c) -> p g c", g=16)
    WCb3 = w["WCb"][:].rearrange("p (g c) -> p g c", g=16)
    for g in range(16):
        pr.dma("Y", wtmp[:, :DI], WdtT[:, g, :])
        pr.copy("V", Wdtb3[:, g, :], wtmp[:, :DI])
        pr.dma("Y", wtmp[:, DI:DI + NST], WBT[:, g, :])
        pr.copy("V", WBb3[:, g, :], wtmp[:, DI:DI + NST])
        pr.dma("Y", wtmp[:, DI + NST:DI + 2 * NST], WCT[:, g, :])
        pr.copy("V", WCb3[:, g, :], wtmp[:, DI + NST:DI + 2 * NST])
    return w


def emit_stage1a_span(pr, w, io, si):
    NCH = SPAN // TCH
    xr = io[f"x_{si}"].rearrange("(g p) t -> p g t", p=128)
    xh = io[f"xhalo_{si}"].rearrange("(g p) t -> p g t", p=128)
    xcbr = io[f"xcb_{si}"].rearrange("(g p) t -> p g t", p=128)
    sgr = io[f"sg_{si}"].rearrange("(g p) t -> p g t", p=128)

    xch = pr.sbuf("s1_xch", [128, 8 * TCH], F32)
    xbch = pr.sbuf("s1_xbch", [128, 8 * TCH], BF16)
    bsc = pr.sbuf("s1_bsc", [128, 8 * TCH], BF16)
    ps_sum = pr.psum("s1_ps_sum", [1, TCH], F32)
    ps_sq = pr.psum("s1_ps_sq", [1, TCH], F32)
    rows = pr.sbuf("s1_rows", [1, 7 * TCH], F32)
    r_m = rows[:][:, 0 * TCH:1 * TCH]
    r_v = rows[:][:, 1 * TCH:2 * TCH]
    r_t = rows[:][:, 2 * TCH:3 * TCH]
    r_q = rows[:][:, 3 * TCH:4 * TCH]
    r_r1 = rows[:][:, 4 * TCH:5 * TCH]
    r_r = rows[:][:, 5 * TCH:6 * TCH]
    r_mr = rows[:][:, 6 * TCH:7 * TCH]
    ps_bc1 = pr.psum("s1_ps_bc1", [128, TCH], F32)
    ps_bc2 = pr.psum("s1_ps_bc2", [128, TCH], F32)
    ps_mm = pr.psum("s1_ps_mm", [128, TCH], F32)
    scr32 = pr.sbuf("s1_scr32", [128, 16 * TCH], F32)
    xbr = pr.sbuf("s1_xbr", [128, 16 * (TCH + 3)], F32)
    xcbt = pr.sbuf("s1_xcbt", [128, 16 * TCH], BF16)

    Winb3 = w["Winb"][:].rearrange("p (g c) -> p g c", g=8)
    cwa3 = w["cwa"][:].rearrange("p (g k) -> p g k", g=16)
    xbr3 = xbr[:].rearrange("p (g t) -> p g t", g=16)
    xcf3 = scr32[:].rearrange("p (g t) -> p g t", g=16)
    xcbt3 = xcbt[:].rearrange("p (g t) -> p g t", g=16)

    def xn_of(ci):
        T = TCH if ci >= 0 else 128
        src = xr[:, :, ci * TCH:ci * TCH + T] if ci >= 0 else xh[:, :, :]
        xch_v = xch[:, :8 * T].rearrange("p (g t) -> p g t", g=8)
        xb_v = xbch[:, :8 * T].rearrange("p (g t) -> p g t", g=8)
        sq_v = bsc[:, :8 * T].rearrange("p (g t) -> p g t", g=8)
        pr.dma("Y", xch_v, src)
        pr.copy("V", xbch[:, :8 * T], xch[:, :8 * T])
        pr.act("S", bsc[:, :8 * T], xbch[:, :8 * T], AF.Square)
        for g in range(8):
            pr.mm(ps_sum[:1, :T], w["onesb"][:], xb_v[:, g, :], start=(g == 0), stop=(g == 7))
        for g in range(8):
            pr.mm(ps_sq[:1, :T], w["onesb"][:], sq_v[:, g, :], start=(g == 0), stop=(g == 7))
        pr.ts("V", r_m[:, :T], ps_sum[:1, :T], 1.0 / D, None, OP.mult)
        pr.ts("V", r_v[:, :T], ps_sq[:1, :T], 1.0 / D, None, OP.mult)
        pr.tt("V", r_t[:, :T], r_m[:, :T], r_m[:, :T], OP.mult)
        pr.tt("V", r_v[:, :T], r_v[:, :T], r_t[:, :T], OP.subtract)
        pr.act("S", r_q[:, :T], r_v[:, :T], AF.Ln, bias=float(LN_EPS))
        pr.act("S", r_r1[:, :T], r_q[:, :T], AF.Exp, scale=-0.5)
        pr.act("S", r_t[:, :T], r_v[:, :T], AF.Ln, scale=float(1.0 + RMS_EPS),
               bias=float(RMS_EPS * LN_EPS))
        pr.act("S", r_r[:, :T], r_t[:, :T], AF.Exp, scale=-0.5)
        pr.tt("V", r_mr[:, :T], r_m[:, :T], r_r[:, :T], OP.mult)
        if ci >= 0:
            st = io[f"stat_{si}"]
            pr.dma("P", st[0:1, ci * TCH:ci * TCH + T], r_m[:, :T])
            pr.dma("P", st[1:2, ci * TCH:ci * TCH + T], r_r1[:, :T])
        pr.mm(ps_bc1[:, :T], w["ones1x128f"][:], r_r[:, :T], start=True, stop=True)
        pr.mm(ps_bc2[:, :T], w["ones1x128f"][:], r_mr[:, :T], start=True, stop=True)
        t1_v = scr32[:, :8 * T].rearrange("p (g t) -> p g t", g=8)
        pr.tt("V", t1_v, xch_v, bc8(ps_bc1[:, :T], 8), OP.mult)
        pr.tt("V", bsc[:, :8 * T].rearrange("p (g t) -> p g t", g=8), t1_v,
              bc8(ps_bc2[:, :T], 8), OP.subtract)
        return T

    for ci in range(-1, NCH):
        T = xn_of(ci)
        xnb3 = bsc[:, :8 * T].rearrange("p (g t) -> p g t", g=8)
        if ci < 0:
            for mc in range(16):
                for kc in range(8):
                    pr.mm(ps_mm[:, :T], Winb3[:, kc, mc * 128:(mc + 1) * 128], xnb3[:, kc, :],
                          start=(kc == 0), stop=(kc == 7))
                pr.copy("V", xbr3[:, mc, 0:3], ps_mm[:, T - 3:T])
            continue
        for mc in range(32):
            for kc in range(8):
                pr.mm(ps_mm[:, :T], Winb3[:, kc, mc * 128:(mc + 1) * 128], xnb3[:, kc, :],
                      start=(kc == 0), stop=(kc == 7))
            if mc < 16:
                pr.copy("V", xbr3[:, mc, 3:3 + T], ps_mm[:, :T])
            else:
                pr.act("S", xcbt3[:, mc - 16, :T], ps_mm[:, :T], AF.Silu)
        pr.dma("P", sgr[:, :, ci * TCH:(ci + 1) * TCH], xcbt3)
        for g in range(16):
            pr.ts("V", xcf3[:, g, :], xbr3[:, g, 3:3 + TCH], cwa3[:, g, 3:4],
                  w["cba"][:, g:g + 1], OP.mult, OP.add)
        for k in (2, 1, 0):
            for g in range(16):
                pr.stt("V", xcf3[:, g, :], xbr3[:, g, k:k + TCH], cwa3[:, g, k:k + 1],
                       xcf3[:, g, :], OP.mult, OP.add)
        pr.act("S", scr32[:], scr32[:], AF.Silu)
        pr.copy("V", xcbt[:], scr32[:])
        pr.dma("P", xcbr[:, :, ci * TCH:(ci + 1) * TCH], xcbt3)
        for g in range(16):
            pr.copy("P", xbr3[:, g, 0:3], xbr3[:, g, TCH:TCH + 3])


def emit_stage1b_span(pr, w, io, si):
    NCH = SPAN // TCH
    dtr = io[f"dt_{si}"].rearrange("(g p) t -> p g t", p=128)
    xcbr = io[f"xcb_{si}"].rearrange("(g p) t -> p g t", p=128)

    xcbt = pr.sbuf("s1_xcbt", [128, 16 * TCH], BF16)
    scr = pr.sbuf("s1_scr", [128, TCH], F32)
    spa = pr.sbuf("s1_spa", [128, 16 * TCH], F32)
    spr = pr.sbuf("s1_spr", [128, 16 * TCH], F32)
    dtst = pr.sbuf("s1_dtst", [128, 16 * TCH], F32)
    bst = pr.sbuf("s1_bst", [16, 2 * TCH], F32)
    ps_mm = pr.psum("s1_ps_mm", [128, TCH], F32)
    ps_bcn = pr.psum("s1_ps_bcn", [16, TCH], F32)
    ps_ccn = pr.psum("s1_ps_ccn", [16, TCH], F32)

    Wdtb3 = w["Wdtb"][:].rearrange("p (g c) -> p g c", g=16)
    WBb3 = w["WBb"][:].rearrange("p (g c) -> p g c", g=16)
    WCb3 = w["WCb"][:].rearrange("p (g c) -> p g c", g=16)
    xcbt3 = xcbt[:].rearrange("p (g t) -> p g t", g=16)
    spa3 = spa[:].rearrange("p (g t) -> p g t", g=16)
    spr3 = spr[:].rearrange("p (g t) -> p g t", g=16)

    for ci in range(NCH):
        pr.dma("Y", xcbt3, xcbr[:, :, ci * TCH:(ci + 1) * TCH])
        for mc in range(16):
            for kc in range(16):
                pr.mm(ps_mm[:, :TCH], Wdtb3[:, kc, mc * 128:(mc + 1) * 128],
                      xcbt3[:, kc, :], start=(kc == 0), stop=(kc == 15))
            pr.ts("V", scr[:, :TCH], ps_mm[:, :TCH], w["bdt"][:, mc:mc + 1], None, OP.add)
            pr.ts("V", spr3[:, mc, :], scr[:, :TCH], 0.0, None, OP.max)
            pr.stt("V", spa3[:, mc, :], spr3[:, mc, :], -2.0, scr[:, :TCH],
                   OP.mult, OP.add)
        pr.act("S", spa[:], spa[:], AF.Exp)
        pr.act("S", spa[:], spa[:], AF.Ln, bias=1.0)
        pr.tt("V", dtst[:], spa[:], spr[:], OP.add)
        pr.act("S", dtst[:], dtst[:], AF.Tanh, scale=float(_SC_DT_S1),
               bias=float(-_SC_DT_C * _SC_DT_S1))
        pr.ts("V", dtst[:], dtst[:], float(_SC_DT_S2), float(_SC_DT_C), OP.mult, OP.add)
        pr.dma("P", dtr[:, :, ci * TCH:(ci + 1) * TCH],
               dtst[:].rearrange("p (g t) -> p g t", g=16))
        for kc in range(16):
            pr.mm(ps_bcn[:, :TCH], WBb3[:, kc, :], xcbt3[:, kc, :],
                  start=(kc == 0), stop=(kc == 15))
        for kc in range(16):
            pr.mm(ps_ccn[:, :TCH], WCb3[:, kc, :], xcbt3[:, kc, :],
                  start=(kc == 0), stop=(kc == 15))
        pr.copy("V", bst[:, :TCH], ps_bcn[:, :TCH])
        pr.copy("V", bst[:, TCH:2 * TCH], ps_ccn[:, :TCH])
        pr.dma("P", io[f"Brow_{si}"][:, ci * TCH:(ci + 1) * TCH], bst[:, :TCH])
        pr.dma("P", io[f"Crow_{si}"][:, ci * TCH:(ci + 1) * TCH], bst[:, TCH:2 * TCH])


def _declare_stage1_io(nc, which):
    io = {}
    if which == "a":
        io["WinT"] = declare_in(nc, "WinT", [D, DH])
        io["normw"] = declare_in(nc, "normw", [128, 8])
        io["cwa"] = declare_in(nc, "cwa", [128, 64])
        io["cba"] = declare_in(nc, "cba", [128, 16])
        io["ones1x128f"] = declare_in(nc, "ones1x128f", [1, 128])
    else:
        io["WdtT"] = declare_in(nc, "WdtT", [DI, DI])
        io["WBT"] = declare_in(nc, "WBT", [DI, NST])
        io["WCT"] = declare_in(nc, "WCT", [DI, NST])
        io["bdt"] = declare_in(nc, "bdt", [128, 16])
    for si in range(2):
        if which == "a":
            io[f"x_{si}"] = declare_in(nc, f"x_{si}", [D, SPAN])
            io[f"xhalo_{si}"] = declare_in(nc, f"xhalo_{si}", [D, 128])
            io[f"xcb_{si}"] = declare_out(nc, f"xcb_{si}", [DI, SPAN // 2], F32).bitcast(BF16)
            io[f"sg_{si}"] = declare_out(nc, f"sg_{si}", [DI, SPAN // 2], F32).bitcast(BF16)
            io[f"stat_{si}"] = declare_out(nc, f"stat_{si}", [2, SPAN])
        else:
            io[f"xcb_{si}"] = declare_in(nc, f"xcb_{si}", [DI, SPAN // 2], F32).bitcast(BF16)
            io[f"dt_{si}"] = declare_out(nc, f"dt_{si}", [DI, SPAN])
            io[f"Brow_{si}"] = declare_out(nc, f"Brow_{si}", [NST, SPAN])
            io[f"Crow_{si}"] = declare_out(nc, f"Crow_{si}", [NST, SPAN])
    return io


def build_launch1a():
    nc = bass.Bass("TRN2", target_bir_lowering=False)
    pr = Prog(nc)
    io = _declare_stage1_io(nc, "a")
    w = emit_weights_1a(pr, io)
    for si in range(2):
        emit_stage1a_span(pr, w, io, si)
    pr.fence_all("P")
    pr.close()
    return nc


def build_launch1b():
    nc = bass.Bass("TRN2", target_bir_lowering=False)
    pr = Prog(nc)
    io = _declare_stage1_io(nc, "b")
    w = emit_weights_1b(pr, io)
    for si in range(2):
        emit_stage1b_span(pr, w, io, si)
    pr.fence_all("P")
    pr.close()
    return nc



PW = 2064
NE = 206


def span_shift(s):
    t0 = SPAN * s
    o = (10 - (t0 % 10)) % 10
    return 10 - o


def span_mask(s):
    s0 = span_shift(s)
    m = np.ones(NE, np.float32)
    for e in range(NE):
        c = 10 * (e + 1) - s0
        if 0 <= c < SPAN:
            m[e] = 0.0
    return m


def emit_scan_consts(pr, io):
    w = {}
    w["Alog"] = pr.sbuf("sc_Alog", [128, 256], F32)
    w["A16"] = pr.sbuf("sc_A16", [128, 256], F32)
    w["onesf"] = pr.sbuf("sc_onesf", [128, 1], F32)
    w["ones1x128f"] = pr.sbuf("w_ones1f", [1, 128], F32)
    pr.memset("V", w["onesf"][:], 1.0)
    pr.dma("Y", w["Alog"][:], io["Alog"])
    pr.dma("Y", w["ones1x128f"][:], io["ones1x128f"])
    pr.act("S", w["A16"][:], w["Alog"][:], AF.Tanh, scale=0.1)
    pr.act("S", w["A16"][:], w["A16"][:], AF.Exp, scale=10.0)
    pr.ts("V", w["A16"][:], w["A16"][:], float(-1.0 / 16.0), None, OP.mult)
    return w


def _S0of(si):
    def f(r):
        return span_shift(_span_of_core(r)[si][1])
    return f


def _prestage_pads(pr, io, si, specs):
    S0 = _S0of(si)
    for dst, srckey, ztag in specs:
        srcv = io[srckey]
        z = io[ztag]
        pr.dma_arms("Y", lambda r: (dst[:, S0(r):S0(r) + SPAN], srcv))
        pr.dma_arms("Y", lambda r: (dst[:, 0:S0(r)], z[:, 0:S0(r)]))
        pr.dma_arms("Y", lambda r: (dst[:, S0(r) + SPAN:PW], z[:, 0:PW - SPAN - S0(r)]))


def emit_pass1_span(pr, w, io, si, s_global, fused=False):
    s0 = span_shift(s_global) if not fused else None
    if fused:
        _prestage_pads(pr, io, si, [(io[f"dtp_{si}"], f"dt_{si}", "zDI"),
                                    (io[f"xcbp_{si}"], f"xcb_{si}", "zDIb"),
                                    (io[f"Bp_{si}"], f"Brow_{si}", "zNST")])
        dtr = io[f"dtp_{si}"].rearrange("(g p) t -> p g t", p=128)
        xcbr = io[f"xcbp_{si}"].rearrange("(g p) t -> p g t", p=128)
    else:
        dtr = io[f"dt_{si}"].rearrange("(g p) t -> p g t", p=128)
        xcbr = io[f"xcb_{si}"].rearrange("(g p) t -> p g t", p=128)
    PQ = io[f"PQ_{si}"].rearrange("w (p dg n e) -> w p dg n e", p=128, dg=16, n=16)

    NBLK = 4
    dt_dg = pr.sbuf("p1_dt", [128, PW], F32)
    xcf_dg = pr.sbuf("p1_xcf", [128, PW], F32)
    wt_dg = pr.sbuf("p1_wt", [128, PW], F32)
    xcb_dg = pr.sbuf("p1_xcb", [128, PW], BF16)
    Bbc = pr.sbuf("p1_Bbc", [128, NBLK * PW], F32)
    dA = pr.sbuf("p1_dA", [128, PW], F32)
    uu = pr.sbuf("p1_u", [128, PW], F32)
    qc = pr.sbuf("p1_qc", [128, PW], F32)
    lam = pr.sbuf("p1_lam", [128, NE], F32)
    pst = pr.sbuf("p1_pst", [128, 16 * NE], F32)
    qst = pr.sbuf("p1_qst", [128, 16 * NE], F32)
    qce = pr.sbuf("p1_qce", [128, NE + 1], F32)

    Bbc3 = Bbc[:].rearrange("p (nb t) -> p nb t", nb=NBLK)
    pst3 = pst[:].rearrange("p (k e) -> p k e", k=16)
    qst3 = qst[:].rearrange("p (k e) -> p k e", k=16)

    pr.memset("V", dt_dg[:], 0.0)
    pr.memset("V", xcb_dg[:], 0.0)
    pr.memset("V", Bbc[:], 0.0)

    for nb in range(16 // NBLK):
        for j in range(NBLK):
            n = nb * NBLK + j
            if fused:
                pr.dma("Y", Bbc3[:, j, :], io[f"Bp_{si}"][n:n + 1, :].broadcast_to([128, PW]))
            else:
                pr.dma("Y", Bbc3[:, j, s0:s0 + SPAN],
                       io[f"Brow_{si}"][n:n + 1, :].broadcast_to([128, SPAN]))
        for dg in range(16):
            if fused:
                pr.dma("Y", dt_dg[:], dtr[:, dg, :])
                pr.dma("Y", xcb_dg[:], xcbr[:, dg, :])
            else:
                pr.dma("Y", dt_dg[:, s0:s0 + SPAN], dtr[:, dg, :])
                pr.dma("Y", xcb_dg[:, s0:s0 + SPAN], xcbr[:, dg, :])
            pr.copy("V", xcf_dg[:], xcb_dg[:])
            pr.tt("V", wt_dg[:], dt_dg[:], xcf_dg[:], OP.mult)
            for j in range(NBLK):
                n = nb * NBLK + j
                pair = dg * 16 + n
                k16 = (dg % 4) * NBLK + j
                acol = w["A16"][:, pair:pair + 1]
                pr.act("S", dA[:], dt_dg[:], AF.Tanh, scale=acol)
                mid = dA[:, 1:1 + NE * 10].rearrange("p (k t) -> p k t", t=10)
                pr.red("V", lam[:], mid, OP.add)
                pr.act("S", dA[:], dA[:], AF.Exp, scale=16.0)
                pr.tt("V", uu[:], wt_dg[:], Bbc3[:, j, :], OP.mult)
                pr.scan("V", qc[:], dA[:], uu[:], 0.0, OP.mult, OP.add)
                pr.act("S", pst3[:, k16, :], lam[:], AF.Exp, scale=16.0)
                pr.memset("V", qce[:, 0:1], 0.0)
                pr.copy("V", qce[:, 1:1 + NE], strided_cols(qc, 10, 10, NE))
                pr.tt("V", qst3[:, k16, :], pst3[:, k16, :], qce[:, 0:NE], OP.mult)
                pr.tt("V", qst3[:, k16, :], qce[:, 1:1 + NE], qst3[:, k16, :], OP.subtract)
            if dg % 4 == 3:
                dg0 = dg - 3
                nb0 = nb * NBLK
                pr.dma("Y", PQ[0, :, dg0:dg0 + 4, nb0:nb0 + NBLK, :],
                       pst3.rearrange("p (dgp nj) e -> p dgp nj e", dgp=4))
                pr.dma("Y", PQ[1, :, dg0:dg0 + 4, nb0:nb0 + NBLK, :],
                       qst3.rearrange("p (dgp nj) e -> p dgp nj e", dgp=4))


def _declare_pass1_io(nc):
    io = {}
    io["Alog"] = declare_in(nc, "Alog", [128, 256])
    io["ones1x128f"] = declare_in(nc, "ones1x128f", [1, 128])
    for si in range(2):
        io[f"dt_{si}"] = declare_in(nc, f"dt_{si}", [DI, SPAN])
        io[f"xcb_{si}"] = declare_in(nc, f"xcb_{si}", [DI, SPAN // 2], F32).bitcast(BF16)
        io[f"Brow_{si}"] = declare_in(nc, f"Brow_{si}", [NST, SPAN])
        io[f"PQ_{si}"] = declare_out(nc, f"PQ_{si}", [2, 128 * 256 * NE])
    return io


def build_launch2(core):
    nc = bass.Bass("TRN2", target_bir_lowering=False)
    pr = Prog(nc)
    io = _declare_pass1_io(nc)
    w = emit_scan_consts(pr, io)
    for si, (b, s) in enumerate(_span_of_core(core)):
        emit_pass1_span(pr, w, io, si, s)
    pr.fence_all("P")
    pr.close()
    return nc



def emit_chain(pr, io, nspans=NSPANS):
    onesf = pr.sbuf("ch_onesf", [128, 1], F32)
    ones1x128f = pr.sbuf("w_ones1f", [1, 128], F32)
    pr.memset("V", onesf[:], 1.0)
    pr.dma("Y", ones1x128f[:], io["ones1x128f"])

    EB = 16
    state = pr.sbuf("ch_state", [128, 256], F32)
    hpre = pr.sbuf("ch_hpre", [128, 256], F32)
    sqscr = pr.sbuf("ch_sq", [128, 256], F32)
    partials = pr.sbuf("ch_part", [128, 1], F32)
    pblk = pr.sbuf("ch_pblk", [128, 256 * EB], F32)
    qblk = pr.sbuf("ch_qblk", [128, 256 * EB], F32)
    rows = pr.sbuf("ch_rows", [1, 4], F32)
    facrow = pr.sbuf("ch_facrow", [1, NE], F32)
    maskrow = pr.sbuf("ch_mask", [1, NE], F32)
    faccol = pr.sbuf("ch_faccol", [128, 1], F32)
    ps_n = pr.psum("ch_ps_n", [1, 1], F32)
    ps_f = pr.psum("ch_ps_f", [128, 1], F32)

    pr.memset("V", state[:], 0.0)
    pb3 = pblk[:].rearrange("p (pr e) -> p pr e", e=EB)
    qb3 = qblk[:].rearrange("p (pr e) -> p pr e", e=EB)

    for s in range(nspans):
        PQ = io[f"PQc_{s}"].rearrange("w (p pr e) -> w p pr e", p=128, pr=256)
        pr.dma("P", io[f"stent_{s}"], state[:])
        pr.dma("Y", maskrow[:], io[f"mask_{s}"])
        for eb in range((NE + EB - 1) // EB):
            e0 = eb * EB
            ne = min(EB, NE - e0)
            pr.dma("Y", pb3[:, :, 0:ne], PQ[0, :, :, e0:e0 + ne])
            pr.dma("Y", qb3[:, :, 0:ne], PQ[1, :, :, e0:e0 + ne])
            for el in range(ne):
                e = e0 + el
                Pv = pb3[:, :, el]
                Qv = qb3[:, :, el]
                pr.tt("V", hpre[:], Pv, state[:], OP.mult)
                pr.tt("V", hpre[:], hpre[:], Qv, OP.add)
                pr.act("S", sqscr[:], hpre[:], AF.Square, accum_out=partials[:])
                pr.mm(ps_n[:1, :1], onesf[:], partials[:], start=True, stop=True)
                pr.act("S", rows[:, 0:1], ps_n[:1, :1], AF.Sqrt)
                pr.ts("V", rows[:, 1:2], rows[:, 0:1], 1e-8, None, OP.add)
                pr.recip("V", rows[:, 2:3], rows[:, 1:2])
                pr.ts("V", rows[:, 3:4], rows[:, 2:3], 20.0, 1.0, OP.mult, OP.min)
                pr.tt("V", rows[:, 3:4], rows[:, 3:4], maskrow[:, e:e + 1], OP.max)
                pr.copy("V", facrow[:, e:e + 1], rows[:, 3:4])
                pr.mm(ps_f[:, :1], ones1x128f[:], rows[:, 3:4], start=True, stop=True)
                pr.copy("V", faccol[:], ps_f[:, :1])
                pr.ts("V", state[:], hpre[:], faccol[:], None, OP.mult)
        pr.dma("P", io[f"facs_{s}"], facrow[:])


def _declare_chain_io(nc, nspans=NSPANS):
    io = {"ones1x128f": declare_in(nc, "ones1x128f", [1, 128])}
    for s in range(nspans):
        io[f"PQc_{s}"] = declare_in(nc, f"PQc_{s}", [2, 128 * 256 * NE])
        io[f"mask_{s}"] = declare_in(nc, f"mask_{s}", [1, NE])
        io[f"stent_{s}"] = declare_out(nc, f"stent_{s}", [128, 256])
        io[f"facs_{s}"] = declare_out(nc, f"facs_{s}", [1, NE])
    return io


def build_launch3():
    nc = bass.Bass("TRN2", target_bir_lowering=False)
    pr = Prog(nc)
    io = _declare_chain_io(nc)
    emit_chain(pr, io)
    pr.fence_all("P")
    pr.close()
    return nc



def emit_pass2_weights(pr, io):
    w = {}
    w["Alog"] = pr.sbuf("sc_Alog", [128, 256], F32)
    w["A16"] = pr.sbuf("sc_A16", [128, 256], F32)
    w["onesf"] = pr.sbuf("sc_onesf", [128, 1], F32)
    w["ones1x128f"] = pr.sbuf("w_ones1f", [1, 128], F32)
    w["Dp"] = pr.sbuf("p2_Dp", [128, 16], F32)
    w["Woutb"] = pr.sbuf("p2_Woutb", [128, 16 * D], BF16)
    w["scol"] = pr.sbuf("p2_scol", [128, 1], F32)
    wtmp = pr.sbuf("p2_wtmp", [128, D], F32)
    srow = pr.sbuf("p2_srow", [1, 1], F32)
    ps_b = pr.psum("p2_ps_b", [128, 1], F32)
    pr.memset("V", w["onesf"][:], 1.0)
    pr.dma("Y", w["Alog"][:], io["Alog"])
    pr.dma("Y", w["ones1x128f"][:], io["ones1x128f"])
    pr.dma("Y", w["Dp"][:], io["Dpar"])
    pr.act("S", w["A16"][:], w["Alog"][:], AF.Tanh, scale=0.1)
    pr.act("S", w["A16"][:], w["A16"][:], AF.Exp, scale=10.0)
    pr.ts("V", w["A16"][:], w["A16"][:], float(-1.0 / 16.0), None, OP.mult)
    WoutT = io["WoutT"].rearrange("(g p) c -> p g c", p=128)
    Wb3 = w["Woutb"][:].rearrange("p (g c) -> p g c", g=16)
    for g in range(16):
        pr.dma("Y", wtmp[:], WoutT[:, g, :])
        pr.copy("V", Wb3[:, g, :], wtmp[:])
    pr.dma("Y", srow[:], io["rscal"])
    pr.act("S", srow[:], srow[:], AF.Tanh, scale=0.5)
    pr.ts("V", srow[:], srow[:], 0.25, 0.25, OP.mult, OP.add)
    pr.mm(ps_b[:, :1], w["ones1x128f"][:], srow[:], start=True, stop=True)
    pr.copy("V", w["scol"][:], ps_b[:, :1])
    return w


def emit_pass2_span(pr, w, io, si, s_global, fused=False):
    s0 = span_shift(s_global) if not fused else None
    if fused:
        _prestage_pads(pr, io, si, [(io[f"dtp_{si}"], f"dt_{si}", "zDI"),
                                    (io[f"xcbp_{si}"], f"xcb_{si}", "zDIb"),
                                    (io[f"sgp_{si}"], f"sg_{si}", "zDIb"),
                                    (io[f"Bp_{si}"], f"Brow_{si}", "zNST"),
                                    (io[f"Cp_{si}"], f"Crow_{si}", "zNST")])
        dtr = io[f"dtp_{si}"].rearrange("(g p) t -> p g t", p=128)
        xcbr = io[f"xcbp_{si}"].rearrange("(g p) t -> p g t", p=128)
        sgr = io[f"sgp_{si}"].rearrange("(g p) t -> p g t", p=128)
        ysgr = io[f"ysgp_{si}"].rearrange("(g p) t -> p g t", p=128)
    else:
        dtr = io[f"dt_{si}"].rearrange("(g p) t -> p g t", p=128)
        xcbr = io[f"xcb_{si}"].rearrange("(g p) t -> p g t", p=128)
        sgr = io[f"sg_{si}"].rearrange("(g p) t -> p g t", p=128)
        ysgr = io[f"ysg_{si}"].rearrange("(g p) t -> p g t", p=128)
    xr = io[f"x_{si}"].rearrange("(g p) t -> p g t", p=128)
    x2r = io[f"x2_{si}"].rearrange("(g p) t -> p g t", p=128)

    NBLK = 4
    dt_dg = pr.sbuf("p2_dt", [128, PW], F32)
    xcf_dg = pr.sbuf("p2_xcf", [128, PW], F32)
    wt_dg = pr.sbuf("p2_wt", [128, PW], F32)
    xcb_dg = pr.sbuf("p2_xcb", [128, PW], BF16)
    Bbc = pr.sbuf("p2_Bbc", [128, NBLK * PW], F32)
    Cbc = pr.sbuf("p2_Cbc", [128, NBLK * PW], F32)
    dA = pr.sbuf("p2_dA", [128, PW], F32)
    uu = pr.sbuf("p2_u", [128, PW], F32)
    hh = pr.sbuf("p2_h", [128, PW], F32)
    acc = pr.sbuf("p2_acc", [128, PW], F32)
    ent = pr.sbuf("p2_ent", [128, 256], F32)
    facr = pr.sbuf("p2_facr", [1, NE], F32)
    facb = pr.sbuf("p2_facb", [128, NE], F32)
    sgf = pr.sbuf("p2_sgf", [128, PW], F32)
    ysgb = pr.sbuf("p2_ysgb", [128, PW], BF16)
    ps_fb = pr.psum("p2_ps_fb", [128, NE], F32)

    Bbc3 = Bbc[:].rearrange("p (nb t) -> p nb t", nb=NBLK)
    Cbc3 = Cbc[:].rearrange("p (nb t) -> p nb t", nb=NBLK)

    pr.memset("V", dt_dg[:], 0.0)
    pr.memset("V", xcb_dg[:], 0.0)
    pr.memset("V", Bbc[:], 0.0)
    pr.memset("V", Cbc[:], 0.0)

    pr.dma("Y", ent[:], io[f"stent_{si}"])
    pr.dma("Y", facr[:], io[f"facs_{si}"])
    pr.mm(ps_fb[:, :], w["ones1x128f"][:], facr[:], start=True, stop=True)
    pr.copy("V", facb[:], ps_fb[:, :])
    facb3 = facb[:].unsqueeze(2)

    yacr = io[f"yac_{si}"].rearrange("(g p) t -> p g t", p=128)
    for nb in range(16 // NBLK):
        for j in range(NBLK):
            n = nb * NBLK + j
            if fused:
                pr.dma("Y", Bbc3[:, j, :], io[f"Bp_{si}"][n:n + 1, :].broadcast_to([128, PW]))
                pr.dma("Y", Cbc3[:, j, :], io[f"Cp_{si}"][n:n + 1, :].broadcast_to([128, PW]))
            else:
                pr.dma("Y", Bbc3[:, j, s0:s0 + SPAN],
                       io[f"Brow_{si}"][n:n + 1, :].broadcast_to([128, SPAN]))
                pr.dma("Y", Cbc3[:, j, s0:s0 + SPAN],
                       io[f"Crow_{si}"][n:n + 1, :].broadcast_to([128, SPAN]))
        for dg in range(16):
            if fused:
                pr.dma("Y", dt_dg[:], dtr[:, dg, :])
                pr.dma("Y", xcb_dg[:], xcbr[:, dg, :])
            else:
                pr.dma("Y", dt_dg[:, s0:s0 + SPAN], dtr[:, dg, :])
                pr.dma("Y", xcb_dg[:], xcbr[:, dg, :]) if False else                     pr.dma("Y", xcb_dg[:, s0:s0 + SPAN], xcbr[:, dg, :])
            pr.copy("V", xcf_dg[:], xcb_dg[:])
            pr.tt("V", wt_dg[:], dt_dg[:], xcf_dg[:], OP.mult)
            if nb == 0:
                pr.ts("V", acc[:], xcf_dg[:], w["Dp"][:, dg:dg + 1], None, OP.mult)
            else:
                if fused:
                    pr.dma("Y", acc[:], yacr[:, dg, :])
                else:
                    pr.dma("Y", acc[:, s0:s0 + SPAN], yacr[:, dg, :])
            for j in range(NBLK):
                n = nb * NBLK + j
                pair = dg * 16 + n
                acol = w["A16"][:, pair:pair + 1]
                pr.act("S", dA[:], dt_dg[:], AF.Tanh, scale=acol)
                pr.act("S", dA[:], dA[:], AF.Exp, scale=16.0)
                dAc = bass.AP(dA[:].tensor, dA[:].offset + 10,
                              [list(dA[:].ap[0]), [10, NE], [1, 1]])
                pr.tt("V", dAc, dAc, facb3, OP.mult)
                pr.tt("V", uu[:], wt_dg[:], Bbc3[:, j, :], OP.mult)
                uuc = bass.AP(uu[:].tensor, uu[:].offset + 10,
                              [list(uu[:].ap[0]), [10, NE], [1, 1]])
                pr.tt("V", uuc, uuc, facb3, OP.mult)
                pr.scan("V", hh[:], dA[:], uu[:], ent[:, pair:pair + 1], OP.mult, OP.add)
                pr.tt("V", hh[:], hh[:], Cbc3[:, j, :], OP.mult)
                pr.tt("V", acc[:], acc[:], hh[:], OP.add)
            if nb < 3:
                if fused:
                    pr.dma("P", yacr[:, dg, :], acc[:])
                else:
                    pr.dma("P", yacr[:, dg, :], acc[:, s0:s0 + SPAN])
            else:
                if fused:
                    pr.dma("Y", xcb_dg[:], sgr[:, dg, :])
                else:
                    pr.dma("Y", xcb_dg[:, s0:s0 + SPAN], sgr[:, dg, :])
                pr.copy("V", sgf[:], xcb_dg[:])
                pr.tt("V", ysgb[:], acc[:], sgf[:], OP.mult)
                if fused:
                    pr.dma("P", ysgr[:, dg, :], ysgb[:])
                else:
                    pr.dma("P", ysgr[:, dg, :], ysgb[:, s0:s0 + SPAN])

    ych = pr.sbuf("p2_ych", [128, 16 * TCH], BF16, alias="p2_Bbc")
    xch = pr.sbuf("p2_xch", [128, 8 * TCH], F32, alias="p2_Cbc")
    x2ch = pr.sbuf("p2_x2ch", [128, 8 * TCH], F32, alias="p2_h")
    strow = pr.sbuf("p2_strow", [1, 4 * TCH], F32, alias="p2_u")
    ps_o = pr.psum("p2_ps_o", [128, TCH], F32)
    ps_a = pr.psum("p2_ps_a", [128, TCH], F32)
    ps_c = pr.psum("p2_ps_c", [128, TCH], F32)
    Wb3 = w["Woutb"][:].rearrange("p (g c) -> p g c", g=16)
    ych3 = ych[:].rearrange("p (g t) -> p g t", g=16)
    xch3 = xch[:].rearrange("p (g t) -> p g t", g=8)
    x2ch3 = x2ch[:].rearrange("p (g t) -> p g t", g=8)
    st = io[f"stat_{si}"]
    S0f = _S0of(si)
    for ci in range(SPAN // TCH):
        tsl = slice(ci * TCH, (ci + 1) * TCH)
        if fused:
            pr.dma_arms("Y", lambda r: (ych3, ysgr[:, :, S0f(r) + ci * TCH:S0f(r) + (ci + 1) * TCH]))
        else:
            pr.dma("Y", ych3, ysgr[:, :, tsl])
        pr.dma("Y", xch3, xr[:, :, tsl])
        pr.dma("Y", strow[:, 0:TCH], st[0:1, tsl])
        pr.dma("Y", strow[:, TCH:2 * TCH], st[1:2, tsl])
        pr.ts("V", strow[:, 2 * TCH:3 * TCH], strow[:, TCH:2 * TCH], 1.0, None, OP.add)
        pr.tt("V", strow[:, 3 * TCH:4 * TCH], strow[:, 0:TCH], strow[:, TCH:2 * TCH], OP.mult)
        pr.mm(ps_a[:, :], w["ones1x128f"][:], strow[:, 2 * TCH:3 * TCH], start=True, stop=True)
        pr.mm(ps_c[:, :], w["ones1x128f"][:], strow[:, 3 * TCH:4 * TCH], start=True, stop=True)
        for mc in range(8):
            for kc in range(16):
                pr.mm(ps_o[:, :], Wb3[:, kc, mc * 128:(mc + 1) * 128], ych3[:, kc, :],
                      start=(kc == 0), stop=(kc == 15))
            pr.tt("V", x2ch3[:, mc, :], xch3[:, mc, :], ps_a[:, :], OP.mult)
            pr.tt("V", x2ch3[:, mc, :], x2ch3[:, mc, :], ps_c[:, :], OP.subtract)
            pr.stt("V", x2ch3[:, mc, :], ps_o[:, :], w["scol"][:], x2ch3[:, mc, :],
                   OP.mult, OP.add)
        pr.dma("P", x2r[:, :, tsl], x2ch3)


def _declare_pass2_io(nc):
    io = {}
    io["Alog"] = declare_in(nc, "Alog", [128, 256])
    io["ones1x128f"] = declare_in(nc, "ones1x128f", [1, 128])
    io["Dpar"] = declare_in(nc, "Dpar", [128, 16])
    io["WoutT"] = declare_in(nc, "WoutT", [DI, D])
    io["rscal"] = declare_in(nc, "rscal", [1, 1])
    for si in range(2):
        io[f"x_{si}"] = declare_in(nc, f"x_{si}", [D, SPAN])
        io[f"dt_{si}"] = declare_in(nc, f"dt_{si}", [DI, SPAN])
        io[f"xcb_{si}"] = declare_in(nc, f"xcb_{si}", [DI, SPAN // 2], F32).bitcast(BF16)
        io[f"sg_{si}"] = declare_in(nc, f"sg_{si}", [DI, SPAN // 2], F32).bitcast(BF16)
        io[f"Brow_{si}"] = declare_in(nc, f"Brow_{si}", [NST, SPAN])
        io[f"Crow_{si}"] = declare_in(nc, f"Crow_{si}", [NST, SPAN])
        io[f"stat_{si}"] = declare_in(nc, f"stat_{si}", [2, SPAN])
        io[f"stent_{si}"] = declare_in(nc, f"stent_{si}", [128, 256])
        io[f"facs_{si}"] = declare_in(nc, f"facs_{si}", [1, NE])
        io[f"yac_{si}"] = nc.dram_tensor(f"yac_{si}", [DI, SPAN], F32).ap()
        io[f"ysg_{si}"] = declare_out(nc, f"ysg_{si}", [DI, SPAN // 2], F32).bitcast(BF16)
        io[f"x2_{si}"] = declare_out(nc, f"x2_{si}", [D, SPAN])
    return io


def build_launch4(core):
    nc = bass.Bass("TRN2", target_bir_lowering=False)
    pr = Prog(nc)
    io = _declare_pass2_io(nc)
    w = emit_pass2_weights(pr, io)
    for si, (b, s) in enumerate(_span_of_core(core)):
        emit_pass2_span(pr, w, io, si, s)
    pr.fence_all("P")
    pr.close()
    return nc



XE = SPAN + 2


def emit_ternary_prep(pr, io, which, nslab, width, wtmp_alias=None, wi8_alias=None):
    wT = io[f"w{which}T"].rearrange("(g p) c -> p g c", p=128)
    wqb = pr.sbuf(f"fw_wq{which}", [128, nslab * width], BF16)
    wtmp = pr.sbuf("fw_wtmp", [128, 4096], F32, alias=wtmp_alias)
    wi8 = pr.sbuf("fw_wi8", [128, 4096], I8, alias=wi8_alias)
    pcol = pr.sbuf("fw_pcol", [128, 1], F32)
    acol = pr.sbuf("fw_acol", [128, 1], F32)
    srow = pr.sbuf(f"fw_srow{which}", [1, 2], F32)
    swcol = pr.sbuf("fw_swcol", [128, 1], F32)
    ps1 = pr.psum("fw_ps1", [1, 1], F32)
    ps_b = pr.psum("fw_ps_b", [128, 1], F32)
    onesf = pr.sbuf("sc_onesf", [128, 1], F32)
    ones1x128f = pr.sbuf("w_ones1f", [1, 128], F32)
    pr.memset("V", onesf[:], 1.0)
    wq3 = wqb[:].rearrange("p (g c) -> p g c", g=nslab)
    pr.memset("V", pcol[:], 0.0)
    for g in range(nslab):
        pr.dma("Y", wtmp[:, :width], wT[:, g, :])
        pr.act("S", wtmp[:, :width], wtmp[:, :width], AF.Abs, accum_out=acol[:])
        pr.tt("V", pcol[:], pcol[:], acol[:], OP.add)
    pr.mm(ps1[:1, :1], onesf[:], pcol[:], start=True, stop=True)
    pr.ts("V", srow[:, 0:1], ps1[:1, :1], float(1.0 / (nslab * 128 * width)), 1e-5,
          OP.mult, OP.max)
    pr.recip("V", srow[:, 1:2], srow[:, 0:1])
    pr.mm(ps_b[:, :1], ones1x128f[:], srow[:, 1:2], start=True, stop=True)
    pr.copy("V", swcol[:], ps_b[:, :1])
    for g in range(nslab):
        pr.dma("Y", wtmp[:, :width], wT[:, g, :])
        pr.ts("V", wtmp[:, :width], wtmp[:, :width], swcol[:], None, OP.mult)
        pr.copy("V", wi8[:, :width], wtmp[:, :width])
        pr.copy("V", wq3[:, g, :], wi8[:, :width])
        pr.ts("V", wq3[:, g, :], wq3[:, g, :], 1.0, -1.0, OP.min, OP.max)
    invswc = pr.sbuf(f"fw_invsw{which}", [128, 1], F32)
    pr.mm(ps_b[:, :1], ones1x128f[:], srow[:, 0:1], start=True, stop=True)
    pr.copy("V", invswc[:], ps_b[:, :1])
    return wqb, invswc


def _ffn_chunks():
    return [(0, TCH), (TCH, TCH), (2 * TCH, TCH), (3 * TCH, TCH), (4 * TCH, 2)]


def emit_ffn_a_span(pr, w, io, si):
    x2e = io[f"x2e_{si}"].rearrange("(g p) t -> p g t", p=128)
    hr = io[f"h_{si}"].rearrange("(g p) t -> p g t", p=128)
    rsc = io[f"rsc_{si}"]

    xch = pr.sbuf("f_xch", [128, 8 * TCH], F32)
    xbch = pr.sbuf("f_xbch", [128, 8 * TCH], BF16)
    sqch = pr.sbuf("f_sqch", [128, 8 * TCH], BF16)
    xnf = pr.sbuf("f_xnf", [128, 8 * TCH], F32)
    xqi = pr.sbuf("f_xqi", [128, TCH], I8)
    xqb = pr.sbuf("f_xqb", [128, 8 * TCH], BF16, alias="f_xbch")
    amx = pr.sbuf("f_amx", [128, TCH], F32)
    tcol = pr.sbuf("f_tcol", [128, 1], F32)
    rowq = pr.sbuf("f_rowq", [1, 2 * TCH], F32)
    hst = pr.sbuf("f_hst", [128, 16 * TCH], BF16)
    rows = pr.sbuf("s1_rows", [1, 7 * TCH], F32)
    ps_sum = pr.psum("s1_ps_sum", [1, TCH], F32)
    ps_sq = pr.psum("s1_ps_sq", [1, TCH], F32)
    ps_bc1 = pr.psum("s1_ps_bc1", [128, TCH], F32)
    ps_bc2 = pr.psum("s1_ps_bc2", [128, TCH], F32)
    ps_mm = pr.psum("s1_ps_mm", [128, TCH], F32)
    ps_t = pr.psum("f_ps_t", [128, 128], F32)
    r_m = rows[:][:, 0:TCH]
    r_v = rows[:][:, TCH:2 * TCH]
    r_t = rows[:][:, 2 * TCH:3 * TCH]
    r_q = rows[:][:, 3 * TCH:4 * TCH]
    r_r1 = rows[:][:, 4 * TCH:5 * TCH]

    w1q3 = w["w1qb"][:].rearrange("p (g c) -> p g c", g=8)
    hst3 = hst[:].rearrange("p (g t) -> p g t", g=16)

    for (c0, T) in _ffn_chunks():
        xch_v = xch[:, :8 * T].rearrange("p (g t) -> p g t", g=8)
        xb_v = xbch[:, :8 * T].rearrange("p (g t) -> p g t", g=8)
        sq_v = sqch[:, :8 * T].rearrange("p (g t) -> p g t", g=8)
        xnf_v = xnf[:, :8 * T].rearrange("p (g t) -> p g t", g=8)
        pr.dma("Y", xch_v, x2e[:, :, c0:c0 + T])
        pr.copy("V", xbch[:, :8 * T], xch[:, :8 * T])
        pr.act("S", sqch[:, :8 * T], xbch[:, :8 * T], AF.Square)
        for g in range(8):
            pr.mm(ps_sum[:1, :T], w["onesb"][:], xb_v[:, g, :], start=(g == 0), stop=(g == 7))
        for g in range(8):
            pr.mm(ps_sq[:1, :T], w["onesb"][:], sq_v[:, g, :], start=(g == 0), stop=(g == 7))
        pr.ts("V", r_m[:, :T], ps_sum[:1, :T], 1.0 / D, None, OP.mult)
        pr.ts("V", r_v[:, :T], ps_sq[:1, :T], 1.0 / D, None, OP.mult)
        pr.tt("V", r_t[:, :T], r_m[:, :T], r_m[:, :T], OP.mult)
        pr.tt("V", r_v[:, :T], r_v[:, :T], r_t[:, :T], OP.subtract)
        pr.act("S", r_q[:, :T], r_v[:, :T], AF.Ln, bias=float(LN_EPS))
        pr.act("S", r_r1[:, :T], r_q[:, :T], AF.Exp, scale=-0.5)
        pr.tt("V", r_t[:, :T], r_m[:, :T], r_r1[:, :T], OP.mult)
        pr.mm(ps_bc1[:, :T], w["ones1x128f"][:], r_r1[:, :T], start=True, stop=True)
        pr.mm(ps_bc2[:, :T], w["ones1x128f"][:], r_t[:, :T], start=True, stop=True)
        pr.tt("V", xnf_v, xch_v, bc8(ps_bc1[:, :T], 8), OP.mult)
        pr.tt("V", xnf_v, xnf_v, bc8(ps_bc2[:, :T], 8), OP.subtract)
        pr.act("S", xch[:, :8 * T], xnf[:, :8 * T], AF.Abs)
        pr.tt("V", amx[:, :T], xch_v[:, 0, :], xch_v[:, 1, :], OP.max)
        for g in range(2, 8):
            pr.tt("V", amx[:, :T], amx[:, :T], xch_v[:, g, :], OP.max)
        nblk = (T + 127) // 128
        for tb in range(nblk):
            tw = min(128, T - tb * 128)
            pr.transpose(ps_t[:tw, :128], amx[:, tb * 128:tb * 128 + tw], w["idenf"][:])
            pr.red("V", tcol[:tw, 0:1], ps_t[:tw, :128], OP.max)
            pr.dma("P", rsc[0:1, c0 + tb * 128:c0 + tb * 128 + tw], tcol[:tw, 0:1])
        pr.dma("Y", rowq[:, 0:T], rsc[0:1, c0:c0 + T])
        pr.ts("V", rowq[:, 0:T], rowq[:, 0:T], 1e-5, None, OP.max)
        pr.recip("V", rowq[:, TCH:TCH + T], rowq[:, 0:T])
        pr.ts("V", rowq[:, TCH:TCH + T], rowq[:, TCH:TCH + T], 127.0, None, OP.mult)
        pr.mm(ps_bc1[:, :T], w["ones1x128f"][:], rowq[:, TCH:TCH + T], start=True, stop=True)
        pr.tt("V", xnf_v, xnf_v, bc8(ps_bc1[:, :T], 8), OP.mult)
        for g in range(8):
            pr.copy("V", xqi[:, :T], xnf_v[:, g, :])
            pr.copy("V", xqb[:, :8 * T].rearrange("p (g t) -> p g t", g=8)[:, g, :], xqi[:, :T])
        pr.ts("V", rowq[:, 0:T], rowq[:, 0:T], float(1.0 / 127.0), None, OP.mult)
        pr.mm(ps_bc2[:, :T], w["ones1x128f"][:], rowq[:, 0:T], start=True, stop=True)
        dqbc = pr.sbuf("f_dqbc", [128, TCH], F32)
        pr.ts("V", dqbc[:, :T], ps_bc2[:, :T], w["invsw1"][:, 0:1], None, OP.mult)
        xq3 = xqb[:, :8 * T].rearrange("p (g t) -> p g t", g=8)
        for half in range(2):
            for mcl in range(16):
                mc = half * 16 + mcl
                for kc in range(8):
                    pr.mm(ps_mm[:, :T], w1q3[:, kc, mc * 128:mc * 128 + 128], xq3[:, kc, :],
                          start=(kc == 0), stop=(kc == 7))
                pr.tt("V", hst3[:, mcl, :T], ps_mm[:, :T], dqbc[:, :T], OP.mult)
            pr.dma("P", hr[:, half * 16:half * 16 + 16, c0:c0 + T], hst3[:, :, :T])


def emit_ffn_b_span(pr, w, io, si):
    hr = io[f"h_{si}"].rearrange("(g p) t -> p g t", p=128)
    x2e = io[f"x2e_{si}"].rearrange("(g p) t -> p g t", p=128)
    x3r = io[f"x3_{si}"].rearrange("(g p) t -> p g t", p=128)
    rsc = io[f"rsc_{si}"]

    hch = pr.sbuf("fb_hch", [128, 32 * (TCH + 2)], BF16)
    hcf = pr.sbuf("fb_hcf", [128, 32 * TCH], F32)
    amx = pr.sbuf("f_amx", [128, TCH], F32)
    tcol = pr.sbuf("f_tcol", [128, 1], F32)
    rowq = pr.sbuf("f_rowq", [1, 2 * TCH], F32)
    hqi = pr.sbuf("fb_hqi", [128, TCH], I8)
    hqb = pr.sbuf("fb_hqb", [128, 32 * TCH], BF16, alias="fb_hcf")
    xch = pr.sbuf("f_xch", [128, 8 * TCH], F32)
    ps_mm = pr.psum("s1_ps_mm", [128, TCH], F32)
    ps_bc1 = pr.psum("s1_ps_bc1", [128, TCH], F32)
    ps_bc2 = pr.psum("s1_ps_bc2", [128, TCH], F32)
    ps_t = pr.psum("f_ps_t", [128, 128], F32)

    w2q3 = w["w2qb"][:].rearrange("p (g c) -> p g c", g=32)
    cwf3 = w["cwf"][:].rearrange("p (g k) -> p g k", g=32)
    hch3 = hch[:].rearrange("p (g t) -> p g t", g=32)
    hcf3 = hcf[:].rearrange("p (g t) -> p g t", g=32)
    hqb3 = hqb[:].rearrange("p (g t) -> p g t", g=32)

    for ci in range(SPAN // TCH):
        c0 = ci * TCH
        T = TCH
        pr.dma("Y", hch3, hr[:, :, c0:c0 + T + 2])
        for g in range(32):
            pr.ts("V", hcf3[:, g, :], hch3[:, g, 0:T], cwf3[:, g, 0:1],
                  w["cbf"][:, g:g + 1], OP.mult, OP.add)
            for k in (1, 2):
                pr.stt("V", hcf3[:, g, :], hch3[:, g, k:k + T], cwf3[:, g, k:k + 1],
                       hcf3[:, g, :], OP.mult, OP.add)
        pr.act("S", hcf[:], hcf[:], AF.Silu)
        absg = pr.sbuf("fb_absg", [128, TCH], F32)
        pr.act("S", absg[:, :T], hcf3[:, 0, :], AF.Abs)
        pr.copy("V", amx[:, :T], absg[:, :T])
        for g in range(1, 32):
            pr.act("S", absg[:, :T], hcf3[:, g, :], AF.Abs)
            pr.tt("V", amx[:, :T], amx[:, :T], absg[:, :T], OP.max)
        nblk = T // 128
        for tb in range(nblk):
            pr.transpose(ps_t[:, :128], amx[:, tb * 128:tb * 128 + 128], w["idenf"][:])
            pr.red("V", tcol[:, 0:1], ps_t[:, :128], OP.max)
            pr.dma("P", rsc[0:1, c0 + tb * 128:c0 + tb * 128 + 128], tcol[:, 0:1])
        pr.dma("Y", rowq[:, 0:T], rsc[0:1, c0:c0 + T])
        pr.ts("V", rowq[:, 0:T], rowq[:, 0:T], 1e-5, None, OP.max)
        pr.recip("V", rowq[:, TCH:TCH + T], rowq[:, 0:T])
        pr.ts("V", rowq[:, TCH:TCH + T], rowq[:, TCH:TCH + T], 127.0, None, OP.mult)
        pr.mm(ps_bc1[:, :T], w["ones1x128f"][:], rowq[:, TCH:TCH + T], start=True, stop=True)
        for g in range(32):
            pr.tt("V", hcf3[:, g, :], hcf3[:, g, :], ps_bc1[:, :T], OP.mult)
            pr.copy("V", hqi[:, :T], hcf3[:, g, :])
            pr.copy("V", hqb3[:, g, :T], hqi[:, :T])
        pr.ts("V", rowq[:, 0:T], rowq[:, 0:T], float(1.0 / 127.0), None, OP.mult)
        pr.mm(ps_bc2[:, :T], w["ones1x128f"][:], rowq[:, 0:T], start=True, stop=True)
        dqbc2 = pr.sbuf("fb_dqbc", [128, TCH], F32)
        pr.ts("V", dqbc2[:, :T], ps_bc2[:, :T], w["invsw2"][:, 0:1], None, OP.mult)
        pr.dma("Y", xch[:].rearrange("p (g t) -> p g t", g=8), x2e[:, :, c0 + 1:c0 + 1 + T])
        for mc in range(8):
            for kc in range(32):
                pr.mm(ps_mm[:, :T], w2q3[:, kc, mc * 128:mc * 128 + 128], hqb3[:, kc, :],
                      start=(kc == 0), stop=(kc == 31))
            xv = xch[:].rearrange("p (g t) -> p g t", g=8)[:, mc, :]
            pr.tt("V", amx[:, :T], ps_mm[:, :T], dqbc2[:, :T], OP.mult)
            pr.tt("V", amx[:, :T], amx[:, :T], xv, OP.add)
            pr.copy("V", xch[:].rearrange("p (g t) -> p g t", g=8)[:, mc, :], amx[:, :T])
        pr.dma("P", x3r[:, :, c0:c0 + T], xch[:].rearrange("p (g t) -> p g t", g=8))


def emit_ffn_weights(pr, io, which):
    w = {}
    w["onesb"] = pr.sbuf("w_onesb", [128, 1], BF16)
    w["ones1x128f"] = pr.sbuf("w_ones1f", [1, 128], F32)
    w["idenf"] = pr.sbuf("w_idenf", [128, 128], F32)
    w["cwf"] = pr.sbuf("w_cwf", [128, 96], F32)
    w["cbf"] = pr.sbuf("w_cbf", [128, 32], F32)
    pr.memset("V", w["onesb"][:], 1.0)
    pr.dma("Y", w["ones1x128f"][:], io["ones1x128f"])
    pr.dma("Y", w["idenf"][:], io["idenf"])
    pr.dma("Y", w["cwf"][:], io["cwf"])
    pr.dma("Y", w["cbf"][:], io["cbf"])
    if which == "a":
        w["w1qb"], w["invsw1"] = emit_ternary_prep(pr, io, 1, 8, DH)
    else:
        pr.sbuf("fb_hch", [128, 32 * (TCH + 2)], BF16)
        pr.sbuf("fb_hcf", [128, 32 * TCH], F32)
        w["w2qb"], w["invsw2"] = emit_ternary_prep(pr, io, 2, 32, D,
                                                   wtmp_alias="fb_hch", wi8_alias="fb_hcf")
    pr.barrier()
    return w


def _declare_ffn_io(nc, which):
    io = {}
    io["ones1x128f"] = declare_in(nc, "ones1x128f", [1, 128])
    io["idenf"] = declare_in(nc, "idenf", [128, 128])
    io["cwf"] = declare_in(nc, "cwf", [128, 96])
    io["cbf"] = declare_in(nc, "cbf", [128, 32])
    if which == "a":
        io["w1T"] = declare_in(nc, "w1T", [D, DH])
    else:
        io["w2T"] = declare_in(nc, "w2T", [DH, D])
    for si in range(2):
        io[f"x2e_{si}"] = declare_in(nc, f"x2e_{si}", [D, XE])
        io[f"rsc_{si}"] = nc.dram_tensor(f"rsc_{si}", [1, XE], F32).ap()
        if which == "a":
            io[f"h_{si}"] = declare_out(nc, f"h_{si}", [DH, XE // 2], F32).bitcast(BF16)
        else:
            io[f"h_{si}"] = declare_in(nc, f"h_{si}", [DH, XE // 2], F32).bitcast(BF16)
            io[f"x3_{si}"] = declare_out(nc, f"x3_{si}", [D, SPAN])
    return io


def build_launch5a():
    nc = bass.Bass("TRN2", target_bir_lowering=False)
    pr = Prog(nc)
    io = _declare_ffn_io(nc, "a")
    w = emit_ffn_weights(pr, io, "a")
    for si in range(2):
        emit_ffn_a_span(pr, w, io, si)
    pr.fence_all("P")
    pr.close()
    return nc


def build_launch5b():
    nc = bass.Bass("TRN2", target_bir_lowering=False)
    pr = Prog(nc)
    io = _declare_ffn_io(nc, "b")
    w = emit_ffn_weights(pr, io, "b")
    for si in range(2):
        emit_ffn_b_span(pr, w, io, si)
    pr.fence_all("P")
    pr.close()
    return nc



def emit_s4d_prep(pr, io):
    w = {}
    for nm in ("ldt", "Are", "Aim", "Cre", "Cim"):
        w[nm] = pr.sbuf(f"s4_{nm}", [128, 256 if nm != "ldt" else 8], F32)
        pr.dma("Y", w[nm][:], io["s4_logdt" if nm == "ldt" else
                                "s4_Ar" if nm == "Are" else
                                "s4_Ai" if nm == "Aim" else
                                "s4_Cr" if nm == "Cre" else "s4_Ci"])
    names = ["dt", "are", "aim", "lre", "lim", "cpre", "cpim", "pre", "pim",
             "t1", "t2", "t3", "clre", "clim", "kst"]
    t = {nm: pr.sbuf(f"s4w_{nm}", [128, 256], F32) for nm in names}
    bst = pr.sbuf("s4w_bst", [128, 8 * 32 * 16], BF16)
    bst2 = pr.sbuf("s4w_bst2", [128, 8 * 32 * 16], BF16)
    bsmall = pr.sbuf("s4w_bs", [128, 256], BF16)
    kstb = pr.sbuf("s4w_kstb", [128, 8 * 128], BF16)
    powGv = io["powG"].rearrange("(s ri hg p n) -> s ri hg p n", s=128, ri=2, hg=8, p=128)
    powEv = io["powE"].rearrange("(ri hg p n k) -> ri hg p n k", ri=2, hg=8, p=128, n=32)
    kpadv = io["kpad"].rearrange("(hg p c) -> hg p c", p=128, c=256)
    lamTv = io["lamT"].rearrange("(ri h n) -> ri h n", ri=2, h=1024)

    pr.act("S", w["ldt"][:], w["ldt"][:], AF.Exp)
    dt_b = w["ldt"][:].unsqueeze(2).broadcast_to([128, 8, 32])

    def v3(ap):
        return ap.rearrange("p (g n) -> p g n", g=8)
    pr.act("S", t["are"][:], w["Are"][:], AF.Exp)
    pr.ts("V", t["are"][:], t["are"][:], -1.0, None, OP.mult)
    pr.tt("V", v3(t["t1"][:]), v3(t["are"][:]), dt_b, OP.mult)
    pr.tt("V", v3(t["t2"][:]), v3(w["Aim"][:]), dt_b, OP.mult)
    inv2pi = float(1.0 / (2.0 * np.pi))
    two_pi = float(2.0 * np.pi)
    i8s = pr.sbuf("s4w_i32", [128, 256], mybir.dt.int32)
    pr.ts("V", t["t3"][:], t["t2"][:], inv2pi, None, OP.mult)
    pr.copy("V", i8s[:], t["t3"][:])
    pr.copy("V", t["t3"][:], i8s[:])
    pr.ts("V", t["t3"][:], t["t3"][:], two_pi, None, OP.mult)
    pr.tt("V", t["t3"][:], t["t2"][:], t["t3"][:], OP.subtract)
    pr.act("S", t["lre"][:], t["t1"][:], AF.Exp)
    snt = pr.sbuf("s4w_snt", [128, 256], F32)
    cst = pr.sbuf("s4w_cst", [128, 256], F32)
    pr.act("S", snt[:], t["t3"][:], AF.Sin)
    pr.act("S", cst[:], t["t3"][:], AF.Sin, scale=0.5)
    pr.tt("V", cst[:], cst[:], cst[:], OP.mult)
    pr.ts("V", cst[:], cst[:], -2.0, 1.0, OP.mult, OP.add)
    pr.tt("V", t["lim"][:], t["lre"][:], snt[:], OP.mult)
    pr.tt("V", t["lre"][:], t["lre"][:], cst[:], OP.mult)
    pr.tt("V", t["t1"][:], t["are"][:], t["are"][:], OP.mult)
    pr.tt("V", t["t2"][:], w["Aim"][:], w["Aim"][:], OP.mult)
    pr.tt("V", t["t1"][:], t["t1"][:], t["t2"][:], OP.add)
    pr.recip("V", t["t1"][:], t["t1"][:])
    lm1re = pr.sbuf("s4w_lm1re", [128, 256], F32)
    pr.ts("V", lm1re[:], t["lre"][:], 1.0, None, OP.subtract)
    pr.tt("V", t["t2"][:], lm1re[:], t["are"][:], OP.mult)
    pr.tt("V", t["t3"][:], t["lim"][:], w["Aim"][:], OP.mult)
    pr.tt("V", t["t2"][:], t["t2"][:], t["t3"][:], OP.add)
    numim = pr.sbuf("s4w_numim", [128, 256], F32)
    pr.tt("V", numim[:], t["lim"][:], t["are"][:], OP.mult)
    pr.tt("V", t["t3"][:], lm1re[:], w["Aim"][:], OP.mult)
    pr.tt("V", numim[:], numim[:], t["t3"][:], OP.subtract)
    pr.tt("V", t["t2"][:], t["t2"][:], t["t1"][:], OP.mult)
    pr.tt("V", numim[:], numim[:], t["t1"][:], OP.mult)
    pr.tt("V", t["cpre"][:], w["Cre"][:], t["t2"][:], OP.mult)
    pr.tt("V", t["t3"][:], w["Cim"][:], numim[:], OP.mult)
    pr.tt("V", t["cpre"][:], t["cpre"][:], t["t3"][:], OP.subtract)
    pr.tt("V", t["cpim"][:], w["Cre"][:], numim[:], OP.mult)
    pr.tt("V", t["t3"][:], w["Cim"][:], t["t2"][:], OP.mult)
    pr.tt("V", t["cpim"][:], t["cpim"][:], t["t3"][:], OP.add)
    pr.tt("V", t["clre"][:], t["cpre"][:], t["lre"][:], OP.mult)
    pr.tt("V", t["t3"][:], t["cpim"][:], t["lim"][:], OP.mult)
    pr.tt("V", t["clre"][:], t["clre"][:], t["t3"][:], OP.subtract)
    pr.tt("V", t["clim"][:], t["cpre"][:], t["lim"][:], OP.mult)
    pr.tt("V", t["t3"][:], t["cpim"][:], t["lre"][:], OP.mult)
    pr.tt("V", t["clim"][:], t["clim"][:], t["t3"][:], OP.add)
    pr.memset("V", t["pre"][:], 1.0)
    pr.memset("V", t["pim"][:], 0.0)
    pr.memset("V", t["kst"][:], 0.0)

    bstv = bst[:].rearrange("p (g n k) -> p g n k", g=8, n=32)
    bstv2 = bst2[:].rearrange("p (g n k) -> p g n k", g=8, n=32)
    kstbv = kstb[:].rearrange("p (g k) -> p g k", g=8)
    for k in range(128):
        pr.tt("V", t["t1"][:], t["cpre"][:], t["pre"][:], OP.mult)
        pr.tt("V", t["t3"][:], t["cpim"][:], t["pim"][:], OP.mult)
        pr.tt("V", t["t1"][:], t["t1"][:], t["t3"][:], OP.subtract)
        r3 = t["t1"][:].rearrange("p (g n) -> p g n", g=8)
        pr.red("V", t["t2"][:, 0:8], r3, OP.add)
        pr.ts("V", kstbv[:, :, k:k+1].rearrange("p g one -> p (g one)"),
              t["t2"][:, 0:8], 2.0, None, OP.mult)
        s = 127 - k
        pr.copy("V", bsmall[:], t["pre"][:])
        pr.dma("P", powGv[s, 0, :, :, :].rearrange("hg p n -> p hg n"),
               bsmall[:].rearrange("p (g n) -> p g n", g=8))
        pr.copy("V", bsmall[:], t["pim"][:])
        pr.dma("P", powGv[s, 1, :, :, :].rearrange("hg p n -> p hg n"),
               bsmall[:].rearrange("p (g n) -> p g n", g=8))
        kb = k % 16
        pr.tt("V", t["t1"][:], t["clre"][:], t["pre"][:], OP.mult)
        pr.tt("V", t["t3"][:], t["clim"][:], t["pim"][:], OP.mult)
        pr.tt("V", t["t1"][:], t["t1"][:], t["t3"][:], OP.subtract)
        pr.ts("V", bstv[:, :, :, kb].rearrange("p g n -> p (g n)"), t["t1"][:], 2.0, None, OP.mult)
        pr.tt("V", t["t1"][:], t["clre"][:], t["pim"][:], OP.mult)
        pr.tt("V", t["t3"][:], t["clim"][:], t["pre"][:], OP.mult)
        pr.tt("V", t["t1"][:], t["t1"][:], t["t3"][:], OP.add)
        pr.ts("V", bstv2[:, :, :, kb].rearrange("p g n -> p (g n)"), t["t1"][:], -2.0, None, OP.mult)
        if kb == 15:
            k0 = k - 15
            for hg in range(8):
                pr.dma("P", powEv[0, hg, :, :, k0:k0 + 16], bstv[:, hg, :, :])
                pr.dma("P", powEv[1, hg, :, :, k0:k0 + 16], bstv2[:, hg, :, :])
        pr.tt("V", t["t1"][:], t["pre"][:], t["lre"][:], OP.mult)
        pr.tt("V", t["t3"][:], t["pim"][:], t["lim"][:], OP.mult)
        pr.tt("V", t["t1"][:], t["t1"][:], t["t3"][:], OP.subtract)
        pr.tt("V", t["t3"][:], t["pim"][:], t["lre"][:], OP.mult)
        pr.tt("V", t["t2"][:], t["pre"][:], t["lim"][:], OP.mult)
        pr.tt("V", t["pim"][:], t["t3"][:], t["t2"][:], OP.add)
        pr.copy("V", t["pre"][:], t["t1"][:])
    Db = pr.sbuf("s4w_Db", [128, 1], F32)
    pr.dma("Y", Db[:], io["s4_D"].broadcast_to([128, 1]))
    k0v = kstbv[:, :, 0:1].rearrange("p g one -> p (g one)")
    pr.ts("V", k0v, k0v, Db[:], None, OP.add)
    zrow = pr.sbuf("s4w_zrow", [128, 256], BF16)
    pr.memset("V", zrow[:], 0.0)
    for hg in range(8):
        pr.dma("P", kpadv[hg, :, :], zrow[:])
    pr.dma("P", kpadv.rearrange("hg p c -> hg p c")[:, :, 127:255].rearrange("hg p c -> p hg c"),
           kstbv)
    pr.dma("P", lamTv[0, :, :].rearrange("(hg p) n -> p hg n", p=128), t["pre"][:].rearrange("p (g n) -> p g n", g=8))
    pr.dma("P", lamTv[1, :, :].rearrange("(hg p) n -> p hg n", p=128), t["pim"][:].rearrange("p (g n) -> p g n", g=8))
    pr.barrier()
    return w


def _declare_s4d_prep_io(nc):
    io = {}
    io["s4_logdt"] = declare_in(nc, "s4_logdt", [128, 8])
    io["s4_Ar"] = declare_in(nc, "s4_Ar", [128, 256])
    io["s4_Ai"] = declare_in(nc, "s4_Ai", [128, 256])
    io["s4_Cr"] = declare_in(nc, "s4_Cr", [128, 256])
    io["s4_Ci"] = declare_in(nc, "s4_Ci", [128, 256])
    io["s4_D"] = declare_in(nc, "s4_D", [1, 1])
    io["powG"] = declare_out(nc, "powG", [128 * 2 * 1024 * 32 // 2], F32).bitcast(BF16)
    io["powE"] = declare_out(nc, "powE", [2 * 1024 * 32 * 128 // 2], F32).bitcast(BF16)
    io["kpad"] = declare_out(nc, "kpad", [1024 * 256 // 2], F32).bitcast(BF16)
    io["lamT"] = declare_out(nc, "lamT", [2 * 32 * 1024])
    return io


def build_s4d_prep_test():
    nc = bass.Bass("TRN2", target_bir_lowering=False)
    pr = Prog(nc)
    io = _declare_s4d_prep_io(nc)
    emit_s4d_prep(pr, io)
    pr.fence_all("P")
    pr.close()
    return nc


NCHS = SPAN // 128
T6 = 256


def emit_lamn(pr, w, io, P, name):
    lamL = pr.sbuf("s4n_lamL", [128, 512], F32)
    lamn = pr.sbuf(name, [P, 2 * HS4], F32)
    ps_lt = pr.psum("s4n_pslt", [32, 128], F32)
    pr.dma("Y", lamL[:].rearrange("p (ri hg n) -> p ri hg n", ri=2, hg=8),
           io["lamT"].rearrange("(ri hg p n) -> p ri hg n", ri=2, hg=8, p=128))
    lv = lamL[:].rearrange("p (ri hg n) -> p ri hg n", ri=2, hg=8)
    for ri in range(2):
        for hg in range(8):
            pr.transpose(ps_lt[:, :], lv[:, ri, hg, :], w["idenf"][:])
            for rep in range(P // 32):
                pr.copy("V", lamn[:][rep * 32:rep * 32 + 32,
                                     ri * HS4 + hg * 128:ri * HS4 + (hg + 1) * 128],
                        ps_lt[:, :])
    return lamn


def emit_ln3_span(pr, w, io, si, uT):
    xr = io[f"x3_{si}"].rearrange("(g p) t -> p g t", p=128)
    xch = pr.sbuf("s6_xch", [128, 8 * T6], F32)
    xbch = pr.sbuf("s6_xbch", [128, 8 * T6], BF16)
    sqch = pr.sbuf("s6_sqch", [128, 8 * T6], BF16)
    xnb = pr.sbuf("s6_xnb", [128, 8 * T6], BF16)
    rows = pr.sbuf("s6_rows", [1, 5 * T6], F32)
    ps_sum = pr.psum("s6_ps_sum", [1, T6], F32)
    ps_sq = pr.psum("s6_ps_sq", [1, T6], F32)
    ps_bc1 = pr.psum("s6_ps_bc1", [128, T6], F32)
    ps_bc2 = pr.psum("s6_ps_bc2", [128, T6], F32)
    ps_tb = pr.psum("s6_ps_tb", [128, 128], BF16)
    r_m = rows[:][:, 0:T6]
    r_v = rows[:][:, T6:2 * T6]
    r_t = rows[:][:, 2 * T6:3 * T6]
    r_q = rows[:][:, 3 * T6:4 * T6]
    r_r1 = rows[:][:, 4 * T6:5 * T6]
    uTv = uT[:]
    for ci in range(SPAN // T6):
        xch_v = xch[:].rearrange("p (g t) -> p g t", g=8)
        xb_v = xbch[:].rearrange("p (g t) -> p g t", g=8)
        sq_v = sqch[:].rearrange("p (g t) -> p g t", g=8)
        xnb_v = xnb[:].rearrange("p (g t) -> p g t", g=8)
        pr.dma("Y", xch_v, xr[:, :, ci * T6:(ci + 1) * T6])
        pr.copy("V", xbch[:], xch[:])
        pr.act("S", sqch[:], xbch[:], AF.Square)
        for g in range(8):
            pr.mm(ps_sum[:1, :], w["onesb"][:], xb_v[:, g, :], start=(g == 0), stop=(g == 7))
        for g in range(8):
            pr.mm(ps_sq[:1, :], w["onesb"][:], sq_v[:, g, :], start=(g == 0), stop=(g == 7))
        pr.ts("V", r_m, ps_sum[:1, :], 1.0 / D, None, OP.mult)
        pr.ts("V", r_v, ps_sq[:1, :], 1.0 / D, None, OP.mult)
        pr.tt("V", r_t, r_m, r_m, OP.mult)
        pr.tt("V", r_v, r_v, r_t, OP.subtract)
        pr.act("S", r_q, r_v, AF.Ln, bias=float(LN_EPS))
        pr.act("S", r_r1, r_q, AF.Exp, scale=-0.5)
        pr.tt("V", r_t, r_m, r_r1, OP.mult)
        pr.mm(ps_bc1[:, :], w["ones1x128f"][:], r_r1, start=True, stop=True)
        pr.mm(ps_bc2[:, :], w["ones1x128f"][:], r_t, start=True, stop=True)
        pr.tt("V", xch_v, xch_v, bc8(ps_bc1[:, :], 8), OP.mult)
        pr.tt("V", xnb_v, xch_v, bc8(ps_bc2[:, :], 8), OP.subtract)
        for tb in range(T6 // 128):
            c = (ci * T6 + tb * 128) // 128
            for hg in range(8):
                pr.transpose(ps_tb[:, :], xnb_v[:, hg, tb * 128:(tb + 1) * 128], w["idenb"][:])
                pr.copy("V", uTv[:, (si * NCHS + c) * HS4 + hg * 128:
                                 (si * NCHS + c) * HS4 + (hg + 1) * 128], ps_tb[:, :])
    pr.dma("P", io[f"uT_{si}"].rearrange("(p f) -> p f", p=128),
           uTv[:, si * NCHS * HS4:(si + 1) * NCHS * HS4])


def emit_s4d_g(pr, w, io, uT):
    powGr = io["powG"].rearrange("(s ri h n) -> s ri h n", s=128, ri=2, h=HS4)
    gst = [pr.sbuf(f"s6_gst{i}", [128, 2048], BF16) for i in range(2)]
    Gsb = pr.sbuf("s6_Gsb", [64, 2 * NCHS * HS4], BF16)
    ps_g = [pr.psum(f"s6_ps_g{i}", [32, 512], F32) for i in range(2)]
    uT4 = uT[:].rearrange("p (si c h) -> p si c h", si=2, c=NCHS)
    Gv = Gsb[:].rearrange("p (ri c h) -> p ri h c", ri=2, c=NCHS)
    for hb in range(HS4 // 32):
        g = gst[hb % 2]
        pr.dma("Y", g[:].rearrange("p (ri hh n) -> p ri hh n", ri=2, hh=32),
               powGr[:, :, hb * 32:(hb + 1) * 32, :])
        for sub in range(4):
            ps = ps_g[sub % 2]
            for hh in range(8):
                h = hb * 32 + sub * 8 + hh
                for ri in range(2):
                    lhsT = g[:][:, ri * 1024 + (sub * 8 + hh) * 32:
                                ri * 1024 + (sub * 8 + hh) * 32 + 32]
                    pr.mm(ps[:, hh * 64 + ri * 32:hh * 64 + ri * 32 + 32],
                          lhsT, uT4[:, :, :, h], start=True, stop=True)
            psv = ps[:].rearrange("p (hh ri si c) -> p hh ri si c", hh=8, ri=2, si=2)
            for ri in range(2):
                for sp in range(2):
                    h0 = hb * 32 + sub * 8
                    pr.copy("V", Gv[sp * 32:sp * 32 + 32, ri, h0:h0 + 8, :],
                            psv[:, :, ri, sp, :])
    return Gsb


def emit_s4d_scan(pr, io, lamn, Gsb):
    histst = pr.sbuf("s6_histst", [64, NCHS * 2 * HS4], BF16, alias="s6_uT")
    Sre = pr.sbuf("s6_Sre", [64, HS4], F32)
    Sim = pr.sbuf("s6_Sim", [64, HS4], F32)
    tA = pr.sbuf("s6_tA", [64, HS4], F32)
    tB = pr.sbuf("s6_tB", [64, HS4], F32)
    pr.barrier()
    pr.memset("V", Sre[:], 0.0)
    pr.memset("V", Sim[:], 0.0)
    lre = lamn[:][:, 0:HS4]
    lim = lamn[:][:, HS4:2 * HS4]
    G3 = Gsb[:].rearrange("p (ri c h) -> p ri c h", ri=2, c=NCHS)
    H3 = histst[:].rearrange("p (c ri h) -> p c ri h", c=NCHS, ri=2)
    for c in range(NCHS):
        pr.copy("V", H3[:, c, 0, :], Sre[:])
        pr.copy("V", H3[:, c, 1, :], Sim[:])
        pr.tt("V", tA[:], lim, Sim[:], OP.mult)
        pr.tt("V", tB[:], lim, Sre[:], OP.mult)
        pr.tt("V", Sre[:], Sre[:], lre, OP.mult)
        pr.tt("V", Sre[:], Sre[:], tA[:], OP.subtract)
        pr.tt("V", Sre[:], Sre[:], G3[:, 0, c, :], OP.add)
        pr.tt("V", Sim[:], Sim[:], lre, OP.mult)
        pr.tt("V", Sim[:], Sim[:], tB[:], OP.add)
        pr.tt("V", Sim[:], Sim[:], G3[:, 1, c, :], OP.add)
    for si in range(2):
        pr.dma("P", io[f"hist_{si}"].rearrange("(c ri n h) -> n c ri h", c=NCHS, ri=2, n=NS4),
               histst[:][si * 32:si * 32 + 32, :].rearrange("p (c ri h) -> p c ri h",
                                                            c=NCHS, ri=2))
        sr = io[f"send_{si}"].rearrange("(ri n h) -> ri n h", ri=2, n=NS4)
        pr.dma("P", sr[0], Sre[:][si * 32:si * 32 + 32, :])
        pr.dma("P", sr[1], Sim[:][si * 32:si * 32 + 32, :])


def _declare_s4d6_io(nc):
    io = {}
    io["idenf"] = declare_in(nc, "idenf", [128, 128])
    io["ones1x128f"] = declare_in(nc, "ones1x128f", [1, 128])
    io["powG"] = declare_in(nc, "powG", [128 * 2 * HS4 * NS4 // 2]).bitcast(BF16)
    io["lamT"] = declare_in(nc, "lamT", [2 * HS4 * NS4])
    for si in range(2):
        io[f"x3_{si}"] = declare_in(nc, f"x3_{si}", [D, SPAN])
        io[f"uT_{si}"] = declare_out(nc, f"uT_{si}", [128 * NCHS * HS4 // 2]).bitcast(BF16)
        io[f"hist_{si}"] = declare_out(nc, f"hist_{si}", [NCHS * 2 * NS4 * HS4 // 2]).bitcast(BF16)
        io[f"send_{si}"] = declare_out(nc, f"send_{si}", [2 * NS4 * HS4])
    return io


def build_launch6():
    nc = bass.Bass("TRN2", target_bir_lowering=False)
    pr = Prog(nc)
    io = _declare_s4d6_io(nc)
    w = {}
    w["idenf"] = pr.sbuf("w_idenf", [128, 128], F32)
    w["idenb"] = pr.sbuf("w_idenb", [128, 128], BF16)
    w["onesb"] = pr.sbuf("w_onesb", [128, 1], BF16)
    w["ones1x128f"] = pr.sbuf("w_ones1f", [1, 128], F32)
    pr.dma("Y", w["idenf"][:], io["idenf"])
    pr.dma("Y", w["ones1x128f"][:], io["ones1x128f"])
    pr.copy("V", w["idenb"][:], w["idenf"][:])
    pr.memset("V", w["onesb"][:], 1.0)
    uT = pr.sbuf("s6_uT", [128, 2 * NCHS * HS4], BF16)
    lamn = emit_lamn(pr, w, io, 64, "s6_lamn")
    for si in range(2):
        emit_ln3_span(pr, w, io, si, uT)
    Gsb = emit_s4d_g(pr, w, io, uT)
    emit_s4d_scan(pr, io, lamn, Gsb)
    pr.fence_all("P")
    pr.close()
    return nc


def _declare_s4d7_io(nc):
    io = {}
    io["idenf"] = declare_in(nc, "idenf", [128, 128])
    io["lamT"] = declare_in(nc, "lamT", [2 * HS4 * NS4])
    for s in range(NSPANS):
        io[f"send_{s}"] = declare_in(nc, f"send_{s}", [2 * NS4 * HS4])
        io[f"Sst_{s}"] = declare_out(nc, f"Sst_{s}", [2 * NS4 * HS4])
    return io


def emit_stitch(pr, io):
    w = {}
    w["idenf"] = pr.sbuf("w_idenf", [128, 128], F32)
    pr.dma("Y", w["idenf"][:], io["idenf"])
    lamn = emit_lamn(pr, w, io, 32, "s7_lamn")
    l16 = pr.sbuf("s7_l16", [32, 2 * HS4], F32)
    t1 = pr.sbuf("s7_t1", [32, HS4], F32)
    t2 = pr.sbuf("s7_t2", [32, HS4], F32)
    pr.copy("V", l16[:], lamn[:])
    re = l16[:][:, 0:HS4]
    im = l16[:][:, HS4:2 * HS4]
    for _ in range(4):
        pr.tt("V", t1[:], re, im, OP.mult)
        pr.tt("V", re, re, re, OP.mult)
        pr.tt("V", t2[:], im, im, OP.mult)
        pr.tt("V", re, re, t2[:], OP.subtract)
        pr.ts("V", im, t1[:], 2.0, None, OP.mult)
    Sre = pr.sbuf("s7_Sre", [32, HS4], F32)
    Sim = pr.sbuf("s7_Sim", [32, HS4], F32)
    sd = pr.sbuf("s7_sd", [32, 2 * HS4], F32)
    pr.memset("V", Sre[:], 0.0)
    pr.memset("V", Sim[:], 0.0)
    for s in range(NSPANS):
        sv = io[f"Sst_{s}"].rearrange("(ri n h) -> ri n h", ri=2, n=NS4)
        pr.dma("P", sv[0], Sre[:])
        pr.dma("P", sv[1], Sim[:])
        pr.dma("Y", sd[:].rearrange("p (ri h) -> p ri h", ri=2),
               io[f"send_{s}"].rearrange("(ri n h) -> n ri h", ri=2, n=NS4))
        pr.tt("V", t1[:], im, Sim[:], OP.mult)
        pr.tt("V", t2[:], im, Sre[:], OP.mult)
        pr.tt("V", Sre[:], Sre[:], re, OP.mult)
        pr.tt("V", Sre[:], Sre[:], t1[:], OP.subtract)
        pr.tt("V", Sre[:], Sre[:], sd[:][:, 0:HS4], OP.add)
        pr.tt("V", Sim[:], Sim[:], re, OP.mult)
        pr.tt("V", Sim[:], Sim[:], t2[:], OP.add)
        pr.tt("V", Sim[:], Sim[:], sd[:][:, HS4:2 * HS4], OP.add)
    pr.fence_all("P")


def build_launch7():
    nc = bass.Bass("TRN2", target_bir_lowering=False)
    pr = Prog(nc)
    io = _declare_s4d7_io(nc)
    emit_stitch(pr, io)
    pr.close()
    return nc


def _declare_s4d8_io(nc):
    io = {}
    io["idenf"] = declare_in(nc, "idenf", [128, 128])
    io["antif"] = declare_in(nc, "antif", [128, 128])
    io["lamT"] = declare_in(nc, "lamT", [2 * HS4 * NS4])
    io["powE"] = declare_in(nc, "powE", [2 * HS4 * NS4 * 128 // 2]).bitcast(BF16)
    io["kpad"] = declare_in(nc, "kpad", [HS4 * 256 // 2]).bitcast(BF16)
    for si in range(2):
        io[f"x3_{si}"] = declare_in(nc, f"x3_{si}", [D, SPAN])
        io[f"uT_{si}"] = declare_in(nc, f"uT_{si}", [128 * NCHS * HS4 // 2]).bitcast(BF16)
        io[f"hist_{si}"] = declare_in(nc, f"hist_{si}", [NCHS * 2 * NS4 * HS4 // 2]).bitcast(BF16)
        io[f"Sst_{si}"] = declare_in(nc, f"Sst_{si}", [2 * NS4 * HS4])
        io[f"x4_{si}"] = declare_out(nc, f"x4_{si}", [D, SPAN])
    return io


def emit_s4d_final(pr, io, kpadb):
    w = {}
    w["idenf"] = pr.sbuf("w_idenf", [128, 128], F32)
    w["idenb"] = pr.sbuf("w_idenb", [128, 128], BF16)
    w["antif"] = pr.sbuf("w_antif", [128, 128], F32)
    w["antib"] = pr.sbuf("w_antib", [128, 128], BF16)
    pr.dma("Y", w["idenf"][:], io["idenf"])
    pr.dma("Y", w["antif"][:], io["antif"])
    pr.copy("V", w["idenb"][:], w["idenf"][:])
    pr.copy("V", w["antib"][:], w["antif"][:])
    pr.dma("Y", kpadb.ap(), io["kpad"])
    lamn = emit_lamn(pr, w, io, 64, "s8_lamn")

    uR = pr.sbuf("s8_uR", [128, 2 * NCHS * HS4], BF16)
    hsb = pr.sbuf("s8_hsb", [64, NCHS * 2 * HS4], BF16, alias="s8_uR")
    y4T = pr.sbuf("s8_y4T", [128, 2 * NCHS * HS4], BF16)
    est = [pr.sbuf(f"s8_est{i}", [64, 2 * 16 * 128], BF16) for i in range(2)]
    hkt = [pr.sbuf(f"s8_hkt{i}", [128, 16 * 128], BF16) for i in range(2)]
    hstg = [pr.sbuf(f"s8_hstg{i}", [32, 2 * HS4], BF16) for i in range(2)]
    Sre = pr.sbuf("s8_Sre", [64, HS4], F32)
    Sim = pr.sbuf("s8_Sim", [64, HS4], F32)
    tA = pr.sbuf("s8_tA", [64, HS4], F32)
    tB = pr.sbuf("s8_tB", [64, HS4], F32)
    uch = [pr.sbuf(f"s8_uch{i}", [128, HS4], BF16) for i in range(2)]
    x3ch = [pr.sbuf(f"s8_x3ch{i}", [128, 1024], F32) for i in range(2)]
    x4ch = [pr.sbuf("s8_x4ch0", [128, 1024], F32, alias="s8_tA"),
            pr.sbuf("s8_x4ch1", [128, 1024], F32, alias="s8_tB")]
    gsl = [pr.sbuf(f"s8_gsl{i}", [128, 1024], BF16) for i in range(2)]
    pA = pr.psum("s8_pA", [128, 512], F32)
    pB = pr.psum("s8_pB", [128, 512], F32)
    pC = pr.psum("s8_pC", [128, 1024], BF16)
    pD = pr.psum("s8_pD", [128, 1024], BF16)
    ps_e = [pA, pB]
    ps_b = [pC, pD]

    hsv = hsb[:]
    for si in range(2):
        hr = io[f"hist_{si}"].rearrange("(c ri n h) -> n c ri h", c=NCHS, ri=2, n=NS4)
        for c in range(NCHS):
            hb = hstg[(si * NCHS + c) % 2]
            pr.dma("Y", hb[:].rearrange("p (ri h) -> p ri h", ri=2), hr[:, c, :, :])
            pr.copy("V", hsv[si * 32:si * 32 + 32, c * 2048:(c + 1) * 2048], hb[:])
    for si in range(2):
        sv = io[f"Sst_{si}"].rearrange("(ri n h) -> ri n h", ri=2, n=NS4)
        pr.dma("Y", Sre[:][si * 32:si * 32 + 32, :], sv[0])
        pr.dma("Y", Sim[:][si * 32:si * 32 + 32, :], sv[1])
    lre = lamn[:][:, 0:HS4]
    lim = lamn[:][:, HS4:2 * HS4]
    for c in range(NCHS):
        pr.tt("V", hsv[:, c * 2048:c * 2048 + HS4],
              hsv[:, c * 2048:c * 2048 + HS4], Sre[:], OP.add)
        pr.tt("V", hsv[:, c * 2048 + HS4:(c + 1) * 2048],
              hsv[:, c * 2048 + HS4:(c + 1) * 2048], Sim[:], OP.add)
        if c < NCHS - 1:
            pr.tt("V", tA[:], lim, Sim[:], OP.mult)
            pr.tt("V", tB[:], lim, Sre[:], OP.mult)
            pr.tt("V", Sre[:], Sre[:], lre, OP.mult)
            pr.tt("V", Sre[:], Sre[:], tA[:], OP.subtract)
            pr.tt("V", Sim[:], Sim[:], lre, OP.mult)
            pr.tt("V", Sim[:], Sim[:], tB[:], OP.add)
    powEr = io["powE"].rearrange("(ri h n k) -> ri h n k", ri=2, h=HS4, n=NS4)
    H4 = hsb[:].rearrange("p (c ri h) -> p ri h c", c=NCHS, ri=2)
    y4f = y4T[:].rearrange("p (si c h) -> p si h c", si=2, c=NCHS)
    y4s = [y4f[:, 0], y4f[:, 1]]
    for hb in range(HS4 // 16):
        eb = est[hb % 2]
        ebv = eb[:]
        for si in range(2):
            for ri in range(2):
                pr.dma("Y", ebv[si * 32:si * 32 + 32, ri * 2048:(ri + 1) * 2048]
                       .rearrange("p (h k) -> p h k", h=16),
                       powEr[ri, hb * 16:(hb + 1) * 16, :, :].rearrange("h n k -> n h k"))
        ps = ps_e[hb % 2]
        for hh in range(16):
            h = hb * 16 + hh
            for si in range(2):
                for ri in range(2):
                    pr.mm(ps[:, hh * 32 + si * 16:hh * 32 + si * 16 + 16],
                          ebv[si * 32:si * 32 + 32,
                              ri * 2048 + hh * 128:ri * 2048 + (hh + 1) * 128],
                          H4[si * 32:si * 32 + 32, ri, h, :],
                          start=(ri == 0), stop=(ri == 1))
        psv = ps[:].rearrange("p (hh si c) -> p si hh c", hh=16, si=2)
        for si in range(2):
            pr.copy("V", y4s[si][:, hb * 16:(hb + 1) * 16, :], psv[:, si])
    pr.barrier()
    uRv = uR[:]
    for si in range(2):
        uTr = io[f"uT_{si}"].rearrange("(p f) -> p f", p=128)
        for c in range(NCHS):
            i = (si * NCHS + c) % 2
            pr.dma("Y", uch[i][:], uTr[:, c * HS4:(c + 1) * HS4])
            for half in range(2):
                ps = ps_e[(2 * (si * NCHS + c) + half) % 2]
                pr.mm(ps[:], w["antib"][:],
                      uch[i][:][:, half * 512:(half + 1) * 512], start=True, stop=True)
                pr.copy("V", uRv[:, (si * NCHS + c) * HS4 + half * 512:
                                (si * NCHS + c) * HS4 + (half + 1) * 512], ps[:])
    pr.barrier()
    uR4 = uR[:].rearrange("p (si c h) -> p si c h", si=2, c=NCHS)
    for hb in range(HS4 // 16):
        hk = hkt[hb % 2]
        pr.dma("Y", hk[:], bass.AP(kpadb, hb * 16 * 256, [[1, 128], [256, 16], [1, 128]]))
        ps = ps_e[hb % 2]
        for hh in range(16):
            h = hb * 16 + hh
            pr.mm(ps[:, hh * 32:hh * 32 + 32],
                  hk[:][:, hh * 128:(hh + 1) * 128], uR4[:, :, :, h],
                  start=True, stop=True)
        psv = ps[:].rearrange("p (hh si c) -> p si hh c", hh=16, si=2)
        for si in range(2):
            pr.tt("V", y4s[si][:, hb * 16:(hb + 1) * 16, :],
                  y4s[si][:, hb * 16:(hb + 1) * 16, :], psv[:, si], OP.add)
    pr.barrier()
    for si in range(2):
        x3r = io[f"x3_{si}"].rearrange("(g p) t -> p g t", p=128)
        x4r = io[f"x4_{si}"].rearrange("(g p) t -> p g t", p=128)
        for c in range(NCHS):
            i = (si * NCHS + c) % 2
            pr.dma("Y", x3ch[i][:].rearrange("p (g t) -> p g t", g=8),
                   x3r[:, :, c * 128:(c + 1) * 128])
            pr.act("S", gsl[i][:], y4T[:][:, (si * NCHS + c) * HS4:
                                          (si * NCHS + c + 1) * HS4], AF.Gelu)
            for hg in range(8):
                pr.transpose(ps_b[i][:, hg * 128:(hg + 1) * 128],
                             gsl[i][:][:, hg * 128:(hg + 1) * 128], w["idenb"][:])
            pr.tt("V", x4ch[i][:], x3ch[i][:], ps_b[i][:], OP.add)
            pr.dma("P", x4r[:, :, c * 128:(c + 1) * 128],
                   x4ch[i][:].rearrange("p (g t) -> p g t", g=8))


def build_launch8():
    nc = bass.Bass("TRN2", target_bir_lowering=False)
    pr = Prog(nc)
    io = _declare_s4d8_io(nc)
    kpadb = nc.dram_tensor("kpadb", [HS4 * 256], BF16)
    emit_s4d_final(pr, io, kpadb)
    pr.fence_all("P")
    pr.close()
    return nc



def emit_relay_chain(pr, io, dr):
    EB = 32
    NBK = (NE + EB - 1) // EB
    onesf = pr.sbuf("ch_onesf", [128, 1], F32)
    ones1x128f = pr.sbuf("ch_ones1f", [1, 128], F32)
    w20 = pr.sbuf("ch_w20", [1, 1], F32)
    pr.memset("V", onesf[:], 1.0)
    pr.memset("V", w20[:], 20.0)
    pr.dma("Y", ones1x128f[:], io["ones1x128f"])
    state = pr.sbuf("ch_state", [128, 256], F32)
    tprod = pr.sbuf("ch_tprod", [128, 256], F32)
    sqscr = pr.sbuf("ch_sq", [128, 256], F32)
    partials = pr.sbuf("ch_part", [128, 1], F32)
    pblk = [pr.sbuf(f"ch_pblk{i}", [128, 256 * EB], F32) for i in range(2)]
    qblk = [pr.sbuf(f"ch_qblk{i}", [128, 256 * EB], F32) for i in range(2)]
    rows = pr.sbuf("ch_rows", [1, 8], F32)
    facrow = pr.sbuf("ch_facrow", [1, NE], F32)
    maskrow = pr.sbuf("ch_mask", [1, NE], F32)
    srecv = pr.sbuf("ch_srecv", [128, 256], F32)
    ps_n = pr.psum("ch_ps_n", [1, 2], F32)
    ps_f = pr.psum("ch_ps_f", [128, 2], F32)
    stA = dr["ch_stentAll"].ap().rearrange("(k p t) -> k p t", k=NSPANS, p=128)
    fcA = dr["ch_facsAll"].ap().rearrange("(k o e) -> k o e", k=NSPANS, o=1)

    for k in range(NSPANS):
        if k == 0:
            pr.memset("V", srecv[:], 0.0)
        else:
            bo = dr[f"ch_bout_{k - 1}"].ap()

            def entry(rank, k=k, bo=bo):
                if rank == k:
                    return (srecv[:], bo[(k - 1) * 128:k * 128, 0:256])
                if rank == 7 - k:
                    return (srecv[:], bo[(8 - k) * 128:(9 - k) * 128, 256:512])
                return (srecv[:], bo[0:128, 0:256])
            pr.dma_arms("Y", entry)
        pr.dma("P", stA[k], srecv[:])
        pr.dma("Y", maskrow[:], io["maskAll"][k:k + 1, :])
        pr.copy("V", state[:], srecv[:])
        pr.memset("V", ps_f[:, 1:2], 1.0)
        for eb in range(NBK):
            e0 = eb * EB
            ne = min(EB, NE - e0)
            pb, qb = pblk[eb % 2], qblk[eb % 2]
            pb3 = pb[:].rearrange("p (pr e) -> p pr e", e=EB)
            qb3 = qb[:].rearrange("p (pr e) -> p pr e", e=EB)

            def pqload(widx, dst, k=k, e0=e0, ne=ne):
                def f(rank):
                    si = 1 if rank == 7 - k else 0
                    PQ = dr[f"PQ_{si}"].ap().rearrange("w (p pr e) -> w p pr e", p=128, pr=256)
                    return (dst, PQ[widx, :, :, e0:e0 + ne])
                return f
            pr.dma_arms("Y", pqload(0, pb3[:, :, 0:ne]))
            pr.dma_arms("Y", pqload(1, qb3[:, :, 0:ne]))
            for el in range(ne):
                e = e0 + el
                Pv = pb3[:, :, el]
                Qv = qb3[:, :, el]
                pr.tt("V", tprod[:], Pv, state[:], OP.mult)
                pr.stt("V", state[:], tprod[:], ps_f[:, (e + 1) % 2:(e + 1) % 2 + 1],
                       Qv, OP.mult, OP.add)
                pr.act("S", sqscr[:], state[:], AF.Square, accum_out=partials[:])
                pr.mm(ps_n[:1, e % 2:e % 2 + 1], onesf[:], partials[:], start=True, stop=True)
                pr.act("S", rows[:, 4:5], ps_n[:1, e % 2:e % 2 + 1], AF.Ln, bias=1e-30)
                pr.act("S", rows[:, 0:1], rows[:, 4:5], AF.Exp, scale=0.5)
                pr.ts("V", rows[:, 1:2], rows[:, 0:1], 1e-8, None, OP.add)
                pr.recip("V", rows[:, 2:3], rows[:, 1:2])
                pr.ts("V", rows[:, 2:3], rows[:, 2:3], 20.0, 1.0, OP.mult, OP.min)
                pr.tt("V", rows[:, 2:3], rows[:, 2:3], maskrow[:, e:e + 1], OP.max)
                pr.copy("V", facrow[:, e:e + 1], rows[:, 2:3])
                pr.mm(ps_f[:, e % 2:e % 2 + 1], ones1x128f[:], rows[:, 2:3],
                      start=True, stop=True)
        pr.ts("V", state[:], state[:], ps_f[:, (NE - 1) % 2:(NE - 1) % 2 + 1], None, OP.mult)
        pr.dma("P", fcA[k], facrow[:])
        cin = dr[f"ch_cin_{k}"].ap()
        pr.dma("P", cin[:, 0:256], state[:])
        pr.dma("P", cin[:, 256:512], state[:])
        pr.collective("AllGather", OP.bypass, [list(range(NCORES))],
                      [cin], [dr[f"ch_bout_{k}"].ap()])
    pr.dma_arms("Y", lambda r: (dr["stent_0"].ap(), stA[r]))
    pr.dma_arms("Y", lambda r: (dr["stent_1"].ap(), stA[7 - r]))
    pr.dma_arms("Y", lambda r: (dr["facs_0"].ap(), fcA[r]))
    pr.dma_arms("Y", lambda r: (dr["facs_1"].ap(), fcA[7 - r]))


def emit_stitch_fused(pr, io, dr):
    w = {}
    w["idenf"] = pr.sbuf("st_idenf", [128, 128], F32)
    pr.dma("Y", w["idenf"][:], io["idenf"])
    bi = dr["st_bin"]
    bo = dr["st_bout"]
    for si in range(2):
        pr.dma("P", bi.ap()[si * 32:(si + 1) * 32, :].rearrange("n (ri h) -> n ri h", ri=2),
               dr[f"send_{si}"].ap().rearrange("(ri n h) -> n ri h", ri=2, n=NS4))
    pr.collective("AllGather", OP.bypass, [list(range(NCORES))], [bi.ap()], [bo.ap()])
    lamn = emit_lamn(pr, w, io, 32, "st_lamn")
    l16 = pr.sbuf("st_l16", [32, 2 * HS4], F32)
    t1 = pr.sbuf("st_t1", [32, HS4], F32)
    t2 = pr.sbuf("st_t2", [32, HS4], F32)
    pr.copy("V", l16[:], lamn[:])
    re = l16[:][:, 0:HS4]
    im = l16[:][:, HS4:2 * HS4]
    for _ in range(4):
        pr.tt("V", t1[:], re, im, OP.mult)
        pr.tt("V", re, re, re, OP.mult)
        pr.tt("V", t2[:], im, im, OP.mult)
        pr.tt("V", re, re, t2[:], OP.subtract)
        pr.ts("V", im, t1[:], 2.0, None, OP.mult)
    Sre = pr.sbuf("st_Sre", [32, HS4], F32)
    Sim = pr.sbuf("st_Sim", [32, HS4], F32)
    sd = pr.sbuf("st_sd", [32, 2 * HS4], F32)
    SstAll = dr["SstAll"].ap().rearrange("(b s f) -> b s f", b=NB, s=NSPANS)
    for b in range(NB):
        pr.memset("V", Sre[:], 0.0)
        pr.memset("V", Sim[:], 0.0)
        for s in range(NSPANS):
            sv = SstAll[b, s].rearrange("(ri n h) -> ri n h", ri=2, n=NS4)
            pr.dma("P", sv[0], Sre[:])
            pr.dma("P", sv[1], Sim[:])
            rowbase = s * 64 if b == 0 else (7 - s) * 64 + 32
            pr.dma("Y", sd[:], bo.ap()[rowbase:rowbase + 32, :])
            pr.tt("V", t1[:], im, Sim[:], OP.mult)
            pr.tt("V", t2[:], im, Sre[:], OP.mult)
            pr.tt("V", Sre[:], Sre[:], re, OP.mult)
            pr.tt("V", Sre[:], Sre[:], t1[:], OP.subtract)
            pr.tt("V", Sre[:], Sre[:], sd[:][:, 0:HS4], OP.add)
            pr.tt("V", Sim[:], Sim[:], re, OP.mult)
            pr.tt("V", Sim[:], Sim[:], t2[:], OP.add)
            pr.tt("V", Sim[:], Sim[:], sd[:][:, HS4:2 * HS4], OP.add)
    for si in range(2):
        pr.dma_arms("Y", lambda r, si=si: (dr[f"Sst_{si}"].ap(),
                                           SstAll[_span_of_core(r)[si][0],
                                                  _span_of_core(r)[si][1]]))


def emit_halo(pr, dr, xekey, binm, boutm):
    bi = dr[binm]
    bo = dr[boutm]
    for si in range(2):
        xe = dr[xekey.format(si=si)].ap().rearrange("(g p) t -> p g t", p=128)
        bv = bi.ap().rearrange("p (s2 g e) -> p s2 g e", s2=2, g=8)
        pr.dma("P", bv[:, si, :, 0], xe[:, :, 1], allow_slow_non_contiguous=True)
        pr.dma("P", bv[:, si, :, 1], xe[:, :, XE - 2], allow_slow_non_contiguous=True)
    pr.collective("AllGather", OP.bypass, [list(range(NCORES))], [bi.ap()], [bo.ap()])
    bog = bo.ap().rearrange("(r p) (s2 g e) -> r p s2 g e", r=NCORES, p=128, s2=2, g=8)

    def pick(si, edge):
        def f(rank):
            b, s = _span_of_core(rank)[si]
            xe = dr[xekey.format(si=si)].ap().rearrange("(g p) t -> p g t", p=128)
            dst = xe[:, :, 0 if edge == 0 else XE - 1]
            ns = s - 1 if edge == 0 else s + 1
            if 0 <= ns < NSPANS:
                oc, osi = _owner(b, ns)
                src = bog[oc, :, osi, :, 1 if edge == 0 else 0]
            else:
                src = dr["zed"].ap()
            return (dst, src)
        return f
    for si in range(2):
        for edge in range(2):
            pr.dma_arms("P", pick(si, edge), allow_slow_non_contiguous=True)


def build_fused():
    nc = bass.Bass("TRN2", target_bir_lowering=False)
    pr = Prog(nc)
    hi = {}
    for nm, shp in [("WinT", [D, DH]), ("WdtT", [DI, DI]), ("WBT", [DI, NST]),
                    ("WCT", [DI, NST]), ("WoutT", [DI, D]), ("w1T", [D, DH]),
                    ("w2T", [DH, D]), ("normw", [128, 8]), ("cwa", [128, 64]),
                    ("cba", [128, 16]), ("bdt", [128, 16]), ("Alog", [128, 256]),
                    ("Dpar", [128, 16]), ("rscal", [1, 1]), ("cwf", [128, 96]),
                    ("cbf", [128, 32]), ("s4_logdt", [128, 8]), ("s4_Ar", [128, 256]),
                    ("s4_Ai", [128, 256]), ("s4_Cr", [128, 256]), ("s4_Ci", [128, 256]),
                    ("s4_D", [1, 1]), ("ones1x128f", [1, 128]), ("idenf", [128, 128]),
                    ("antif", [128, 128]), ("maskAll", [NSPANS, NE])]:
        hi[nm] = declare_in(nc, nm, shp)
    for si in range(2):
        hi[f"x_{si}"] = declare_in(nc, f"x_{si}", [D, SPAN])
        hi[f"xhalo_{si}"] = declare_in(nc, f"xhalo_{si}", [D, 128])
        hi[f"out_{si}"] = declare_out(nc, f"out_{si}", [D, SPAN])
    dr = {}

    def mkdr(name, shape, dtype=F32):
        dr[name] = nc.dram_tensor(name, shape, dtype)
        return dr[name]
    for si in range(2):
        mkdr(f"xcb_{si}", [DI, SPAN], BF16)
        mkdr(f"sg_{si}", [DI, SPAN], BF16)
        mkdr(f"stat_{si}", [2, SPAN])
        mkdr(f"dt_{si}", [DI, SPAN])
        mkdr(f"Brow_{si}", [NST, SPAN])
        mkdr(f"Crow_{si}", [NST, SPAN])
        mkdr(f"dtp_{si}", [DI, PW])
        mkdr(f"xcbp_{si}", [DI, PW], BF16)
        mkdr(f"sgp_{si}", [DI, PW], BF16)
        mkdr(f"Bp_{si}", [NST, PW])
        mkdr(f"Cp_{si}", [NST, PW])
        mkdr(f"PQ_{si}", [2, 128 * 256 * NE])
        mkdr(f"stent_{si}", [128, 256])
        mkdr(f"facs_{si}", [1, NE])
        mkdr(f"yac_{si}", [DI, PW])
        mkdr(f"ysgp_{si}", [DI, PW], BF16)
        mkdr(f"x2e_{si}", [D, XE])
        mkdr(f"rsc2_{si}", [1, XE])
        mkdr(f"h1_{si}", [DH, XE], BF16)
        mkdr(f"x3_{si}", [D, SPAN])
        mkdr(f"uT_{si}", [128 * NCHS * HS4], BF16)
        mkdr(f"hist_{si}", [NCHS * 2 * NS4 * HS4], BF16)
        mkdr(f"send_{si}", [2 * NS4 * HS4])
        mkdr(f"Sst_{si}", [2 * NS4 * HS4])
        mkdr(f"x4e_{si}", [D, XE])
        mkdr(f"rsc4_{si}", [1, XE])
        mkdr(f"h2_{si}", [DH, XE], BF16)
    mkdr("zDI", [DI, 16])
    mkdr("zDIb", [DI, 16], BF16)
    mkdr("zNST", [NST, 16])
    mkdr("zed", [128, 8])
    mkdr("powG", [128 * 2 * HS4 * NS4], BF16)
    mkdr("powE", [2 * HS4 * NS4 * 128], BF16)
    mkdr("kpad", [HS4 * 256], BF16)
    mkdr("kpadb", [HS4 * 256], BF16)
    mkdr("lamT", [2 * HS4 * NS4])
    mkdr("ch_stentAll", [NSPANS * 128 * 256])
    mkdr("ch_facsAll", [NSPANS * 1 * NE])
    for k in range(NSPANS):
        mkdr(f"ch_cin_{k}", [128, 512])
        mkdr(f"ch_bout_{k}", [128 * NCORES, 512])
    mkdr("SstAll", [NB * NSPANS * 2 * NS4 * HS4])
    mkdr("st_bin", [64, 2048])
    mkdr("st_bout", [64 * NCORES, 2048])
    mkdr("hb2_in", [128, 32])
    mkdr("hb2_out", [128 * NCORES, 32])
    mkdr("hb4_in", [128, 32])
    mkdr("hb4_out", [128 * NCORES, 32])

    A = lambda n: dr[n].ap()

    pr.set_prefix("z_")
    zt = pr.sbuf("zt", [128, 16], F32)
    ztb = pr.sbuf("ztb", [128, 16], BF16)
    pr.memset("V", zt[:], 0.0)
    pr.memset("V", ztb[:], 0.0)
    for g in range(DI // 128):
        pr.dma("P", A("zDI").rearrange("(g p) c -> g p c", p=128)[g], zt[:])
        pr.dma("P", A("zDIb").rearrange("(g p) c -> g p c", p=128)[g], ztb[:])
    pr.dma("P", A("zNST"), zt[:][0:NST, :])
    pr.dma("P", A("zed"), zt[:][:, 0:8])

    pr.phase()
    pr.set_prefix("pp_")
    io = {k: hi[k] for k in ("s4_logdt", "s4_Ar", "s4_Ai", "s4_Cr", "s4_Ci", "s4_D")}
    io["powG"] = A("powG")
    io["powE"] = A("powE")
    io["kpad"] = A("kpad")
    io["lamT"] = A("lamT")
    emit_s4d_prep(pr, io)
    pr.dma("Y", A("kpadb"), A("kpad"))

    pr.phase()
    pr.set_prefix("sa_")
    io = {k: hi[k] for k in ("WinT", "normw", "cwa", "cba", "ones1x128f")}
    for si in range(2):
        io[f"x_{si}"] = hi[f"x_{si}"]
        io[f"xhalo_{si}"] = hi[f"xhalo_{si}"]
        io[f"xcb_{si}"] = A(f"xcb_{si}")
        io[f"sg_{si}"] = A(f"sg_{si}")
        io[f"stat_{si}"] = A(f"stat_{si}")
    w = emit_weights_1a(pr, io)
    for si in range(2):
        emit_stage1a_span(pr, w, io, si)

    pr.phase()
    pr.set_prefix("sb_")
    io = {k: hi[k] for k in ("WdtT", "WBT", "WCT", "bdt")}
    for si in range(2):
        io[f"xcb_{si}"] = A(f"xcb_{si}")
        io[f"dt_{si}"] = A(f"dt_{si}")
        io[f"Brow_{si}"] = A(f"Brow_{si}")
        io[f"Crow_{si}"] = A(f"Crow_{si}")
    w = emit_weights_1b(pr, io)
    for si in range(2):
        emit_stage1b_span(pr, w, io, si)

    pr.phase()
    pr.set_prefix("p1_")
    io = {"Alog": hi["Alog"], "ones1x128f": hi["ones1x128f"],
          "zDI": A("zDI"), "zDIb": A("zDIb"), "zNST": A("zNST")}
    for si in range(2):
        for k in ("dt", "xcb", "Brow", "dtp", "xcbp", "Bp", "PQ"):
            io[f"{k}_{si}"] = A(f"{k}_{si}")
    w = emit_scan_consts(pr, io)
    for si in range(2):
        emit_pass1_span(pr, w, io, si, None, fused=True)

    pr.phase()
    pr.set_prefix("ch_")
    io = {"ones1x128f": hi["ones1x128f"], "maskAll": hi["maskAll"]}
    emit_relay_chain(pr, io, dr)

    pr.phase()
    pr.set_prefix("p2_")
    io = {k: hi[k] for k in ("Alog", "ones1x128f", "Dpar", "WoutT", "rscal")}
    io.update({"zDI": A("zDI"), "zDIb": A("zDIb"), "zNST": A("zNST")})
    for si in range(2):
        for k in ("dt", "xcb", "sg", "Brow", "Crow", "stat", "stent", "facs",
                  "dtp", "xcbp", "sgp", "Bp", "Cp", "yac", "ysgp"):
            io[f"{k}_{si}"] = A(f"{k}_{si}")
        io[f"x_{si}"] = hi[f"x_{si}"]
        io[f"ysg_{si}"] = A(f"ysgp_{si}")
        io[f"x2_{si}"] = A(f"x2e_{si}")[:, 1:XE - 1]
    w = emit_pass2_weights(pr, io)
    for si in range(2):
        emit_pass2_span(pr, w, io, si, None, fused=True)

    pr.phase()
    pr.set_prefix("h2_")
    emit_halo(pr, dr, "x2e_{si}", "hb2_in", "hb2_out")

    for which, xk, hk, rk, outk in (("a", "x2e", "h1", "rsc2", None),
                                    ("b", "x2e", "h1", "rsc2", "x3")):
        pr.phase()
        pr.set_prefix(f"f1{which}_")
        io = {k: hi[k] for k in ("ones1x128f", "idenf", "cwf", "cbf")}
        io["w1T"] = hi["w1T"]
        io["w2T"] = hi["w2T"]
        for si in range(2):
            io[f"x2e_{si}"] = A(f"{xk}_{si}")
            io[f"h_{si}"] = A(f"{hk}_{si}")
            io[f"rsc_{si}"] = A(f"{rk}_{si}")
            if outk:
                io[f"x3_{si}"] = A(f"{outk}_{si}")
        w = emit_ffn_weights(pr, io, which)
        for si in range(2):
            if which == "a":
                emit_ffn_a_span(pr, w, io, si)
            else:
                emit_ffn_b_span(pr, w, io, si)

    pr.phase()
    pr.set_prefix("s6_")
    io = {"idenf": hi["idenf"], "ones1x128f": hi["ones1x128f"],
          "powG": A("powG"), "lamT": A("lamT")}
    for si in range(2):
        io[f"x3_{si}"] = A(f"x3_{si}")
        io[f"uT_{si}"] = A(f"uT_{si}")
        io[f"hist_{si}"] = A(f"hist_{si}")
        io[f"send_{si}"] = A(f"send_{si}")
    w = {}
    w["idenf"] = pr.sbuf("w_idenf", [128, 128], F32)
    w["idenb"] = pr.sbuf("w_idenb", [128, 128], BF16)
    w["onesb"] = pr.sbuf("w_onesb", [128, 1], BF16)
    w["ones1x128f"] = pr.sbuf("w_ones1f", [1, 128], F32)
    pr.dma("Y", w["idenf"][:], io["idenf"])
    pr.dma("Y", w["ones1x128f"][:], io["ones1x128f"])
    pr.copy("V", w["idenb"][:], w["idenf"][:])
    pr.memset("V", w["onesb"][:], 1.0)
    uT = pr.sbuf("s6_uT", [128, 2 * NCHS * HS4], BF16)
    lamn = emit_lamn(pr, w, io, 64, "s6_lamn")
    for si in range(2):
        emit_ln3_span(pr, w, io, si, uT)
    Gsb = emit_s4d_g(pr, w, io, uT)
    emit_s4d_scan(pr, io, lamn, Gsb)

    pr.phase()
    pr.set_prefix("st_")
    io = {"idenf": hi["idenf"], "lamT": A("lamT")}
    emit_stitch_fused(pr, io, dr)

    pr.phase()
    pr.set_prefix("s8_")
    io = {"idenf": hi["idenf"], "antif": hi["antif"], "lamT": A("lamT"),
          "powE": A("powE"), "kpad": A("kpad")}
    for si in range(2):
        io[f"x3_{si}"] = A(f"x3_{si}")
        io[f"uT_{si}"] = A(f"uT_{si}")
        io[f"hist_{si}"] = A(f"hist_{si}")
        io[f"Sst_{si}"] = A(f"Sst_{si}")
        io[f"x4_{si}"] = A(f"x4e_{si}")[:, 1:XE - 1]
    emit_s4d_final(pr, io, dr["kpadb"])

    pr.phase()
    pr.set_prefix("h4_")
    emit_halo(pr, dr, "x4e_{si}", "hb4_in", "hb4_out")

    for which in ("a", "b"):
        pr.phase()
        pr.set_prefix(f"f2{which}_")
        io = {k: hi[k] for k in ("ones1x128f", "idenf", "cwf", "cbf")}
        io["w1T"] = hi["w1T"]
        io["w2T"] = hi["w2T"]
        for si in range(2):
            io[f"x2e_{si}"] = A(f"x4e_{si}")
            io[f"h_{si}"] = A(f"h2_{si}")
            io[f"rsc_{si}"] = A(f"rsc4_{si}")
            io[f"x3_{si}"] = hi[f"out_{si}"]
        w = emit_ffn_weights(pr, io, which)
        for si in range(2):
            if which == "a":
                emit_ffn_a_span(pr, w, io, si)
            else:
                emit_ffn_b_span(pr, w, io, si)
    for e in Prog.ENGINES:
        pr.fence_all(e)
    pr.close()
    return nc


def kernel(**inputs):
    maps = host_prepare(inputs)
    mask = np.stack([span_mask(s) for s in range(NSPANS)])
    for c in range(NCORES):
        maps[c]["maskAll"] = mask
    nc = build_fused()
    res = run_bass_kernel_spmd(nc, maps, core_ids=list(range(NCORES)))
    out = np.empty((NB, L, D), np.float32)
    for c in range(NCORES):
        for si, (b, s) in enumerate(_span_of_core(c)):
            t0 = s * SPAN
            out[b, t0:t0 + SPAN] = np.asarray(res.results[c][f"out_{si}"]) \
                .reshape(8, 128, SPAN).transpose(2, 0, 1).reshape(SPAN, D)
    return out


def _mk_x_e(xg, c, si):
    b, s = _span_of_core(c)[si]
    t0 = s * SPAN
    xe = np.zeros((D, XE), np.float32)
    xe[:, 1:XE - 1] = xg[b, t0:t0 + SPAN].T
    if t0 > 0:
        xe[:, 0] = xg[b, t0 - 1]
    if t0 + SPAN < L:
        xe[:, XE - 1] = xg[b, t0 + SPAN]
    return xe


def _gather_fm(results, key):
    xg = np.zeros((NB, L, D), np.float32)
    for c in range(NCORES):
        for si, (b, s) in enumerate(_span_of_core(c)):
            t0 = s * SPAN
            xg[b, t0:t0 + SPAN] = np.asarray(results[c][key.format(si=si)]) \
                .reshape(8, 128, SPAN).transpose(2, 0, 1).reshape(SPAN, D)
    return xg


def _run_groups(build_fn, maps, key_of_core):
    out = [None] * NCORES
    groups = {}
    for c in range(NCORES):
        groups.setdefault(key_of_core(c), []).append(c)
    for key, cores in groups.items():
        nc = build_fn(cores[0])
        res = run_bass_kernel_spmd(nc, [maps[c] for c in cores], core_ids=cores)
        for i, c in enumerate(cores):
            out[c] = dict(res.results[i])
    return out


def _owner(b, s):
    for c in range(NCORES):
        for si, (bb, ss) in enumerate(_span_of_core(c)):
            if bb == b and ss == s:
                return c, si
    raise KeyError


def _ffn_pass(maps, xg):
    maps5 = []
    for c in range(NCORES):
        m = {k: maps[c][k] for k in ("ones1x128f", "idenf", "cwf", "cbf", "w1T", "w2T")}
        for si in range(2):
            m[f"x2e_{si}"] = _mk_x_e(xg, c, si)
        maps5.append(m)
    res = run_bass_kernel_spmd(build_launch5a(), maps5, core_ids=list(range(NCORES)))
    ra = [dict(r) for r in res.results]
    for c in range(NCORES):
        for si in range(2):
            maps5[c][f"h_{si}"] = ra[c][f"h_{si}"]
    res = run_bass_kernel_spmd(build_launch5b(), maps5, core_ids=list(range(NCORES)))
    rb = [dict(r) for r in res.results]
    return _gather_fm(rb, "x3_{si}")


def kernel_multi(**inputs):
    maps = host_prepare(inputs)
    cores = list(range(NCORES))
    r1 = [dict(r) for r in run_bass_kernel_spmd(build_launch1a(), maps, core_ids=cores).results]
    for c in cores:
        for si in range(2):
            maps[c][f"xcb_{si}"] = r1[c][f"xcb_{si}"]
    rb = [dict(r) for r in run_bass_kernel_spmd(build_launch1b(), maps, core_ids=cores).results]
    for c in cores:
        r1[c].update(rb[c])
        for si in range(2):
            for k in ("dt", "Brow"):
                maps[c][f"{k}_{si}"] = r1[c][f"{k}_{si}"]
    r2 = _run_groups(build_launch2, maps, lambda c: tuple(span_shift(s) for _, s in _span_of_core(c)))
    maps3 = []
    for b in range(NB):
        m = {"ones1x128f": maps[0]["ones1x128f"]}
        for s in range(NSPANS):
            oc, osi = _owner(b, s)
            m[f"PQc_{s}"] = r2[oc][f"PQ_{osi}"]
            m[f"mask_{s}"] = span_mask(s).reshape(1, NE)
        maps3.append(m)
    r3 = [dict(r) for r in run_bass_kernel_spmd(build_launch3(), maps3, core_ids=[0, 1]).results]
    for c in cores:
        for si, (b, s) in enumerate(_span_of_core(c)):
            for k in ("sg", "Crow", "stat"):
                maps[c][f"{k}_{si}"] = r1[c][f"{k}_{si}"]
            maps[c][f"stent_{si}"] = r3[b][f"stent_{s}"]
            maps[c][f"facs_{si}"] = r3[b][f"facs_{s}"]
    r4 = _run_groups(build_launch4, maps, lambda c: tuple(span_shift(s) for _, s in _span_of_core(c)))
    x2 = _gather_fm(r4, "x2_{si}")
    x3 = _ffn_pass(maps, x2)
    rp = [dict(r) for r in run_bass_kernel_spmd(build_s4d_prep_test(), maps, core_ids=cores).results]
    maps6 = []
    for c in cores:
        m = {"idenf": maps[c]["idenf"], "ones1x128f": maps[c]["ones1x128f"],
             "powG": rp[c]["powG"], "lamT": rp[c]["lamT"]}
        for si, (b, s) in enumerate(_span_of_core(c)):
            m[f"x3_{si}"] = np.ascontiguousarray(x3[b, s * SPAN:(s + 1) * SPAN].T)
        maps6.append(m)
    r6 = [dict(r) for r in run_bass_kernel_spmd(build_launch6(), maps6, core_ids=cores).results]
    maps7 = []
    for b in range(NB):
        m = {"idenf": maps[0]["idenf"], "lamT": rp[b]["lamT"]}
        for s in range(NSPANS):
            oc, osi = _owner(b, s)
            m[f"send_{s}"] = r6[oc][f"send_{osi}"]
        maps7.append(m)
    r7 = [dict(r) for r in run_bass_kernel_spmd(build_launch7(), maps7, core_ids=[0, 1]).results]
    maps8 = []
    for c in cores:
        m = {"idenf": maps[c]["idenf"], "antif": maps[c]["antif"],
             "lamT": rp[c]["lamT"], "powE": rp[c]["powE"], "kpad": rp[c]["kpad"]}
        for si, (b, s) in enumerate(_span_of_core(c)):
            m[f"x3_{si}"] = maps6[c][f"x3_{si}"]
            m[f"uT_{si}"] = r6[c][f"uT_{si}"]
            m[f"hist_{si}"] = r6[c][f"hist_{si}"]
            m[f"Sst_{si}"] = r7[b][f"Sst_{s}"]
        maps8.append(m)
    r8 = [dict(r) for r in run_bass_kernel_spmd(build_launch8(), maps8, core_ids=cores).results]
    x4 = _gather_fm(r8, "x4_{si}")
    x5 = _ffn_pass(maps, x4)
    return np.ascontiguousarray(x5)


# version 06
# speedup vs baseline: 1.0572x; 1.0572x over previous
import numpy as np
import math
import concourse.bass as bass
import concourse.mybir as mybir
from concourse.alu_op_type import AluOpType as OP
from concourse.bass_utils import run_bass_kernel_spmd

F32 = mybir.dt.float32
BF16 = mybir.dt.bfloat16
I8 = mybir.dt.int8
AF = mybir.ActivationFunctionType
AX = mybir.AxisListType

D = 1024
DI = 2048
DH = 4096
NST = 16
HS4 = 1024
NS4 = 32
SPAN = 2048
NSPANS = 8
NB = 2
L = 16384
NCORES = 8
TCH = 512

LN_EPS = 1e-5
RMS_EPS = 1e-6

_SC_DT_C = (0.1 + 0.001) / 2.0
_SC_DT_R = (0.1 - 0.001) / 2.0
_SC_DT_S1 = 0.5 / (_SC_DT_R + 1e-8)
_SC_DT_S2 = _SC_DT_R / 0.5
_SC_A_S1 = 0.5 / (5.0 + 1e-8)
_SC_A_S2 = 5.0 / 0.5



class Prog:
    ENGINES = ("V", "S", "P", "T", "Y")

    def __init__(self, nc):
        self.nc = nc
        self.eng = {"V": nc.vector, "S": nc.scalar, "P": nc.gpsimd,
                    "T": nc.tensor, "Y": nc.sync}
        self._ctx = []
        self.prog, self.progv = {}, {}
        for e in self.ENGINES:
            self.prog[e] = self._sem(f"prog{e}")
            self.progv[e] = 0
        self.dsem, self.dsemv = {}, {}
        for e in ("P", "Y", "S", "V"):
            self.dsem[e] = self._sem(f"dma{e}")
            self.dsemv[e] = 0
        self.ccsem = self._sem("ccsem")
        self.ccv = 0
        self.writer = {}
        self.readers = {}
        self.waited = {e: {} for e in self.ENGINES}
        self._aliases = {}
        self._prefix = ""
        self._phase_psums = []

    def _sem(self, name):
        g = self.nc.semaphore(name)
        s = g.__enter__()
        self._ctx.append(g)
        return s

    def set_prefix(self, p):
        self._prefix = p

    def phase(self):
        self.barrier()
        if hasattr(self, "_tiles"):
            self._sb_off = self._sb_base
            self._tiles = {}
            self._tile_off = {}
        for g in reversed(self._phase_psums):
            g.__exit__(None, None, None)
        self._phase_psums = []
        self._psum_tiles = {}

    def sbuf(self, name, shape, dtype, alias=None):
        name = self._prefix + name
        if alias is not None:
            alias = self._prefix + alias
        if not hasattr(self, "_tiles"):
            self._tiles = {}
            self._tile_off = {}
            self._sb_off = (int(self.nc.sbuf_base) + 31) // 32 * 32
            self._sb_base = self._sb_off
            self._sb_top = int(self.nc.sbuf_top)
        if name in self._tiles:
            return self._tiles[name]
        import math
        nb = int(np.prod(shape[1:])) * mybir.dt.size(dtype)
        if alias is not None:
            off = self._tile_off[alias]
            grp = self._aliases.get(alias)
            if grp is None:
                grp = (alias,)
            grp = tuple(set(grp + (name, alias)))
            for m in grp:
                self._aliases[m] = grp
        else:
            off = self._sb_off
            self._sb_off = (off + nb + 31) // 32 * 32
            assert self._sb_off <= self._sb_top, \
                f"SBUF overflow at {name}: {self._sb_off} > {self._sb_top}"
        t = self.nc.alloc_sbuf_tensor_at(name, shape, dtype, offset=off)
        self._tile_off[name] = off
        self._tiles[name] = t
        return t

    def psum(self, name, shape, dtype):
        name = self._prefix + name
        if not hasattr(self, "_psum_tiles"):
            self._psum_tiles = {}
        if name in self._psum_tiles:
            return self._psum_tiles[name]
        g = self.nc.psum_tensor(name, shape, dtype)
        t = g.__enter__()
        self._phase_psums.append(g)
        self._psum_tiles[name] = t
        return t

    def close(self):
        for g in reversed(self._phase_psums):
            g.__exit__(None, None, None)
        self._phase_psums = []
        for g in reversed(self._ctx):
            g.__exit__(None, None, None)

    @staticmethod
    def _key(x):
        if isinstance(x, bass.AP):
            return x.tensor.name
        return x.name

    def _alias_group(self, k):
        return self._aliases.get(k, (k,))

    def _sem_of(self, tag):
        if tag.startswith("d"):
            return self.dsem[tag[1]], tag
        if tag == "cc":
            return self.ccsem, "cc"
        return self.prog[tag], tag

    def _wait(self, e, tag, val):
        if val <= 0:
            return
        w = self.waited[e]
        if w.get(tag, 0) >= val:
            return
        sem, _ = self._sem_of(tag)
        self.eng[e].wait_ge(sem, val)
        w[tag] = val

    def _pre(self, e, reads, writes):
        for x in reads:
            for k in self._alias_group(self._key(x)):
                wr = self.writer.get(k)
                if wr:
                    self._wait(e, wr[0], wr[1])
        for x in writes:
            for k in self._alias_group(self._key(x)):
                wr = self.writer.get(k)
                if wr:
                    self._wait(e, wr[0], wr[1])
                for rt, rv in self.readers.get(k, []):
                    self._wait(e, rt, rv)

    def _post(self, tag, val, reads, writes):
        for x in writes:
            k = self._key(x)
            self.writer[k] = (tag, val)
            self.readers[k] = []
        for x in reads:
            self.readers.setdefault(self._key(x), []).append((tag, val))

    def _emit(self, e, reads, writes, fn):
        self._pre(e, reads, writes)
        inst = fn(self.eng[e])
        self.progv[e] += 1
        inst.then_inc(self.prog[e], 1)
        self._post(e, self.progv[e], reads, writes)
        return inst

    def copy(self, e, out, in_):
        self._emit(e, [in_], [out], lambda g: g.tensor_copy(out, in_))

    def memset(self, e, out, v):
        self._emit(e, [], [out], lambda g: g.memset(out, v))

    def tt(self, e, out, a, b, op):
        self._emit(e, [a, b], [out], lambda g: g.tensor_tensor(out, a, b, op))

    def ts(self, e, out, a, s1, s2, op0, op1=None, extra_reads=()):
        rd = [a] + [s for s in (s1, s2) if isinstance(s, bass.AP)] + list(extra_reads)
        if op1 is None:
            self._emit(e, rd, [out], lambda g: g.tensor_scalar(out, a, s1, None, op0))
        else:
            self._emit(e, rd, [out], lambda g: g.tensor_scalar(out, a, s1, s2, op0, op1))

    def stt(self, e, out, in0, scalar, in1, op0, op1):
        rd = [in0, in1] + ([scalar] if isinstance(scalar, bass.AP) else [])
        self._emit(e, rd, [out], lambda g: g.scalar_tensor_tensor(out, in0, scalar, in1, op0, op1))

    def act(self, e, out, in_, func, bias=None, scale=None, accum_out=None):
        kw = {}
        rd = [in_]
        if bias is not None:
            if isinstance(bias, bass.AP):
                rd.append(bias)
            elif isinstance(bias, float):
                bias = np.float32(bias)
            kw["bias"] = bias
        if scale is not None:
            if isinstance(scale, bass.AP):
                rd.append(scale)
            elif isinstance(scale, float):
                scale = np.float32(scale)
            kw["scale"] = scale
        wr = [out]
        if accum_out is not None:
            kw["accum_out"] = accum_out
            wr.append(accum_out)
        self._emit(e, rd, wr, lambda g: g.activation(out=out, in_=in_, func=func, **kw))

    def red(self, e, out, in_, op, absval=False):
        kw = {}
        if absval:
            kw["apply_absolute_value"] = True
        self._emit(e, [in_], [out], lambda g: g.tensor_reduce(out, in_, axis=AX.X, op=op, **kw))

    def recip(self, e, out, in_):
        self._emit(e, [in_], [out], lambda g: g.reciprocal(out, in_))

    def scan(self, e, out, d0, d1, init, op0, op1):
        rd = [d0, d1] + ([init] if isinstance(init, bass.AP) else [])
        self._emit(e, rd, [out], lambda g: g.tensor_tensor_scan(out, d0, d1, init, op0, op1))

    def select(self, e, out, mask, a, b):
        self._emit(e, [mask, a, b], [out], lambda g: g.select(out, mask, a, b))

    def mm(self, psum, lhsT, rhs, start, stop):
        self._emit("T", [lhsT, rhs], [psum],
                   lambda g: g.matmul(psum, lhsT, rhs, start=start, stop=stop))

    def transpose(self, psum, in_, iden):
        self._emit("T", [in_, iden], [psum], lambda g: g.transpose(psum, in_, iden))

    def dma(self, e, out, in_, **kw):
        self._pre(e, [in_], [out])
        inst = self.eng[e].dma_start(out=out, in_=in_, **kw)
        self.dsemv[e] += 16
        inst.then_inc(self.dsem[e], 16)
        self._post("d" + e, self.dsemv[e], [in_], [out])
        return inst

    def dma_arms(self, e, fn, nranks=NCORES, **kw):
        pairs = [fn(r) for r in range(nranks)]
        reads, writes, seen_r, seen_w = [], [], set(), set()
        for o, i in pairs:
            kr, kw2 = self._key(i), self._key(o)
            if kr not in seen_r:
                seen_r.add(kr)
                reads.append(i)
            if kw2 not in seen_w:
                seen_w.add(kw2)
                writes.append(o)
        self._pre(e, reads, writes)
        eng = self.eng[e]
        for arm in self.nc.switch_core_id(engines=[eng], n=128):
            o, i = pairs[arm.logical % nranks]
            eng.dma_start(out=o, in_=i, **kw).then_inc(self.dsem[e], 16)
        self.dsemv[e] += 16
        self._post("d" + e, self.dsemv[e], reads, writes)

    def dma_t(self, e, out, in_):
        self._pre(e, [in_], [out])
        inst = self.eng[e].dma_start_transpose(out, in_)
        self.dsemv[e] += 16
        inst.then_inc(self.dsem[e], 16)
        self._post("d" + e, self.dsemv[e], [in_], [out])
        return inst

    def collective(self, kind, op, groups, ins, outs):
        self._pre("P", ins, outs)
        inst = self.eng["P"].collective_compute(
            kind, op, replica_groups=groups,
            ins=[a.opt() for a in ins], outs=[a.opt() for a in outs])
        self.ccv += 1
        inst.then_inc(self.ccsem)
        self._post("cc", self.ccv, ins, outs)
        return inst

    def fence_read(self, e, aps):
        self._pre(e, aps, [])

    def fence_all(self, e):
        for k, (tag, val) in list(self.writer.items()):
            self._wait(e, tag, val)

    def barrier(self):
        for e in self.ENGINES:
            for t, v in [("V", self.progv["V"]), ("S", self.progv["S"]),
                         ("P", self.progv["P"]), ("T", self.progv["T"]),
                         ("Y", self.progv["Y"])] +                         [("d" + q, self.dsemv[q]) for q in self.dsemv] +                         [("cc", self.ccv)]:
                self._wait(e, t, v)


def strided_cols(t, start, step, count):
    ap = t[:]
    part = ap.ap[0]
    return bass.AP(ap.tensor, ap.offset + start, [list(part), [step, count]])


def bc8(ap2d, G):
    return ap2d.unsqueeze(1).broadcast_to([ap2d.shape[0], G, ap2d.shape[1]])



def _span_of_core(c):
    return [(0, c), (1, NSPANS - 1 - c)]


def _fm(a):
    return np.ascontiguousarray(np.asarray(a).T)


def host_prepare(inputs):
    x = np.asarray(inputs["L0_attn_a_x"])
    base = {}
    base["WinT"] = _fm(inputs["L0_attn_a_W_in"])
    base["WdtT"] = _fm(inputs["L0_attn_a_W_dt"])
    base["WBT"] = _fm(inputs["L0_attn_a_W_B"])
    base["WCT"] = _fm(inputs["L0_attn_a_W_C"])
    base["WoutT"] = _fm(inputs["L0_attn_a_W_out"])
    base["w1T"] = _fm(inputs["L0_ffn_w1"])
    base["w2T"] = _fm(inputs["L0_ffn_w2"])
    base["normw"] = np.ascontiguousarray(
        np.asarray(inputs["L0_attn_a_norm_w"]).reshape(8, 128).T)
    base["cwa"] = np.ascontiguousarray(
        np.asarray(inputs["L0_attn_a_conv_w"]).reshape(16, 128, 4).transpose(1, 0, 2).reshape(128, 64))
    base["cba"] = np.ascontiguousarray(
        np.asarray(inputs["L0_attn_a_conv_b"]).reshape(16, 128).T)
    base["bdt"] = np.ascontiguousarray(
        np.asarray(inputs["L0_attn_a_b_dt"]).reshape(16, 128).T)
    base["Alog"] = np.ascontiguousarray(
        np.asarray(inputs["L0_attn_a_A_log"]).reshape(16, 128, 16).transpose(1, 0, 2).reshape(128, 256))
    base["Dpar"] = np.ascontiguousarray(
        np.asarray(inputs["L0_attn_a_D_param"]).reshape(16, 128).T)
    base["rscal"] = np.asarray(inputs["L0_attn_a_residual_scale"]).reshape(1, 1)
    base["cwf"] = np.ascontiguousarray(
        np.asarray(inputs["L0_ffn_conv_w"]).reshape(32, 128, 3).transpose(1, 0, 2).reshape(128, 96))
    base["cbf"] = np.ascontiguousarray(
        np.asarray(inputs["L0_ffn_conv_b"]).reshape(32, 128).T)
    base["s4_logdt"] = np.asarray(inputs["L0_attn_b_log_dt"]).reshape(8, 128).T.copy()
    base["s4_Ar"] = np.ascontiguousarray(
        np.asarray(inputs["L0_attn_b_A_real"]).reshape(8, 128, 32).transpose(1, 0, 2).reshape(128, 256))
    base["s4_Ai"] = np.ascontiguousarray(
        np.asarray(inputs["L0_attn_b_A_imag"]).reshape(8, 128, 32).transpose(1, 0, 2).reshape(128, 256))
    base["s4_Cr"] = np.ascontiguousarray(
        np.asarray(inputs["L0_attn_b_C_real"]).reshape(8, 128, 32).transpose(1, 0, 2).reshape(128, 256))
    base["s4_Ci"] = np.ascontiguousarray(
        np.asarray(inputs["L0_attn_b_C_imag"]).reshape(8, 128, 32).transpose(1, 0, 2).reshape(128, 256))
    base["s4_D"] = np.asarray(inputs["L0_attn_b_D"]).reshape(1, 1)
    base["ones1x128f"] = np.ones((1, 128), np.float32)
    base["idenf"] = np.eye(128, dtype=np.float32)
    base["antif"] = np.ascontiguousarray(np.eye(128, dtype=np.float32)[::-1])

    maps = [dict(base) for _ in range(NCORES)]
    for c in range(NCORES):
        for si, (b, s) in enumerate(_span_of_core(c)):
            t0 = s * SPAN
            maps[c][f"x_{si}"] = np.ascontiguousarray(x[b, t0:t0 + SPAN].T)
            halo = np.zeros((D, 128), np.float32)
            if t0 > 0:
                halo[:, :] = x[b, t0 - 128:t0].T
            maps[c][f"xhalo_{si}"] = halo
    return maps


def ml_bf16():
    import ml_dtypes
    return ml_dtypes.bfloat16


def host_assemble(results):
    out = np.empty((NB, L, D), np.float32)
    for c in range(NCORES):
        for si, (b, s) in enumerate(_span_of_core(c)):
            t0 = s * SPAN
            out[b, t0:t0 + SPAN] = np.asarray(results[c][f"out_{si}"]).T
    return out



def declare_in(nc, name, shape, dtype=F32):
    return nc.dram_tensor(name, shape, dtype, kind="ExternalInput").ap()


def declare_out(nc, name, shape, dtype=F32):
    return nc.dram_tensor(name, shape, dtype, kind="ExternalOutput").ap()



def emit_weights_1a(pr, io):
    w = {}
    w["normw"] = pr.sbuf("w_normw", [128, 8], F32)
    w["cwa"] = pr.sbuf("w_cwa", [128, 64], F32)
    w["cba"] = pr.sbuf("w_cba", [128, 16], F32)
    w["onesb"] = pr.sbuf("w_onesb", [128, 1], BF16)
    w["ones1x128f"] = pr.sbuf("w_ones1f", [1, 128], F32)
    pr.memset("V", w["onesb"][:], 1.0)
    w["Winb"] = pr.sbuf("w_Winb", [128, 8 * DH], BF16)
    wtmp = pr.sbuf("s1_xch", [128, 8 * TCH], F32)
    for k in ("normw", "cwa", "cba", "ones1x128f"):
        pr.dma("Y", w[k][:], io[k])
    WinT = io["WinT"].rearrange("(g p) c -> p g c", p=128)
    Winb3 = w["Winb"][:].rearrange("p (g c) -> p g c", g=8)
    for g in range(8):
        pr.dma("Y", wtmp[:, :DH], WinT[:, g, :])
        pr.ts("V", Winb3[:, g, :], wtmp[:, :DH], w["normw"][:, g:g + 1], None, OP.mult)
    return w


def emit_weights_1b(pr, io):
    w = {}
    w["bdt"] = pr.sbuf("w_bdt", [128, 16], F32)
    w["Wdtb"] = pr.sbuf("w_Wdtb", [128, 16 * DI], BF16)
    w["WBb"] = pr.sbuf("w_WBb", [128, 16 * NST], BF16)
    w["WCb"] = pr.sbuf("w_WCb", [128, 16 * NST], BF16)
    wtmp = pr.sbuf("s1_wtmp", [128, DI + 2 * NST], F32)
    pr.dma("Y", w["bdt"][:], io["bdt"])
    WdtT = io["WdtT"].rearrange("(g p) c -> p g c", p=128)
    Wdtb3 = w["Wdtb"][:].rearrange("p (g c) -> p g c", g=16)
    WBT = io["WBT"].rearrange("(g p) c -> p g c", p=128)
    WCT = io["WCT"].rearrange("(g p) c -> p g c", p=128)
    WBb3 = w["WBb"][:].rearrange("p (g c) -> p g c", g=16)
    WCb3 = w["WCb"][:].rearrange("p (g c) -> p g c", g=16)
    for g in range(16):
        pr.dma("Y", wtmp[:, :DI], WdtT[:, g, :])
        pr.copy("V", Wdtb3[:, g, :], wtmp[:, :DI])
        pr.dma("Y", wtmp[:, DI:DI + NST], WBT[:, g, :])
        pr.copy("V", WBb3[:, g, :], wtmp[:, DI:DI + NST])
        pr.dma("Y", wtmp[:, DI + NST:DI + 2 * NST], WCT[:, g, :])
        pr.copy("V", WCb3[:, g, :], wtmp[:, DI + NST:DI + 2 * NST])
    return w


def emit_stage1a_span(pr, w, io, si):
    NCH = SPAN // TCH
    xr = io[f"x_{si}"].rearrange("(g p) t -> p g t", p=128)
    xh = io[f"xhalo_{si}"].rearrange("(g p) t -> p g t", p=128)
    xcbr = io[f"xcb_{si}"].rearrange("(g p) t -> p g t", p=128)
    sgr = io[f"sg_{si}"].rearrange("(g p) t -> p g t", p=128)

    xch = pr.sbuf("s1_xch", [128, 8 * TCH], F32)
    xbch = pr.sbuf("s1_xbch", [128, 8 * TCH], BF16)
    bsc = pr.sbuf("s1_bsc", [128, 8 * TCH], BF16)
    ps_sum = pr.psum("s1_ps_sum", [1, TCH], F32)
    ps_sq = pr.psum("s1_ps_sq", [1, TCH], F32)
    rows = pr.sbuf("s1_rows", [1, 7 * TCH], F32)
    r_m = rows[:][:, 0 * TCH:1 * TCH]
    r_v = rows[:][:, 1 * TCH:2 * TCH]
    r_t = rows[:][:, 2 * TCH:3 * TCH]
    r_q = rows[:][:, 3 * TCH:4 * TCH]
    r_r1 = rows[:][:, 4 * TCH:5 * TCH]
    r_r = rows[:][:, 5 * TCH:6 * TCH]
    r_mr = rows[:][:, 6 * TCH:7 * TCH]
    ps_bc1 = pr.psum("s1_ps_bc1", [128, TCH], F32)
    ps_bc2 = pr.psum("s1_ps_bc2", [128, TCH], F32)
    ps_mm = pr.psum("s1_ps_mm", [128, TCH], F32)
    scr32 = pr.sbuf("s1_scr32", [128, 16 * TCH], F32)
    xbr = pr.sbuf("s1_xbr", [128, 16 * (TCH + 3)], F32)
    xcbt = pr.sbuf("s1_xcbt", [128, 16 * TCH], BF16)

    Winb3 = w["Winb"][:].rearrange("p (g c) -> p g c", g=8)
    cwa3 = w["cwa"][:].rearrange("p (g k) -> p g k", g=16)
    xbr3 = xbr[:].rearrange("p (g t) -> p g t", g=16)
    xcf3 = scr32[:].rearrange("p (g t) -> p g t", g=16)
    xcbt3 = xcbt[:].rearrange("p (g t) -> p g t", g=16)

    def xn_of(ci):
        T = TCH if ci >= 0 else 128
        src = xr[:, :, ci * TCH:ci * TCH + T] if ci >= 0 else xh[:, :, :]
        xch_v = xch[:, :8 * T].rearrange("p (g t) -> p g t", g=8)
        xb_v = xbch[:, :8 * T].rearrange("p (g t) -> p g t", g=8)
        sq_v = bsc[:, :8 * T].rearrange("p (g t) -> p g t", g=8)
        pr.dma("Y", xch_v, src)
        pr.copy("V", xbch[:, :8 * T], xch[:, :8 * T])
        pr.act("S", bsc[:, :8 * T], xbch[:, :8 * T], AF.Square)
        for g in range(8):
            pr.mm(ps_sum[:1, :T], w["onesb"][:], xb_v[:, g, :], start=(g == 0), stop=(g == 7))
        for g in range(8):
            pr.mm(ps_sq[:1, :T], w["onesb"][:], sq_v[:, g, :], start=(g == 0), stop=(g == 7))
        pr.ts("V", r_m[:, :T], ps_sum[:1, :T], 1.0 / D, None, OP.mult)
        pr.ts("V", r_v[:, :T], ps_sq[:1, :T], 1.0 / D, None, OP.mult)
        pr.tt("V", r_t[:, :T], r_m[:, :T], r_m[:, :T], OP.mult)
        pr.tt("V", r_v[:, :T], r_v[:, :T], r_t[:, :T], OP.subtract)
        pr.act("S", r_q[:, :T], r_v[:, :T], AF.Ln, bias=float(LN_EPS))
        pr.act("S", r_r1[:, :T], r_q[:, :T], AF.Exp, scale=-0.5)
        pr.act("S", r_t[:, :T], r_v[:, :T], AF.Ln, scale=float(1.0 + RMS_EPS),
               bias=float(RMS_EPS * LN_EPS))
        pr.act("S", r_r[:, :T], r_t[:, :T], AF.Exp, scale=-0.5)
        pr.tt("V", r_mr[:, :T], r_m[:, :T], r_r[:, :T], OP.mult)
        if ci >= 0:
            st = io[f"stat_{si}"]
            pr.dma("P", st[0:1, ci * TCH:ci * TCH + T], r_m[:, :T])
            pr.dma("P", st[1:2, ci * TCH:ci * TCH + T], r_r1[:, :T])
        pr.mm(ps_bc1[:, :T], w["ones1x128f"][:], r_r[:, :T], start=True, stop=True)
        pr.mm(ps_bc2[:, :T], w["ones1x128f"][:], r_mr[:, :T], start=True, stop=True)
        t1_v = scr32[:, :8 * T].rearrange("p (g t) -> p g t", g=8)
        pr.tt("V", t1_v, xch_v, bc8(ps_bc1[:, :T], 8), OP.mult)
        pr.tt("V", bsc[:, :8 * T].rearrange("p (g t) -> p g t", g=8), t1_v,
              bc8(ps_bc2[:, :T], 8), OP.subtract)
        return T

    for ci in range(-1, NCH):
        T = xn_of(ci)
        xnb3 = bsc[:, :8 * T].rearrange("p (g t) -> p g t", g=8)
        if ci < 0:
            for mc in range(16):
                for kc in range(8):
                    pr.mm(ps_mm[:, :T], Winb3[:, kc, mc * 128:(mc + 1) * 128], xnb3[:, kc, :],
                          start=(kc == 0), stop=(kc == 7))
                pr.copy("V", xbr3[:, mc, 0:3], ps_mm[:, T - 3:T])
            continue
        for mc in range(32):
            for kc in range(8):
                pr.mm(ps_mm[:, :T], Winb3[:, kc, mc * 128:(mc + 1) * 128], xnb3[:, kc, :],
                      start=(kc == 0), stop=(kc == 7))
            if mc < 16:
                pr.copy("V", xbr3[:, mc, 3:3 + T], ps_mm[:, :T])
            else:
                pr.act("S", xcbt3[:, mc - 16, :T], ps_mm[:, :T], AF.Silu)
        pr.dma("P", sgr[:, :, ci * TCH:(ci + 1) * TCH], xcbt3)
        for g in range(16):
            pr.ts("V", xcf3[:, g, :], xbr3[:, g, 3:3 + TCH], cwa3[:, g, 3:4],
                  w["cba"][:, g:g + 1], OP.mult, OP.add)
        for k in (2, 1, 0):
            for g in range(16):
                pr.stt("V", xcf3[:, g, :], xbr3[:, g, k:k + TCH], cwa3[:, g, k:k + 1],
                       xcf3[:, g, :], OP.mult, OP.add)
        pr.act("S", scr32[:], scr32[:], AF.Silu)
        pr.copy("V", xcbt[:], scr32[:])
        pr.dma("P", xcbr[:, :, ci * TCH:(ci + 1) * TCH], xcbt3)
        for g in range(16):
            pr.copy("P", xbr3[:, g, 0:3], xbr3[:, g, TCH:TCH + 3])


def emit_stage1b_span(pr, w, io, si):
    NCH = SPAN // TCH
    dtr = io[f"dt_{si}"].rearrange("(g p) t -> p g t", p=128)
    xcbr = io[f"xcb_{si}"].rearrange("(g p) t -> p g t", p=128)

    xcbt = pr.sbuf("s1_xcbt", [128, 16 * TCH], BF16)
    scr = pr.sbuf("s1_scr", [128, TCH], F32)
    spa = pr.sbuf("s1_spa", [128, 16 * TCH], F32)
    spr = pr.sbuf("s1_spr", [128, 16 * TCH], F32)
    dtst = pr.sbuf("s1_dtst", [128, 16 * TCH], F32)
    bst = pr.sbuf("s1_bst", [16, 2 * TCH], F32)
    ps_mm2 = [pr.psum(f"sb_ps_mm{i}", [128, TCH], F32) for i in range(2)]
    ps_bcn = pr.psum("s1_ps_bcn", [16, TCH], F32)
    ps_ccn = pr.psum("s1_ps_ccn", [16, TCH], F32)

    Wdtb3 = w["Wdtb"][:].rearrange("p (g c) -> p g c", g=16)
    WBb3 = w["WBb"][:].rearrange("p (g c) -> p g c", g=16)
    WCb3 = w["WCb"][:].rearrange("p (g c) -> p g c", g=16)
    xcbt3 = xcbt[:].rearrange("p (g t) -> p g t", g=16)
    spa3 = spa[:].rearrange("p (g t) -> p g t", g=16)
    spr3 = spr[:].rearrange("p (g t) -> p g t", g=16)

    for ci in range(NCH):
        pr.dma("Y", xcbt3, xcbr[:, :, ci * TCH:(ci + 1) * TCH])
        for mc in range(16):
            ps_mm = ps_mm2[mc % 2]
            for kc in range(16):
                pr.mm(ps_mm[:, :TCH], Wdtb3[:, kc, mc * 128:(mc + 1) * 128],
                      xcbt3[:, kc, :], start=(kc == 0), stop=(kc == 15))
            pr.ts("V", scr[:, :TCH], ps_mm[:, :TCH], w["bdt"][:, mc:mc + 1], None, OP.add)
            pr.ts("V", spr3[:, mc, :], scr[:, :TCH], 0.0, None, OP.max)
            pr.stt("V", spa3[:, mc, :], spr3[:, mc, :], -2.0, scr[:, :TCH],
                   OP.mult, OP.add)
        pr.act("S", spa[:], spa[:], AF.Exp)
        pr.act("S", spa[:], spa[:], AF.Ln, bias=1.0)
        pr.tt("V", dtst[:], spa[:], spr[:], OP.add)
        pr.act("S", dtst[:], dtst[:], AF.Tanh, scale=float(_SC_DT_S1),
               bias=float(-_SC_DT_C * _SC_DT_S1))
        pr.ts("V", dtst[:], dtst[:], float(_SC_DT_S2), float(_SC_DT_C), OP.mult, OP.add)
        pr.dma("P", dtr[:, :, ci * TCH:(ci + 1) * TCH],
               dtst[:].rearrange("p (g t) -> p g t", g=16))
        for kc in range(16):
            pr.mm(ps_bcn[:, :TCH], WBb3[:, kc, :], xcbt3[:, kc, :],
                  start=(kc == 0), stop=(kc == 15))
        for kc in range(16):
            pr.mm(ps_ccn[:, :TCH], WCb3[:, kc, :], xcbt3[:, kc, :],
                  start=(kc == 0), stop=(kc == 15))
        pr.copy("V", bst[:, :TCH], ps_bcn[:, :TCH])
        pr.copy("V", bst[:, TCH:2 * TCH], ps_ccn[:, :TCH])
        pr.dma("P", io[f"Brow_{si}"][:, ci * TCH:(ci + 1) * TCH], bst[:, :TCH])
        pr.dma("P", io[f"Crow_{si}"][:, ci * TCH:(ci + 1) * TCH], bst[:, TCH:2 * TCH])


def _declare_stage1_io(nc, which):
    io = {}
    if which == "a":
        io["WinT"] = declare_in(nc, "WinT", [D, DH])
        io["normw"] = declare_in(nc, "normw", [128, 8])
        io["cwa"] = declare_in(nc, "cwa", [128, 64])
        io["cba"] = declare_in(nc, "cba", [128, 16])
        io["ones1x128f"] = declare_in(nc, "ones1x128f", [1, 128])
    else:
        io["WdtT"] = declare_in(nc, "WdtT", [DI, DI])
        io["WBT"] = declare_in(nc, "WBT", [DI, NST])
        io["WCT"] = declare_in(nc, "WCT", [DI, NST])
        io["bdt"] = declare_in(nc, "bdt", [128, 16])
    for si in range(2):
        if which == "a":
            io[f"x_{si}"] = declare_in(nc, f"x_{si}", [D, SPAN])
            io[f"xhalo_{si}"] = declare_in(nc, f"xhalo_{si}", [D, 128])
            io[f"xcb_{si}"] = declare_out(nc, f"xcb_{si}", [DI, SPAN // 2], F32).bitcast(BF16)
            io[f"sg_{si}"] = declare_out(nc, f"sg_{si}", [DI, SPAN // 2], F32).bitcast(BF16)
            io[f"stat_{si}"] = declare_out(nc, f"stat_{si}", [2, SPAN])
        else:
            io[f"xcb_{si}"] = declare_in(nc, f"xcb_{si}", [DI, SPAN // 2], F32).bitcast(BF16)
            io[f"dt_{si}"] = declare_out(nc, f"dt_{si}", [DI, SPAN])
            io[f"Brow_{si}"] = declare_out(nc, f"Brow_{si}", [NST, SPAN])
            io[f"Crow_{si}"] = declare_out(nc, f"Crow_{si}", [NST, SPAN])
    return io


def build_launch1a():
    nc = bass.Bass("TRN2", target_bir_lowering=False)
    pr = Prog(nc)
    io = _declare_stage1_io(nc, "a")
    w = emit_weights_1a(pr, io)
    for si in range(2):
        emit_stage1a_span(pr, w, io, si)
    pr.fence_all("P")
    pr.close()
    return nc


def build_launch1b():
    nc = bass.Bass("TRN2", target_bir_lowering=False)
    pr = Prog(nc)
    io = _declare_stage1_io(nc, "b")
    w = emit_weights_1b(pr, io)
    for si in range(2):
        emit_stage1b_span(pr, w, io, si)
    pr.fence_all("P")
    pr.close()
    return nc



PW = 2064
NE = 206


def span_shift(s):
    t0 = SPAN * s
    o = (10 - (t0 % 10)) % 10
    return 10 - o


def span_mask(s):
    s0 = span_shift(s)
    m = np.ones(NE, np.float32)
    for e in range(NE):
        c = 10 * (e + 1) - s0
        if 0 <= c < SPAN:
            m[e] = 0.0
    return m


def emit_scan_consts(pr, io):
    w = {}
    w["Alog"] = pr.sbuf("sc_Alog", [128, 256], F32)
    w["A16"] = pr.sbuf("sc_A16", [128, 256], F32)
    w["onesf"] = pr.sbuf("sc_onesf", [128, 1], F32)
    w["ones1x128f"] = pr.sbuf("w_ones1f", [1, 128], F32)
    pr.memset("V", w["onesf"][:], 1.0)
    pr.dma("Y", w["Alog"][:], io["Alog"])
    pr.dma("Y", w["ones1x128f"][:], io["ones1x128f"])
    pr.act("S", w["A16"][:], w["Alog"][:], AF.Tanh, scale=0.1)
    pr.act("S", w["A16"][:], w["A16"][:], AF.Exp, scale=10.0)
    pr.ts("V", w["A16"][:], w["A16"][:], float(-1.0 / 16.0), None, OP.mult)
    return w


def _S0of(si):
    def f(r):
        return span_shift(_span_of_core(r)[si][1])
    return f


def _prestage_pads(pr, io, si, specs):
    S0 = _S0of(si)
    for dst, srckey, ztag in specs:
        srcv = io[srckey]
        z = io[ztag]
        pr.dma_arms("Y", lambda r: (dst[:, S0(r):S0(r) + SPAN], srcv))
        pr.dma_arms("Y", lambda r: (dst[:, 0:S0(r)], z[:, 0:S0(r)]))
        pr.dma_arms("Y", lambda r: (dst[:, S0(r) + SPAN:PW], z[:, 0:PW - SPAN - S0(r)]))


def emit_pass1_span(pr, w, io, si, s_global, fused=False):
    s0 = span_shift(s_global) if not fused else None
    if fused:
        _prestage_pads(pr, io, si, [(io[f"dtp_{si}"], f"dt_{si}", "zDI"),
                                    (io[f"xcbp_{si}"], f"xcb_{si}", "zDIb"),
                                    (io[f"Bp_{si}"], f"Brow_{si}", "zNST")])
        dtr = io[f"dtp_{si}"].rearrange("(g p) t -> p g t", p=128)
        xcbr = io[f"xcbp_{si}"].rearrange("(g p) t -> p g t", p=128)
    else:
        dtr = io[f"dt_{si}"].rearrange("(g p) t -> p g t", p=128)
        xcbr = io[f"xcb_{si}"].rearrange("(g p) t -> p g t", p=128)
    PQ = io[f"PQ_{si}"].rearrange("w (p dg n e) -> w p dg n e", p=128, dg=16, n=16)

    NBLK = 4
    dt_db = [pr.sbuf(f"p1_dt{i}", [128, PW], F32) for i in range(2)]
    xcf_dg = pr.sbuf("p1_xcf", [128, PW], F32)
    wt_dg = pr.sbuf("p1_wt", [128, PW], F32)
    xcb_db = [pr.sbuf(f"p1_xcb{i}", [128, PW], BF16) for i in range(2)]
    Bbc = pr.sbuf("p1_Bbc", [128, NBLK * PW], F32)
    dA = pr.sbuf("p1_dA", [128, PW], F32)
    uu = pr.sbuf("p1_u", [128, PW], F32)
    qc = pr.sbuf("p1_qc", [128, PW], F32)
    lam = pr.sbuf("p1_lam", [128, NE], F32)
    pst = pr.sbuf("p1_pst", [128, 16 * NE], F32)
    qst = pr.sbuf("p1_qst", [128, 16 * NE], F32)
    qce = pr.sbuf("p1_qce", [128, NE + 1], F32)

    Bbc3 = Bbc[:].rearrange("p (nb t) -> p nb t", nb=NBLK)
    pst3 = pst[:].rearrange("p (k e) -> p k e", k=16)
    qst3 = qst[:].rearrange("p (k e) -> p k e", k=16)

    for i in range(2):
        pr.memset("V", dt_db[i][:], 0.0)
        pr.memset("V", xcb_db[i][:], 0.0)
    pr.memset("V", Bbc[:], 0.0)

    for nb in range(16 // NBLK):
        for j in range(NBLK):
            n = nb * NBLK + j
            if fused:
                pr.dma("Y", Bbc3[:, j, :], io[f"Bp_{si}"][n:n + 1, :].broadcast_to([128, PW]))
            else:
                pr.dma("Y", Bbc3[:, j, s0:s0 + SPAN],
                       io[f"Brow_{si}"][n:n + 1, :].broadcast_to([128, SPAN]))
        for dg in range(16):
            dt_dg = dt_db[dg % 2]
            xcb_dg = xcb_db[dg % 2]
            if fused:
                pr.dma("Y", dt_dg[:], dtr[:, dg, :])
                pr.dma("Y", xcb_dg[:], xcbr[:, dg, :])
            else:
                pr.dma("Y", dt_dg[:, s0:s0 + SPAN], dtr[:, dg, :])
                pr.dma("Y", xcb_dg[:, s0:s0 + SPAN], xcbr[:, dg, :])
            pr.copy("V", xcf_dg[:], xcb_dg[:])
            pr.tt("V", wt_dg[:], dt_dg[:], xcf_dg[:], OP.mult)
            for j in range(NBLK):
                n = nb * NBLK + j
                pair = dg * 16 + n
                k16 = (dg % 4) * NBLK + j
                acol = w["A16"][:, pair:pair + 1]
                pr.act("S", dA[:], dt_dg[:], AF.Tanh, scale=acol)
                mid = dA[:, 1:1 + NE * 10].rearrange("p (k t) -> p k t", t=10)
                pr.red("V", lam[:], mid, OP.add)
                pr.act("S", dA[:], dA[:], AF.Exp, scale=16.0)
                pr.tt("V", uu[:], wt_dg[:], Bbc3[:, j, :], OP.mult)
                pr.scan("V", qc[:], dA[:], uu[:], 0.0, OP.mult, OP.add)
                pr.act("S", pst3[:, k16, :], lam[:], AF.Exp, scale=16.0)
                pr.memset("V", qce[:, 0:1], 0.0)
                pr.copy("V", qce[:, 1:1 + NE], strided_cols(qc, 10, 10, NE))
                pr.tt("V", qst3[:, k16, :], pst3[:, k16, :], qce[:, 0:NE], OP.mult)
                pr.tt("V", qst3[:, k16, :], qce[:, 1:1 + NE], qst3[:, k16, :], OP.subtract)
            if dg % 4 == 3:
                dg0 = dg - 3
                nb0 = nb * NBLK
                pr.dma("Y", PQ[0, :, dg0:dg0 + 4, nb0:nb0 + NBLK, :],
                       pst3.rearrange("p (dgp nj) e -> p dgp nj e", dgp=4))
                pr.dma("Y", PQ[1, :, dg0:dg0 + 4, nb0:nb0 + NBLK, :],
                       qst3.rearrange("p (dgp nj) e -> p dgp nj e", dgp=4))


def _declare_pass1_io(nc):
    io = {}
    io["Alog"] = declare_in(nc, "Alog", [128, 256])
    io["ones1x128f"] = declare_in(nc, "ones1x128f", [1, 128])
    for si in range(2):
        io[f"dt_{si}"] = declare_in(nc, f"dt_{si}", [DI, SPAN])
        io[f"xcb_{si}"] = declare_in(nc, f"xcb_{si}", [DI, SPAN // 2], F32).bitcast(BF16)
        io[f"Brow_{si}"] = declare_in(nc, f"Brow_{si}", [NST, SPAN])
        io[f"PQ_{si}"] = declare_out(nc, f"PQ_{si}", [2, 128 * 256 * NE])
    return io


def build_launch2(core):
    nc = bass.Bass("TRN2", target_bir_lowering=False)
    pr = Prog(nc)
    io = _declare_pass1_io(nc)
    w = emit_scan_consts(pr, io)
    for si, (b, s) in enumerate(_span_of_core(core)):
        emit_pass1_span(pr, w, io, si, s)
    pr.fence_all("P")
    pr.close()
    return nc



def emit_chain(pr, io, nspans=NSPANS):
    onesf = pr.sbuf("ch_onesf", [128, 1], F32)
    ones1x128f = pr.sbuf("w_ones1f", [1, 128], F32)
    pr.memset("V", onesf[:], 1.0)
    pr.dma("Y", ones1x128f[:], io["ones1x128f"])

    EB = 16
    state = pr.sbuf("ch_state", [128, 256], F32)
    hpre = pr.sbuf("ch_hpre", [128, 256], F32)
    sqscr = pr.sbuf("ch_sq", [128, 256], F32)
    partials = pr.sbuf("ch_part", [128, 1], F32)
    pblk = pr.sbuf("ch_pblk", [128, 256 * EB], F32)
    qblk = pr.sbuf("ch_qblk", [128, 256 * EB], F32)
    rows = pr.sbuf("ch_rows", [1, 4], F32)
    facrow = pr.sbuf("ch_facrow", [1, NE], F32)
    maskrow = pr.sbuf("ch_mask", [1, NE], F32)
    faccol = pr.sbuf("ch_faccol", [128, 1], F32)
    ps_n = pr.psum("ch_ps_n", [1, 1], F32)
    ps_f = pr.psum("ch_ps_f", [128, 1], F32)

    pr.memset("V", state[:], 0.0)
    pb3 = pblk[:].rearrange("p (pr e) -> p pr e", e=EB)
    qb3 = qblk[:].rearrange("p (pr e) -> p pr e", e=EB)

    for s in range(nspans):
        PQ = io[f"PQc_{s}"].rearrange("w (p pr e) -> w p pr e", p=128, pr=256)
        pr.dma("P", io[f"stent_{s}"], state[:])
        pr.dma("Y", maskrow[:], io[f"mask_{s}"])
        for eb in range((NE + EB - 1) // EB):
            e0 = eb * EB
            ne = min(EB, NE - e0)
            pr.dma("Y", pb3[:, :, 0:ne], PQ[0, :, :, e0:e0 + ne])
            pr.dma("Y", qb3[:, :, 0:ne], PQ[1, :, :, e0:e0 + ne])
            for el in range(ne):
                e = e0 + el
                Pv = pb3[:, :, el]
                Qv = qb3[:, :, el]
                pr.tt("V", hpre[:], Pv, state[:], OP.mult)
                pr.tt("V", hpre[:], hpre[:], Qv, OP.add)
                pr.act("S", sqscr[:], hpre[:], AF.Square, accum_out=partials[:])
                pr.mm(ps_n[:1, :1], onesf[:], partials[:], start=True, stop=True)
                pr.act("S", rows[:, 0:1], ps_n[:1, :1], AF.Sqrt)
                pr.ts("V", rows[:, 1:2], rows[:, 0:1], 1e-8, None, OP.add)
                pr.recip("V", rows[:, 2:3], rows[:, 1:2])
                pr.ts("V", rows[:, 3:4], rows[:, 2:3], 20.0, 1.0, OP.mult, OP.min)
                pr.tt("V", rows[:, 3:4], rows[:, 3:4], maskrow[:, e:e + 1], OP.max)
                pr.copy("V", facrow[:, e:e + 1], rows[:, 3:4])
                pr.mm(ps_f[:, :1], ones1x128f[:], rows[:, 3:4], start=True, stop=True)
                pr.copy("V", faccol[:], ps_f[:, :1])
                pr.ts("V", state[:], hpre[:], faccol[:], None, OP.mult)
        pr.dma("P", io[f"facs_{s}"], facrow[:])


def _declare_chain_io(nc, nspans=NSPANS):
    io = {"ones1x128f": declare_in(nc, "ones1x128f", [1, 128])}
    for s in range(nspans):
        io[f"PQc_{s}"] = declare_in(nc, f"PQc_{s}", [2, 128 * 256 * NE])
        io[f"mask_{s}"] = declare_in(nc, f"mask_{s}", [1, NE])
        io[f"stent_{s}"] = declare_out(nc, f"stent_{s}", [128, 256])
        io[f"facs_{s}"] = declare_out(nc, f"facs_{s}", [1, NE])
    return io


def build_launch3():
    nc = bass.Bass("TRN2", target_bir_lowering=False)
    pr = Prog(nc)
    io = _declare_chain_io(nc)
    emit_chain(pr, io)
    pr.fence_all("P")
    pr.close()
    return nc



def emit_pass2_weights(pr, io):
    w = {}
    w["Alog"] = pr.sbuf("sc_Alog", [128, 256], F32)
    w["A16"] = pr.sbuf("sc_A16", [128, 256], F32)
    w["onesf"] = pr.sbuf("sc_onesf", [128, 1], F32)
    w["ones1x128f"] = pr.sbuf("w_ones1f", [1, 128], F32)
    w["Dp"] = pr.sbuf("p2_Dp", [128, 16], F32)
    w["Woutb"] = pr.sbuf("p2_Woutb", [128, 16 * D], BF16)
    w["scol"] = pr.sbuf("p2_scol", [128, 1], F32)
    wtmp = pr.sbuf("p2_wtmp", [128, D], F32)
    srow = pr.sbuf("p2_srow", [1, 1], F32)
    ps_b = pr.psum("p2_ps_b", [128, 1], F32)
    pr.memset("V", w["onesf"][:], 1.0)
    pr.dma("Y", w["Alog"][:], io["Alog"])
    pr.dma("Y", w["ones1x128f"][:], io["ones1x128f"])
    pr.dma("Y", w["Dp"][:], io["Dpar"])
    pr.act("S", w["A16"][:], w["Alog"][:], AF.Tanh, scale=0.1)
    pr.act("S", w["A16"][:], w["A16"][:], AF.Exp, scale=10.0)
    pr.ts("V", w["A16"][:], w["A16"][:], float(-1.0 / 16.0), None, OP.mult)
    WoutT = io["WoutT"].rearrange("(g p) c -> p g c", p=128)
    Wb3 = w["Woutb"][:].rearrange("p (g c) -> p g c", g=16)
    for g in range(16):
        pr.dma("Y", wtmp[:], WoutT[:, g, :])
        pr.copy("V", Wb3[:, g, :], wtmp[:])
    pr.dma("Y", srow[:], io["rscal"])
    pr.act("S", srow[:], srow[:], AF.Tanh, scale=0.5)
    pr.ts("V", srow[:], srow[:], 0.25, 0.25, OP.mult, OP.add)
    pr.mm(ps_b[:, :1], w["ones1x128f"][:], srow[:], start=True, stop=True)
    pr.copy("V", w["scol"][:], ps_b[:, :1])
    return w


def emit_pass2_span(pr, w, io, si, s_global, fused=False):
    s0 = span_shift(s_global) if not fused else None
    if fused:
        _prestage_pads(pr, io, si, [(io[f"dtp_{si}"], f"dt_{si}", "zDI"),
                                    (io[f"xcbp_{si}"], f"xcb_{si}", "zDIb"),
                                    (io[f"sgp_{si}"], f"sg_{si}", "zDIb"),
                                    (io[f"Bp_{si}"], f"Brow_{si}", "zNST"),
                                    (io[f"Cp_{si}"], f"Crow_{si}", "zNST")])
        dtr = io[f"dtp_{si}"].rearrange("(g p) t -> p g t", p=128)
        xcbr = io[f"xcbp_{si}"].rearrange("(g p) t -> p g t", p=128)
        sgr = io[f"sgp_{si}"].rearrange("(g p) t -> p g t", p=128)
        ysgr = io[f"ysgp_{si}"].rearrange("(g p) t -> p g t", p=128)
    else:
        dtr = io[f"dt_{si}"].rearrange("(g p) t -> p g t", p=128)
        xcbr = io[f"xcb_{si}"].rearrange("(g p) t -> p g t", p=128)
        sgr = io[f"sg_{si}"].rearrange("(g p) t -> p g t", p=128)
        ysgr = io[f"ysg_{si}"].rearrange("(g p) t -> p g t", p=128)
    xr = io[f"x_{si}"].rearrange("(g p) t -> p g t", p=128)
    x2r = io[f"x2_{si}"].rearrange("(g p) t -> p g t", p=128)

    NBLK = 4
    dt_db = [pr.sbuf(f"p2_dt{i}", [128, PW], F32) for i in range(2)]
    xcf_dg = pr.sbuf("p2_xcf", [128, PW], F32)
    wt_dg = pr.sbuf("p2_wt", [128, PW], F32)
    xcb_db = [pr.sbuf(f"p2_xcb{i}", [128, PW], BF16) for i in range(2)]
    Bbc = pr.sbuf("p2_Bbc", [128, NBLK * PW], F32)
    Cbc = pr.sbuf("p2_Cbc", [128, NBLK * PW], F32)
    dA = pr.sbuf("p2_dA", [128, PW], F32)
    uu = pr.sbuf("p2_u", [128, PW], F32)
    hh = pr.sbuf("p2_h", [128, PW], F32)
    acc = pr.sbuf("p2_acc", [128, PW], F32)
    ent = pr.sbuf("p2_ent", [128, 256], F32)
    facr = pr.sbuf("p2_facr", [1, NE], F32)
    facb = pr.sbuf("p2_facb", [128, NE], F32)
    sgf = pr.sbuf("p2_sgf", [128, PW], F32)
    ysgb = pr.sbuf("p2_ysgb", [128, PW], BF16)
    ps_fb = pr.psum("p2_ps_fb", [128, NE], F32)

    Bbc3 = Bbc[:].rearrange("p (nb t) -> p nb t", nb=NBLK)
    Cbc3 = Cbc[:].rearrange("p (nb t) -> p nb t", nb=NBLK)

    for i in range(2):
        pr.memset("V", dt_db[i][:], 0.0)
        pr.memset("V", xcb_db[i][:], 0.0)
    pr.memset("V", Bbc[:], 0.0)
    pr.memset("V", Cbc[:], 0.0)

    pr.dma("Y", ent[:], io[f"stent_{si}"])
    pr.dma("Y", facr[:], io[f"facs_{si}"])
    pr.mm(ps_fb[:, :], w["ones1x128f"][:], facr[:], start=True, stop=True)
    pr.copy("V", facb[:], ps_fb[:, :])
    facb3 = facb[:].unsqueeze(2)

    yacr = io[f"yac_{si}"].rearrange("(g p) t -> p g t", p=128)
    for nb in range(16 // NBLK):
        for j in range(NBLK):
            n = nb * NBLK + j
            if fused:
                pr.dma("Y", Bbc3[:, j, :], io[f"Bp_{si}"][n:n + 1, :].broadcast_to([128, PW]))
                pr.dma("Y", Cbc3[:, j, :], io[f"Cp_{si}"][n:n + 1, :].broadcast_to([128, PW]))
            else:
                pr.dma("Y", Bbc3[:, j, s0:s0 + SPAN],
                       io[f"Brow_{si}"][n:n + 1, :].broadcast_to([128, SPAN]))
                pr.dma("Y", Cbc3[:, j, s0:s0 + SPAN],
                       io[f"Crow_{si}"][n:n + 1, :].broadcast_to([128, SPAN]))
        for dg in range(16):
            dt_dg = dt_db[dg % 2]
            xcb_dg = xcb_db[dg % 2]
            if fused:
                pr.dma("Y", dt_dg[:], dtr[:, dg, :])
                pr.dma("Y", xcb_dg[:], xcbr[:, dg, :])
            else:
                pr.dma("Y", dt_dg[:, s0:s0 + SPAN], dtr[:, dg, :])
                pr.dma("Y", xcb_dg[:, s0:s0 + SPAN], xcbr[:, dg, :])
            pr.copy("V", xcf_dg[:], xcb_dg[:])
            pr.tt("V", wt_dg[:], dt_dg[:], xcf_dg[:], OP.mult)
            if nb == 0:
                pr.ts("V", acc[:], xcf_dg[:], w["Dp"][:, dg:dg + 1], None, OP.mult)
            else:
                if fused:
                    pr.dma("Y", acc[:], yacr[:, dg, :])
                else:
                    pr.dma("Y", acc[:, s0:s0 + SPAN], yacr[:, dg, :])
            for j in range(NBLK):
                n = nb * NBLK + j
                pair = dg * 16 + n
                acol = w["A16"][:, pair:pair + 1]
                pr.act("S", dA[:], dt_dg[:], AF.Tanh, scale=acol)
                pr.act("S", dA[:], dA[:], AF.Exp, scale=16.0)
                dAc = bass.AP(dA[:].tensor, dA[:].offset + 10,
                              [list(dA[:].ap[0]), [10, NE], [1, 1]])
                pr.tt("V", dAc, dAc, facb3, OP.mult)
                pr.tt("V", uu[:], wt_dg[:], Bbc3[:, j, :], OP.mult)
                uuc = bass.AP(uu[:].tensor, uu[:].offset + 10,
                              [list(uu[:].ap[0]), [10, NE], [1, 1]])
                pr.tt("V", uuc, uuc, facb3, OP.mult)
                pr.scan("V", hh[:], dA[:], uu[:], ent[:, pair:pair + 1], OP.mult, OP.add)
                pr.tt("V", hh[:], hh[:], Cbc3[:, j, :], OP.mult)
                pr.tt("V", acc[:], acc[:], hh[:], OP.add)
            if nb < 3:
                if fused:
                    pr.dma("P", yacr[:, dg, :], acc[:])
                else:
                    pr.dma("P", yacr[:, dg, :], acc[:, s0:s0 + SPAN])
            else:
                if fused:
                    pr.dma("Y", xcb_dg[:], sgr[:, dg, :])
                else:
                    pr.dma("Y", xcb_dg[:, s0:s0 + SPAN], sgr[:, dg, :])
                pr.copy("V", sgf[:], xcb_dg[:])
                pr.tt("V", ysgb[:], acc[:], sgf[:], OP.mult)
                if fused:
                    pr.dma("P", ysgr[:, dg, :], ysgb[:])
                else:
                    pr.dma("P", ysgr[:, dg, :], ysgb[:, s0:s0 + SPAN])

    ych = pr.sbuf("p2_ych", [128, 16 * TCH], BF16, alias="p2_Bbc")
    xch = pr.sbuf("p2_xch", [128, 8 * TCH], F32, alias="p2_Cbc")
    x2ch = pr.sbuf("p2_x2ch", [128, 8 * TCH], F32, alias="p2_h")
    strow = pr.sbuf("p2_strow", [1, 4 * TCH], F32, alias="p2_u")
    ps_o2 = [pr.psum(f"p2_ps_o{i}", [128, TCH], F32) for i in range(2)]
    ps_a = pr.psum("p2_ps_a", [128, TCH], F32)
    ps_c = pr.psum("p2_ps_c", [128, TCH], F32)
    Wb3 = w["Woutb"][:].rearrange("p (g c) -> p g c", g=16)
    ych3 = ych[:].rearrange("p (g t) -> p g t", g=16)
    xch3 = xch[:].rearrange("p (g t) -> p g t", g=8)
    x2ch3 = x2ch[:].rearrange("p (g t) -> p g t", g=8)
    st = io[f"stat_{si}"]
    S0f = _S0of(si)
    for ci in range(SPAN // TCH):
        tsl = slice(ci * TCH, (ci + 1) * TCH)
        if fused:
            pr.dma_arms("Y", lambda r: (ych3, ysgr[:, :, S0f(r) + ci * TCH:S0f(r) + (ci + 1) * TCH]))
        else:
            pr.dma("Y", ych3, ysgr[:, :, tsl])
        pr.dma("Y", xch3, xr[:, :, tsl])
        pr.dma("Y", strow[:, 0:TCH], st[0:1, tsl])
        pr.dma("Y", strow[:, TCH:2 * TCH], st[1:2, tsl])
        pr.ts("V", strow[:, 2 * TCH:3 * TCH], strow[:, TCH:2 * TCH], 1.0, None, OP.add)
        pr.tt("V", strow[:, 3 * TCH:4 * TCH], strow[:, 0:TCH], strow[:, TCH:2 * TCH], OP.mult)
        pr.mm(ps_a[:, :], w["ones1x128f"][:], strow[:, 2 * TCH:3 * TCH], start=True, stop=True)
        pr.mm(ps_c[:, :], w["ones1x128f"][:], strow[:, 3 * TCH:4 * TCH], start=True, stop=True)
        for mc in range(8):
            ps_o = ps_o2[mc % 2]
            for kc in range(16):
                pr.mm(ps_o[:, :], Wb3[:, kc, mc * 128:(mc + 1) * 128], ych3[:, kc, :],
                      start=(kc == 0), stop=(kc == 15))
            pr.tt("V", x2ch3[:, mc, :], xch3[:, mc, :], ps_a[:, :], OP.mult)
            pr.tt("V", x2ch3[:, mc, :], x2ch3[:, mc, :], ps_c[:, :], OP.subtract)
            pr.stt("V", x2ch3[:, mc, :], ps_o[:, :], w["scol"][:], x2ch3[:, mc, :],
                   OP.mult, OP.add)
        pr.dma("P", x2r[:, :, tsl], x2ch3)


def _declare_pass2_io(nc):
    io = {}
    io["Alog"] = declare_in(nc, "Alog", [128, 256])
    io["ones1x128f"] = declare_in(nc, "ones1x128f", [1, 128])
    io["Dpar"] = declare_in(nc, "Dpar", [128, 16])
    io["WoutT"] = declare_in(nc, "WoutT", [DI, D])
    io["rscal"] = declare_in(nc, "rscal", [1, 1])
    for si in range(2):
        io[f"x_{si}"] = declare_in(nc, f"x_{si}", [D, SPAN])
        io[f"dt_{si}"] = declare_in(nc, f"dt_{si}", [DI, SPAN])
        io[f"xcb_{si}"] = declare_in(nc, f"xcb_{si}", [DI, SPAN // 2], F32).bitcast(BF16)
        io[f"sg_{si}"] = declare_in(nc, f"sg_{si}", [DI, SPAN // 2], F32).bitcast(BF16)
        io[f"Brow_{si}"] = declare_in(nc, f"Brow_{si}", [NST, SPAN])
        io[f"Crow_{si}"] = declare_in(nc, f"Crow_{si}", [NST, SPAN])
        io[f"stat_{si}"] = declare_in(nc, f"stat_{si}", [2, SPAN])
        io[f"stent_{si}"] = declare_in(nc, f"stent_{si}", [128, 256])
        io[f"facs_{si}"] = declare_in(nc, f"facs_{si}", [1, NE])
        io[f"yac_{si}"] = nc.dram_tensor(f"yac_{si}", [DI, SPAN], F32).ap()
        io[f"ysg_{si}"] = declare_out(nc, f"ysg_{si}", [DI, SPAN // 2], F32).bitcast(BF16)
        io[f"x2_{si}"] = declare_out(nc, f"x2_{si}", [D, SPAN])
    return io


def build_launch4(core):
    nc = bass.Bass("TRN2", target_bir_lowering=False)
    pr = Prog(nc)
    io = _declare_pass2_io(nc)
    w = emit_pass2_weights(pr, io)
    for si, (b, s) in enumerate(_span_of_core(core)):
        emit_pass2_span(pr, w, io, si, s)
    pr.fence_all("P")
    pr.close()
    return nc



XE = SPAN + 2


def emit_ternary_prep(pr, io, which, nslab, width, wtmp_alias=None, wi8_alias=None):
    wT = io[f"w{which}T"].rearrange("(g p) c -> p g c", p=128)
    wqb = pr.sbuf(f"fw_wq{which}", [128, nslab * width], BF16)
    wtmp = pr.sbuf("fw_wtmp", [128, 4096], F32, alias=wtmp_alias)
    wi8 = pr.sbuf("fw_wi8", [128, 4096], I8, alias=wi8_alias)
    pcol = pr.sbuf("fw_pcol", [128, 1], F32)
    acol = pr.sbuf("fw_acol", [128, 1], F32)
    srow = pr.sbuf(f"fw_srow{which}", [1, 2], F32)
    swcol = pr.sbuf("fw_swcol", [128, 1], F32)
    ps_w = pr.psum("fw_ps", [128, 2], F32)
    ps1 = ps_w[:]
    ps_b = ps_w[:]
    onesf = pr.sbuf("sc_onesf", [128, 1], F32)
    ones1x128f = pr.sbuf("w_ones1f", [1, 128], F32)
    pr.memset("V", onesf[:], 1.0)
    wq3 = wqb[:].rearrange("p (g c) -> p g c", g=nslab)
    pr.memset("V", pcol[:], 0.0)
    for g in range(nslab):
        pr.dma("Y", wtmp[:, :width], wT[:, g, :])
        pr.act("S", wtmp[:, :width], wtmp[:, :width], AF.Abs, accum_out=acol[:])
        pr.tt("V", pcol[:], pcol[:], acol[:], OP.add)
    pr.mm(ps1[0:1, 0:1], onesf[:], pcol[:], start=True, stop=True)
    pr.ts("V", srow[:, 0:1], ps1[:1, :1], float(1.0 / (nslab * 128 * width)), 1e-5,
          OP.mult, OP.max)
    pr.recip("V", srow[:, 1:2], srow[:, 0:1])
    pr.mm(ps_b[:, 1:2], ones1x128f[:], srow[:, 1:2], start=True, stop=True)
    pr.copy("V", swcol[:], ps_b[:, 1:2])
    for g in range(nslab):
        pr.dma("Y", wtmp[:, :width], wT[:, g, :])
        pr.ts("V", wtmp[:, :width], wtmp[:, :width], swcol[:], None, OP.mult)
        pr.copy("V", wi8[:, :width], wtmp[:, :width])
        pr.copy("V", wq3[:, g, :], wi8[:, :width])
        pr.ts("V", wq3[:, g, :], wq3[:, g, :], 1.0, -1.0, OP.min, OP.max)
    invswc = pr.sbuf(f"fw_invsw{which}", [128, 1], F32)
    pr.mm(ps_b[:, 1:2], ones1x128f[:], srow[:, 0:1], start=True, stop=True)
    pr.copy("V", invswc[:], ps_b[:, 1:2])
    return wqb, invswc


def _ffn_chunks():
    return [(0, TCH), (TCH, TCH), (2 * TCH, TCH), (3 * TCH, TCH), (4 * TCH, 2)]


def emit_ffn_a_span(pr, w, io, si):
    x2e = io[f"x2e_{si}"].rearrange("(g p) t -> p g t", p=128)
    hr = io[f"h_{si}"].rearrange("(g p) t -> p g t", p=128)
    rsc = io[f"rsc_{si}"]

    xch = pr.sbuf("f_xch", [128, 8 * TCH], F32)
    xbch = pr.sbuf("f_xbch", [128, 8 * TCH], BF16)
    sqch = pr.sbuf("f_sqch", [128, 8 * TCH], BF16)
    xnf = pr.sbuf("f_xnf", [128, 8 * TCH], F32)
    xqi = pr.sbuf("f_xqi", [128, TCH], I8)
    xqb = pr.sbuf("f_xqb", [128, 8 * TCH], BF16, alias="f_xbch")
    amx = pr.sbuf("f_amx", [128, TCH], F32)
    tcol = pr.sbuf("f_tcol", [128, 1], F32)
    rowq = pr.sbuf("f_rowq", [1, 2 * TCH], F32)
    hst = pr.sbuf("f_hst", [128, 16 * TCH], BF16)
    rows = pr.sbuf("s1_rows", [1, 7 * TCH], F32)
    ps_sum = pr.psum("s1_ps_sum", [1, TCH], F32)
    ps_sq = pr.psum("s1_ps_sq", [1, TCH], F32)
    ps_bc1 = pr.psum("s1_ps_bc1", [128, TCH], F32)
    ps_bc2 = pr.psum("s1_ps_bc2", [128, TCH], F32)
    ps_mm2 = [pr.psum(f"f_ps_mm{i}", [128, TCH], F32) for i in range(2)]
    ps_t = pr.psum("f_ps_t", [128, 128], F32)
    r_m = rows[:][:, 0:TCH]
    r_v = rows[:][:, TCH:2 * TCH]
    r_t = rows[:][:, 2 * TCH:3 * TCH]
    r_q = rows[:][:, 3 * TCH:4 * TCH]
    r_r1 = rows[:][:, 4 * TCH:5 * TCH]

    w1q3 = w["w1qb"][:].rearrange("p (g c) -> p g c", g=8)
    hst3 = hst[:].rearrange("p (g t) -> p g t", g=16)

    for (c0, T) in _ffn_chunks():
        xch_v = xch[:, :8 * T].rearrange("p (g t) -> p g t", g=8)
        xb_v = xbch[:, :8 * T].rearrange("p (g t) -> p g t", g=8)
        sq_v = sqch[:, :8 * T].rearrange("p (g t) -> p g t", g=8)
        xnf_v = xnf[:, :8 * T].rearrange("p (g t) -> p g t", g=8)
        pr.dma("Y", xch_v, x2e[:, :, c0:c0 + T])
        pr.copy("V", xbch[:, :8 * T], xch[:, :8 * T])
        pr.act("S", sqch[:, :8 * T], xbch[:, :8 * T], AF.Square)
        for g in range(8):
            pr.mm(ps_sum[:1, :T], w["onesb"][:], xb_v[:, g, :], start=(g == 0), stop=(g == 7))
        for g in range(8):
            pr.mm(ps_sq[:1, :T], w["onesb"][:], sq_v[:, g, :], start=(g == 0), stop=(g == 7))
        pr.ts("V", r_m[:, :T], ps_sum[:1, :T], 1.0 / D, None, OP.mult)
        pr.ts("V", r_v[:, :T], ps_sq[:1, :T], 1.0 / D, None, OP.mult)
        pr.tt("V", r_t[:, :T], r_m[:, :T], r_m[:, :T], OP.mult)
        pr.tt("V", r_v[:, :T], r_v[:, :T], r_t[:, :T], OP.subtract)
        pr.act("S", r_q[:, :T], r_v[:, :T], AF.Ln, bias=float(LN_EPS))
        pr.act("S", r_r1[:, :T], r_q[:, :T], AF.Exp, scale=-0.5)
        pr.tt("V", r_t[:, :T], r_m[:, :T], r_r1[:, :T], OP.mult)
        pr.mm(ps_bc1[:, :T], w["ones1x128f"][:], r_r1[:, :T], start=True, stop=True)
        pr.mm(ps_bc2[:, :T], w["ones1x128f"][:], r_t[:, :T], start=True, stop=True)
        pr.tt("V", xnf_v, xch_v, bc8(ps_bc1[:, :T], 8), OP.mult)
        pr.tt("V", xnf_v, xnf_v, bc8(ps_bc2[:, :T], 8), OP.subtract)
        pr.act("S", xch[:, :8 * T], xnf[:, :8 * T], AF.Abs)
        pr.tt("V", amx[:, :T], xch_v[:, 0, :], xch_v[:, 1, :], OP.max)
        for g in range(2, 8):
            pr.tt("V", amx[:, :T], amx[:, :T], xch_v[:, g, :], OP.max)
        nblk = (T + 127) // 128
        for tb in range(nblk):
            tw = min(128, T - tb * 128)
            pr.transpose(ps_t[:tw, :128], amx[:, tb * 128:tb * 128 + tw], w["idenf"][:])
            pr.red("V", tcol[:tw, 0:1], ps_t[:tw, :128], OP.max)
            pr.dma("P", rsc[0:1, c0 + tb * 128:c0 + tb * 128 + tw], tcol[:tw, 0:1])
        pr.dma("Y", rowq[:, 0:T], rsc[0:1, c0:c0 + T])
        pr.ts("V", rowq[:, 0:T], rowq[:, 0:T], 1e-5, None, OP.max)
        pr.recip("V", rowq[:, TCH:TCH + T], rowq[:, 0:T])
        pr.ts("V", rowq[:, TCH:TCH + T], rowq[:, TCH:TCH + T], 127.0, None, OP.mult)
        pr.mm(ps_bc1[:, :T], w["ones1x128f"][:], rowq[:, TCH:TCH + T], start=True, stop=True)
        pr.tt("V", xnf_v, xnf_v, bc8(ps_bc1[:, :T], 8), OP.mult)
        for g in range(8):
            pr.copy("V", xqi[:, :T], xnf_v[:, g, :])
            pr.copy("V", xqb[:, :8 * T].rearrange("p (g t) -> p g t", g=8)[:, g, :], xqi[:, :T])
        pr.ts("V", rowq[:, 0:T], rowq[:, 0:T], float(1.0 / 127.0), None, OP.mult)
        pr.mm(ps_bc2[:, :T], w["ones1x128f"][:], rowq[:, 0:T], start=True, stop=True)
        dqbc = pr.sbuf("f_dqbc", [128, TCH], F32)
        pr.ts("V", dqbc[:, :T], ps_bc2[:, :T], w["invsw1"][:, 0:1], None, OP.mult)
        xq3 = xqb[:, :8 * T].rearrange("p (g t) -> p g t", g=8)
        for half in range(2):
            for mcl in range(16):
                mc = half * 16 + mcl
                ps_mm = ps_mm2[mc % 2]
                for kc in range(8):
                    pr.mm(ps_mm[:, :T], w1q3[:, kc, mc * 128:mc * 128 + 128], xq3[:, kc, :],
                          start=(kc == 0), stop=(kc == 7))
                pr.tt("V", hst3[:, mcl, :T], ps_mm[:, :T], dqbc[:, :T], OP.mult)
            pr.dma("P", hr[:, half * 16:half * 16 + 16, c0:c0 + T], hst3[:, :, :T])


def emit_ffn_b_span(pr, w, io, si):
    hr = io[f"h_{si}"].rearrange("(g p) t -> p g t", p=128)
    x2e = io[f"x2e_{si}"].rearrange("(g p) t -> p g t", p=128)
    x3r = io[f"x3_{si}"].rearrange("(g p) t -> p g t", p=128)
    rsc = io[f"rsc_{si}"]

    hch = pr.sbuf("fb_hch", [128, 32 * (TCH + 2)], BF16)
    hcf = pr.sbuf("fb_hcf", [128, 32 * TCH], F32)
    amx = pr.sbuf("f_amx", [128, TCH], F32)
    tcol = pr.sbuf("f_tcol", [128, 1], F32)
    rowq = pr.sbuf("f_rowq", [1, 2 * TCH], F32)
    hqi = pr.sbuf("fb_hqi", [128, TCH], I8)
    hqb = pr.sbuf("fb_hqb", [128, 32 * TCH], BF16, alias="fb_hcf")
    xch = pr.sbuf("f_xch", [128, 8 * TCH], F32)
    ps_mm2 = [pr.psum(f"fb_ps_mm{i}", [128, TCH], F32) for i in range(2)]
    ps_bc1 = pr.psum("s1_ps_bc1", [128, TCH], F32)
    ps_bc2 = pr.psum("s1_ps_bc2", [128, TCH], F32)
    ps_t = pr.psum("f_ps_t", [128, 128], F32)

    w2q3 = w["w2qb"][:].rearrange("p (g c) -> p g c", g=32)
    cwf3 = w["cwf"][:].rearrange("p (g k) -> p g k", g=32)
    hch3 = hch[:].rearrange("p (g t) -> p g t", g=32)
    hcf3 = hcf[:].rearrange("p (g t) -> p g t", g=32)
    hqb3 = hqb[:].rearrange("p (g t) -> p g t", g=32)

    for ci in range(SPAN // TCH):
        c0 = ci * TCH
        T = TCH
        pr.dma("Y", hch3, hr[:, :, c0:c0 + T + 2])
        for g in range(32):
            pr.ts("V", hcf3[:, g, :], hch3[:, g, 0:T], cwf3[:, g, 0:1],
                  w["cbf"][:, g:g + 1], OP.mult, OP.add)
            for k in (1, 2):
                pr.stt("V", hcf3[:, g, :], hch3[:, g, k:k + T], cwf3[:, g, k:k + 1],
                       hcf3[:, g, :], OP.mult, OP.add)
        pr.act("S", hcf[:], hcf[:], AF.Silu)
        absg = pr.sbuf("fb_absg", [128, TCH], F32)
        pr.act("S", absg[:, :T], hcf3[:, 0, :], AF.Abs)
        pr.copy("V", amx[:, :T], absg[:, :T])
        for g in range(1, 32):
            pr.act("S", absg[:, :T], hcf3[:, g, :], AF.Abs)
            pr.tt("V", amx[:, :T], amx[:, :T], absg[:, :T], OP.max)
        nblk = T // 128
        for tb in range(nblk):
            pr.transpose(ps_t[:, :128], amx[:, tb * 128:tb * 128 + 128], w["idenf"][:])
            pr.red("V", tcol[:, 0:1], ps_t[:, :128], OP.max)
            pr.dma("P", rsc[0:1, c0 + tb * 128:c0 + tb * 128 + 128], tcol[:, 0:1])
        pr.dma("Y", rowq[:, 0:T], rsc[0:1, c0:c0 + T])
        pr.ts("V", rowq[:, 0:T], rowq[:, 0:T], 1e-5, None, OP.max)
        pr.recip("V", rowq[:, TCH:TCH + T], rowq[:, 0:T])
        pr.ts("V", rowq[:, TCH:TCH + T], rowq[:, TCH:TCH + T], 127.0, None, OP.mult)
        pr.mm(ps_bc1[:, :T], w["ones1x128f"][:], rowq[:, TCH:TCH + T], start=True, stop=True)
        for g in range(32):
            pr.tt("V", hcf3[:, g, :], hcf3[:, g, :], ps_bc1[:, :T], OP.mult)
            pr.copy("V", hqi[:, :T], hcf3[:, g, :])
            pr.copy("V", hqb3[:, g, :T], hqi[:, :T])
        pr.ts("V", rowq[:, 0:T], rowq[:, 0:T], float(1.0 / 127.0), None, OP.mult)
        pr.mm(ps_bc2[:, :T], w["ones1x128f"][:], rowq[:, 0:T], start=True, stop=True)
        dqbc2 = pr.sbuf("fb_dqbc", [128, TCH], F32)
        pr.ts("V", dqbc2[:, :T], ps_bc2[:, :T], w["invsw2"][:, 0:1], None, OP.mult)
        pr.dma("Y", xch[:].rearrange("p (g t) -> p g t", g=8), x2e[:, :, c0 + 1:c0 + 1 + T])
        for mc in range(8):
            ps_mm = ps_mm2[mc % 2]
            for kc in range(32):
                pr.mm(ps_mm[:, :T], w2q3[:, kc, mc * 128:mc * 128 + 128], hqb3[:, kc, :],
                      start=(kc == 0), stop=(kc == 31))
            xv = xch[:].rearrange("p (g t) -> p g t", g=8)[:, mc, :]
            pr.tt("V", amx[:, :T], ps_mm[:, :T], dqbc2[:, :T], OP.mult)
            pr.tt("V", amx[:, :T], amx[:, :T], xv, OP.add)
            pr.copy("V", xch[:].rearrange("p (g t) -> p g t", g=8)[:, mc, :], amx[:, :T])
        pr.dma("P", x3r[:, :, c0:c0 + T], xch[:].rearrange("p (g t) -> p g t", g=8))


def emit_ffn_weights(pr, io, which):
    w = {}
    w["onesb"] = pr.sbuf("w_onesb", [128, 1], BF16)
    w["ones1x128f"] = pr.sbuf("w_ones1f", [1, 128], F32)
    w["idenf"] = pr.sbuf("w_idenf", [128, 128], F32)
    w["cwf"] = pr.sbuf("w_cwf", [128, 96], F32)
    w["cbf"] = pr.sbuf("w_cbf", [128, 32], F32)
    pr.memset("V", w["onesb"][:], 1.0)
    pr.dma("Y", w["ones1x128f"][:], io["ones1x128f"])
    pr.dma("Y", w["idenf"][:], io["idenf"])
    pr.dma("Y", w["cwf"][:], io["cwf"])
    pr.dma("Y", w["cbf"][:], io["cbf"])
    if which == "a":
        w["w1qb"], w["invsw1"] = emit_ternary_prep(pr, io, 1, 8, DH)
    else:
        pr.sbuf("fb_hch", [128, 32 * (TCH + 2)], BF16)
        pr.sbuf("fb_hcf", [128, 32 * TCH], F32)
        w["w2qb"], w["invsw2"] = emit_ternary_prep(pr, io, 2, 32, D,
                                                   wtmp_alias="fb_hch", wi8_alias="fb_hcf")
    pr.barrier()
    return w


def _declare_ffn_io(nc, which):
    io = {}
    io["ones1x128f"] = declare_in(nc, "ones1x128f", [1, 128])
    io["idenf"] = declare_in(nc, "idenf", [128, 128])
    io["cwf"] = declare_in(nc, "cwf", [128, 96])
    io["cbf"] = declare_in(nc, "cbf", [128, 32])
    if which == "a":
        io["w1T"] = declare_in(nc, "w1T", [D, DH])
    else:
        io["w2T"] = declare_in(nc, "w2T", [DH, D])
    for si in range(2):
        io[f"x2e_{si}"] = declare_in(nc, f"x2e_{si}", [D, XE])
        io[f"rsc_{si}"] = nc.dram_tensor(f"rsc_{si}", [1, XE], F32).ap()
        if which == "a":
            io[f"h_{si}"] = declare_out(nc, f"h_{si}", [DH, XE // 2], F32).bitcast(BF16)
        else:
            io[f"h_{si}"] = declare_in(nc, f"h_{si}", [DH, XE // 2], F32).bitcast(BF16)
            io[f"x3_{si}"] = declare_out(nc, f"x3_{si}", [D, SPAN])
    return io


def build_launch5a():
    nc = bass.Bass("TRN2", target_bir_lowering=False)
    pr = Prog(nc)
    io = _declare_ffn_io(nc, "a")
    w = emit_ffn_weights(pr, io, "a")
    for si in range(2):
        emit_ffn_a_span(pr, w, io, si)
    pr.fence_all("P")
    pr.close()
    return nc


def build_launch5b():
    nc = bass.Bass("TRN2", target_bir_lowering=False)
    pr = Prog(nc)
    io = _declare_ffn_io(nc, "b")
    w = emit_ffn_weights(pr, io, "b")
    for si in range(2):
        emit_ffn_b_span(pr, w, io, si)
    pr.fence_all("P")
    pr.close()
    return nc



def emit_s4d_prep(pr, io):
    w = {}
    for nm in ("ldt", "Are", "Aim", "Cre", "Cim"):
        w[nm] = pr.sbuf(f"s4_{nm}", [128, 256 if nm != "ldt" else 8], F32)
        pr.dma("Y", w[nm][:], io["s4_logdt" if nm == "ldt" else
                                "s4_Ar" if nm == "Are" else
                                "s4_Ai" if nm == "Aim" else
                                "s4_Cr" if nm == "Cre" else "s4_Ci"])
    names = ["dt", "are", "aim", "lre", "lim", "cpre", "cpim", "pre", "pim",
             "t1", "t2", "t3", "clre", "clim", "kst"]
    t = {nm: pr.sbuf(f"s4w_{nm}", [128, 256], F32) for nm in names}
    bst = pr.sbuf("s4w_bst", [128, 8 * 32 * 16], BF16)
    bst2 = pr.sbuf("s4w_bst2", [128, 8 * 32 * 16], BF16)
    bsmall = [pr.sbuf(f"s4w_bs{i}", [128, 256], BF16) for i in range(2)]
    kstb = pr.sbuf("s4w_kstb", [128, 8 * 128], BF16)
    powGv = io["powG"].rearrange("(s ri hg p n) -> s ri hg p n", s=128, ri=2, hg=8, p=128)
    powEv = io["powE"].rearrange("(ri hg p n k) -> ri hg p n k", ri=2, hg=8, p=128, n=32)
    kpadv = io["kpad"].rearrange("(hg p c) -> hg p c", p=128, c=256)
    lamTv = io["lamT"].rearrange("(ri h n) -> ri h n", ri=2, h=1024)

    pr.act("S", w["ldt"][:], w["ldt"][:], AF.Exp)
    dt_b = w["ldt"][:].unsqueeze(2).broadcast_to([128, 8, 32])

    def v3(ap):
        return ap.rearrange("p (g n) -> p g n", g=8)
    pr.act("S", t["are"][:], w["Are"][:], AF.Exp)
    pr.ts("V", t["are"][:], t["are"][:], -1.0, None, OP.mult)
    pr.tt("V", v3(t["t1"][:]), v3(t["are"][:]), dt_b, OP.mult)
    pr.tt("V", v3(t["t2"][:]), v3(w["Aim"][:]), dt_b, OP.mult)
    inv2pi = float(1.0 / (2.0 * np.pi))
    two_pi = float(2.0 * np.pi)
    i8s = pr.sbuf("s4w_i32", [128, 256], mybir.dt.int32)
    pr.ts("V", t["t3"][:], t["t2"][:], inv2pi, None, OP.mult)
    pr.copy("V", i8s[:], t["t3"][:])
    pr.copy("V", t["t3"][:], i8s[:])
    pr.ts("V", t["t3"][:], t["t3"][:], two_pi, None, OP.mult)
    pr.tt("V", t["t3"][:], t["t2"][:], t["t3"][:], OP.subtract)
    pr.act("S", t["lre"][:], t["t1"][:], AF.Exp)
    snt = pr.sbuf("s4w_snt", [128, 256], F32)
    cst = pr.sbuf("s4w_cst", [128, 256], F32)
    pr.act("S", snt[:], t["t3"][:], AF.Sin)
    pr.act("S", cst[:], t["t3"][:], AF.Sin, scale=0.5)
    pr.tt("V", cst[:], cst[:], cst[:], OP.mult)
    pr.ts("V", cst[:], cst[:], -2.0, 1.0, OP.mult, OP.add)
    pr.tt("V", t["lim"][:], t["lre"][:], snt[:], OP.mult)
    pr.tt("V", t["lre"][:], t["lre"][:], cst[:], OP.mult)
    pr.tt("V", t["t1"][:], t["are"][:], t["are"][:], OP.mult)
    pr.tt("V", t["t2"][:], w["Aim"][:], w["Aim"][:], OP.mult)
    pr.tt("V", t["t1"][:], t["t1"][:], t["t2"][:], OP.add)
    pr.recip("V", t["t1"][:], t["t1"][:])
    lm1re = pr.sbuf("s4w_lm1re", [128, 256], F32)
    pr.ts("V", lm1re[:], t["lre"][:], 1.0, None, OP.subtract)
    pr.tt("V", t["t2"][:], lm1re[:], t["are"][:], OP.mult)
    pr.tt("V", t["t3"][:], t["lim"][:], w["Aim"][:], OP.mult)
    pr.tt("V", t["t2"][:], t["t2"][:], t["t3"][:], OP.add)
    numim = pr.sbuf("s4w_numim", [128, 256], F32)
    pr.tt("V", numim[:], t["lim"][:], t["are"][:], OP.mult)
    pr.tt("V", t["t3"][:], lm1re[:], w["Aim"][:], OP.mult)
    pr.tt("V", numim[:], numim[:], t["t3"][:], OP.subtract)
    pr.tt("V", t["t2"][:], t["t2"][:], t["t1"][:], OP.mult)
    pr.tt("V", numim[:], numim[:], t["t1"][:], OP.mult)
    pr.tt("V", t["cpre"][:], w["Cre"][:], t["t2"][:], OP.mult)
    pr.tt("V", t["t3"][:], w["Cim"][:], numim[:], OP.mult)
    pr.tt("V", t["cpre"][:], t["cpre"][:], t["t3"][:], OP.subtract)
    pr.tt("V", t["cpim"][:], w["Cre"][:], numim[:], OP.mult)
    pr.tt("V", t["t3"][:], w["Cim"][:], t["t2"][:], OP.mult)
    pr.tt("V", t["cpim"][:], t["cpim"][:], t["t3"][:], OP.add)
    pr.tt("V", t["clre"][:], t["cpre"][:], t["lre"][:], OP.mult)
    pr.tt("V", t["t3"][:], t["cpim"][:], t["lim"][:], OP.mult)
    pr.tt("V", t["clre"][:], t["clre"][:], t["t3"][:], OP.subtract)
    pr.tt("V", t["clim"][:], t["cpre"][:], t["lim"][:], OP.mult)
    pr.tt("V", t["t3"][:], t["cpim"][:], t["lre"][:], OP.mult)
    pr.tt("V", t["clim"][:], t["clim"][:], t["t3"][:], OP.add)
    pr.memset("V", t["pre"][:], 1.0)
    pr.memset("V", t["pim"][:], 0.0)
    pr.memset("V", t["kst"][:], 0.0)

    bstv = bst[:].rearrange("p (g n k) -> p g n k", g=8, n=32)
    bstv2 = bst2[:].rearrange("p (g n k) -> p g n k", g=8, n=32)
    kstbv = kstb[:].rearrange("p (g k) -> p g k", g=8)
    for k in range(128):
        pr.tt("V", t["t1"][:], t["cpre"][:], t["pre"][:], OP.mult)
        pr.tt("V", t["t3"][:], t["cpim"][:], t["pim"][:], OP.mult)
        pr.tt("V", t["t1"][:], t["t1"][:], t["t3"][:], OP.subtract)
        r3 = t["t1"][:].rearrange("p (g n) -> p g n", g=8)
        pr.red("V", t["t2"][:, 0:8], r3, OP.add)
        pr.ts("V", kstbv[:, :, k:k+1].rearrange("p g one -> p (g one)"),
              t["t2"][:, 0:8], 2.0, None, OP.mult)
        s = 127 - k
        q = ("P", "Y", "S")[k % 3]
        pr.copy("V", bsmall[0][:], t["pre"][:])
        pr.dma(q, powGv[s, 0, :, :, :].rearrange("hg p n -> p hg n"),
               bsmall[0][:].rearrange("p (g n) -> p g n", g=8))
        pr.copy("V", bsmall[1][:], t["pim"][:])
        pr.dma(q, powGv[s, 1, :, :, :].rearrange("hg p n -> p hg n"),
               bsmall[1][:].rearrange("p (g n) -> p g n", g=8))
        kb = k % 16
        pr.tt("V", t["t1"][:], t["clre"][:], t["pre"][:], OP.mult)
        pr.tt("V", t["t3"][:], t["clim"][:], t["pim"][:], OP.mult)
        pr.tt("V", t["t1"][:], t["t1"][:], t["t3"][:], OP.subtract)
        pr.ts("V", bstv[:, :, :, kb].rearrange("p g n -> p (g n)"), t["t1"][:], 2.0, None, OP.mult)
        pr.tt("V", t["t1"][:], t["clre"][:], t["pim"][:], OP.mult)
        pr.tt("V", t["t3"][:], t["clim"][:], t["pre"][:], OP.mult)
        pr.tt("V", t["t1"][:], t["t1"][:], t["t3"][:], OP.add)
        pr.ts("V", bstv2[:, :, :, kb].rearrange("p g n -> p (g n)"), t["t1"][:], -2.0, None, OP.mult)
        if kb == 15:
            k0 = k - 15
            for hg in range(8):
                q = ("P", "Y", "S")[hg % 3]
                pr.dma(q, powEv[0, hg, :, :, k0:k0 + 16], bstv[:, hg, :, :])
                pr.dma(q, powEv[1, hg, :, :, k0:k0 + 16], bstv2[:, hg, :, :])
        pr.tt("V", t["t1"][:], t["pre"][:], t["lre"][:], OP.mult)
        pr.tt("V", t["t3"][:], t["pim"][:], t["lim"][:], OP.mult)
        pr.tt("V", t["t1"][:], t["t1"][:], t["t3"][:], OP.subtract)
        pr.tt("V", t["t3"][:], t["pim"][:], t["lre"][:], OP.mult)
        pr.tt("V", t["t2"][:], t["pre"][:], t["lim"][:], OP.mult)
        pr.tt("V", t["pim"][:], t["t3"][:], t["t2"][:], OP.add)
        pr.copy("V", t["pre"][:], t["t1"][:])
    Db = pr.sbuf("s4w_Db", [128, 1], F32)
    pr.dma("Y", Db[:], io["s4_D"].broadcast_to([128, 1]))
    k0v = kstbv[:, :, 0:1].rearrange("p g one -> p (g one)")
    pr.ts("V", k0v, k0v, Db[:], None, OP.add)
    zrow = pr.sbuf("s4w_zrow", [128, 256], BF16)
    pr.memset("V", zrow[:], 0.0)
    for hg in range(8):
        pr.dma("P", kpadv[hg, :, :], zrow[:])
    pr.dma("P", kpadv.rearrange("hg p c -> hg p c")[:, :, 127:255].rearrange("hg p c -> p hg c"),
           kstbv)
    pr.dma("P", lamTv[0, :, :].rearrange("(hg p) n -> p hg n", p=128), t["pre"][:].rearrange("p (g n) -> p g n", g=8))
    pr.dma("P", lamTv[1, :, :].rearrange("(hg p) n -> p hg n", p=128), t["pim"][:].rearrange("p (g n) -> p g n", g=8))
    pr.barrier()
    return w


def _declare_s4d_prep_io(nc):
    io = {}
    io["s4_logdt"] = declare_in(nc, "s4_logdt", [128, 8])
    io["s4_Ar"] = declare_in(nc, "s4_Ar", [128, 256])
    io["s4_Ai"] = declare_in(nc, "s4_Ai", [128, 256])
    io["s4_Cr"] = declare_in(nc, "s4_Cr", [128, 256])
    io["s4_Ci"] = declare_in(nc, "s4_Ci", [128, 256])
    io["s4_D"] = declare_in(nc, "s4_D", [1, 1])
    io["powG"] = declare_out(nc, "powG", [128 * 2 * 1024 * 32 // 2], F32).bitcast(BF16)
    io["powE"] = declare_out(nc, "powE", [2 * 1024 * 32 * 128 // 2], F32).bitcast(BF16)
    io["kpad"] = declare_out(nc, "kpad", [1024 * 256 // 2], F32).bitcast(BF16)
    io["lamT"] = declare_out(nc, "lamT", [2 * 32 * 1024])
    return io


def build_s4d_prep_test():
    nc = bass.Bass("TRN2", target_bir_lowering=False)
    pr = Prog(nc)
    io = _declare_s4d_prep_io(nc)
    emit_s4d_prep(pr, io)
    pr.fence_all("P")
    pr.close()
    return nc


NCHS = SPAN // 128
T6 = 256


def emit_lamn(pr, w, io, P, name):
    lamL = pr.sbuf("s4n_lamL", [128, 512], F32)
    lamn = pr.sbuf(name, [P, 2 * HS4], F32)
    ps_lt = pr.psum("s4n_pslt", [32, 128], F32)
    pr.dma("Y", lamL[:].rearrange("p (ri hg n) -> p ri hg n", ri=2, hg=8),
           io["lamT"].rearrange("(ri hg p n) -> p ri hg n", ri=2, hg=8, p=128))
    lv = lamL[:].rearrange("p (ri hg n) -> p ri hg n", ri=2, hg=8)
    for ri in range(2):
        for hg in range(8):
            pr.transpose(ps_lt[:, :], lv[:, ri, hg, :], w["idenf"][:])
            for rep in range(P // 32):
                pr.copy("V", lamn[:][rep * 32:rep * 32 + 32,
                                     ri * HS4 + hg * 128:ri * HS4 + (hg + 1) * 128],
                        ps_lt[:, :])
    return lamn


def emit_ln3_span(pr, w, io, si, uT):
    xr = io[f"x3_{si}"].rearrange("(g p) t -> p g t", p=128)
    xch = pr.sbuf("s6_xch", [128, 8 * T6], F32)
    xbch = pr.sbuf("s6_xbch", [128, 8 * T6], BF16)
    sqch = pr.sbuf("s6_sqch", [128, 8 * T6], BF16)
    xnb = pr.sbuf("s6_xnb", [128, 8 * T6], BF16)
    rows = pr.sbuf("s6_rows", [1, 5 * T6], F32)
    ps_sum = pr.psum("s6_ps_sum", [1, T6], F32)
    ps_sq = pr.psum("s6_ps_sq", [1, T6], F32)
    ps_bc1 = pr.psum("s6_ps_bc1", [128, T6], F32)
    ps_bc2 = pr.psum("s6_ps_bc2", [128, T6], F32)
    ps_tb = pr.psum("s6_ps_tb", [128, 128], BF16)
    r_m = rows[:][:, 0:T6]
    r_v = rows[:][:, T6:2 * T6]
    r_t = rows[:][:, 2 * T6:3 * T6]
    r_q = rows[:][:, 3 * T6:4 * T6]
    r_r1 = rows[:][:, 4 * T6:5 * T6]
    uTv = uT[:]
    for ci in range(SPAN // T6):
        xch_v = xch[:].rearrange("p (g t) -> p g t", g=8)
        xb_v = xbch[:].rearrange("p (g t) -> p g t", g=8)
        sq_v = sqch[:].rearrange("p (g t) -> p g t", g=8)
        xnb_v = xnb[:].rearrange("p (g t) -> p g t", g=8)
        pr.dma("Y", xch_v, xr[:, :, ci * T6:(ci + 1) * T6])
        pr.copy("V", xbch[:], xch[:])
        pr.act("S", sqch[:], xbch[:], AF.Square)
        for g in range(8):
            pr.mm(ps_sum[:1, :], w["onesb"][:], xb_v[:, g, :], start=(g == 0), stop=(g == 7))
        for g in range(8):
            pr.mm(ps_sq[:1, :], w["onesb"][:], sq_v[:, g, :], start=(g == 0), stop=(g == 7))
        pr.ts("V", r_m, ps_sum[:1, :], 1.0 / D, None, OP.mult)
        pr.ts("V", r_v, ps_sq[:1, :], 1.0 / D, None, OP.mult)
        pr.tt("V", r_t, r_m, r_m, OP.mult)
        pr.tt("V", r_v, r_v, r_t, OP.subtract)
        pr.act("S", r_q, r_v, AF.Ln, bias=float(LN_EPS))
        pr.act("S", r_r1, r_q, AF.Exp, scale=-0.5)
        pr.tt("V", r_t, r_m, r_r1, OP.mult)
        pr.mm(ps_bc1[:, :], w["ones1x128f"][:], r_r1, start=True, stop=True)
        pr.mm(ps_bc2[:, :], w["ones1x128f"][:], r_t, start=True, stop=True)
        pr.tt("V", xch_v, xch_v, bc8(ps_bc1[:, :], 8), OP.mult)
        pr.tt("V", xnb_v, xch_v, bc8(ps_bc2[:, :], 8), OP.subtract)
        for tb in range(T6 // 128):
            c = (ci * T6 + tb * 128) // 128
            for hg in range(8):
                pr.transpose(ps_tb[:, :], xnb_v[:, hg, tb * 128:(tb + 1) * 128], w["idenb"][:])
                pr.copy("V", uTv[:, (si * NCHS + c) * HS4 + hg * 128:
                                 (si * NCHS + c) * HS4 + (hg + 1) * 128], ps_tb[:, :])
    pr.dma("P", io[f"uT_{si}"].rearrange("(p f) -> p f", p=128),
           uTv[:, si * NCHS * HS4:(si + 1) * NCHS * HS4])


def emit_s4d_g(pr, w, io, uT):
    powGr = io["powG"].rearrange("(s ri h n) -> s ri h n", s=128, ri=2, h=HS4)
    gst = [pr.sbuf(f"s6_gst{i}", [128, 2048], BF16) for i in range(2)]
    Gsb = pr.sbuf("s6_Gsb", [64, 2 * NCHS * HS4], BF16)
    ps_g = [pr.psum(f"s6_ps_g{i}", [32, 512], F32) for i in range(2)]
    uT4 = uT[:].rearrange("p (si c h) -> p si c h", si=2, c=NCHS)
    Gv = Gsb[:].rearrange("p (ri c h) -> p ri h c", ri=2, c=NCHS)
    for hb in range(HS4 // 32):
        g = gst[hb % 2]
        pr.dma("Y", g[:].rearrange("p (ri hh n) -> p ri hh n", ri=2, hh=32),
               powGr[:, :, hb * 32:(hb + 1) * 32, :])
        for sub in range(4):
            ps = ps_g[sub % 2]
            for hh in range(8):
                h = hb * 32 + sub * 8 + hh
                for ri in range(2):
                    lhsT = g[:][:, ri * 1024 + (sub * 8 + hh) * 32:
                                ri * 1024 + (sub * 8 + hh) * 32 + 32]
                    pr.mm(ps[:, hh * 64 + ri * 32:hh * 64 + ri * 32 + 32],
                          lhsT, uT4[:, :, :, h], start=True, stop=True)
            psv = ps[:].rearrange("p (hh ri si c) -> p hh ri si c", hh=8, ri=2, si=2)
            for ri in range(2):
                for sp in range(2):
                    h0 = hb * 32 + sub * 8
                    pr.copy("V", Gv[sp * 32:sp * 32 + 32, ri, h0:h0 + 8, :],
                            psv[:, :, ri, sp, :])
    return Gsb


def emit_s4d_scan(pr, io, lamn, Gsb):
    histst = pr.sbuf("s6_histst", [64, NCHS * 2 * HS4], BF16, alias="s6_uT")
    Sre = pr.sbuf("s6_Sre", [64, HS4], F32)
    Sim = pr.sbuf("s6_Sim", [64, HS4], F32)
    tA = pr.sbuf("s6_tA", [64, HS4], F32)
    tB = pr.sbuf("s6_tB", [64, HS4], F32)
    pr.barrier()
    pr.memset("V", Sre[:], 0.0)
    pr.memset("V", Sim[:], 0.0)
    lre = lamn[:][:, 0:HS4]
    lim = lamn[:][:, HS4:2 * HS4]
    G3 = Gsb[:].rearrange("p (ri c h) -> p ri c h", ri=2, c=NCHS)
    H3 = histst[:].rearrange("p (c ri h) -> p c ri h", c=NCHS, ri=2)
    for c in range(NCHS):
        pr.copy("V", H3[:, c, 0, :], Sre[:])
        pr.copy("V", H3[:, c, 1, :], Sim[:])
        pr.tt("V", tA[:], lim, Sim[:], OP.mult)
        pr.tt("V", tB[:], lim, Sre[:], OP.mult)
        pr.tt("V", Sre[:], Sre[:], lre, OP.mult)
        pr.tt("V", Sre[:], Sre[:], tA[:], OP.subtract)
        pr.tt("V", Sre[:], Sre[:], G3[:, 0, c, :], OP.add)
        pr.tt("V", Sim[:], Sim[:], lre, OP.mult)
        pr.tt("V", Sim[:], Sim[:], tB[:], OP.add)
        pr.tt("V", Sim[:], Sim[:], G3[:, 1, c, :], OP.add)
    for si in range(2):
        pr.dma("P", io[f"hist_{si}"].rearrange("(c ri n h) -> n c ri h", c=NCHS, ri=2, n=NS4),
               histst[:][si * 32:si * 32 + 32, :].rearrange("p (c ri h) -> p c ri h",
                                                            c=NCHS, ri=2))
        sr = io[f"send_{si}"].rearrange("(ri n h) -> ri n h", ri=2, n=NS4)
        pr.dma("P", sr[0], Sre[:][si * 32:si * 32 + 32, :])
        pr.dma("P", sr[1], Sim[:][si * 32:si * 32 + 32, :])


def _declare_s4d6_io(nc):
    io = {}
    io["idenf"] = declare_in(nc, "idenf", [128, 128])
    io["ones1x128f"] = declare_in(nc, "ones1x128f", [1, 128])
    io["powG"] = declare_in(nc, "powG", [128 * 2 * HS4 * NS4 // 2]).bitcast(BF16)
    io["lamT"] = declare_in(nc, "lamT", [2 * HS4 * NS4])
    for si in range(2):
        io[f"x3_{si}"] = declare_in(nc, f"x3_{si}", [D, SPAN])
        io[f"uT_{si}"] = declare_out(nc, f"uT_{si}", [128 * NCHS * HS4 // 2]).bitcast(BF16)
        io[f"hist_{si}"] = declare_out(nc, f"hist_{si}", [NCHS * 2 * NS4 * HS4 // 2]).bitcast(BF16)
        io[f"send_{si}"] = declare_out(nc, f"send_{si}", [2 * NS4 * HS4])
    return io


def build_launch6():
    nc = bass.Bass("TRN2", target_bir_lowering=False)
    pr = Prog(nc)
    io = _declare_s4d6_io(nc)
    w = {}
    w["idenf"] = pr.sbuf("w_idenf", [128, 128], F32)
    w["idenb"] = pr.sbuf("w_idenb", [128, 128], BF16)
    w["onesb"] = pr.sbuf("w_onesb", [128, 1], BF16)
    w["ones1x128f"] = pr.sbuf("w_ones1f", [1, 128], F32)
    pr.dma("Y", w["idenf"][:], io["idenf"])
    pr.dma("Y", w["ones1x128f"][:], io["ones1x128f"])
    pr.copy("V", w["idenb"][:], w["idenf"][:])
    pr.memset("V", w["onesb"][:], 1.0)
    uT = pr.sbuf("s6_uT", [128, 2 * NCHS * HS4], BF16)
    lamn = emit_lamn(pr, w, io, 64, "s6_lamn")
    for si in range(2):
        emit_ln3_span(pr, w, io, si, uT)
    Gsb = emit_s4d_g(pr, w, io, uT)
    emit_s4d_scan(pr, io, lamn, Gsb)
    pr.fence_all("P")
    pr.close()
    return nc


def _declare_s4d7_io(nc):
    io = {}
    io["idenf"] = declare_in(nc, "idenf", [128, 128])
    io["lamT"] = declare_in(nc, "lamT", [2 * HS4 * NS4])
    for s in range(NSPANS):
        io[f"send_{s}"] = declare_in(nc, f"send_{s}", [2 * NS4 * HS4])
        io[f"Sst_{s}"] = declare_out(nc, f"Sst_{s}", [2 * NS4 * HS4])
    return io


def emit_stitch(pr, io):
    w = {}
    w["idenf"] = pr.sbuf("w_idenf", [128, 128], F32)
    pr.dma("Y", w["idenf"][:], io["idenf"])
    lamn = emit_lamn(pr, w, io, 32, "s7_lamn")
    l16 = pr.sbuf("s7_l16", [32, 2 * HS4], F32)
    t1 = pr.sbuf("s7_t1", [32, HS4], F32)
    t2 = pr.sbuf("s7_t2", [32, HS4], F32)
    pr.copy("V", l16[:], lamn[:])
    re = l16[:][:, 0:HS4]
    im = l16[:][:, HS4:2 * HS4]
    for _ in range(4):
        pr.tt("V", t1[:], re, im, OP.mult)
        pr.tt("V", re, re, re, OP.mult)
        pr.tt("V", t2[:], im, im, OP.mult)
        pr.tt("V", re, re, t2[:], OP.subtract)
        pr.ts("V", im, t1[:], 2.0, None, OP.mult)
    Sre = pr.sbuf("s7_Sre", [32, HS4], F32)
    Sim = pr.sbuf("s7_Sim", [32, HS4], F32)
    sd = pr.sbuf("s7_sd", [32, 2 * HS4], F32)
    pr.memset("V", Sre[:], 0.0)
    pr.memset("V", Sim[:], 0.0)
    for s in range(NSPANS):
        sv = io[f"Sst_{s}"].rearrange("(ri n h) -> ri n h", ri=2, n=NS4)
        pr.dma("P", sv[0], Sre[:])
        pr.dma("P", sv[1], Sim[:])
        pr.dma("Y", sd[:].rearrange("p (ri h) -> p ri h", ri=2),
               io[f"send_{s}"].rearrange("(ri n h) -> n ri h", ri=2, n=NS4))
        pr.tt("V", t1[:], im, Sim[:], OP.mult)
        pr.tt("V", t2[:], im, Sre[:], OP.mult)
        pr.tt("V", Sre[:], Sre[:], re, OP.mult)
        pr.tt("V", Sre[:], Sre[:], t1[:], OP.subtract)
        pr.tt("V", Sre[:], Sre[:], sd[:][:, 0:HS4], OP.add)
        pr.tt("V", Sim[:], Sim[:], re, OP.mult)
        pr.tt("V", Sim[:], Sim[:], t2[:], OP.add)
        pr.tt("V", Sim[:], Sim[:], sd[:][:, HS4:2 * HS4], OP.add)
    pr.fence_all("P")


def build_launch7():
    nc = bass.Bass("TRN2", target_bir_lowering=False)
    pr = Prog(nc)
    io = _declare_s4d7_io(nc)
    emit_stitch(pr, io)
    pr.close()
    return nc


def _declare_s4d8_io(nc):
    io = {}
    io["idenf"] = declare_in(nc, "idenf", [128, 128])
    io["antif"] = declare_in(nc, "antif", [128, 128])
    io["lamT"] = declare_in(nc, "lamT", [2 * HS4 * NS4])
    io["powE"] = declare_in(nc, "powE", [2 * HS4 * NS4 * 128 // 2]).bitcast(BF16)
    io["kpad"] = declare_in(nc, "kpad", [HS4 * 256 // 2]).bitcast(BF16)
    for si in range(2):
        io[f"x3_{si}"] = declare_in(nc, f"x3_{si}", [D, SPAN])
        io[f"uT_{si}"] = declare_in(nc, f"uT_{si}", [128 * NCHS * HS4 // 2]).bitcast(BF16)
        io[f"hist_{si}"] = declare_in(nc, f"hist_{si}", [NCHS * 2 * NS4 * HS4 // 2]).bitcast(BF16)
        io[f"Sst_{si}"] = declare_in(nc, f"Sst_{si}", [2 * NS4 * HS4])
        io[f"x4_{si}"] = declare_out(nc, f"x4_{si}", [D, SPAN])
    return io


def emit_s4d_final(pr, io, kpadb):
    w = {}
    w["idenf"] = pr.sbuf("w_idenf", [128, 128], F32)
    w["idenb"] = pr.sbuf("w_idenb", [128, 128], BF16)
    w["antif"] = pr.sbuf("w_antif", [128, 128], F32)
    w["antib"] = pr.sbuf("w_antib", [128, 128], BF16)
    pr.dma("Y", w["idenf"][:], io["idenf"])
    pr.dma("Y", w["antif"][:], io["antif"])
    pr.copy("V", w["idenb"][:], w["idenf"][:])
    pr.copy("V", w["antib"][:], w["antif"][:])
    pr.dma("Y", kpadb.ap(), io["kpad"])
    lamn = emit_lamn(pr, w, io, 64, "s8_lamn")

    uR = pr.sbuf("s8_uR", [128, 2 * NCHS * HS4], BF16)
    hsb = pr.sbuf("s8_hsb", [64, NCHS * 2 * HS4], BF16, alias="s8_uR")
    y4T = pr.sbuf("s8_y4T", [128, 2 * NCHS * HS4], BF16)
    est = [pr.sbuf(f"s8_est{i}", [64, 2 * 16 * 128], BF16) for i in range(2)]
    hkt = [pr.sbuf(f"s8_hkt{i}", [128, 16 * 128], BF16) for i in range(2)]
    hstg = [pr.sbuf(f"s8_hstg{i}", [32, 2 * HS4], BF16) for i in range(2)]
    Sre = pr.sbuf("s8_Sre", [64, HS4], F32)
    Sim = pr.sbuf("s8_Sim", [64, HS4], F32)
    tA = pr.sbuf("s8_tA", [64, HS4], F32)
    tB = pr.sbuf("s8_tB", [64, HS4], F32)
    uch = [pr.sbuf(f"s8_uch{i}", [128, HS4], BF16) for i in range(2)]
    x3ch = [pr.sbuf(f"s8_x3ch{i}", [128, 1024], F32) for i in range(2)]
    x4ch = [pr.sbuf("s8_x4ch0", [128, 1024], F32, alias="s8_tA"),
            pr.sbuf("s8_x4ch1", [128, 1024], F32, alias="s8_tB")]
    gsl = [pr.sbuf(f"s8_gsl{i}", [128, 1024], BF16) for i in range(2)]
    pA = pr.psum("s8_pA", [128, 512], F32)
    pB = pr.psum("s8_pB", [128, 512], F32)
    pC = pr.psum("s8_pC", [128, 1024], BF16)
    pD = pr.psum("s8_pD", [128, 1024], BF16)
    ps_e = [pA, pB]
    ps_b = [pC, pD]

    hsv = hsb[:]
    for si in range(2):
        hr = io[f"hist_{si}"].rearrange("(c ri n h) -> n c ri h", c=NCHS, ri=2, n=NS4)
        for c in range(NCHS):
            hb = hstg[(si * NCHS + c) % 2]
            pr.dma("Y", hb[:].rearrange("p (ri h) -> p ri h", ri=2), hr[:, c, :, :])
            pr.copy("V", hsv[si * 32:si * 32 + 32, c * 2048:(c + 1) * 2048], hb[:])
    for si in range(2):
        sv = io[f"Sst_{si}"].rearrange("(ri n h) -> ri n h", ri=2, n=NS4)
        pr.dma("Y", Sre[:][si * 32:si * 32 + 32, :], sv[0])
        pr.dma("Y", Sim[:][si * 32:si * 32 + 32, :], sv[1])
    lre = lamn[:][:, 0:HS4]
    lim = lamn[:][:, HS4:2 * HS4]
    for c in range(NCHS):
        pr.tt("V", hsv[:, c * 2048:c * 2048 + HS4],
              hsv[:, c * 2048:c * 2048 + HS4], Sre[:], OP.add)
        pr.tt("V", hsv[:, c * 2048 + HS4:(c + 1) * 2048],
              hsv[:, c * 2048 + HS4:(c + 1) * 2048], Sim[:], OP.add)
        if c < NCHS - 1:
            pr.tt("V", tA[:], lim, Sim[:], OP.mult)
            pr.tt("V", tB[:], lim, Sre[:], OP.mult)
            pr.tt("V", Sre[:], Sre[:], lre, OP.mult)
            pr.tt("V", Sre[:], Sre[:], tA[:], OP.subtract)
            pr.tt("V", Sim[:], Sim[:], lre, OP.mult)
            pr.tt("V", Sim[:], Sim[:], tB[:], OP.add)
    powEr = io["powE"].rearrange("(ri h n k) -> ri h n k", ri=2, h=HS4, n=NS4)
    H4 = hsb[:].rearrange("p (c ri h) -> p ri h c", c=NCHS, ri=2)
    y4f = y4T[:].rearrange("p (si c h) -> p si h c", si=2, c=NCHS)
    y4s = [y4f[:, 0], y4f[:, 1]]
    for hb in range(HS4 // 16):
        eb = est[hb % 2]
        ebv = eb[:]
        for si in range(2):
            for ri in range(2):
                pr.dma("Y", ebv[si * 32:si * 32 + 32, ri * 2048:(ri + 1) * 2048]
                       .rearrange("p (h k) -> p h k", h=16),
                       powEr[ri, hb * 16:(hb + 1) * 16, :, :].rearrange("h n k -> n h k"))
        ps = ps_e[hb % 2]
        for hh in range(16):
            h = hb * 16 + hh
            for si in range(2):
                for ri in range(2):
                    pr.mm(ps[:, hh * 32 + si * 16:hh * 32 + si * 16 + 16],
                          ebv[si * 32:si * 32 + 32,
                              ri * 2048 + hh * 128:ri * 2048 + (hh + 1) * 128],
                          H4[si * 32:si * 32 + 32, ri, h, :],
                          start=(ri == 0), stop=(ri == 1))
        psv = ps[:].rearrange("p (hh si c) -> p si hh c", hh=16, si=2)
        for si in range(2):
            pr.copy("V", y4s[si][:, hb * 16:(hb + 1) * 16, :], psv[:, si])
    pr.barrier()
    uRv = uR[:]
    for si in range(2):
        uTr = io[f"uT_{si}"].rearrange("(p f) -> p f", p=128)
        for c in range(NCHS):
            i = (si * NCHS + c) % 2
            pr.dma("Y", uch[i][:], uTr[:, c * HS4:(c + 1) * HS4])
            for half in range(2):
                ps = ps_e[(2 * (si * NCHS + c) + half) % 2]
                pr.mm(ps[:], w["antib"][:],
                      uch[i][:][:, half * 512:(half + 1) * 512], start=True, stop=True)
                pr.copy("V", uRv[:, (si * NCHS + c) * HS4 + half * 512:
                                (si * NCHS + c) * HS4 + (half + 1) * 512], ps[:])
    pr.barrier()
    uR4 = uR[:].rearrange("p (si c h) -> p si c h", si=2, c=NCHS)
    for hb in range(HS4 // 16):
        hk = hkt[hb % 2]
        pr.dma("Y", hk[:], bass.AP(kpadb, hb * 16 * 256, [[1, 128], [256, 16], [1, 128]]))
        ps = ps_e[hb % 2]
        for hh in range(16):
            h = hb * 16 + hh
            pr.mm(ps[:, hh * 32:hh * 32 + 32],
                  hk[:][:, hh * 128:(hh + 1) * 128], uR4[:, :, :, h],
                  start=True, stop=True)
        psv = ps[:].rearrange("p (hh si c) -> p si hh c", hh=16, si=2)
        for si in range(2):
            pr.tt("V", y4s[si][:, hb * 16:(hb + 1) * 16, :],
                  y4s[si][:, hb * 16:(hb + 1) * 16, :], psv[:, si], OP.add)
    pr.barrier()
    for si in range(2):
        x3r = io[f"x3_{si}"].rearrange("(g p) t -> p g t", p=128)
        x4r = io[f"x4_{si}"].rearrange("(g p) t -> p g t", p=128)
        for c in range(NCHS):
            i = (si * NCHS + c) % 2
            pr.dma("Y", x3ch[i][:].rearrange("p (g t) -> p g t", g=8),
                   x3r[:, :, c * 128:(c + 1) * 128])
            pr.act("S", gsl[i][:], y4T[:][:, (si * NCHS + c) * HS4:
                                          (si * NCHS + c + 1) * HS4], AF.Gelu)
            for hg in range(8):
                pr.transpose(ps_b[i][:, hg * 128:(hg + 1) * 128],
                             gsl[i][:][:, hg * 128:(hg + 1) * 128], w["idenb"][:])
            pr.tt("V", x4ch[i][:], x3ch[i][:], ps_b[i][:], OP.add)
            pr.dma("P", x4r[:, :, c * 128:(c + 1) * 128],
                   x4ch[i][:].rearrange("p (g t) -> p g t", g=8))


def build_launch8():
    nc = bass.Bass("TRN2", target_bir_lowering=False)
    pr = Prog(nc)
    io = _declare_s4d8_io(nc)
    kpadb = nc.dram_tensor("kpadb", [HS4 * 256], BF16)
    emit_s4d_final(pr, io, kpadb)
    pr.fence_all("P")
    pr.close()
    return nc



def emit_relay_chain(pr, io, dr):
    EB = 32
    NBK = (NE + EB - 1) // EB
    onesf = pr.sbuf("ch_onesf", [128, 1], F32)
    ones1x128f = pr.sbuf("ch_ones1f", [1, 128], F32)
    w20 = pr.sbuf("ch_w20", [1, 1], F32)
    pr.memset("V", onesf[:], 1.0)
    pr.memset("V", w20[:], 20.0)
    pr.dma("Y", ones1x128f[:], io["ones1x128f"])
    state = pr.sbuf("ch_state", [128, 256], F32)
    tprod = pr.sbuf("ch_tprod", [128, 256], F32)
    sqscr = pr.sbuf("ch_sq", [128, 256], F32)
    partials = pr.sbuf("ch_part", [128, 1], F32)
    pblk = [pr.sbuf(f"ch_pblk{i}", [128, 256 * EB], F32) for i in range(2)]
    qblk = [pr.sbuf(f"ch_qblk{i}", [128, 256 * EB], F32) for i in range(2)]
    rows = pr.sbuf("ch_rows", [1, 8], F32)
    facrow = pr.sbuf("ch_facrow", [1, NE], F32)
    maskrow = pr.sbuf("ch_mask", [1, NE], F32)
    srecv = pr.sbuf("ch_srecv", [128, 256], F32)
    ps_n = pr.psum("ch_ps_n", [1, 2], F32)
    ps_f = pr.psum("ch_ps_f", [128, 2], F32)
    stA = dr["ch_stentAll"].ap().rearrange("(k p t) -> k p t", k=NSPANS, p=128)
    fcA = dr["ch_facsAll"].ap().rearrange("(k o e) -> k o e", k=NSPANS, o=1)

    for k in range(NSPANS):
        if k == 0:
            pr.memset("V", srecv[:], 0.0)
        else:
            bo = dr[f"ch_bout_{k - 1}"].ap()

            def entry(rank, k=k, bo=bo):
                if rank == k:
                    return (srecv[:], bo[(k - 1) * 128:k * 128, 0:256])
                if rank == 7 - k:
                    return (srecv[:], bo[(8 - k) * 128:(9 - k) * 128, 256:512])
                return (srecv[:], bo[0:128, 0:256])
            pr.dma_arms("Y", entry)
        pr.dma("P", stA[k], srecv[:])
        pr.dma("Y", maskrow[:], io["maskAll"][k:k + 1, :])
        pr.copy("V", state[:], srecv[:])
        pr.memset("V", ps_f[:, 1:2], 1.0)
        for eb in range(NBK):
            e0 = eb * EB
            ne = min(EB, NE - e0)
            pb, qb = pblk[eb % 2], qblk[eb % 2]
            pb3 = pb[:].rearrange("p (pr e) -> p pr e", e=EB)
            qb3 = qb[:].rearrange("p (pr e) -> p pr e", e=EB)

            def pqload(widx, dst, k=k, e0=e0, ne=ne):
                def f(rank):
                    si = 1 if rank == 7 - k else 0
                    PQ = dr[f"PQ_{si}"].ap().rearrange("w (p pr e) -> w p pr e", p=128, pr=256)
                    return (dst, PQ[widx, :, :, e0:e0 + ne])
                return f
            pr.dma_arms("Y", pqload(0, pb3[:, :, 0:ne]))
            pr.dma_arms("Y", pqload(1, qb3[:, :, 0:ne]))
            for el in range(ne):
                e = e0 + el
                Pv = pb3[:, :, el]
                Qv = qb3[:, :, el]
                pr.tt("V", tprod[:], Pv, state[:], OP.mult)
                pr.stt("V", state[:], tprod[:], ps_f[:, (e + 1) % 2:(e + 1) % 2 + 1],
                       Qv, OP.mult, OP.add)
                pr.act("S", sqscr[:], state[:], AF.Square, accum_out=partials[:])
                pr.mm(ps_n[:1, e % 2:e % 2 + 1], onesf[:], partials[:], start=True, stop=True)
                pr.act("S", rows[:, 4:5], ps_n[:1, e % 2:e % 2 + 1], AF.Ln, bias=1e-30)
                pr.act("S", rows[:, 0:1], rows[:, 4:5], AF.Exp, scale=-0.5,
                       bias=float(math.log(20.0)))
                pr.ts("V", rows[:, 2:3], rows[:, 0:1], 1.0, None, OP.min)
                pr.tt("V", rows[:, 2:3], rows[:, 2:3], maskrow[:, e:e + 1], OP.max)
                pr.mm(ps_f[:, e % 2:e % 2 + 1], ones1x128f[:], rows[:, 2:3],
                      start=True, stop=True)
                pr.copy("V", facrow[:, e:e + 1], rows[:, 2:3])
        pr.ts("V", state[:], state[:], ps_f[:, (NE - 1) % 2:(NE - 1) % 2 + 1], None, OP.mult)
        pr.dma("P", fcA[k], facrow[:])
        cin = dr[f"ch_cin_{k}"].ap()
        pr.dma("P", cin[:, 0:256], state[:])
        pr.dma("P", cin[:, 256:512], state[:])
        pr.collective("AllGather", OP.bypass, [list(range(NCORES))],
                      [cin], [dr[f"ch_bout_{k}"].ap()])
    pr.dma_arms("Y", lambda r: (dr["stent_0"].ap(), stA[r]))
    pr.dma_arms("Y", lambda r: (dr["stent_1"].ap(), stA[7 - r]))
    pr.dma_arms("Y", lambda r: (dr["facs_0"].ap(), fcA[r]))
    pr.dma_arms("Y", lambda r: (dr["facs_1"].ap(), fcA[7 - r]))


def emit_stitch_fused(pr, io, dr):
    w = {}
    w["idenf"] = pr.sbuf("st_idenf", [128, 128], F32)
    pr.dma("Y", w["idenf"][:], io["idenf"])
    bi = dr["st_bin"]
    bo = dr["st_bout"]
    for si in range(2):
        pr.dma("P", bi.ap()[si * 32:(si + 1) * 32, :].rearrange("n (ri h) -> n ri h", ri=2),
               dr[f"send_{si}"].ap().rearrange("(ri n h) -> n ri h", ri=2, n=NS4))
    pr.collective("AllGather", OP.bypass, [list(range(NCORES))], [bi.ap()], [bo.ap()])
    lamn = emit_lamn(pr, w, io, 32, "st_lamn")
    l16 = pr.sbuf("st_l16", [32, 2 * HS4], F32)
    t1 = pr.sbuf("st_t1", [32, HS4], F32)
    t2 = pr.sbuf("st_t2", [32, HS4], F32)
    pr.copy("V", l16[:], lamn[:])
    re = l16[:][:, 0:HS4]
    im = l16[:][:, HS4:2 * HS4]
    for _ in range(4):
        pr.tt("V", t1[:], re, im, OP.mult)
        pr.tt("V", re, re, re, OP.mult)
        pr.tt("V", t2[:], im, im, OP.mult)
        pr.tt("V", re, re, t2[:], OP.subtract)
        pr.ts("V", im, t1[:], 2.0, None, OP.mult)
    Sre = pr.sbuf("st_Sre", [32, HS4], F32)
    Sim = pr.sbuf("st_Sim", [32, HS4], F32)
    sd = pr.sbuf("st_sd", [32, 2 * HS4], F32)
    SstAll = dr["SstAll"].ap().rearrange("(b s f) -> b s f", b=NB, s=NSPANS)
    for b in range(NB):
        pr.memset("V", Sre[:], 0.0)
        pr.memset("V", Sim[:], 0.0)
        for s in range(NSPANS):
            sv = SstAll[b, s].rearrange("(ri n h) -> ri n h", ri=2, n=NS4)
            pr.dma("P", sv[0], Sre[:])
            pr.dma("P", sv[1], Sim[:])
            rowbase = s * 64 if b == 0 else (7 - s) * 64 + 32
            pr.dma("Y", sd[:], bo.ap()[rowbase:rowbase + 32, :])
            pr.tt("V", t1[:], im, Sim[:], OP.mult)
            pr.tt("V", t2[:], im, Sre[:], OP.mult)
            pr.tt("V", Sre[:], Sre[:], re, OP.mult)
            pr.tt("V", Sre[:], Sre[:], t1[:], OP.subtract)
            pr.tt("V", Sre[:], Sre[:], sd[:][:, 0:HS4], OP.add)
            pr.tt("V", Sim[:], Sim[:], re, OP.mult)
            pr.tt("V", Sim[:], Sim[:], t2[:], OP.add)
            pr.tt("V", Sim[:], Sim[:], sd[:][:, HS4:2 * HS4], OP.add)
    for si in range(2):
        pr.dma_arms("Y", lambda r, si=si: (dr[f"Sst_{si}"].ap(),
                                           SstAll[_span_of_core(r)[si][0],
                                                  _span_of_core(r)[si][1]]))


def emit_halo(pr, dr, xekey, binm, boutm):
    bi = dr[binm]
    bo = dr[boutm]
    for si in range(2):
        xe = dr[xekey.format(si=si)].ap().rearrange("(g p) t -> p g t", p=128)
        bv = bi.ap().rearrange("p (s2 g e) -> p s2 g e", s2=2, g=8)
        pr.dma("P", bv[:, si, :, 0], xe[:, :, 1], allow_slow_non_contiguous=True)
        pr.dma("P", bv[:, si, :, 1], xe[:, :, XE - 2], allow_slow_non_contiguous=True)
    pr.collective("AllGather", OP.bypass, [list(range(NCORES))], [bi.ap()], [bo.ap()])
    bog = bo.ap().rearrange("(r p) (s2 g e) -> r p s2 g e", r=NCORES, p=128, s2=2, g=8)

    def pick(si, edge):
        def f(rank):
            b, s = _span_of_core(rank)[si]
            xe = dr[xekey.format(si=si)].ap().rearrange("(g p) t -> p g t", p=128)
            dst = xe[:, :, 0 if edge == 0 else XE - 1]
            ns = s - 1 if edge == 0 else s + 1
            if 0 <= ns < NSPANS:
                oc, osi = _owner(b, ns)
                src = bog[oc, :, osi, :, 1 if edge == 0 else 0]
            else:
                src = dr["zed"].ap()
            return (dst, src)
        return f
    for si in range(2):
        for edge in range(2):
            pr.dma_arms("P", pick(si, edge), allow_slow_non_contiguous=True)


def build_fused():
    nc = bass.Bass("TRN2", target_bir_lowering=False)
    pr = Prog(nc)
    hi = {}
    for nm, shp in [("WinT", [D, DH]), ("WdtT", [DI, DI]), ("WBT", [DI, NST]),
                    ("WCT", [DI, NST]), ("WoutT", [DI, D]), ("w1T", [D, DH]),
                    ("w2T", [DH, D]), ("normw", [128, 8]), ("cwa", [128, 64]),
                    ("cba", [128, 16]), ("bdt", [128, 16]), ("Alog", [128, 256]),
                    ("Dpar", [128, 16]), ("rscal", [1, 1]), ("cwf", [128, 96]),
                    ("cbf", [128, 32]), ("s4_logdt", [128, 8]), ("s4_Ar", [128, 256]),
                    ("s4_Ai", [128, 256]), ("s4_Cr", [128, 256]), ("s4_Ci", [128, 256]),
                    ("s4_D", [1, 1]), ("ones1x128f", [1, 128]), ("idenf", [128, 128]),
                    ("antif", [128, 128]), ("maskAll", [NSPANS, NE])]:
        hi[nm] = declare_in(nc, nm, shp)
    for si in range(2):
        hi[f"x_{si}"] = declare_in(nc, f"x_{si}", [D, SPAN])
        hi[f"xhalo_{si}"] = declare_in(nc, f"xhalo_{si}", [D, 128])
        hi[f"out_{si}"] = declare_out(nc, f"out_{si}", [D, SPAN])
    dr = {}

    def mkdr(name, shape, dtype=F32):
        dr[name] = nc.dram_tensor(name, shape, dtype)
        return dr[name]
    for si in range(2):
        mkdr(f"xcb_{si}", [DI, SPAN], BF16)
        mkdr(f"sg_{si}", [DI, SPAN], BF16)
        mkdr(f"stat_{si}", [2, SPAN])
        mkdr(f"dt_{si}", [DI, SPAN])
        mkdr(f"Brow_{si}", [NST, SPAN])
        mkdr(f"Crow_{si}", [NST, SPAN])
        mkdr(f"dtp_{si}", [DI, PW])
        mkdr(f"xcbp_{si}", [DI, PW], BF16)
        mkdr(f"sgp_{si}", [DI, PW], BF16)
        mkdr(f"Bp_{si}", [NST, PW])
        mkdr(f"Cp_{si}", [NST, PW])
        mkdr(f"PQ_{si}", [2, 128 * 256 * NE])
        mkdr(f"stent_{si}", [128, 256])
        mkdr(f"facs_{si}", [1, NE])
        mkdr(f"yac_{si}", [DI, PW])
        mkdr(f"ysgp_{si}", [DI, PW], BF16)
        mkdr(f"x2e_{si}", [D, XE])
        mkdr(f"rsc2_{si}", [1, XE])
        mkdr(f"h1_{si}", [DH, XE], BF16)
        mkdr(f"x3_{si}", [D, SPAN])
        mkdr(f"uT_{si}", [128 * NCHS * HS4], BF16)
        mkdr(f"hist_{si}", [NCHS * 2 * NS4 * HS4], BF16)
        mkdr(f"send_{si}", [2 * NS4 * HS4])
        mkdr(f"Sst_{si}", [2 * NS4 * HS4])
        mkdr(f"x4e_{si}", [D, XE])
        mkdr(f"rsc4_{si}", [1, XE])
        mkdr(f"h2_{si}", [DH, XE], BF16)
    mkdr("zDI", [DI, 16])
    mkdr("zDIb", [DI, 16], BF16)
    mkdr("zNST", [NST, 16])
    mkdr("zed", [128, 8])
    mkdr("powG", [128 * 2 * HS4 * NS4], BF16)
    mkdr("powE", [2 * HS4 * NS4 * 128], BF16)
    mkdr("kpad", [HS4 * 256], BF16)
    mkdr("kpadb", [HS4 * 256], BF16)
    mkdr("lamT", [2 * HS4 * NS4])
    mkdr("ch_stentAll", [NSPANS * 128 * 256])
    mkdr("ch_facsAll", [NSPANS * 1 * NE])
    for k in range(NSPANS):
        mkdr(f"ch_cin_{k}", [128, 512])
        mkdr(f"ch_bout_{k}", [128 * NCORES, 512])
    mkdr("SstAll", [NB * NSPANS * 2 * NS4 * HS4])
    mkdr("st_bin", [64, 2048])
    mkdr("st_bout", [64 * NCORES, 2048])
    mkdr("hb2_in", [128, 32])
    mkdr("hb2_out", [128 * NCORES, 32])
    mkdr("hb4_in", [128, 32])
    mkdr("hb4_out", [128 * NCORES, 32])

    A = lambda n: dr[n].ap()

    pr.set_prefix("z_")
    zt = pr.sbuf("zt", [128, 16], F32)
    ztb = pr.sbuf("ztb", [128, 16], BF16)
    pr.memset("V", zt[:], 0.0)
    pr.memset("V", ztb[:], 0.0)
    for g in range(DI // 128):
        pr.dma("P", A("zDI").rearrange("(g p) c -> g p c", p=128)[g], zt[:])
        pr.dma("P", A("zDIb").rearrange("(g p) c -> g p c", p=128)[g], ztb[:])
    pr.dma("P", A("zNST"), zt[:][0:NST, :])
    pr.dma("P", A("zed"), zt[:][:, 0:8])

    pr.phase()
    pr.set_prefix("pp_")
    io = {k: hi[k] for k in ("s4_logdt", "s4_Ar", "s4_Ai", "s4_Cr", "s4_Ci", "s4_D")}
    io["powG"] = A("powG")
    io["powE"] = A("powE")
    io["kpad"] = A("kpad")
    io["lamT"] = A("lamT")
    emit_s4d_prep(pr, io)
    pr.dma("Y", A("kpadb"), A("kpad"))

    pr.phase()
    pr.set_prefix("sa_")
    io = {k: hi[k] for k in ("WinT", "normw", "cwa", "cba", "ones1x128f")}
    for si in range(2):
        io[f"x_{si}"] = hi[f"x_{si}"]
        io[f"xhalo_{si}"] = hi[f"xhalo_{si}"]
        io[f"xcb_{si}"] = A(f"xcb_{si}")
        io[f"sg_{si}"] = A(f"sg_{si}")
        io[f"stat_{si}"] = A(f"stat_{si}")
    w = emit_weights_1a(pr, io)
    for si in range(2):
        emit_stage1a_span(pr, w, io, si)

    pr.phase()
    pr.set_prefix("sb_")
    io = {k: hi[k] for k in ("WdtT", "WBT", "WCT", "bdt")}
    for si in range(2):
        io[f"xcb_{si}"] = A(f"xcb_{si}")
        io[f"dt_{si}"] = A(f"dt_{si}")
        io[f"Brow_{si}"] = A(f"Brow_{si}")
        io[f"Crow_{si}"] = A(f"Crow_{si}")
    w = emit_weights_1b(pr, io)
    for si in range(2):
        emit_stage1b_span(pr, w, io, si)

    pr.phase()
    pr.set_prefix("p1_")
    io = {"Alog": hi["Alog"], "ones1x128f": hi["ones1x128f"],
          "zDI": A("zDI"), "zDIb": A("zDIb"), "zNST": A("zNST")}
    for si in range(2):
        for k in ("dt", "xcb", "Brow", "dtp", "xcbp", "Bp", "PQ"):
            io[f"{k}_{si}"] = A(f"{k}_{si}")
    w = emit_scan_consts(pr, io)
    for si in range(2):
        emit_pass1_span(pr, w, io, si, None, fused=True)

    pr.phase()
    pr.set_prefix("ch_")
    io = {"ones1x128f": hi["ones1x128f"], "maskAll": hi["maskAll"]}
    emit_relay_chain(pr, io, dr)

    pr.phase()
    pr.set_prefix("p2_")
    io = {k: hi[k] for k in ("Alog", "ones1x128f", "Dpar", "WoutT", "rscal")}
    io.update({"zDI": A("zDI"), "zDIb": A("zDIb"), "zNST": A("zNST")})
    for si in range(2):
        for k in ("dt", "xcb", "sg", "Brow", "Crow", "stat", "stent", "facs",
                  "dtp", "xcbp", "sgp", "Bp", "Cp", "yac", "ysgp"):
            io[f"{k}_{si}"] = A(f"{k}_{si}")
        io[f"x_{si}"] = hi[f"x_{si}"]
        io[f"ysg_{si}"] = A(f"ysgp_{si}")
        io[f"x2_{si}"] = A(f"x2e_{si}")[:, 1:XE - 1]
    w = emit_pass2_weights(pr, io)
    for si in range(2):
        emit_pass2_span(pr, w, io, si, None, fused=True)

    pr.phase()
    pr.set_prefix("h2_")
    emit_halo(pr, dr, "x2e_{si}", "hb2_in", "hb2_out")

    for which, xk, hk, rk, outk in (("a", "x2e", "h1", "rsc2", None),
                                    ("b", "x2e", "h1", "rsc2", "x3")):
        pr.phase()
        pr.set_prefix(f"f1{which}_")
        io = {k: hi[k] for k in ("ones1x128f", "idenf", "cwf", "cbf")}
        io["w1T"] = hi["w1T"]
        io["w2T"] = hi["w2T"]
        for si in range(2):
            io[f"x2e_{si}"] = A(f"{xk}_{si}")
            io[f"h_{si}"] = A(f"{hk}_{si}")
            io[f"rsc_{si}"] = A(f"{rk}_{si}")
            if outk:
                io[f"x3_{si}"] = A(f"{outk}_{si}")
        w = emit_ffn_weights(pr, io, which)
        for si in range(2):
            if which == "a":
                emit_ffn_a_span(pr, w, io, si)
            else:
                emit_ffn_b_span(pr, w, io, si)

    pr.phase()
    pr.set_prefix("s6_")
    io = {"idenf": hi["idenf"], "ones1x128f": hi["ones1x128f"],
          "powG": A("powG"), "lamT": A("lamT")}
    for si in range(2):
        io[f"x3_{si}"] = A(f"x3_{si}")
        io[f"uT_{si}"] = A(f"uT_{si}")
        io[f"hist_{si}"] = A(f"hist_{si}")
        io[f"send_{si}"] = A(f"send_{si}")
    w = {}
    w["idenf"] = pr.sbuf("w_idenf", [128, 128], F32)
    w["idenb"] = pr.sbuf("w_idenb", [128, 128], BF16)
    w["onesb"] = pr.sbuf("w_onesb", [128, 1], BF16)
    w["ones1x128f"] = pr.sbuf("w_ones1f", [1, 128], F32)
    pr.dma("Y", w["idenf"][:], io["idenf"])
    pr.dma("Y", w["ones1x128f"][:], io["ones1x128f"])
    pr.copy("V", w["idenb"][:], w["idenf"][:])
    pr.memset("V", w["onesb"][:], 1.0)
    uT = pr.sbuf("s6_uT", [128, 2 * NCHS * HS4], BF16)
    lamn = emit_lamn(pr, w, io, 64, "s6_lamn")
    for si in range(2):
        emit_ln3_span(pr, w, io, si, uT)
    Gsb = emit_s4d_g(pr, w, io, uT)
    emit_s4d_scan(pr, io, lamn, Gsb)

    pr.phase()
    pr.set_prefix("st_")
    io = {"idenf": hi["idenf"], "lamT": A("lamT")}
    emit_stitch_fused(pr, io, dr)

    pr.phase()
    pr.set_prefix("s8_")
    io = {"idenf": hi["idenf"], "antif": hi["antif"], "lamT": A("lamT"),
          "powE": A("powE"), "kpad": A("kpad")}
    for si in range(2):
        io[f"x3_{si}"] = A(f"x3_{si}")
        io[f"uT_{si}"] = A(f"uT_{si}")
        io[f"hist_{si}"] = A(f"hist_{si}")
        io[f"Sst_{si}"] = A(f"Sst_{si}")
        io[f"x4_{si}"] = A(f"x4e_{si}")[:, 1:XE - 1]
    emit_s4d_final(pr, io, dr["kpadb"])

    pr.phase()
    pr.set_prefix("h4_")
    emit_halo(pr, dr, "x4e_{si}", "hb4_in", "hb4_out")

    for which in ("a", "b"):
        pr.phase()
        pr.set_prefix(f"f2{which}_")
        io = {k: hi[k] for k in ("ones1x128f", "idenf", "cwf", "cbf")}
        io["w1T"] = hi["w1T"]
        io["w2T"] = hi["w2T"]
        for si in range(2):
            io[f"x2e_{si}"] = A(f"x4e_{si}")
            io[f"h_{si}"] = A(f"h2_{si}")
            io[f"rsc_{si}"] = A(f"rsc4_{si}")
            io[f"x3_{si}"] = hi[f"out_{si}"]
        w = emit_ffn_weights(pr, io, which)
        for si in range(2):
            if which == "a":
                emit_ffn_a_span(pr, w, io, si)
            else:
                emit_ffn_b_span(pr, w, io, si)
    for e in Prog.ENGINES:
        pr.fence_all(e)
    pr.close()
    return nc


def kernel(**inputs):
    maps = host_prepare(inputs)
    mask = np.stack([span_mask(s) for s in range(NSPANS)])
    for c in range(NCORES):
        maps[c]["maskAll"] = mask
    nc = build_fused()
    res = run_bass_kernel_spmd(nc, maps, core_ids=list(range(NCORES)))
    out = np.empty((NB, L, D), np.float32)
    for c in range(NCORES):
        for si, (b, s) in enumerate(_span_of_core(c)):
            t0 = s * SPAN
            out[b, t0:t0 + SPAN] = np.asarray(res.results[c][f"out_{si}"]) \
                .reshape(8, 128, SPAN).transpose(2, 0, 1).reshape(SPAN, D)
    return out


def _mk_x_e(xg, c, si):
    b, s = _span_of_core(c)[si]
    t0 = s * SPAN
    xe = np.zeros((D, XE), np.float32)
    xe[:, 1:XE - 1] = xg[b, t0:t0 + SPAN].T
    if t0 > 0:
        xe[:, 0] = xg[b, t0 - 1]
    if t0 + SPAN < L:
        xe[:, XE - 1] = xg[b, t0 + SPAN]
    return xe


def _gather_fm(results, key):
    xg = np.zeros((NB, L, D), np.float32)
    for c in range(NCORES):
        for si, (b, s) in enumerate(_span_of_core(c)):
            t0 = s * SPAN
            xg[b, t0:t0 + SPAN] = np.asarray(results[c][key.format(si=si)]) \
                .reshape(8, 128, SPAN).transpose(2, 0, 1).reshape(SPAN, D)
    return xg


def _run_groups(build_fn, maps, key_of_core):
    out = [None] * NCORES
    groups = {}
    for c in range(NCORES):
        groups.setdefault(key_of_core(c), []).append(c)
    for key, cores in groups.items():
        nc = build_fn(cores[0])
        res = run_bass_kernel_spmd(nc, [maps[c] for c in cores], core_ids=cores)
        for i, c in enumerate(cores):
            out[c] = dict(res.results[i])
    return out


def _owner(b, s):
    for c in range(NCORES):
        for si, (bb, ss) in enumerate(_span_of_core(c)):
            if bb == b and ss == s:
                return c, si
    raise KeyError


def _ffn_pass(maps, xg):
    maps5 = []
    for c in range(NCORES):
        m = {k: maps[c][k] for k in ("ones1x128f", "idenf", "cwf", "cbf", "w1T", "w2T")}
        for si in range(2):
            m[f"x2e_{si}"] = _mk_x_e(xg, c, si)
        maps5.append(m)
    res = run_bass_kernel_spmd(build_launch5a(), maps5, core_ids=list(range(NCORES)))
    ra = [dict(r) for r in res.results]
    for c in range(NCORES):
        for si in range(2):
            maps5[c][f"h_{si}"] = ra[c][f"h_{si}"]
    res = run_bass_kernel_spmd(build_launch5b(), maps5, core_ids=list(range(NCORES)))
    rb = [dict(r) for r in res.results]
    return _gather_fm(rb, "x3_{si}")


def kernel_multi(**inputs):
    maps = host_prepare(inputs)
    cores = list(range(NCORES))
    r1 = [dict(r) for r in run_bass_kernel_spmd(build_launch1a(), maps, core_ids=cores).results]
    for c in cores:
        for si in range(2):
            maps[c][f"xcb_{si}"] = r1[c][f"xcb_{si}"]
    rb = [dict(r) for r in run_bass_kernel_spmd(build_launch1b(), maps, core_ids=cores).results]
    for c in cores:
        r1[c].update(rb[c])
        for si in range(2):
            for k in ("dt", "Brow"):
                maps[c][f"{k}_{si}"] = r1[c][f"{k}_{si}"]
    r2 = _run_groups(build_launch2, maps, lambda c: tuple(span_shift(s) for _, s in _span_of_core(c)))
    maps3 = []
    for b in range(NB):
        m = {"ones1x128f": maps[0]["ones1x128f"]}
        for s in range(NSPANS):
            oc, osi = _owner(b, s)
            m[f"PQc_{s}"] = r2[oc][f"PQ_{osi}"]
            m[f"mask_{s}"] = span_mask(s).reshape(1, NE)
        maps3.append(m)
    r3 = [dict(r) for r in run_bass_kernel_spmd(build_launch3(), maps3, core_ids=[0, 1]).results]
    for c in cores:
        for si, (b, s) in enumerate(_span_of_core(c)):
            for k in ("sg", "Crow", "stat"):
                maps[c][f"{k}_{si}"] = r1[c][f"{k}_{si}"]
            maps[c][f"stent_{si}"] = r3[b][f"stent_{s}"]
            maps[c][f"facs_{si}"] = r3[b][f"facs_{s}"]
    r4 = _run_groups(build_launch4, maps, lambda c: tuple(span_shift(s) for _, s in _span_of_core(c)))
    x2 = _gather_fm(r4, "x2_{si}")
    x3 = _ffn_pass(maps, x2)
    rp = [dict(r) for r in run_bass_kernel_spmd(build_s4d_prep_test(), maps, core_ids=cores).results]
    maps6 = []
    for c in cores:
        m = {"idenf": maps[c]["idenf"], "ones1x128f": maps[c]["ones1x128f"],
             "powG": rp[c]["powG"], "lamT": rp[c]["lamT"]}
        for si, (b, s) in enumerate(_span_of_core(c)):
            m[f"x3_{si}"] = np.ascontiguousarray(x3[b, s * SPAN:(s + 1) * SPAN].T)
        maps6.append(m)
    r6 = [dict(r) for r in run_bass_kernel_spmd(build_launch6(), maps6, core_ids=cores).results]
    maps7 = []
    for b in range(NB):
        m = {"idenf": maps[0]["idenf"], "lamT": rp[b]["lamT"]}
        for s in range(NSPANS):
            oc, osi = _owner(b, s)
            m[f"send_{s}"] = r6[oc][f"send_{osi}"]
        maps7.append(m)
    r7 = [dict(r) for r in run_bass_kernel_spmd(build_launch7(), maps7, core_ids=[0, 1]).results]
    maps8 = []
    for c in cores:
        m = {"idenf": maps[c]["idenf"], "antif": maps[c]["antif"],
             "lamT": rp[c]["lamT"], "powE": rp[c]["powE"], "kpad": rp[c]["kpad"]}
        for si, (b, s) in enumerate(_span_of_core(c)):
            m[f"x3_{si}"] = maps6[c][f"x3_{si}"]
            m[f"uT_{si}"] = r6[c][f"uT_{si}"]
            m[f"hist_{si}"] = r6[c][f"hist_{si}"]
            m[f"Sst_{si}"] = r7[b][f"Sst_{s}"]
        maps8.append(m)
    r8 = [dict(r) for r in run_bass_kernel_spmd(build_launch8(), maps8, core_ids=cores).results]
    x4 = _gather_fm(r8, "x4_{si}")
    x5 = _ffn_pass(maps, x4)
    return np.ascontiguousarray(x5)


# version 11
# speedup vs baseline: 1.0613x; 1.0039x over previous
import numpy as np
import math
import concourse.bass as bass
import concourse.mybir as mybir
from concourse.alu_op_type import AluOpType as OP
from concourse.bass_utils import run_bass_kernel_spmd

F32 = mybir.dt.float32
BF16 = mybir.dt.bfloat16
I8 = mybir.dt.int8
AF = mybir.ActivationFunctionType
AX = mybir.AxisListType

D = 1024
DI = 2048
DH = 4096
NST = 16
HS4 = 1024
NS4 = 32
SPAN = 2048
NSPANS = 8
NB = 2
L = 16384
NCORES = 8
TCH = 512

LN_EPS = 1e-5
RMS_EPS = 1e-6

_SC_DT_C = (0.1 + 0.001) / 2.0
_SC_DT_R = (0.1 - 0.001) / 2.0
_SC_DT_S1 = 0.5 / (_SC_DT_R + 1e-8)
_SC_DT_S2 = _SC_DT_R / 0.5
_SC_A_S1 = 0.5 / (5.0 + 1e-8)
_SC_A_S2 = 5.0 / 0.5



class Prog:
    ENGINES = ("V", "S", "P", "T", "Y")

    def __init__(self, nc):
        self.nc = nc
        self.eng = {"V": nc.vector, "S": nc.scalar, "P": nc.gpsimd,
                    "T": nc.tensor, "Y": nc.sync}
        self._ctx = []
        self.prog, self.progv = {}, {}
        for e in self.ENGINES:
            self.prog[e] = self._sem(f"prog{e}")
            self.progv[e] = 0
        self.dsem, self.dsemv = {}, {}
        for e in ("P", "Y", "S", "V"):
            self.dsem[e] = self._sem(f"dma{e}")
            self.dsemv[e] = 0
        self.ccsem = self._sem("ccsem")
        self.ccv = 0
        self.writer = {}
        self.readers = {}
        self.waited = {e: {} for e in self.ENGINES}
        self._aliases = {}
        self._prefix = ""
        self._phase_psums = []

    def _sem(self, name):
        g = self.nc.semaphore(name)
        s = g.__enter__()
        self._ctx.append(g)
        return s

    def set_prefix(self, p):
        self._prefix = p

    def phase(self):
        self.barrier()
        if hasattr(self, "_tiles"):
            self._sb_off = self._sb_base
            self._tiles = {}
            self._tile_off = {}
        for g in reversed(self._phase_psums):
            g.__exit__(None, None, None)
        self._phase_psums = []
        self._psum_tiles = {}

    def sbuf(self, name, shape, dtype, alias=None):
        name = self._prefix + name
        if alias is not None:
            alias = self._prefix + alias
        if not hasattr(self, "_tiles"):
            self._tiles = {}
            self._tile_off = {}
            self._sb_off = (int(self.nc.sbuf_base) + 31) // 32 * 32
            self._sb_base = self._sb_off
            self._sb_top = int(self.nc.sbuf_top)
        if name in self._tiles:
            return self._tiles[name]
        import math
        nb = int(np.prod(shape[1:])) * mybir.dt.size(dtype)
        if alias is not None:
            off = self._tile_off[alias]
            grp = self._aliases.get(alias)
            if grp is None:
                grp = (alias,)
            grp = tuple(set(grp + (name, alias)))
            for m in grp:
                self._aliases[m] = grp
        else:
            off = self._sb_off
            self._sb_off = (off + nb + 31) // 32 * 32
            assert self._sb_off <= self._sb_top, \
                f"SBUF overflow at {name}: {self._sb_off} > {self._sb_top}"
        t = self.nc.alloc_sbuf_tensor_at(name, shape, dtype, offset=off)
        self._tile_off[name] = off
        self._tiles[name] = t
        return t

    def psum(self, name, shape, dtype):
        name = self._prefix + name
        if not hasattr(self, "_psum_tiles"):
            self._psum_tiles = {}
        if name in self._psum_tiles:
            return self._psum_tiles[name]
        g = self.nc.psum_tensor(name, shape, dtype)
        t = g.__enter__()
        self._phase_psums.append(g)
        self._psum_tiles[name] = t
        return t

    def close(self):
        for g in reversed(self._phase_psums):
            g.__exit__(None, None, None)
        self._phase_psums = []
        for g in reversed(self._ctx):
            g.__exit__(None, None, None)

    @staticmethod
    def _key(x):
        if isinstance(x, bass.AP):
            return x.tensor.name
        return x.name

    def _alias_group(self, k):
        return self._aliases.get(k, (k,))

    def _sem_of(self, tag):
        if tag.startswith("d"):
            return self.dsem[tag[1]], tag
        if tag == "cc":
            return self.ccsem, "cc"
        return self.prog[tag], tag

    def _wait(self, e, tag, val):
        if val <= 0:
            return
        w = self.waited[e]
        if w.get(tag, 0) >= val:
            return
        sem, _ = self._sem_of(tag)
        self.eng[e].wait_ge(sem, val)
        w[tag] = val

    def _pre(self, e, reads, writes):
        for x in reads:
            for k in self._alias_group(self._key(x)):
                wr = self.writer.get(k)
                if wr:
                    self._wait(e, wr[0], wr[1])
        for x in writes:
            for k in self._alias_group(self._key(x)):
                wr = self.writer.get(k)
                if wr:
                    self._wait(e, wr[0], wr[1])
                for rt, rv in self.readers.get(k, []):
                    self._wait(e, rt, rv)

    def _post(self, tag, val, reads, writes):
        for x in writes:
            k = self._key(x)
            self.writer[k] = (tag, val)
            self.readers[k] = []
        for x in reads:
            self.readers.setdefault(self._key(x), []).append((tag, val))

    def _emit(self, e, reads, writes, fn):
        self._pre(e, reads, writes)
        inst = fn(self.eng[e])
        self.progv[e] += 1
        inst.then_inc(self.prog[e], 1)
        self._post(e, self.progv[e], reads, writes)
        return inst

    def copy(self, e, out, in_):
        self._emit(e, [in_], [out], lambda g: g.tensor_copy(out, in_))

    def memset(self, e, out, v):
        self._emit(e, [], [out], lambda g: g.memset(out, v))

    def tt(self, e, out, a, b, op):
        self._emit(e, [a, b], [out], lambda g: g.tensor_tensor(out, a, b, op))

    def ts(self, e, out, a, s1, s2, op0, op1=None, extra_reads=()):
        rd = [a] + [s for s in (s1, s2) if isinstance(s, bass.AP)] + list(extra_reads)
        if op1 is None:
            self._emit(e, rd, [out], lambda g: g.tensor_scalar(out, a, s1, None, op0))
        else:
            self._emit(e, rd, [out], lambda g: g.tensor_scalar(out, a, s1, s2, op0, op1))

    def stt(self, e, out, in0, scalar, in1, op0, op1):
        rd = [in0, in1] + ([scalar] if isinstance(scalar, bass.AP) else [])
        self._emit(e, rd, [out], lambda g: g.scalar_tensor_tensor(out, in0, scalar, in1, op0, op1))

    def act(self, e, out, in_, func, bias=None, scale=None, accum_out=None):
        kw = {}
        rd = [in_]
        if bias is not None:
            if isinstance(bias, bass.AP):
                rd.append(bias)
            elif isinstance(bias, float):
                bias = np.float32(bias)
            kw["bias"] = bias
        if scale is not None:
            if isinstance(scale, bass.AP):
                rd.append(scale)
            elif isinstance(scale, float):
                scale = np.float32(scale)
            kw["scale"] = scale
        wr = [out]
        if accum_out is not None:
            kw["accum_out"] = accum_out
            wr.append(accum_out)
        self._emit(e, rd, wr, lambda g: g.activation(out=out, in_=in_, func=func, **kw))

    def red(self, e, out, in_, op, absval=False):
        kw = {}
        if absval:
            kw["apply_absolute_value"] = True
        self._emit(e, [in_], [out], lambda g: g.tensor_reduce(out, in_, axis=AX.X, op=op, **kw))

    def recip(self, e, out, in_):
        self._emit(e, [in_], [out], lambda g: g.reciprocal(out, in_))

    def scan(self, e, out, d0, d1, init, op0, op1):
        rd = [d0, d1] + ([init] if isinstance(init, bass.AP) else [])
        self._emit(e, rd, [out], lambda g: g.tensor_tensor_scan(out, d0, d1, init, op0, op1))

    def select(self, e, out, mask, a, b):
        self._emit(e, [mask, a, b], [out], lambda g: g.select(out, mask, a, b))

    def mm(self, psum, lhsT, rhs, start, stop):
        self._emit("T", [lhsT, rhs], [psum],
                   lambda g: g.matmul(psum, lhsT, rhs, start=start, stop=stop))

    def transpose(self, psum, in_, iden):
        self._emit("T", [in_, iden], [psum], lambda g: g.transpose(psum, in_, iden))

    def dma(self, e, out, in_, **kw):
        self._pre(e, [in_], [out])
        inst = self.eng[e].dma_start(out=out, in_=in_, **kw)
        self.dsemv[e] += 16
        inst.then_inc(self.dsem[e], 16)
        self._post("d" + e, self.dsemv[e], [in_], [out])
        return inst

    def dma_arms(self, e, fn, nranks=NCORES, **kw):
        pairs = [fn(r) for r in range(nranks)]
        reads, writes, seen_r, seen_w = [], [], set(), set()
        for o, i in pairs:
            kr, kw2 = self._key(i), self._key(o)
            if kr not in seen_r:
                seen_r.add(kr)
                reads.append(i)
            if kw2 not in seen_w:
                seen_w.add(kw2)
                writes.append(o)
        self._pre(e, reads, writes)
        eng = self.eng[e]
        for arm in self.nc.switch_core_id(engines=[eng], n=128):
            o, i = pairs[arm.logical % nranks]
            eng.dma_start(out=o, in_=i, **kw).then_inc(self.dsem[e], 16)
        self.dsemv[e] += 16
        self._post("d" + e, self.dsemv[e], reads, writes)

    def dma_t(self, e, out, in_):
        self._pre(e, [in_], [out])
        inst = self.eng[e].dma_start_transpose(out, in_)
        self.dsemv[e] += 16
        inst.then_inc(self.dsem[e], 16)
        self._post("d" + e, self.dsemv[e], [in_], [out])
        return inst

    def collective(self, kind, op, groups, ins, outs):
        self._pre("P", ins, outs)
        inst = self.eng["P"].collective_compute(
            kind, op, replica_groups=groups,
            ins=[a.opt() for a in ins], outs=[a.opt() for a in outs])
        self.ccv += 1
        inst.then_inc(self.ccsem)
        self._post("cc", self.ccv, ins, outs)
        return inst

    def fence_read(self, e, aps):
        self._pre(e, aps, [])

    def fence_all(self, e):
        for k, (tag, val) in list(self.writer.items()):
            self._wait(e, tag, val)

    def barrier(self):
        for e in self.ENGINES:
            for t, v in [("V", self.progv["V"]), ("S", self.progv["S"]),
                         ("P", self.progv["P"]), ("T", self.progv["T"]),
                         ("Y", self.progv["Y"])] +                         [("d" + q, self.dsemv[q]) for q in self.dsemv] +                         [("cc", self.ccv)]:
                self._wait(e, t, v)


def strided_cols(t, start, step, count):
    ap = t[:]
    part = ap.ap[0]
    return bass.AP(ap.tensor, ap.offset + start, [list(part), [step, count]])


def bc8(ap2d, G):
    return ap2d.unsqueeze(1).broadcast_to([ap2d.shape[0], G, ap2d.shape[1]])



def _span_of_core(c):
    return [(0, c), (1, NSPANS - 1 - c)]


def _fm(a):
    return np.ascontiguousarray(np.asarray(a).T)


def host_prepare(inputs):
    x = np.asarray(inputs["L0_attn_a_x"])
    base = {}
    base["WinT"] = _fm(inputs["L0_attn_a_W_in"])
    base["WdtT"] = _fm(inputs["L0_attn_a_W_dt"])
    base["WBT"] = _fm(inputs["L0_attn_a_W_B"])
    base["WCT"] = _fm(inputs["L0_attn_a_W_C"])
    base["WoutT"] = _fm(inputs["L0_attn_a_W_out"])
    base["w1T"] = _fm(inputs["L0_ffn_w1"])
    base["w2T"] = _fm(inputs["L0_ffn_w2"])
    base["normw"] = np.ascontiguousarray(
        np.asarray(inputs["L0_attn_a_norm_w"]).reshape(8, 128).T)
    base["cwa"] = np.ascontiguousarray(
        np.asarray(inputs["L0_attn_a_conv_w"]).reshape(16, 128, 4).transpose(1, 0, 2).reshape(128, 64))
    base["cba"] = np.ascontiguousarray(
        np.asarray(inputs["L0_attn_a_conv_b"]).reshape(16, 128).T)
    base["bdt"] = np.ascontiguousarray(
        np.asarray(inputs["L0_attn_a_b_dt"]).reshape(16, 128).T)
    base["Alog"] = np.ascontiguousarray(
        np.asarray(inputs["L0_attn_a_A_log"]).reshape(16, 128, 16).transpose(1, 0, 2).reshape(128, 256))
    base["Dpar"] = np.ascontiguousarray(
        np.asarray(inputs["L0_attn_a_D_param"]).reshape(16, 128).T)
    base["rscal"] = np.asarray(inputs["L0_attn_a_residual_scale"]).reshape(1, 1)
    base["cwf"] = np.ascontiguousarray(
        np.asarray(inputs["L0_ffn_conv_w"]).reshape(32, 128, 3).transpose(1, 0, 2).reshape(128, 96))
    base["cbf"] = np.ascontiguousarray(
        np.asarray(inputs["L0_ffn_conv_b"]).reshape(32, 128).T)
    base["s4_logdt"] = np.asarray(inputs["L0_attn_b_log_dt"]).reshape(8, 128).T.copy()
    base["s4_Ar"] = np.ascontiguousarray(
        np.asarray(inputs["L0_attn_b_A_real"]).reshape(8, 128, 32).transpose(1, 0, 2).reshape(128, 256))
    base["s4_Ai"] = np.ascontiguousarray(
        np.asarray(inputs["L0_attn_b_A_imag"]).reshape(8, 128, 32).transpose(1, 0, 2).reshape(128, 256))
    base["s4_Cr"] = np.ascontiguousarray(
        np.asarray(inputs["L0_attn_b_C_real"]).reshape(8, 128, 32).transpose(1, 0, 2).reshape(128, 256))
    base["s4_Ci"] = np.ascontiguousarray(
        np.asarray(inputs["L0_attn_b_C_imag"]).reshape(8, 128, 32).transpose(1, 0, 2).reshape(128, 256))
    base["s4_D"] = np.asarray(inputs["L0_attn_b_D"]).reshape(1, 1)
    base["ones1x128f"] = np.ones((1, 128), np.float32)
    base["idenf"] = np.eye(128, dtype=np.float32)
    base["antif"] = np.ascontiguousarray(np.eye(128, dtype=np.float32)[::-1])

    maps = [dict(base) for _ in range(NCORES)]
    for c in range(NCORES):
        for si, (b, s) in enumerate(_span_of_core(c)):
            t0 = s * SPAN
            maps[c][f"x_{si}"] = np.ascontiguousarray(x[b, t0:t0 + SPAN].T)
            halo = np.zeros((D, 128), np.float32)
            if t0 > 0:
                halo[:, :] = x[b, t0 - 128:t0].T
            maps[c][f"xhalo_{si}"] = halo
    return maps


def ml_bf16():
    import ml_dtypes
    return ml_dtypes.bfloat16


def host_assemble(results):
    out = np.empty((NB, L, D), np.float32)
    for c in range(NCORES):
        for si, (b, s) in enumerate(_span_of_core(c)):
            t0 = s * SPAN
            out[b, t0:t0 + SPAN] = np.asarray(results[c][f"out_{si}"]).T
    return out



def declare_in(nc, name, shape, dtype=F32):
    return nc.dram_tensor(name, shape, dtype, kind="ExternalInput").ap()


def declare_out(nc, name, shape, dtype=F32):
    return nc.dram_tensor(name, shape, dtype, kind="ExternalOutput").ap()



def emit_weights_1a(pr, io):
    w = {}
    w["normw"] = pr.sbuf("w_normw", [128, 8], F32)
    w["cwa"] = pr.sbuf("w_cwa", [128, 64], F32)
    w["cba"] = pr.sbuf("w_cba", [128, 16], F32)
    w["onesb"] = pr.sbuf("w_onesb", [128, 1], BF16)
    w["ones1x128f"] = pr.sbuf("w_ones1f", [1, 128], F32)
    pr.memset("V", w["onesb"][:], 1.0)
    w["Winb"] = pr.sbuf("w_Winb", [128, 8 * DH], BF16)
    wtmp = pr.sbuf("s1_xch", [128, 8 * TCH], F32)
    for k in ("normw", "cwa", "cba", "ones1x128f"):
        pr.dma("Y", w[k][:], io[k])
    WinT = io["WinT"].rearrange("(g p) c -> p g c", p=128)
    Winb3 = w["Winb"][:].rearrange("p (g c) -> p g c", g=8)
    for g in range(8):
        pr.dma("Y", wtmp[:, :DH], WinT[:, g, :])
        pr.ts("V", Winb3[:, g, :], wtmp[:, :DH], w["normw"][:, g:g + 1], None, OP.mult)
    return w


def emit_weights_1b(pr, io):
    w = {}
    w["bdt"] = pr.sbuf("w_bdt", [128, 16], F32)
    w["Wdtb"] = pr.sbuf("w_Wdtb", [128, 16 * DI], BF16)
    w["WBb"] = pr.sbuf("w_WBb", [128, 16 * NST], BF16)
    w["WCb"] = pr.sbuf("w_WCb", [128, 16 * NST], BF16)
    wtmp = pr.sbuf("s1_wtmp", [128, DI + 2 * NST], F32)
    pr.dma("Y", w["bdt"][:], io["bdt"])
    WdtT = io["WdtT"].rearrange("(g p) c -> p g c", p=128)
    Wdtb3 = w["Wdtb"][:].rearrange("p (g c) -> p g c", g=16)
    WBT = io["WBT"].rearrange("(g p) c -> p g c", p=128)
    WCT = io["WCT"].rearrange("(g p) c -> p g c", p=128)
    WBb3 = w["WBb"][:].rearrange("p (g c) -> p g c", g=16)
    WCb3 = w["WCb"][:].rearrange("p (g c) -> p g c", g=16)
    for g in range(16):
        pr.dma("Y", wtmp[:, :DI], WdtT[:, g, :])
        pr.copy("V", Wdtb3[:, g, :], wtmp[:, :DI])
        pr.dma("Y", wtmp[:, DI:DI + NST], WBT[:, g, :])
        pr.copy("V", WBb3[:, g, :], wtmp[:, DI:DI + NST])
        pr.dma("Y", wtmp[:, DI + NST:DI + 2 * NST], WCT[:, g, :])
        pr.copy("V", WCb3[:, g, :], wtmp[:, DI + NST:DI + 2 * NST])
    return w


def emit_stage1a_span(pr, w, io, si):
    NCH = SPAN // TCH
    xr = io[f"x_{si}"].rearrange("(g p) t -> p g t", p=128)
    xh = io[f"xhalo_{si}"].rearrange("(g p) t -> p g t", p=128)
    xcbr = io[f"xcb_{si}"].rearrange("(g p) t -> p g t", p=128)
    sgr = io[f"sg_{si}"].rearrange("(g p) t -> p g t", p=128)

    xch = pr.sbuf("s1_xch", [128, 8 * TCH], F32)
    xbch = pr.sbuf("s1_xbch", [128, 8 * TCH], BF16)
    bsc = pr.sbuf("s1_bsc", [128, 8 * TCH], BF16)
    ps_sum = pr.psum("s1_ps_sum", [1, TCH], F32)
    ps_sq = pr.psum("s1_ps_sq", [1, TCH], F32)
    rows = pr.sbuf("s1_rows", [1, 7 * TCH], F32)
    r_m = rows[:][:, 0 * TCH:1 * TCH]
    r_v = rows[:][:, 1 * TCH:2 * TCH]
    r_t = rows[:][:, 2 * TCH:3 * TCH]
    r_q = rows[:][:, 3 * TCH:4 * TCH]
    r_r1 = rows[:][:, 4 * TCH:5 * TCH]
    r_r = rows[:][:, 5 * TCH:6 * TCH]
    r_mr = rows[:][:, 6 * TCH:7 * TCH]
    ps_bc1 = pr.psum("s1_ps_bc1", [128, TCH], F32)
    ps_bc2 = pr.psum("s1_ps_bc2", [128, TCH], F32)
    ps_mm = pr.psum("s1_ps_mm", [128, TCH], F32)
    scr32 = pr.sbuf("s1_scr32", [128, 16 * TCH], F32)
    xbr = pr.sbuf("s1_xbr", [128, 16 * (TCH + 3)], F32)
    xcbt = pr.sbuf("s1_xcbt", [128, 16 * TCH], BF16)

    Winb3 = w["Winb"][:].rearrange("p (g c) -> p g c", g=8)
    cwa3 = w["cwa"][:].rearrange("p (g k) -> p g k", g=16)
    xbr3 = xbr[:].rearrange("p (g t) -> p g t", g=16)
    xcf3 = scr32[:].rearrange("p (g t) -> p g t", g=16)
    xcbt3 = xcbt[:].rearrange("p (g t) -> p g t", g=16)

    def xn_of(ci):
        T = TCH if ci >= 0 else 128
        src = xr[:, :, ci * TCH:ci * TCH + T] if ci >= 0 else xh[:, :, :]
        xch_v = xch[:, :8 * T].rearrange("p (g t) -> p g t", g=8)
        xb_v = xbch[:, :8 * T].rearrange("p (g t) -> p g t", g=8)
        sq_v = bsc[:, :8 * T].rearrange("p (g t) -> p g t", g=8)
        pr.dma("Y", xch_v, src)
        pr.copy("V", xbch[:, :8 * T], xch[:, :8 * T])
        pr.act("S", bsc[:, :8 * T], xbch[:, :8 * T], AF.Square)
        for g in range(8):
            pr.mm(ps_sum[:1, :T], w["onesb"][:], xb_v[:, g, :], start=(g == 0), stop=(g == 7))
        for g in range(8):
            pr.mm(ps_sq[:1, :T], w["onesb"][:], sq_v[:, g, :], start=(g == 0), stop=(g == 7))
        pr.ts("V", r_m[:, :T], ps_sum[:1, :T], 1.0 / D, None, OP.mult)
        pr.ts("V", r_v[:, :T], ps_sq[:1, :T], 1.0 / D, None, OP.mult)
        pr.tt("V", r_t[:, :T], r_m[:, :T], r_m[:, :T], OP.mult)
        pr.tt("V", r_v[:, :T], r_v[:, :T], r_t[:, :T], OP.subtract)
        pr.act("S", r_q[:, :T], r_v[:, :T], AF.Ln, bias=float(LN_EPS))
        pr.act("S", r_r1[:, :T], r_q[:, :T], AF.Exp, scale=-0.5)
        pr.act("S", r_t[:, :T], r_v[:, :T], AF.Ln, scale=float(1.0 + RMS_EPS),
               bias=float(RMS_EPS * LN_EPS))
        pr.act("S", r_r[:, :T], r_t[:, :T], AF.Exp, scale=-0.5)
        pr.tt("V", r_mr[:, :T], r_m[:, :T], r_r[:, :T], OP.mult)
        if ci >= 0:
            st = io[f"stat_{si}"]
            pr.dma("P", st[0:1, ci * TCH:ci * TCH + T], r_m[:, :T])
            pr.dma("P", st[1:2, ci * TCH:ci * TCH + T], r_r1[:, :T])
        pr.mm(ps_bc1[:, :T], w["ones1x128f"][:], r_r[:, :T], start=True, stop=True)
        pr.mm(ps_bc2[:, :T], w["ones1x128f"][:], r_mr[:, :T], start=True, stop=True)
        t1_v = scr32[:, :8 * T].rearrange("p (g t) -> p g t", g=8)
        pr.tt("V", t1_v, xch_v, bc8(ps_bc1[:, :T], 8), OP.mult)
        pr.tt("V", bsc[:, :8 * T].rearrange("p (g t) -> p g t", g=8), t1_v,
              bc8(ps_bc2[:, :T], 8), OP.subtract)
        return T

    for ci in range(-1, NCH):
        T = xn_of(ci)
        xnb3 = bsc[:, :8 * T].rearrange("p (g t) -> p g t", g=8)
        if ci < 0:
            for mc in range(16):
                for kc in range(8):
                    pr.mm(ps_mm[:, :T], Winb3[:, kc, mc * 128:(mc + 1) * 128], xnb3[:, kc, :],
                          start=(kc == 0), stop=(kc == 7))
                pr.copy("V", xbr3[:, mc, 0:3], ps_mm[:, T - 3:T])
            continue
        for mc in range(32):
            for kc in range(8):
                pr.mm(ps_mm[:, :T], Winb3[:, kc, mc * 128:(mc + 1) * 128], xnb3[:, kc, :],
                      start=(kc == 0), stop=(kc == 7))
            if mc < 16:
                pr.copy("V", xbr3[:, mc, 3:3 + T], ps_mm[:, :T])
            else:
                pr.act("S", xcbt3[:, mc - 16, :T], ps_mm[:, :T], AF.Silu)
        pr.dma("P", sgr[:, :, ci * TCH:(ci + 1) * TCH], xcbt3)
        for g in range(16):
            pr.ts("V", xcf3[:, g, :], xbr3[:, g, 3:3 + TCH], cwa3[:, g, 3:4],
                  w["cba"][:, g:g + 1], OP.mult, OP.add)
        for k in (2, 1, 0):
            for g in range(16):
                pr.stt("V", xcf3[:, g, :], xbr3[:, g, k:k + TCH], cwa3[:, g, k:k + 1],
                       xcf3[:, g, :], OP.mult, OP.add)
        pr.act("S", scr32[:], scr32[:], AF.Silu)
        pr.copy("V", xcbt[:], scr32[:])
        pr.dma("P", xcbr[:, :, ci * TCH:(ci + 1) * TCH], xcbt3)
        for g in range(16):
            pr.copy("P", xbr3[:, g, 0:3], xbr3[:, g, TCH:TCH + 3])


def emit_stage1b_span(pr, w, io, si):
    NCH = SPAN // TCH
    dtr = io[f"dt_{si}"].rearrange("(g p) t -> p g t", p=128)
    xcbr = io[f"xcb_{si}"].rearrange("(g p) t -> p g t", p=128)

    xcbt = pr.sbuf("s1_xcbt", [128, 16 * TCH], BF16)
    scr = pr.sbuf("s1_scr", [128, TCH], F32)
    spa = pr.sbuf("s1_spa", [128, 16 * TCH], F32)
    spr = pr.sbuf("s1_spr", [128, 16 * TCH], F32)
    dtst = pr.sbuf("s1_dtst", [128, 16 * TCH], F32)
    bst = pr.sbuf("s1_bst", [16, 2 * TCH], F32)
    ps_mm2 = [pr.psum(f"sb_ps_mm{i}", [128, TCH], F32) for i in range(2)]
    ps_bcn = pr.psum("s1_ps_bcn", [16, TCH], F32)
    ps_ccn = pr.psum("s1_ps_ccn", [16, TCH], F32)

    Wdtb3 = w["Wdtb"][:].rearrange("p (g c) -> p g c", g=16)
    WBb3 = w["WBb"][:].rearrange("p (g c) -> p g c", g=16)
    WCb3 = w["WCb"][:].rearrange("p (g c) -> p g c", g=16)
    xcbt3 = xcbt[:].rearrange("p (g t) -> p g t", g=16)
    spa3 = spa[:].rearrange("p (g t) -> p g t", g=16)
    spr3 = spr[:].rearrange("p (g t) -> p g t", g=16)

    for ci in range(NCH):
        pr.dma("Y", xcbt3, xcbr[:, :, ci * TCH:(ci + 1) * TCH])
        for mc in range(16):
            ps_mm = ps_mm2[mc % 2]
            for kc in range(16):
                pr.mm(ps_mm[:, :TCH], Wdtb3[:, kc, mc * 128:(mc + 1) * 128],
                      xcbt3[:, kc, :], start=(kc == 0), stop=(kc == 15))
            pr.ts("V", scr[:, :TCH], ps_mm[:, :TCH], w["bdt"][:, mc:mc + 1], None, OP.add)
            pr.ts("V", spr3[:, mc, :], scr[:, :TCH], 0.0, None, OP.max)
            pr.stt("V", spa3[:, mc, :], spr3[:, mc, :], -2.0, scr[:, :TCH],
                   OP.mult, OP.add)
        pr.act("S", spa[:], spa[:], AF.Exp)
        pr.act("S", spa[:], spa[:], AF.Ln, bias=1.0)
        pr.tt("V", dtst[:], spa[:], spr[:], OP.add)
        pr.act("S", dtst[:], dtst[:], AF.Tanh, scale=float(_SC_DT_S1),
               bias=float(-_SC_DT_C * _SC_DT_S1))
        pr.ts("V", dtst[:], dtst[:], float(_SC_DT_S2), float(_SC_DT_C), OP.mult, OP.add)
        pr.dma("P", dtr[:, :, ci * TCH:(ci + 1) * TCH],
               dtst[:].rearrange("p (g t) -> p g t", g=16))
        for kc in range(16):
            pr.mm(ps_bcn[:, :TCH], WBb3[:, kc, :], xcbt3[:, kc, :],
                  start=(kc == 0), stop=(kc == 15))
        for kc in range(16):
            pr.mm(ps_ccn[:, :TCH], WCb3[:, kc, :], xcbt3[:, kc, :],
                  start=(kc == 0), stop=(kc == 15))
        pr.copy("V", bst[:, :TCH], ps_bcn[:, :TCH])
        pr.copy("V", bst[:, TCH:2 * TCH], ps_ccn[:, :TCH])
        pr.dma("P", io[f"Brow_{si}"][:, ci * TCH:(ci + 1) * TCH], bst[:, :TCH])
        pr.dma("P", io[f"Crow_{si}"][:, ci * TCH:(ci + 1) * TCH], bst[:, TCH:2 * TCH])


def _declare_stage1_io(nc, which):
    io = {}
    if which == "a":
        io["WinT"] = declare_in(nc, "WinT", [D, DH])
        io["normw"] = declare_in(nc, "normw", [128, 8])
        io["cwa"] = declare_in(nc, "cwa", [128, 64])
        io["cba"] = declare_in(nc, "cba", [128, 16])
        io["ones1x128f"] = declare_in(nc, "ones1x128f", [1, 128])
    else:
        io["WdtT"] = declare_in(nc, "WdtT", [DI, DI])
        io["WBT"] = declare_in(nc, "WBT", [DI, NST])
        io["WCT"] = declare_in(nc, "WCT", [DI, NST])
        io["bdt"] = declare_in(nc, "bdt", [128, 16])
    for si in range(2):
        if which == "a":
            io[f"x_{si}"] = declare_in(nc, f"x_{si}", [D, SPAN])
            io[f"xhalo_{si}"] = declare_in(nc, f"xhalo_{si}", [D, 128])
            io[f"xcb_{si}"] = declare_out(nc, f"xcb_{si}", [DI, SPAN // 2], F32).bitcast(BF16)
            io[f"sg_{si}"] = declare_out(nc, f"sg_{si}", [DI, SPAN // 2], F32).bitcast(BF16)
            io[f"stat_{si}"] = declare_out(nc, f"stat_{si}", [2, SPAN])
        else:
            io[f"xcb_{si}"] = declare_in(nc, f"xcb_{si}", [DI, SPAN // 2], F32).bitcast(BF16)
            io[f"dt_{si}"] = declare_out(nc, f"dt_{si}", [DI, SPAN])
            io[f"Brow_{si}"] = declare_out(nc, f"Brow_{si}", [NST, SPAN])
            io[f"Crow_{si}"] = declare_out(nc, f"Crow_{si}", [NST, SPAN])
    return io


def build_launch1a():
    nc = bass.Bass("TRN2", target_bir_lowering=False)
    pr = Prog(nc)
    io = _declare_stage1_io(nc, "a")
    w = emit_weights_1a(pr, io)
    for si in range(2):
        emit_stage1a_span(pr, w, io, si)
    pr.fence_all("P")
    pr.close()
    return nc


def build_launch1b():
    nc = bass.Bass("TRN2", target_bir_lowering=False)
    pr = Prog(nc)
    io = _declare_stage1_io(nc, "b")
    w = emit_weights_1b(pr, io)
    for si in range(2):
        emit_stage1b_span(pr, w, io, si)
    pr.fence_all("P")
    pr.close()
    return nc



PW = 2064
NE = 206


def span_shift(s):
    t0 = SPAN * s
    o = (10 - (t0 % 10)) % 10
    return 10 - o


def span_mask(s):
    s0 = span_shift(s)
    m = np.ones(NE, np.float32)
    for e in range(NE):
        c = 10 * (e + 1) - s0
        if 0 <= c < SPAN:
            m[e] = 0.0
    return m


def emit_scan_consts(pr, io):
    w = {}
    w["Alog"] = pr.sbuf("sc_Alog", [128, 256], F32)
    w["A16"] = pr.sbuf("sc_A16", [128, 256], F32)
    w["onesf"] = pr.sbuf("sc_onesf", [128, 1], F32)
    w["ones1x128f"] = pr.sbuf("w_ones1f", [1, 128], F32)
    pr.memset("V", w["onesf"][:], 1.0)
    pr.dma("Y", w["Alog"][:], io["Alog"])
    pr.dma("Y", w["ones1x128f"][:], io["ones1x128f"])
    pr.act("S", w["A16"][:], w["Alog"][:], AF.Tanh, scale=0.1)
    pr.act("S", w["A16"][:], w["A16"][:], AF.Exp, scale=10.0)
    pr.ts("V", w["A16"][:], w["A16"][:], float(-1.0 / 16.0), None, OP.mult)
    return w


def _S0of(si):
    def f(r):
        return span_shift(_span_of_core(r)[si][1])
    return f


def _prestage_pads(pr, io, si, specs):
    S0 = _S0of(si)
    for dst, srckey, ztag in specs:
        srcv = io[srckey]
        z = io[ztag]
        pr.dma_arms("Y", lambda r: (dst[:, S0(r):S0(r) + SPAN], srcv))
        pr.dma_arms("Y", lambda r: (dst[:, 0:S0(r)], z[:, 0:S0(r)]))
        pr.dma_arms("Y", lambda r: (dst[:, S0(r) + SPAN:PW], z[:, 0:PW - SPAN - S0(r)]))


def emit_pass1_span(pr, w, io, si, s_global, fused=False):
    s0 = span_shift(s_global) if not fused else None
    if fused:
        _prestage_pads(pr, io, si, [(io[f"dtp_{si}"], f"dt_{si}", "zDI"),
                                    (io[f"xcbp_{si}"], f"xcb_{si}", "zDIb"),
                                    (io[f"Bp_{si}"], f"Brow_{si}", "zNST")])
        dtr = io[f"dtp_{si}"].rearrange("(g p) t -> p g t", p=128)
        xcbr = io[f"xcbp_{si}"].rearrange("(g p) t -> p g t", p=128)
    else:
        dtr = io[f"dt_{si}"].rearrange("(g p) t -> p g t", p=128)
        xcbr = io[f"xcb_{si}"].rearrange("(g p) t -> p g t", p=128)
    PQ = io[f"PQ_{si}"].rearrange("w (p dg n e) -> w p dg n e", p=128, dg=16, n=16)

    NBLK = 4
    dt_db = [pr.sbuf(f"p1_dt{i}", [128, PW], F32) for i in range(2)]
    xcf_dg = pr.sbuf("p1_xcf", [128, PW], F32)
    wt_dg = pr.sbuf("p1_wt", [128, PW], F32)
    xcb_db = [pr.sbuf(f"p1_xcb{i}", [128, PW], BF16) for i in range(2)]
    Bbc = pr.sbuf("p1_Bbc", [128, NBLK * PW], F32)
    dA2 = [pr.sbuf(f"p1_dA{i}", [128, PW], F32) for i in range(2)]
    uu2 = [pr.sbuf(f"p1_u{i}", [128, PW], F32) for i in range(2)]
    qc2 = [pr.sbuf(f"p1_qc{i}", [128, PW], F32) for i in range(2)]
    lam2 = [pr.sbuf(f"p1_lam{i}", [128, NE], F32) for i in range(2)]
    pst = pr.sbuf("p1_pst", [128, 16 * NE], F32)
    qst = pr.sbuf("p1_qst", [128, 16 * NE], F32)
    qce2 = [pr.sbuf(f"p1_qce{i}", [128, NE + 1], F32) for i in range(2)]

    Bbc3 = Bbc[:].rearrange("p (nb t) -> p nb t", nb=NBLK)
    pst3 = pst[:].rearrange("p (k e) -> p k e", k=16)
    qst3 = qst[:].rearrange("p (k e) -> p k e", k=16)

    for i in range(2):
        pr.memset("V", dt_db[i][:], 0.0)
        pr.memset("V", xcb_db[i][:], 0.0)
    pr.memset("V", Bbc[:], 0.0)

    for nb in range(16 // NBLK):
        for j in range(NBLK):
            n = nb * NBLK + j
            if fused:
                pr.dma("Y", Bbc3[:, j, :], io[f"Bp_{si}"][n:n + 1, :].broadcast_to([128, PW]))
            else:
                pr.dma("Y", Bbc3[:, j, s0:s0 + SPAN],
                       io[f"Brow_{si}"][n:n + 1, :].broadcast_to([128, SPAN]))
        for dg in range(16):
            dt_dg = dt_db[dg % 2]
            xcb_dg = xcb_db[dg % 2]
            if fused:
                pr.dma("Y", dt_dg[:], dtr[:, dg, :])
                pr.dma("Y", xcb_dg[:], xcbr[:, dg, :])
            else:
                pr.dma("Y", dt_dg[:, s0:s0 + SPAN], dtr[:, dg, :])
                pr.dma("Y", xcb_dg[:, s0:s0 + SPAN], xcbr[:, dg, :])
            pr.copy("V", xcf_dg[:], xcb_dg[:])
            pr.tt("V", wt_dg[:], dt_dg[:], xcf_dg[:], OP.mult)
            for j in range(NBLK):
                n = nb * NBLK + j
                pair = dg * 16 + n
                k16 = (dg % 4) * NBLK + j
                acol = w["A16"][:, pair:pair + 1]
                dA = dA2[j % 2]
                uu = uu2[j % 2]
                qc = qc2[j % 2]
                lam = lam2[j % 2]
                qce = qce2[j % 2]
                pr.act("S", dA[:], dt_dg[:], AF.Tanh, scale=acol)
                mid = dA[:, 1:1 + NE * 10].rearrange("p (k t) -> p k t", t=10)
                pr.red("V", lam[:], mid, OP.add)
                pr.act("S", dA[:], dA[:], AF.Exp, scale=16.0)
                pr.tt("V", uu[:], wt_dg[:], Bbc3[:, j, :], OP.mult)
                pr.scan("V", qc[:], dA[:], uu[:], 0.0, OP.mult, OP.add)
                pr.act("S", pst3[:, k16, :], lam[:], AF.Exp, scale=16.0)
                pr.memset("V", qce[:, 0:1], 0.0)
                pr.copy("V", qce[:, 1:1 + NE], strided_cols(qc, 10, 10, NE))
                pr.tt("V", qst3[:, k16, :], pst3[:, k16, :], qce[:, 0:NE], OP.mult)
                pr.tt("V", qst3[:, k16, :], qce[:, 1:1 + NE], qst3[:, k16, :], OP.subtract)
            if dg % 4 == 3:
                dg0 = dg - 3
                nb0 = nb * NBLK
                pr.dma("Y", PQ[0, :, dg0:dg0 + 4, nb0:nb0 + NBLK, :],
                       pst3.rearrange("p (dgp nj) e -> p dgp nj e", dgp=4))
                pr.dma("Y", PQ[1, :, dg0:dg0 + 4, nb0:nb0 + NBLK, :],
                       qst3.rearrange("p (dgp nj) e -> p dgp nj e", dgp=4))


def _declare_pass1_io(nc):
    io = {}
    io["Alog"] = declare_in(nc, "Alog", [128, 256])
    io["ones1x128f"] = declare_in(nc, "ones1x128f", [1, 128])
    for si in range(2):
        io[f"dt_{si}"] = declare_in(nc, f"dt_{si}", [DI, SPAN])
        io[f"xcb_{si}"] = declare_in(nc, f"xcb_{si}", [DI, SPAN // 2], F32).bitcast(BF16)
        io[f"Brow_{si}"] = declare_in(nc, f"Brow_{si}", [NST, SPAN])
        io[f"PQ_{si}"] = declare_out(nc, f"PQ_{si}", [2, 128 * 256 * NE])
    return io


def build_launch2(core):
    nc = bass.Bass("TRN2", target_bir_lowering=False)
    pr = Prog(nc)
    io = _declare_pass1_io(nc)
    w = emit_scan_consts(pr, io)
    for si, (b, s) in enumerate(_span_of_core(core)):
        emit_pass1_span(pr, w, io, si, s)
    pr.fence_all("P")
    pr.close()
    return nc



def emit_chain(pr, io, nspans=NSPANS):
    onesf = pr.sbuf("ch_onesf", [128, 1], F32)
    ones1x128f = pr.sbuf("w_ones1f", [1, 128], F32)
    pr.memset("V", onesf[:], 1.0)
    pr.dma("Y", ones1x128f[:], io["ones1x128f"])

    EB = 16
    state = pr.sbuf("ch_state", [128, 256], F32)
    hpre = pr.sbuf("ch_hpre", [128, 256], F32)
    sqscr = pr.sbuf("ch_sq", [128, 256], F32)
    partials = pr.sbuf("ch_part", [128, 1], F32)
    pblk = pr.sbuf("ch_pblk", [128, 256 * EB], F32)
    qblk = pr.sbuf("ch_qblk", [128, 256 * EB], F32)
    rows = pr.sbuf("ch_rows", [1, 4], F32)
    facrow = pr.sbuf("ch_facrow", [1, NE], F32)
    maskrow = pr.sbuf("ch_mask", [1, NE], F32)
    faccol = pr.sbuf("ch_faccol", [128, 1], F32)
    ps_n = pr.psum("ch_ps_n", [1, 1], F32)
    ps_f = pr.psum("ch_ps_f", [128, 1], F32)

    pr.memset("V", state[:], 0.0)
    pb3 = pblk[:].rearrange("p (pr e) -> p pr e", e=EB)
    qb3 = qblk[:].rearrange("p (pr e) -> p pr e", e=EB)

    for s in range(nspans):
        PQ = io[f"PQc_{s}"].rearrange("w (p pr e) -> w p pr e", p=128, pr=256)
        pr.dma("P", io[f"stent_{s}"], state[:])
        pr.dma("Y", maskrow[:], io[f"mask_{s}"])
        for eb in range((NE + EB - 1) // EB):
            e0 = eb * EB
            ne = min(EB, NE - e0)
            pr.dma("Y", pb3[:, :, 0:ne], PQ[0, :, :, e0:e0 + ne])
            pr.dma("Y", qb3[:, :, 0:ne], PQ[1, :, :, e0:e0 + ne])
            for el in range(ne):
                e = e0 + el
                Pv = pb3[:, :, el]
                Qv = qb3[:, :, el]
                pr.tt("V", hpre[:], Pv, state[:], OP.mult)
                pr.tt("V", hpre[:], hpre[:], Qv, OP.add)
                pr.act("S", sqscr[:], hpre[:], AF.Square, accum_out=partials[:])
                pr.mm(ps_n[:1, :1], onesf[:], partials[:], start=True, stop=True)
                pr.act("S", rows[:, 0:1], ps_n[:1, :1], AF.Sqrt)
                pr.ts("V", rows[:, 1:2], rows[:, 0:1], 1e-8, None, OP.add)
                pr.recip("V", rows[:, 2:3], rows[:, 1:2])
                pr.ts("V", rows[:, 3:4], rows[:, 2:3], 20.0, 1.0, OP.mult, OP.min)
                pr.tt("V", rows[:, 3:4], rows[:, 3:4], maskrow[:, e:e + 1], OP.max)
                pr.copy("V", facrow[:, e:e + 1], rows[:, 3:4])
                pr.mm(ps_f[:, :1], ones1x128f[:], rows[:, 3:4], start=True, stop=True)
                pr.copy("V", faccol[:], ps_f[:, :1])
                pr.ts("V", state[:], hpre[:], faccol[:], None, OP.mult)
        pr.dma("P", io[f"facs_{s}"], facrow[:])


def _declare_chain_io(nc, nspans=NSPANS):
    io = {"ones1x128f": declare_in(nc, "ones1x128f", [1, 128])}
    for s in range(nspans):
        io[f"PQc_{s}"] = declare_in(nc, f"PQc_{s}", [2, 128 * 256 * NE])
        io[f"mask_{s}"] = declare_in(nc, f"mask_{s}", [1, NE])
        io[f"stent_{s}"] = declare_out(nc, f"stent_{s}", [128, 256])
        io[f"facs_{s}"] = declare_out(nc, f"facs_{s}", [1, NE])
    return io


def build_launch3():
    nc = bass.Bass("TRN2", target_bir_lowering=False)
    pr = Prog(nc)
    io = _declare_chain_io(nc)
    emit_chain(pr, io)
    pr.fence_all("P")
    pr.close()
    return nc



def emit_pass2_weights(pr, io):
    w = {}
    w["Alog"] = pr.sbuf("sc_Alog", [128, 256], F32)
    w["A16"] = pr.sbuf("sc_A16", [128, 256], F32)
    w["onesf"] = pr.sbuf("sc_onesf", [128, 1], F32)
    w["ones1x128f"] = pr.sbuf("w_ones1f", [1, 128], F32)
    w["Dp"] = pr.sbuf("p2_Dp", [128, 16], F32)
    w["Woutb"] = pr.sbuf("p2_Woutb", [128, 16 * D], BF16)
    w["scol"] = pr.sbuf("p2_scol", [128, 1], F32)
    wtmp = pr.sbuf("p2_wtmp", [128, D], F32)
    srow = pr.sbuf("p2_srow", [1, 1], F32)
    ps_b = pr.psum("p2_ps_b", [128, 1], F32)
    pr.memset("V", w["onesf"][:], 1.0)
    pr.dma("Y", w["Alog"][:], io["Alog"])
    pr.dma("Y", w["ones1x128f"][:], io["ones1x128f"])
    pr.dma("Y", w["Dp"][:], io["Dpar"])
    pr.act("S", w["A16"][:], w["Alog"][:], AF.Tanh, scale=0.1)
    pr.act("S", w["A16"][:], w["A16"][:], AF.Exp, scale=10.0)
    pr.ts("V", w["A16"][:], w["A16"][:], float(-1.0 / 16.0), None, OP.mult)
    WoutT = io["WoutT"].rearrange("(g p) c -> p g c", p=128)
    Wb3 = w["Woutb"][:].rearrange("p (g c) -> p g c", g=16)
    for g in range(16):
        pr.dma("Y", wtmp[:], WoutT[:, g, :])
        pr.copy("V", Wb3[:, g, :], wtmp[:])
    pr.dma("Y", srow[:], io["rscal"])
    pr.act("S", srow[:], srow[:], AF.Tanh, scale=0.5)
    pr.ts("V", srow[:], srow[:], 0.25, 0.25, OP.mult, OP.add)
    pr.mm(ps_b[:, :1], w["ones1x128f"][:], srow[:], start=True, stop=True)
    pr.copy("V", w["scol"][:], ps_b[:, :1])
    return w


def emit_pass2_span(pr, w, io, si, s_global, fused=False):
    s0 = span_shift(s_global) if not fused else None
    if fused:
        _prestage_pads(pr, io, si, [(io[f"dtp_{si}"], f"dt_{si}", "zDI"),
                                    (io[f"xcbp_{si}"], f"xcb_{si}", "zDIb"),
                                    (io[f"sgp_{si}"], f"sg_{si}", "zDIb"),
                                    (io[f"Bp_{si}"], f"Brow_{si}", "zNST"),
                                    (io[f"Cp_{si}"], f"Crow_{si}", "zNST")])
        dtr = io[f"dtp_{si}"].rearrange("(g p) t -> p g t", p=128)
        xcbr = io[f"xcbp_{si}"].rearrange("(g p) t -> p g t", p=128)
        sgr = io[f"sgp_{si}"].rearrange("(g p) t -> p g t", p=128)
        ysgr = io[f"ysgp_{si}"].rearrange("(g p) t -> p g t", p=128)
    else:
        dtr = io[f"dt_{si}"].rearrange("(g p) t -> p g t", p=128)
        xcbr = io[f"xcb_{si}"].rearrange("(g p) t -> p g t", p=128)
        sgr = io[f"sg_{si}"].rearrange("(g p) t -> p g t", p=128)
        ysgr = io[f"ysg_{si}"].rearrange("(g p) t -> p g t", p=128)
    xr = io[f"x_{si}"].rearrange("(g p) t -> p g t", p=128)
    x2r = io[f"x2_{si}"].rearrange("(g p) t -> p g t", p=128)

    NBLK = 4
    dt_db = [pr.sbuf(f"p2_dt{i}", [128, PW], F32) for i in range(2)]
    xcf_dg = pr.sbuf("p2_xcf", [128, PW], F32)
    wt_dg = pr.sbuf("p2_wt", [128, PW], F32)
    xcb_db = [pr.sbuf(f"p2_xcb{i}", [128, PW], BF16) for i in range(2)]
    Bbc = pr.sbuf("p2_Bbc", [128, NBLK * PW], F32)
    Cbc = pr.sbuf("p2_Cbc", [128, NBLK * PW], F32)
    dA2 = [pr.sbuf(f"p2_dA{i}", [128, PW], F32) for i in range(2)]
    uu2 = [pr.sbuf(f"p2_u{i}", [128, PW], F32) for i in range(2)]
    hh = pr.sbuf("p2_h", [128, PW], F32)
    acc = pr.sbuf("p2_acc", [128, PW], F32)
    ent = pr.sbuf("p2_ent", [128, 256], F32)
    facr = pr.sbuf("p2_facr", [1, NE], F32)
    facb = pr.sbuf("p2_facb", [128, NE], F32)
    sgf = pr.sbuf("p2_sgf", [128, PW], F32)
    ysgb = pr.sbuf("p2_ysgb", [128, PW], BF16)
    ps_fb = pr.psum("p2_ps_fb", [128, NE], F32)

    Bbc3 = Bbc[:].rearrange("p (nb t) -> p nb t", nb=NBLK)
    Cbc3 = Cbc[:].rearrange("p (nb t) -> p nb t", nb=NBLK)

    for i in range(2):
        pr.memset("V", dt_db[i][:], 0.0)
        pr.memset("V", xcb_db[i][:], 0.0)
    pr.memset("V", Bbc[:], 0.0)
    pr.memset("V", Cbc[:], 0.0)

    pr.dma("Y", ent[:], io[f"stent_{si}"])
    pr.dma("Y", facr[:], io[f"facs_{si}"])
    pr.mm(ps_fb[:, :], w["ones1x128f"][:], facr[:], start=True, stop=True)
    pr.copy("V", facb[:], ps_fb[:, :])
    facb3 = facb[:].unsqueeze(2)

    yacr = io[f"yac_{si}"].rearrange("(g p) t -> p g t", p=128)
    for nb in range(16 // NBLK):
        for j in range(NBLK):
            n = nb * NBLK + j
            if fused:
                pr.dma("Y", Bbc3[:, j, :], io[f"Bp_{si}"][n:n + 1, :].broadcast_to([128, PW]))
                pr.dma("Y", Cbc3[:, j, :], io[f"Cp_{si}"][n:n + 1, :].broadcast_to([128, PW]))
            else:
                pr.dma("Y", Bbc3[:, j, s0:s0 + SPAN],
                       io[f"Brow_{si}"][n:n + 1, :].broadcast_to([128, SPAN]))
                pr.dma("Y", Cbc3[:, j, s0:s0 + SPAN],
                       io[f"Crow_{si}"][n:n + 1, :].broadcast_to([128, SPAN]))
        for dg in range(16):
            dt_dg = dt_db[dg % 2]
            xcb_dg = xcb_db[dg % 2]
            if fused:
                pr.dma("Y", dt_dg[:], dtr[:, dg, :])
                pr.dma("Y", xcb_dg[:], xcbr[:, dg, :])
            else:
                pr.dma("Y", dt_dg[:, s0:s0 + SPAN], dtr[:, dg, :])
                pr.dma("Y", xcb_dg[:, s0:s0 + SPAN], xcbr[:, dg, :])
            pr.copy("V", xcf_dg[:], xcb_dg[:])
            pr.tt("V", wt_dg[:], dt_dg[:], xcf_dg[:], OP.mult)
            if nb == 0:
                pr.ts("V", acc[:], xcf_dg[:], w["Dp"][:, dg:dg + 1], None, OP.mult)
            else:
                if fused:
                    pr.dma("Y", acc[:], yacr[:, dg, :])
                else:
                    pr.dma("Y", acc[:, s0:s0 + SPAN], yacr[:, dg, :])
            for j in range(NBLK):
                n = nb * NBLK + j
                pair = dg * 16 + n
                acol = w["A16"][:, pair:pair + 1]
                dA = dA2[j % 2]
                uu = uu2[j % 2]
                pr.act("S", dA[:], dt_dg[:], AF.Tanh, scale=acol)
                pr.act("S", dA[:], dA[:], AF.Exp, scale=16.0)
                dAc = bass.AP(dA[:].tensor, dA[:].offset + 10,
                              [list(dA[:].ap[0]), [10, NE], [1, 1]])
                pr.tt("V", dAc, dAc, facb3, OP.mult)
                pr.tt("V", uu[:], wt_dg[:], Bbc3[:, j, :], OP.mult)
                uuc = bass.AP(uu[:].tensor, uu[:].offset + 10,
                              [list(uu[:].ap[0]), [10, NE], [1, 1]])
                pr.tt("V", uuc, uuc, facb3, OP.mult)
                pr.scan("V", hh[:], dA[:], uu[:], ent[:, pair:pair + 1], OP.mult, OP.add)
                pr.tt("V", hh[:], hh[:], Cbc3[:, j, :], OP.mult)
                pr.tt("V", acc[:], acc[:], hh[:], OP.add)
            if nb < 3:
                if fused:
                    pr.dma("P", yacr[:, dg, :], acc[:])
                else:
                    pr.dma("P", yacr[:, dg, :], acc[:, s0:s0 + SPAN])
            else:
                if fused:
                    pr.dma("Y", xcb_dg[:], sgr[:, dg, :])
                else:
                    pr.dma("Y", xcb_dg[:, s0:s0 + SPAN], sgr[:, dg, :])
                pr.copy("V", sgf[:], xcb_dg[:])
                pr.tt("V", ysgb[:], acc[:], sgf[:], OP.mult)
                if fused:
                    pr.dma("P", ysgr[:, dg, :], ysgb[:])
                else:
                    pr.dma("P", ysgr[:, dg, :], ysgb[:, s0:s0 + SPAN])

    ych = pr.sbuf("p2_ych", [128, 16 * TCH], BF16, alias="p2_Bbc")
    xch = pr.sbuf("p2_xch", [128, 8 * TCH], F32, alias="p2_Cbc")
    x2ch = pr.sbuf("p2_x2ch", [128, 8 * TCH], F32, alias="p2_h")
    strow = pr.sbuf("p2_strow", [1, 4 * TCH], F32, alias="p2_u0")
    ps_o2 = [pr.psum(f"p2_ps_o{i}", [128, TCH], F32) for i in range(2)]
    ps_a = pr.psum("p2_ps_a", [128, TCH], F32)
    ps_c = pr.psum("p2_ps_c", [128, TCH], F32)
    Wb3 = w["Woutb"][:].rearrange("p (g c) -> p g c", g=16)
    ych3 = ych[:].rearrange("p (g t) -> p g t", g=16)
    xch3 = xch[:].rearrange("p (g t) -> p g t", g=8)
    x2ch3 = x2ch[:].rearrange("p (g t) -> p g t", g=8)
    st = io[f"stat_{si}"]
    S0f = _S0of(si)
    for ci in range(SPAN // TCH):
        tsl = slice(ci * TCH, (ci + 1) * TCH)
        if fused:
            pr.dma_arms("Y", lambda r: (ych3, ysgr[:, :, S0f(r) + ci * TCH:S0f(r) + (ci + 1) * TCH]))
        else:
            pr.dma("Y", ych3, ysgr[:, :, tsl])
        pr.dma("Y", xch3, xr[:, :, tsl])
        pr.dma("Y", strow[:, 0:TCH], st[0:1, tsl])
        pr.dma("Y", strow[:, TCH:2 * TCH], st[1:2, tsl])
        pr.ts("V", strow[:, 2 * TCH:3 * TCH], strow[:, TCH:2 * TCH], 1.0, None, OP.add)
        pr.tt("V", strow[:, 3 * TCH:4 * TCH], strow[:, 0:TCH], strow[:, TCH:2 * TCH], OP.mult)
        pr.mm(ps_a[:, :], w["ones1x128f"][:], strow[:, 2 * TCH:3 * TCH], start=True, stop=True)
        pr.mm(ps_c[:, :], w["ones1x128f"][:], strow[:, 3 * TCH:4 * TCH], start=True, stop=True)
        for mc in range(8):
            ps_o = ps_o2[mc % 2]
            for kc in range(16):
                pr.mm(ps_o[:, :], Wb3[:, kc, mc * 128:(mc + 1) * 128], ych3[:, kc, :],
                      start=(kc == 0), stop=(kc == 15))
            pr.tt("V", x2ch3[:, mc, :], xch3[:, mc, :], ps_a[:, :], OP.mult)
            pr.tt("V", x2ch3[:, mc, :], x2ch3[:, mc, :], ps_c[:, :], OP.subtract)
            pr.stt("V", x2ch3[:, mc, :], ps_o[:, :], w["scol"][:], x2ch3[:, mc, :],
                   OP.mult, OP.add)
        pr.dma("P", x2r[:, :, tsl], x2ch3)


def _declare_pass2_io(nc):
    io = {}
    io["Alog"] = declare_in(nc, "Alog", [128, 256])
    io["ones1x128f"] = declare_in(nc, "ones1x128f", [1, 128])
    io["Dpar"] = declare_in(nc, "Dpar", [128, 16])
    io["WoutT"] = declare_in(nc, "WoutT", [DI, D])
    io["rscal"] = declare_in(nc, "rscal", [1, 1])
    for si in range(2):
        io[f"x_{si}"] = declare_in(nc, f"x_{si}", [D, SPAN])
        io[f"dt_{si}"] = declare_in(nc, f"dt_{si}", [DI, SPAN])
        io[f"xcb_{si}"] = declare_in(nc, f"xcb_{si}", [DI, SPAN // 2], F32).bitcast(BF16)
        io[f"sg_{si}"] = declare_in(nc, f"sg_{si}", [DI, SPAN // 2], F32).bitcast(BF16)
        io[f"Brow_{si}"] = declare_in(nc, f"Brow_{si}", [NST, SPAN])
        io[f"Crow_{si}"] = declare_in(nc, f"Crow_{si}", [NST, SPAN])
        io[f"stat_{si}"] = declare_in(nc, f"stat_{si}", [2, SPAN])
        io[f"stent_{si}"] = declare_in(nc, f"stent_{si}", [128, 256])
        io[f"facs_{si}"] = declare_in(nc, f"facs_{si}", [1, NE])
        io[f"yac_{si}"] = nc.dram_tensor(f"yac_{si}", [DI, SPAN], F32).ap()
        io[f"ysg_{si}"] = declare_out(nc, f"ysg_{si}", [DI, SPAN // 2], F32).bitcast(BF16)
        io[f"x2_{si}"] = declare_out(nc, f"x2_{si}", [D, SPAN])
    return io


def build_launch4(core):
    nc = bass.Bass("TRN2", target_bir_lowering=False)
    pr = Prog(nc)
    io = _declare_pass2_io(nc)
    w = emit_pass2_weights(pr, io)
    for si, (b, s) in enumerate(_span_of_core(core)):
        emit_pass2_span(pr, w, io, si, s)
    pr.fence_all("P")
    pr.close()
    return nc



XE = SPAN + 2


def emit_ternary_prep(pr, io, which, nslab, width, wtmp_alias=None, wi8_alias=None):
    wT = io[f"w{which}T"].rearrange("(g p) c -> p g c", p=128)
    wqb = pr.sbuf(f"fw_wq{which}", [128, nslab * width], BF16)
    wtmp = pr.sbuf("fw_wtmp", [128, 4096], F32, alias=wtmp_alias)
    wi8 = pr.sbuf("fw_wi8", [128, 4096], I8, alias=wi8_alias)
    pcol = pr.sbuf("fw_pcol", [128, 1], F32)
    acol = pr.sbuf("fw_acol", [128, 1], F32)
    srow = pr.sbuf(f"fw_srow{which}", [1, 2], F32)
    swcol = pr.sbuf("fw_swcol", [128, 1], F32)
    ps_w = pr.psum("fw_ps", [128, 2], F32)
    ps1 = ps_w[:]
    ps_b = ps_w[:]
    onesf = pr.sbuf("sc_onesf", [128, 1], F32)
    ones1x128f = pr.sbuf("w_ones1f", [1, 128], F32)
    pr.memset("V", onesf[:], 1.0)
    wq3 = wqb[:].rearrange("p (g c) -> p g c", g=nslab)
    pr.memset("V", pcol[:], 0.0)
    for g in range(nslab):
        pr.dma("Y", wtmp[:, :width], wT[:, g, :])
        pr.act("S", wtmp[:, :width], wtmp[:, :width], AF.Abs, accum_out=acol[:])
        pr.tt("V", pcol[:], pcol[:], acol[:], OP.add)
    pr.mm(ps1[0:1, 0:1], onesf[:], pcol[:], start=True, stop=True)
    pr.ts("V", srow[:, 0:1], ps1[:1, :1], float(1.0 / (nslab * 128 * width)), 1e-5,
          OP.mult, OP.max)
    pr.recip("V", srow[:, 1:2], srow[:, 0:1])
    pr.mm(ps_b[:, 1:2], ones1x128f[:], srow[:, 1:2], start=True, stop=True)
    pr.copy("V", swcol[:], ps_b[:, 1:2])
    for g in range(nslab):
        pr.dma("Y", wtmp[:, :width], wT[:, g, :])
        pr.ts("V", wtmp[:, :width], wtmp[:, :width], swcol[:], None, OP.mult)
        pr.copy("V", wi8[:, :width], wtmp[:, :width])
        pr.copy("V", wq3[:, g, :], wi8[:, :width])
        pr.ts("V", wq3[:, g, :], wq3[:, g, :], 1.0, -1.0, OP.min, OP.max)
    invswc = pr.sbuf(f"fw_invsw{which}", [128, 1], F32)
    pr.mm(ps_b[:, 1:2], ones1x128f[:], srow[:, 0:1], start=True, stop=True)
    pr.copy("V", invswc[:], ps_b[:, 1:2])
    return wqb, invswc


def _ffn_chunks():
    return [(0, TCH), (TCH, TCH), (2 * TCH, TCH), (3 * TCH, TCH), (4 * TCH, 2)]


def emit_ffn_a_span(pr, w, io, si):
    x2e = io[f"x2e_{si}"].rearrange("(g p) t -> p g t", p=128)
    hr = io[f"h_{si}"].rearrange("(g p) t -> p g t", p=128)
    rsc = io[f"rsc_{si}"]

    xch = pr.sbuf("f_xch", [128, 8 * TCH], F32)
    xbch = pr.sbuf("f_xbch", [128, 8 * TCH], BF16)
    sqch = pr.sbuf("f_sqch", [128, 8 * TCH], BF16)
    xnf = pr.sbuf("f_xnf", [128, 8 * TCH], F32)
    xqi = pr.sbuf("f_xqi", [128, TCH], I8)
    xqb = pr.sbuf("f_xqb", [128, 8 * TCH], BF16, alias="f_xbch")
    amx = pr.sbuf("f_amx", [128, TCH], F32)
    tcol = pr.sbuf("f_tcol", [128, 1], F32)
    rowq = pr.sbuf("f_rowq", [1, 2 * TCH], F32)
    hst = pr.sbuf("f_hst", [128, 16 * TCH], BF16)
    rows = pr.sbuf("s1_rows", [1, 7 * TCH], F32)
    ps_sum = pr.psum("s1_ps_sum", [1, TCH], F32)
    ps_sq = pr.psum("s1_ps_sq", [1, TCH], F32)
    ps_bc1 = pr.psum("s1_ps_bc1", [128, TCH], F32)
    ps_bc2 = pr.psum("s1_ps_bc2", [128, TCH], F32)
    ps_mm2 = [pr.psum(f"f_ps_mm{i}", [128, TCH], F32) for i in range(2)]
    ps_t = pr.psum("f_ps_t", [128, 128], F32)
    r_m = rows[:][:, 0:TCH]
    r_v = rows[:][:, TCH:2 * TCH]
    r_t = rows[:][:, 2 * TCH:3 * TCH]
    r_q = rows[:][:, 3 * TCH:4 * TCH]
    r_r1 = rows[:][:, 4 * TCH:5 * TCH]

    w1q3 = w["w1qb"][:].rearrange("p (g c) -> p g c", g=8)
    hst3 = hst[:].rearrange("p (g t) -> p g t", g=16)

    for (c0, T) in _ffn_chunks():
        xch_v = xch[:, :8 * T].rearrange("p (g t) -> p g t", g=8)
        xb_v = xbch[:, :8 * T].rearrange("p (g t) -> p g t", g=8)
        sq_v = sqch[:, :8 * T].rearrange("p (g t) -> p g t", g=8)
        xnf_v = xnf[:, :8 * T].rearrange("p (g t) -> p g t", g=8)
        pr.dma("Y", xch_v, x2e[:, :, c0:c0 + T])
        pr.copy("V", xbch[:, :8 * T], xch[:, :8 * T])
        pr.act("S", sqch[:, :8 * T], xbch[:, :8 * T], AF.Square)
        for g in range(8):
            pr.mm(ps_sum[:1, :T], w["onesb"][:], xb_v[:, g, :], start=(g == 0), stop=(g == 7))
        for g in range(8):
            pr.mm(ps_sq[:1, :T], w["onesb"][:], sq_v[:, g, :], start=(g == 0), stop=(g == 7))
        pr.ts("V", r_m[:, :T], ps_sum[:1, :T], 1.0 / D, None, OP.mult)
        pr.ts("V", r_v[:, :T], ps_sq[:1, :T], 1.0 / D, None, OP.mult)
        pr.tt("V", r_t[:, :T], r_m[:, :T], r_m[:, :T], OP.mult)
        pr.tt("V", r_v[:, :T], r_v[:, :T], r_t[:, :T], OP.subtract)
        pr.act("S", r_q[:, :T], r_v[:, :T], AF.Ln, bias=float(LN_EPS))
        pr.act("S", r_r1[:, :T], r_q[:, :T], AF.Exp, scale=-0.5)
        pr.tt("V", r_t[:, :T], r_m[:, :T], r_r1[:, :T], OP.mult)
        pr.mm(ps_bc1[:, :T], w["ones1x128f"][:], r_r1[:, :T], start=True, stop=True)
        pr.mm(ps_bc2[:, :T], w["ones1x128f"][:], r_t[:, :T], start=True, stop=True)
        pr.tt("V", xnf_v, xch_v, bc8(ps_bc1[:, :T], 8), OP.mult)
        pr.tt("V", xnf_v, xnf_v, bc8(ps_bc2[:, :T], 8), OP.subtract)
        pr.act("S", xch[:, :8 * T], xnf[:, :8 * T], AF.Abs)
        pr.tt("V", amx[:, :T], xch_v[:, 0, :], xch_v[:, 1, :], OP.max)
        for g in range(2, 8):
            pr.tt("V", amx[:, :T], amx[:, :T], xch_v[:, g, :], OP.max)
        nblk = (T + 127) // 128
        for tb in range(nblk):
            tw = min(128, T - tb * 128)
            pr.transpose(ps_t[:tw, :128], amx[:, tb * 128:tb * 128 + tw], w["idenf"][:])
            pr.red("V", tcol[:tw, 0:1], ps_t[:tw, :128], OP.max)
            pr.dma("P", rsc[0:1, c0 + tb * 128:c0 + tb * 128 + tw], tcol[:tw, 0:1])
        pr.dma("Y", rowq[:, 0:T], rsc[0:1, c0:c0 + T])
        pr.ts("V", rowq[:, 0:T], rowq[:, 0:T], 1e-5, None, OP.max)
        pr.recip("V", rowq[:, TCH:TCH + T], rowq[:, 0:T])
        pr.ts("V", rowq[:, TCH:TCH + T], rowq[:, TCH:TCH + T], 127.0, None, OP.mult)
        pr.mm(ps_bc1[:, :T], w["ones1x128f"][:], rowq[:, TCH:TCH + T], start=True, stop=True)
        pr.tt("V", xnf_v, xnf_v, bc8(ps_bc1[:, :T], 8), OP.mult)
        for g in range(8):
            pr.copy("V", xqi[:, :T], xnf_v[:, g, :])
            pr.copy("V", xqb[:, :8 * T].rearrange("p (g t) -> p g t", g=8)[:, g, :], xqi[:, :T])
        pr.ts("V", rowq[:, 0:T], rowq[:, 0:T], float(1.0 / 127.0), None, OP.mult)
        pr.mm(ps_bc2[:, :T], w["ones1x128f"][:], rowq[:, 0:T], start=True, stop=True)
        dqbc = pr.sbuf("f_dqbc", [128, TCH], F32)
        pr.ts("V", dqbc[:, :T], ps_bc2[:, :T], w["invsw1"][:, 0:1], None, OP.mult)
        xq3 = xqb[:, :8 * T].rearrange("p (g t) -> p g t", g=8)
        for half in range(2):
            for mcl in range(16):
                mc = half * 16 + mcl
                ps_mm = ps_mm2[mc % 2]
                for kc in range(8):
                    pr.mm(ps_mm[:, :T], w1q3[:, kc, mc * 128:mc * 128 + 128], xq3[:, kc, :],
                          start=(kc == 0), stop=(kc == 7))
                pr.tt("V", hst3[:, mcl, :T], ps_mm[:, :T], dqbc[:, :T], OP.mult)
            pr.dma("P", hr[:, half * 16:half * 16 + 16, c0:c0 + T], hst3[:, :, :T])


def emit_ffn_b_span(pr, w, io, si):
    hr = io[f"h_{si}"].rearrange("(g p) t -> p g t", p=128)
    x2e = io[f"x2e_{si}"].rearrange("(g p) t -> p g t", p=128)
    x3r = io[f"x3_{si}"].rearrange("(g p) t -> p g t", p=128)
    rsc = io[f"rsc_{si}"]

    hch = pr.sbuf("fb_hch", [128, 32 * (TCH + 2)], BF16)
    hcf = pr.sbuf("fb_hcf", [128, 32 * TCH], F32)
    amx = pr.sbuf("f_amx", [128, TCH], F32)
    tcol = pr.sbuf("f_tcol", [128, 1], F32)
    rowq = pr.sbuf("f_rowq", [1, 2 * TCH], F32)
    hqi = pr.sbuf("fb_hqi", [128, TCH], I8)
    hqb = pr.sbuf("fb_hqb", [128, 32 * TCH], BF16, alias="fb_hcf")
    xch = pr.sbuf("f_xch", [128, 8 * TCH], F32)
    ps_mm2 = [pr.psum(f"fb_ps_mm{i}", [128, TCH], F32) for i in range(2)]
    ps_bc1 = pr.psum("s1_ps_bc1", [128, TCH], F32)
    ps_bc2 = pr.psum("s1_ps_bc2", [128, TCH], F32)
    ps_t = pr.psum("f_ps_t", [128, 128], F32)

    w2q3 = w["w2qb"][:].rearrange("p (g c) -> p g c", g=32)
    cwf3 = w["cwf"][:].rearrange("p (g k) -> p g k", g=32)
    hch3 = hch[:].rearrange("p (g t) -> p g t", g=32)
    hcf3 = hcf[:].rearrange("p (g t) -> p g t", g=32)
    hqb3 = hqb[:].rearrange("p (g t) -> p g t", g=32)

    for ci in range(SPAN // TCH):
        c0 = ci * TCH
        T = TCH
        pr.dma("Y", hch3, hr[:, :, c0:c0 + T + 2])
        for g in range(32):
            e_ = "V"
            pr.ts(e_, hcf3[:, g, :], hch3[:, g, 0:T], cwf3[:, g, 0:1],
                  w["cbf"][:, g:g + 1], OP.mult, OP.add)
            for k in (1, 2):
                pr.stt(e_, hcf3[:, g, :], hch3[:, g, k:k + T], cwf3[:, g, k:k + 1],
                       hcf3[:, g, :], OP.mult, OP.add)
        pr.act("S", hcf[:], hcf[:], AF.Silu)
        absg = pr.sbuf("fb_absg", [128, TCH], F32)
        pr.act("S", absg[:, :T], hcf3[:, 0, :], AF.Abs)
        pr.copy("V", amx[:, :T], absg[:, :T])
        for g in range(1, 32):
            pr.act("S", absg[:, :T], hcf3[:, g, :], AF.Abs)
            pr.tt("V", amx[:, :T], amx[:, :T], absg[:, :T], OP.max)
        nblk = T // 128
        for tb in range(nblk):
            pr.transpose(ps_t[:, :128], amx[:, tb * 128:tb * 128 + 128], w["idenf"][:])
            pr.red("V", tcol[:, 0:1], ps_t[:, :128], OP.max)
            pr.dma("P", rsc[0:1, c0 + tb * 128:c0 + tb * 128 + 128], tcol[:, 0:1])
        pr.dma("Y", rowq[:, 0:T], rsc[0:1, c0:c0 + T])
        pr.ts("V", rowq[:, 0:T], rowq[:, 0:T], 1e-5, None, OP.max)
        pr.recip("V", rowq[:, TCH:TCH + T], rowq[:, 0:T])
        pr.ts("V", rowq[:, TCH:TCH + T], rowq[:, TCH:TCH + T], 127.0, None, OP.mult)
        pr.mm(ps_bc1[:, :T], w["ones1x128f"][:], rowq[:, TCH:TCH + T], start=True, stop=True)
        for g in range(32):
            pr.tt("V", hcf3[:, g, :], hcf3[:, g, :], ps_bc1[:, :T], OP.mult)
            pr.copy("V", hqi[:, :T], hcf3[:, g, :])
            pr.copy("V", hqb3[:, g, :T], hqi[:, :T])
        pr.ts("V", rowq[:, 0:T], rowq[:, 0:T], float(1.0 / 127.0), None, OP.mult)
        pr.mm(ps_bc2[:, :T], w["ones1x128f"][:], rowq[:, 0:T], start=True, stop=True)
        dqbc2 = pr.sbuf("fb_dqbc", [128, TCH], F32)
        pr.ts("V", dqbc2[:, :T], ps_bc2[:, :T], w["invsw2"][:, 0:1], None, OP.mult)
        pr.dma("Y", xch[:].rearrange("p (g t) -> p g t", g=8), x2e[:, :, c0 + 1:c0 + 1 + T])
        for mc in range(8):
            ps_mm = ps_mm2[mc % 2]
            for kc in range(32):
                pr.mm(ps_mm[:, :T], w2q3[:, kc, mc * 128:mc * 128 + 128], hqb3[:, kc, :],
                      start=(kc == 0), stop=(kc == 31))
            xv = xch[:].rearrange("p (g t) -> p g t", g=8)[:, mc, :]
            pr.tt("V", amx[:, :T], ps_mm[:, :T], dqbc2[:, :T], OP.mult)
            pr.tt("V", amx[:, :T], amx[:, :T], xv, OP.add)
            pr.copy("V", xch[:].rearrange("p (g t) -> p g t", g=8)[:, mc, :], amx[:, :T])
        pr.dma("P", x3r[:, :, c0:c0 + T], xch[:].rearrange("p (g t) -> p g t", g=8))


def emit_ffn_weights(pr, io, which):
    w = {}
    w["onesb"] = pr.sbuf("w_onesb", [128, 1], BF16)
    w["ones1x128f"] = pr.sbuf("w_ones1f", [1, 128], F32)
    w["idenf"] = pr.sbuf("w_idenf", [128, 128], F32)
    w["cwf"] = pr.sbuf("w_cwf", [128, 96], F32)
    w["cbf"] = pr.sbuf("w_cbf", [128, 32], F32)
    pr.memset("V", w["onesb"][:], 1.0)
    pr.dma("Y", w["ones1x128f"][:], io["ones1x128f"])
    pr.dma("Y", w["idenf"][:], io["idenf"])
    pr.dma("Y", w["cwf"][:], io["cwf"])
    pr.dma("Y", w["cbf"][:], io["cbf"])
    if which == "a":
        w["w1qb"], w["invsw1"] = emit_ternary_prep(pr, io, 1, 8, DH)
    else:
        pr.sbuf("fb_hch", [128, 32 * (TCH + 2)], BF16)
        pr.sbuf("fb_hcf", [128, 32 * TCH], F32)
        w["w2qb"], w["invsw2"] = emit_ternary_prep(pr, io, 2, 32, D,
                                                   wtmp_alias="fb_hch", wi8_alias="fb_hcf")
    pr.barrier()
    return w


def _declare_ffn_io(nc, which):
    io = {}
    io["ones1x128f"] = declare_in(nc, "ones1x128f", [1, 128])
    io["idenf"] = declare_in(nc, "idenf", [128, 128])
    io["cwf"] = declare_in(nc, "cwf", [128, 96])
    io["cbf"] = declare_in(nc, "cbf", [128, 32])
    if which == "a":
        io["w1T"] = declare_in(nc, "w1T", [D, DH])
    else:
        io["w2T"] = declare_in(nc, "w2T", [DH, D])
    for si in range(2):
        io[f"x2e_{si}"] = declare_in(nc, f"x2e_{si}", [D, XE])
        io[f"rsc_{si}"] = nc.dram_tensor(f"rsc_{si}", [1, XE], F32).ap()
        if which == "a":
            io[f"h_{si}"] = declare_out(nc, f"h_{si}", [DH, XE // 2], F32).bitcast(BF16)
        else:
            io[f"h_{si}"] = declare_in(nc, f"h_{si}", [DH, XE // 2], F32).bitcast(BF16)
            io[f"x3_{si}"] = declare_out(nc, f"x3_{si}", [D, SPAN])
    return io


def build_launch5a():
    nc = bass.Bass("TRN2", target_bir_lowering=False)
    pr = Prog(nc)
    io = _declare_ffn_io(nc, "a")
    w = emit_ffn_weights(pr, io, "a")
    for si in range(2):
        emit_ffn_a_span(pr, w, io, si)
    pr.fence_all("P")
    pr.close()
    return nc


def build_launch5b():
    nc = bass.Bass("TRN2", target_bir_lowering=False)
    pr = Prog(nc)
    io = _declare_ffn_io(nc, "b")
    w = emit_ffn_weights(pr, io, "b")
    for si in range(2):
        emit_ffn_b_span(pr, w, io, si)
    pr.fence_all("P")
    pr.close()
    return nc



def emit_s4d_prep(pr, io):
    w = {}
    for nm in ("ldt", "Are", "Aim", "Cre", "Cim"):
        w[nm] = pr.sbuf(f"s4_{nm}", [128, 256 if nm != "ldt" else 8], F32)
        pr.dma("Y", w[nm][:], io["s4_logdt" if nm == "ldt" else
                                "s4_Ar" if nm == "Are" else
                                "s4_Ai" if nm == "Aim" else
                                "s4_Cr" if nm == "Cre" else "s4_Ci"])
    names = ["dt", "are", "aim", "lre", "lim", "cpre", "cpim", "pre", "pim",
             "t1", "t2", "t3", "clre", "clim", "kst"]
    t = {nm: pr.sbuf(f"s4w_{nm}", [128, 256], F32) for nm in names}
    bst = pr.sbuf("s4w_bst", [128, 8 * 32 * 16], BF16)
    bst2 = pr.sbuf("s4w_bst2", [128, 8 * 32 * 16], BF16)
    bsmall = [pr.sbuf(f"s4w_bs{i}", [128, 256], BF16) for i in range(2)]
    kstb = pr.sbuf("s4w_kstb", [128, 8 * 128], BF16)
    powGv = io["powG"].rearrange("(s ri hg p n) -> s ri hg p n", s=128, ri=2, hg=8, p=128)
    powEv = io["powE"].rearrange("(ri hg p n k) -> ri hg p n k", ri=2, hg=8, p=128, n=32)
    kpadv = io["kpad"].rearrange("(hg p c) -> hg p c", p=128, c=256)
    lamTv = io["lamT"].rearrange("(ri h n) -> ri h n", ri=2, h=1024)

    pr.act("S", w["ldt"][:], w["ldt"][:], AF.Exp)
    dt_b = w["ldt"][:].unsqueeze(2).broadcast_to([128, 8, 32])

    def v3(ap):
        return ap.rearrange("p (g n) -> p g n", g=8)
    pr.act("S", t["are"][:], w["Are"][:], AF.Exp)
    pr.ts("V", t["are"][:], t["are"][:], -1.0, None, OP.mult)
    pr.tt("V", v3(t["t1"][:]), v3(t["are"][:]), dt_b, OP.mult)
    pr.tt("V", v3(t["t2"][:]), v3(w["Aim"][:]), dt_b, OP.mult)
    inv2pi = float(1.0 / (2.0 * np.pi))
    two_pi = float(2.0 * np.pi)
    i8s = pr.sbuf("s4w_i32", [128, 256], mybir.dt.int32)
    pr.ts("V", t["t3"][:], t["t2"][:], inv2pi, None, OP.mult)
    pr.copy("V", i8s[:], t["t3"][:])
    pr.copy("V", t["t3"][:], i8s[:])
    pr.ts("V", t["t3"][:], t["t3"][:], two_pi, None, OP.mult)
    pr.tt("V", t["t3"][:], t["t2"][:], t["t3"][:], OP.subtract)
    pr.act("S", t["lre"][:], t["t1"][:], AF.Exp)
    snt = pr.sbuf("s4w_snt", [128, 256], F32)
    cst = pr.sbuf("s4w_cst", [128, 256], F32)
    pr.act("S", snt[:], t["t3"][:], AF.Sin)
    pr.act("S", cst[:], t["t3"][:], AF.Sin, scale=0.5)
    pr.tt("V", cst[:], cst[:], cst[:], OP.mult)
    pr.ts("V", cst[:], cst[:], -2.0, 1.0, OP.mult, OP.add)
    pr.tt("V", t["lim"][:], t["lre"][:], snt[:], OP.mult)
    pr.tt("V", t["lre"][:], t["lre"][:], cst[:], OP.mult)
    pr.tt("V", t["t1"][:], t["are"][:], t["are"][:], OP.mult)
    pr.tt("V", t["t2"][:], w["Aim"][:], w["Aim"][:], OP.mult)
    pr.tt("V", t["t1"][:], t["t1"][:], t["t2"][:], OP.add)
    pr.recip("V", t["t1"][:], t["t1"][:])
    lm1re = pr.sbuf("s4w_lm1re", [128, 256], F32)
    pr.ts("V", lm1re[:], t["lre"][:], 1.0, None, OP.subtract)
    pr.tt("V", t["t2"][:], lm1re[:], t["are"][:], OP.mult)
    pr.tt("V", t["t3"][:], t["lim"][:], w["Aim"][:], OP.mult)
    pr.tt("V", t["t2"][:], t["t2"][:], t["t3"][:], OP.add)
    numim = pr.sbuf("s4w_numim", [128, 256], F32)
    pr.tt("V", numim[:], t["lim"][:], t["are"][:], OP.mult)
    pr.tt("V", t["t3"][:], lm1re[:], w["Aim"][:], OP.mult)
    pr.tt("V", numim[:], numim[:], t["t3"][:], OP.subtract)
    pr.tt("V", t["t2"][:], t["t2"][:], t["t1"][:], OP.mult)
    pr.tt("V", numim[:], numim[:], t["t1"][:], OP.mult)
    pr.tt("V", t["cpre"][:], w["Cre"][:], t["t2"][:], OP.mult)
    pr.tt("V", t["t3"][:], w["Cim"][:], numim[:], OP.mult)
    pr.tt("V", t["cpre"][:], t["cpre"][:], t["t3"][:], OP.subtract)
    pr.tt("V", t["cpim"][:], w["Cre"][:], numim[:], OP.mult)
    pr.tt("V", t["t3"][:], w["Cim"][:], t["t2"][:], OP.mult)
    pr.tt("V", t["cpim"][:], t["cpim"][:], t["t3"][:], OP.add)
    pr.tt("V", t["clre"][:], t["cpre"][:], t["lre"][:], OP.mult)
    pr.tt("V", t["t3"][:], t["cpim"][:], t["lim"][:], OP.mult)
    pr.tt("V", t["clre"][:], t["clre"][:], t["t3"][:], OP.subtract)
    pr.tt("V", t["clim"][:], t["cpre"][:], t["lim"][:], OP.mult)
    pr.tt("V", t["t3"][:], t["cpim"][:], t["lre"][:], OP.mult)
    pr.tt("V", t["clim"][:], t["clim"][:], t["t3"][:], OP.add)
    pr.memset("V", t["pre"][:], 1.0)
    pr.memset("V", t["pim"][:], 0.0)
    pr.memset("V", t["kst"][:], 0.0)

    bstv = bst[:].rearrange("p (g n k) -> p g n k", g=8, n=32)
    bstv2 = bst2[:].rearrange("p (g n k) -> p g n k", g=8, n=32)
    kstbv = kstb[:].rearrange("p (g k) -> p g k", g=8)
    for k in range(128):
        pr.tt("V", t["t1"][:], t["cpre"][:], t["pre"][:], OP.mult)
        pr.tt("V", t["t3"][:], t["cpim"][:], t["pim"][:], OP.mult)
        pr.tt("V", t["t1"][:], t["t1"][:], t["t3"][:], OP.subtract)
        r3 = t["t1"][:].rearrange("p (g n) -> p g n", g=8)
        pr.red("V", t["t2"][:, 0:8], r3, OP.add)
        pr.ts("V", kstbv[:, :, k:k+1].rearrange("p g one -> p (g one)"),
              t["t2"][:, 0:8], 2.0, None, OP.mult)
        s = 127 - k
        q = ("P", "Y", "S")[k % 3]
        pr.copy("V", bsmall[0][:], t["pre"][:])
        pr.dma(q, powGv[s, 0, :, :, :].rearrange("hg p n -> p hg n"),
               bsmall[0][:].rearrange("p (g n) -> p g n", g=8))
        pr.copy("V", bsmall[1][:], t["pim"][:])
        pr.dma(q, powGv[s, 1, :, :, :].rearrange("hg p n -> p hg n"),
               bsmall[1][:].rearrange("p (g n) -> p g n", g=8))
        kb = k % 16
        pr.tt("V", t["t1"][:], t["clre"][:], t["pre"][:], OP.mult)
        pr.tt("V", t["t3"][:], t["clim"][:], t["pim"][:], OP.mult)
        pr.tt("V", t["t1"][:], t["t1"][:], t["t3"][:], OP.subtract)
        pr.ts("V", bstv[:, :, :, kb].rearrange("p g n -> p (g n)"), t["t1"][:], 2.0, None, OP.mult)
        pr.tt("V", t["t1"][:], t["clre"][:], t["pim"][:], OP.mult)
        pr.tt("V", t["t3"][:], t["clim"][:], t["pre"][:], OP.mult)
        pr.tt("V", t["t1"][:], t["t1"][:], t["t3"][:], OP.add)
        pr.ts("V", bstv2[:, :, :, kb].rearrange("p g n -> p (g n)"), t["t1"][:], -2.0, None, OP.mult)
        if kb == 15:
            k0 = k - 15
            for hg in range(8):
                q = ("P", "Y", "S")[hg % 3]
                pr.dma(q, powEv[0, hg, :, :, k0:k0 + 16], bstv[:, hg, :, :])
                pr.dma(q, powEv[1, hg, :, :, k0:k0 + 16], bstv2[:, hg, :, :])
        pr.tt("V", t["t1"][:], t["pre"][:], t["lre"][:], OP.mult)
        pr.tt("V", t["t3"][:], t["pim"][:], t["lim"][:], OP.mult)
        pr.tt("V", t["t1"][:], t["t1"][:], t["t3"][:], OP.subtract)
        pr.tt("V", t["t3"][:], t["pim"][:], t["lre"][:], OP.mult)
        pr.tt("V", t["t2"][:], t["pre"][:], t["lim"][:], OP.mult)
        pr.tt("V", t["pim"][:], t["t3"][:], t["t2"][:], OP.add)
        pr.copy("V", t["pre"][:], t["t1"][:])
    Db = pr.sbuf("s4w_Db", [128, 1], F32)
    pr.dma("Y", Db[:], io["s4_D"].broadcast_to([128, 1]))
    k0v = kstbv[:, :, 0:1].rearrange("p g one -> p (g one)")
    pr.ts("V", k0v, k0v, Db[:], None, OP.add)
    zrow = pr.sbuf("s4w_zrow", [128, 256], BF16)
    pr.memset("V", zrow[:], 0.0)
    for hg in range(8):
        pr.dma("P", kpadv[hg, :, :], zrow[:])
    pr.dma("P", kpadv.rearrange("hg p c -> hg p c")[:, :, 127:255].rearrange("hg p c -> p hg c"),
           kstbv)
    pr.dma("P", lamTv[0, :, :].rearrange("(hg p) n -> p hg n", p=128), t["pre"][:].rearrange("p (g n) -> p g n", g=8))
    pr.dma("P", lamTv[1, :, :].rearrange("(hg p) n -> p hg n", p=128), t["pim"][:].rearrange("p (g n) -> p g n", g=8))
    pr.barrier()
    return w


def _declare_s4d_prep_io(nc):
    io = {}
    io["s4_logdt"] = declare_in(nc, "s4_logdt", [128, 8])
    io["s4_Ar"] = declare_in(nc, "s4_Ar", [128, 256])
    io["s4_Ai"] = declare_in(nc, "s4_Ai", [128, 256])
    io["s4_Cr"] = declare_in(nc, "s4_Cr", [128, 256])
    io["s4_Ci"] = declare_in(nc, "s4_Ci", [128, 256])
    io["s4_D"] = declare_in(nc, "s4_D", [1, 1])
    io["powG"] = declare_out(nc, "powG", [128 * 2 * 1024 * 32 // 2], F32).bitcast(BF16)
    io["powE"] = declare_out(nc, "powE", [2 * 1024 * 32 * 128 // 2], F32).bitcast(BF16)
    io["kpad"] = declare_out(nc, "kpad", [1024 * 256 // 2], F32).bitcast(BF16)
    io["lamT"] = declare_out(nc, "lamT", [2 * 32 * 1024])
    return io


def build_s4d_prep_test():
    nc = bass.Bass("TRN2", target_bir_lowering=False)
    pr = Prog(nc)
    io = _declare_s4d_prep_io(nc)
    emit_s4d_prep(pr, io)
    pr.fence_all("P")
    pr.close()
    return nc


NCHS = SPAN // 128
T6 = 256


def emit_lamn(pr, w, io, P, name):
    lamL = pr.sbuf("s4n_lamL", [128, 512], F32)
    lamn = pr.sbuf(name, [P, 2 * HS4], F32)
    ps_lt = pr.psum("s4n_pslt", [32, 128], F32)
    pr.dma("Y", lamL[:].rearrange("p (ri hg n) -> p ri hg n", ri=2, hg=8),
           io["lamT"].rearrange("(ri hg p n) -> p ri hg n", ri=2, hg=8, p=128))
    lv = lamL[:].rearrange("p (ri hg n) -> p ri hg n", ri=2, hg=8)
    for ri in range(2):
        for hg in range(8):
            pr.transpose(ps_lt[:, :], lv[:, ri, hg, :], w["idenf"][:])
            for rep in range(P // 32):
                pr.copy("V", lamn[:][rep * 32:rep * 32 + 32,
                                     ri * HS4 + hg * 128:ri * HS4 + (hg + 1) * 128],
                        ps_lt[:, :])
    return lamn


def emit_ln3_span(pr, w, io, si, uT):
    xr = io[f"x3_{si}"].rearrange("(g p) t -> p g t", p=128)
    xch = pr.sbuf("s6_xch", [128, 8 * T6], F32)
    xbch = pr.sbuf("s6_xbch", [128, 8 * T6], BF16)
    sqch = pr.sbuf("s6_sqch", [128, 8 * T6], BF16)
    xnb = pr.sbuf("s6_xnb", [128, 8 * T6], BF16)
    rows = pr.sbuf("s6_rows", [1, 5 * T6], F32)
    ps_sum = pr.psum("s6_ps_sum", [1, T6], F32)
    ps_sq = pr.psum("s6_ps_sq", [1, T6], F32)
    ps_bc1 = pr.psum("s6_ps_bc1", [128, T6], F32)
    ps_bc2 = pr.psum("s6_ps_bc2", [128, T6], F32)
    ps_tb = pr.psum("s6_ps_tb", [128, 128], BF16)
    r_m = rows[:][:, 0:T6]
    r_v = rows[:][:, T6:2 * T6]
    r_t = rows[:][:, 2 * T6:3 * T6]
    r_q = rows[:][:, 3 * T6:4 * T6]
    r_r1 = rows[:][:, 4 * T6:5 * T6]
    uTv = uT[:]
    for ci in range(SPAN // T6):
        xch_v = xch[:].rearrange("p (g t) -> p g t", g=8)
        xb_v = xbch[:].rearrange("p (g t) -> p g t", g=8)
        sq_v = sqch[:].rearrange("p (g t) -> p g t", g=8)
        xnb_v = xnb[:].rearrange("p (g t) -> p g t", g=8)
        pr.dma("Y", xch_v, xr[:, :, ci * T6:(ci + 1) * T6])
        pr.copy("V", xbch[:], xch[:])
        pr.act("S", sqch[:], xbch[:], AF.Square)
        for g in range(8):
            pr.mm(ps_sum[:1, :], w["onesb"][:], xb_v[:, g, :], start=(g == 0), stop=(g == 7))
        for g in range(8):
            pr.mm(ps_sq[:1, :], w["onesb"][:], sq_v[:, g, :], start=(g == 0), stop=(g == 7))
        pr.ts("V", r_m, ps_sum[:1, :], 1.0 / D, None, OP.mult)
        pr.ts("V", r_v, ps_sq[:1, :], 1.0 / D, None, OP.mult)
        pr.tt("V", r_t, r_m, r_m, OP.mult)
        pr.tt("V", r_v, r_v, r_t, OP.subtract)
        pr.act("S", r_q, r_v, AF.Ln, bias=float(LN_EPS))
        pr.act("S", r_r1, r_q, AF.Exp, scale=-0.5)
        pr.tt("V", r_t, r_m, r_r1, OP.mult)
        pr.mm(ps_bc1[:, :], w["ones1x128f"][:], r_r1, start=True, stop=True)
        pr.mm(ps_bc2[:, :], w["ones1x128f"][:], r_t, start=True, stop=True)
        pr.tt("V", xch_v, xch_v, bc8(ps_bc1[:, :], 8), OP.mult)
        pr.tt("V", xnb_v, xch_v, bc8(ps_bc2[:, :], 8), OP.subtract)
        for tb in range(T6 // 128):
            c = (ci * T6 + tb * 128) // 128
            for hg in range(8):
                pr.transpose(ps_tb[:, :], xnb_v[:, hg, tb * 128:(tb + 1) * 128], w["idenb"][:])
                pr.copy("V", uTv[:, (si * NCHS + c) * HS4 + hg * 128:
                                 (si * NCHS + c) * HS4 + (hg + 1) * 128], ps_tb[:, :])
    pr.dma("P", io[f"uT_{si}"].rearrange("(p f) -> p f", p=128),
           uTv[:, si * NCHS * HS4:(si + 1) * NCHS * HS4])


def emit_s4d_g(pr, w, io, uT):
    powGr = io["powG"].rearrange("(s ri h n) -> s ri h n", s=128, ri=2, h=HS4)
    gst = [pr.sbuf(f"s6_gst{i}", [128, 2048], BF16) for i in range(2)]
    Gsb = pr.sbuf("s6_Gsb", [64, 2 * NCHS * HS4], BF16)
    ps_g = [pr.psum(f"s6_ps_g{i}", [32, 512], F32) for i in range(2)]
    uT4 = uT[:].rearrange("p (si c h) -> p si c h", si=2, c=NCHS)
    Gv = Gsb[:].rearrange("p (ri c h) -> p ri h c", ri=2, c=NCHS)
    for hb in range(HS4 // 32):
        g = gst[hb % 2]
        pr.dma("Y", g[:].rearrange("p (ri hh n) -> p ri hh n", ri=2, hh=32),
               powGr[:, :, hb * 32:(hb + 1) * 32, :])
        for sub in range(4):
            ps = ps_g[sub % 2]
            for hh in range(8):
                h = hb * 32 + sub * 8 + hh
                for ri in range(2):
                    lhsT = g[:][:, ri * 1024 + (sub * 8 + hh) * 32:
                                ri * 1024 + (sub * 8 + hh) * 32 + 32]
                    pr.mm(ps[:, hh * 64 + ri * 32:hh * 64 + ri * 32 + 32],
                          lhsT, uT4[:, :, :, h], start=True, stop=True)
            psv = ps[:].rearrange("p (hh ri si c) -> p hh ri si c", hh=8, ri=2, si=2)
            for ri in range(2):
                for sp in range(2):
                    h0 = hb * 32 + sub * 8
                    pr.copy("V", Gv[sp * 32:sp * 32 + 32, ri, h0:h0 + 8, :],
                            psv[:, :, ri, sp, :])
    return Gsb


def emit_s4d_scan(pr, io, lamn, Gsb):
    histst = pr.sbuf("s6_histst", [64, NCHS * 2 * HS4], BF16, alias="s6_uT")
    Sre = pr.sbuf("s6_Sre", [64, HS4], F32)
    Sim = pr.sbuf("s6_Sim", [64, HS4], F32)
    tA = pr.sbuf("s6_tA", [64, HS4], F32)
    tB = pr.sbuf("s6_tB", [64, HS4], F32)
    pr.barrier()
    pr.memset("V", Sre[:], 0.0)
    pr.memset("V", Sim[:], 0.0)
    lre = lamn[:][:, 0:HS4]
    lim = lamn[:][:, HS4:2 * HS4]
    G3 = Gsb[:].rearrange("p (ri c h) -> p ri c h", ri=2, c=NCHS)
    H3 = histst[:].rearrange("p (c ri h) -> p c ri h", c=NCHS, ri=2)
    for c in range(NCHS):
        pr.copy("V", H3[:, c, 0, :], Sre[:])
        pr.copy("V", H3[:, c, 1, :], Sim[:])
        pr.tt("V", tA[:], lim, Sim[:], OP.mult)
        pr.tt("V", tB[:], lim, Sre[:], OP.mult)
        pr.tt("V", Sre[:], Sre[:], lre, OP.mult)
        pr.tt("V", Sre[:], Sre[:], tA[:], OP.subtract)
        pr.tt("V", Sre[:], Sre[:], G3[:, 0, c, :], OP.add)
        pr.tt("V", Sim[:], Sim[:], lre, OP.mult)
        pr.tt("V", Sim[:], Sim[:], tB[:], OP.add)
        pr.tt("V", Sim[:], Sim[:], G3[:, 1, c, :], OP.add)
    for si in range(2):
        pr.dma("P", io[f"hist_{si}"].rearrange("(c ri n h) -> n c ri h", c=NCHS, ri=2, n=NS4),
               histst[:][si * 32:si * 32 + 32, :].rearrange("p (c ri h) -> p c ri h",
                                                            c=NCHS, ri=2))
        sr = io[f"send_{si}"].rearrange("(ri n h) -> ri n h", ri=2, n=NS4)
        pr.dma("P", sr[0], Sre[:][si * 32:si * 32 + 32, :])
        pr.dma("P", sr[1], Sim[:][si * 32:si * 32 + 32, :])


def _declare_s4d6_io(nc):
    io = {}
    io["idenf"] = declare_in(nc, "idenf", [128, 128])
    io["ones1x128f"] = declare_in(nc, "ones1x128f", [1, 128])
    io["powG"] = declare_in(nc, "powG", [128 * 2 * HS4 * NS4 // 2]).bitcast(BF16)
    io["lamT"] = declare_in(nc, "lamT", [2 * HS4 * NS4])
    for si in range(2):
        io[f"x3_{si}"] = declare_in(nc, f"x3_{si}", [D, SPAN])
        io[f"uT_{si}"] = declare_out(nc, f"uT_{si}", [128 * NCHS * HS4 // 2]).bitcast(BF16)
        io[f"hist_{si}"] = declare_out(nc, f"hist_{si}", [NCHS * 2 * NS4 * HS4 // 2]).bitcast(BF16)
        io[f"send_{si}"] = declare_out(nc, f"send_{si}", [2 * NS4 * HS4])
    return io


def build_launch6():
    nc = bass.Bass("TRN2", target_bir_lowering=False)
    pr = Prog(nc)
    io = _declare_s4d6_io(nc)
    w = {}
    w["idenf"] = pr.sbuf("w_idenf", [128, 128], F32)
    w["idenb"] = pr.sbuf("w_idenb", [128, 128], BF16)
    w["onesb"] = pr.sbuf("w_onesb", [128, 1], BF16)
    w["ones1x128f"] = pr.sbuf("w_ones1f", [1, 128], F32)
    pr.dma("Y", w["idenf"][:], io["idenf"])
    pr.dma("Y", w["ones1x128f"][:], io["ones1x128f"])
    pr.copy("V", w["idenb"][:], w["idenf"][:])
    pr.memset("V", w["onesb"][:], 1.0)
    uT = pr.sbuf("s6_uT", [128, 2 * NCHS * HS4], BF16)
    lamn = emit_lamn(pr, w, io, 64, "s6_lamn")
    for si in range(2):
        emit_ln3_span(pr, w, io, si, uT)
    Gsb = emit_s4d_g(pr, w, io, uT)
    emit_s4d_scan(pr, io, lamn, Gsb)
    pr.fence_all("P")
    pr.close()
    return nc


def _declare_s4d7_io(nc):
    io = {}
    io["idenf"] = declare_in(nc, "idenf", [128, 128])
    io["lamT"] = declare_in(nc, "lamT", [2 * HS4 * NS4])
    for s in range(NSPANS):
        io[f"send_{s}"] = declare_in(nc, f"send_{s}", [2 * NS4 * HS4])
        io[f"Sst_{s}"] = declare_out(nc, f"Sst_{s}", [2 * NS4 * HS4])
    return io


def emit_stitch(pr, io):
    w = {}
    w["idenf"] = pr.sbuf("w_idenf", [128, 128], F32)
    pr.dma("Y", w["idenf"][:], io["idenf"])
    lamn = emit_lamn(pr, w, io, 32, "s7_lamn")
    l16 = pr.sbuf("s7_l16", [32, 2 * HS4], F32)
    t1 = pr.sbuf("s7_t1", [32, HS4], F32)
    t2 = pr.sbuf("s7_t2", [32, HS4], F32)
    pr.copy("V", l16[:], lamn[:])
    re = l16[:][:, 0:HS4]
    im = l16[:][:, HS4:2 * HS4]
    for _ in range(4):
        pr.tt("V", t1[:], re, im, OP.mult)
        pr.tt("V", re, re, re, OP.mult)
        pr.tt("V", t2[:], im, im, OP.mult)
        pr.tt("V", re, re, t2[:], OP.subtract)
        pr.ts("V", im, t1[:], 2.0, None, OP.mult)
    Sre = pr.sbuf("s7_Sre", [32, HS4], F32)
    Sim = pr.sbuf("s7_Sim", [32, HS4], F32)
    sd = pr.sbuf("s7_sd", [32, 2 * HS4], F32)
    pr.memset("V", Sre[:], 0.0)
    pr.memset("V", Sim[:], 0.0)
    for s in range(NSPANS):
        sv = io[f"Sst_{s}"].rearrange("(ri n h) -> ri n h", ri=2, n=NS4)
        pr.dma("P", sv[0], Sre[:])
        pr.dma("P", sv[1], Sim[:])
        pr.dma("Y", sd[:].rearrange("p (ri h) -> p ri h", ri=2),
               io[f"send_{s}"].rearrange("(ri n h) -> n ri h", ri=2, n=NS4))
        pr.tt("V", t1[:], im, Sim[:], OP.mult)
        pr.tt("V", t2[:], im, Sre[:], OP.mult)
        pr.tt("V", Sre[:], Sre[:], re, OP.mult)
        pr.tt("V", Sre[:], Sre[:], t1[:], OP.subtract)
        pr.tt("V", Sre[:], Sre[:], sd[:][:, 0:HS4], OP.add)
        pr.tt("V", Sim[:], Sim[:], re, OP.mult)
        pr.tt("V", Sim[:], Sim[:], t2[:], OP.add)
        pr.tt("V", Sim[:], Sim[:], sd[:][:, HS4:2 * HS4], OP.add)
    pr.fence_all("P")


def build_launch7():
    nc = bass.Bass("TRN2", target_bir_lowering=False)
    pr = Prog(nc)
    io = _declare_s4d7_io(nc)
    emit_stitch(pr, io)
    pr.close()
    return nc


def _declare_s4d8_io(nc):
    io = {}
    io["idenf"] = declare_in(nc, "idenf", [128, 128])
    io["antif"] = declare_in(nc, "antif", [128, 128])
    io["lamT"] = declare_in(nc, "lamT", [2 * HS4 * NS4])
    io["powE"] = declare_in(nc, "powE", [2 * HS4 * NS4 * 128 // 2]).bitcast(BF16)
    io["kpad"] = declare_in(nc, "kpad", [HS4 * 256 // 2]).bitcast(BF16)
    for si in range(2):
        io[f"x3_{si}"] = declare_in(nc, f"x3_{si}", [D, SPAN])
        io[f"uT_{si}"] = declare_in(nc, f"uT_{si}", [128 * NCHS * HS4 // 2]).bitcast(BF16)
        io[f"hist_{si}"] = declare_in(nc, f"hist_{si}", [NCHS * 2 * NS4 * HS4 // 2]).bitcast(BF16)
        io[f"Sst_{si}"] = declare_in(nc, f"Sst_{si}", [2 * NS4 * HS4])
        io[f"x4_{si}"] = declare_out(nc, f"x4_{si}", [D, SPAN])
    return io


def emit_s4d_final(pr, io, kpadb):
    w = {}
    w["idenf"] = pr.sbuf("w_idenf", [128, 128], F32)
    w["idenb"] = pr.sbuf("w_idenb", [128, 128], BF16)
    w["antif"] = pr.sbuf("w_antif", [128, 128], F32)
    w["antib"] = pr.sbuf("w_antib", [128, 128], BF16)
    pr.dma("Y", w["idenf"][:], io["idenf"])
    pr.dma("Y", w["antif"][:], io["antif"])
    pr.copy("V", w["idenb"][:], w["idenf"][:])
    pr.copy("V", w["antib"][:], w["antif"][:])
    pr.dma("Y", kpadb.ap(), io["kpad"])
    lamn = emit_lamn(pr, w, io, 64, "s8_lamn")

    uR = pr.sbuf("s8_uR", [128, 2 * NCHS * HS4], BF16)
    hsb = pr.sbuf("s8_hsb", [64, NCHS * 2 * HS4], BF16, alias="s8_uR")
    y4T = pr.sbuf("s8_y4T", [128, 2 * NCHS * HS4], BF16)
    est = [pr.sbuf(f"s8_est{i}", [64, 2 * 16 * 128], BF16) for i in range(2)]
    hkt = [pr.sbuf(f"s8_hkt{i}", [128, 16 * 128], BF16) for i in range(2)]
    hstg = [pr.sbuf(f"s8_hstg{i}", [32, 2 * HS4], BF16) for i in range(2)]
    Sre = pr.sbuf("s8_Sre", [64, HS4], F32)
    Sim = pr.sbuf("s8_Sim", [64, HS4], F32)
    tA = pr.sbuf("s8_tA", [64, HS4], F32)
    tB = pr.sbuf("s8_tB", [64, HS4], F32)
    uch = [pr.sbuf(f"s8_uch{i}", [128, HS4], BF16) for i in range(2)]
    x3ch = [pr.sbuf(f"s8_x3ch{i}", [128, 1024], F32) for i in range(2)]
    x4ch = [pr.sbuf("s8_x4ch0", [128, 1024], F32, alias="s8_tA"),
            pr.sbuf("s8_x4ch1", [128, 1024], F32, alias="s8_tB")]
    gsl = [pr.sbuf(f"s8_gsl{i}", [128, 1024], BF16) for i in range(2)]
    pA = pr.psum("s8_pA", [128, 512], F32)
    pB = pr.psum("s8_pB", [128, 512], F32)
    pC = pr.psum("s8_pC", [128, 1024], BF16)
    pD = pr.psum("s8_pD", [128, 1024], BF16)
    ps_e = [pA, pB]
    ps_b = [pC, pD]

    hsv = hsb[:]
    for si in range(2):
        hr = io[f"hist_{si}"].rearrange("(c ri n h) -> n c ri h", c=NCHS, ri=2, n=NS4)
        for c in range(NCHS):
            hb = hstg[(si * NCHS + c) % 2]
            pr.dma("Y", hb[:].rearrange("p (ri h) -> p ri h", ri=2), hr[:, c, :, :])
            pr.copy("V", hsv[si * 32:si * 32 + 32, c * 2048:(c + 1) * 2048], hb[:])
    for si in range(2):
        sv = io[f"Sst_{si}"].rearrange("(ri n h) -> ri n h", ri=2, n=NS4)
        pr.dma("Y", Sre[:][si * 32:si * 32 + 32, :], sv[0])
        pr.dma("Y", Sim[:][si * 32:si * 32 + 32, :], sv[1])
    lre = lamn[:][:, 0:HS4]
    lim = lamn[:][:, HS4:2 * HS4]
    for c in range(NCHS):
        pr.tt("V", hsv[:, c * 2048:c * 2048 + HS4],
              hsv[:, c * 2048:c * 2048 + HS4], Sre[:], OP.add)
        pr.tt("V", hsv[:, c * 2048 + HS4:(c + 1) * 2048],
              hsv[:, c * 2048 + HS4:(c + 1) * 2048], Sim[:], OP.add)
        if c < NCHS - 1:
            pr.tt("V", tA[:], lim, Sim[:], OP.mult)
            pr.tt("V", tB[:], lim, Sre[:], OP.mult)
            pr.tt("V", Sre[:], Sre[:], lre, OP.mult)
            pr.tt("V", Sre[:], Sre[:], tA[:], OP.subtract)
            pr.tt("V", Sim[:], Sim[:], lre, OP.mult)
            pr.tt("V", Sim[:], Sim[:], tB[:], OP.add)
    powEr = io["powE"].rearrange("(ri h n k) -> ri h n k", ri=2, h=HS4, n=NS4)
    H4 = hsb[:].rearrange("p (c ri h) -> p ri h c", c=NCHS, ri=2)
    y4f = y4T[:].rearrange("p (si c h) -> p si h c", si=2, c=NCHS)
    y4s = [y4f[:, 0], y4f[:, 1]]
    for hb in range(HS4 // 16):
        eb = est[hb % 2]
        ebv = eb[:]
        for si in range(2):
            for ri in range(2):
                pr.dma("Y", ebv[si * 32:si * 32 + 32, ri * 2048:(ri + 1) * 2048]
                       .rearrange("p (h k) -> p h k", h=16),
                       powEr[ri, hb * 16:(hb + 1) * 16, :, :].rearrange("h n k -> n h k"))
        ps = ps_e[hb % 2]
        for hh in range(16):
            h = hb * 16 + hh
            for si in range(2):
                for ri in range(2):
                    pr.mm(ps[:, hh * 32 + si * 16:hh * 32 + si * 16 + 16],
                          ebv[si * 32:si * 32 + 32,
                              ri * 2048 + hh * 128:ri * 2048 + (hh + 1) * 128],
                          H4[si * 32:si * 32 + 32, ri, h, :],
                          start=(ri == 0), stop=(ri == 1))
        psv = ps[:].rearrange("p (hh si c) -> p si hh c", hh=16, si=2)
        for si in range(2):
            pr.copy("V", y4s[si][:, hb * 16:(hb + 1) * 16, :], psv[:, si])
    pr.barrier()
    uRv = uR[:]
    for si in range(2):
        uTr = io[f"uT_{si}"].rearrange("(p f) -> p f", p=128)
        for c in range(NCHS):
            i = (si * NCHS + c) % 2
            pr.dma("Y", uch[i][:], uTr[:, c * HS4:(c + 1) * HS4])
            for half in range(2):
                ps = ps_e[(2 * (si * NCHS + c) + half) % 2]
                pr.mm(ps[:], w["antib"][:],
                      uch[i][:][:, half * 512:(half + 1) * 512], start=True, stop=True)
                pr.copy("V", uRv[:, (si * NCHS + c) * HS4 + half * 512:
                                (si * NCHS + c) * HS4 + (half + 1) * 512], ps[:])
    pr.barrier()
    uR4 = uR[:].rearrange("p (si c h) -> p si c h", si=2, c=NCHS)
    for hb in range(HS4 // 16):
        hk = hkt[hb % 2]
        pr.dma("Y", hk[:], bass.AP(kpadb, hb * 16 * 256, [[1, 128], [256, 16], [1, 128]]))
        ps = ps_e[hb % 2]
        for hh in range(16):
            h = hb * 16 + hh
            pr.mm(ps[:, hh * 32:hh * 32 + 32],
                  hk[:][:, hh * 128:(hh + 1) * 128], uR4[:, :, :, h],
                  start=True, stop=True)
        psv = ps[:].rearrange("p (hh si c) -> p si hh c", hh=16, si=2)
        for si in range(2):
            pr.tt("V", y4s[si][:, hb * 16:(hb + 1) * 16, :],
                  y4s[si][:, hb * 16:(hb + 1) * 16, :], psv[:, si], OP.add)
    pr.barrier()
    for si in range(2):
        x3r = io[f"x3_{si}"].rearrange("(g p) t -> p g t", p=128)
        x4r = io[f"x4_{si}"].rearrange("(g p) t -> p g t", p=128)
        for c in range(NCHS):
            i = (si * NCHS + c) % 2
            pr.dma("Y", x3ch[i][:].rearrange("p (g t) -> p g t", g=8),
                   x3r[:, :, c * 128:(c + 1) * 128])
            pr.act("S", gsl[i][:], y4T[:][:, (si * NCHS + c) * HS4:
                                          (si * NCHS + c + 1) * HS4], AF.Gelu)
            for hg in range(8):
                pr.transpose(ps_b[i][:, hg * 128:(hg + 1) * 128],
                             gsl[i][:][:, hg * 128:(hg + 1) * 128], w["idenb"][:])
            pr.tt("V", x4ch[i][:], x3ch[i][:], ps_b[i][:], OP.add)
            pr.dma("P", x4r[:, :, c * 128:(c + 1) * 128],
                   x4ch[i][:].rearrange("p (g t) -> p g t", g=8))


def build_launch8():
    nc = bass.Bass("TRN2", target_bir_lowering=False)
    pr = Prog(nc)
    io = _declare_s4d8_io(nc)
    kpadb = nc.dram_tensor("kpadb", [HS4 * 256], BF16)
    emit_s4d_final(pr, io, kpadb)
    pr.fence_all("P")
    pr.close()
    return nc



def emit_relay_chain(pr, io, dr):
    EB = 32
    NBK = (NE + EB - 1) // EB
    onesf = pr.sbuf("ch_onesf", [128, 1], F32)
    ones1x128f = pr.sbuf("ch_ones1f", [1, 128], F32)
    w20 = pr.sbuf("ch_w20", [1, 1], F32)
    pr.memset("V", onesf[:], 1.0)
    pr.memset("V", w20[:], 20.0)
    pr.dma("Y", ones1x128f[:], io["ones1x128f"])
    state = pr.sbuf("ch_state", [128, 256], F32)
    tprod = pr.sbuf("ch_tprod", [128, 256], F32)
    sqscr = pr.sbuf("ch_sq", [128, 256], F32)
    partials = pr.sbuf("ch_part", [128, 1], F32)
    pblk = [pr.sbuf(f"ch_pblk{i}", [128, 256 * EB], F32) for i in range(2)]
    qblk = [pr.sbuf(f"ch_qblk{i}", [128, 256 * EB], F32) for i in range(2)]
    rows = pr.sbuf("ch_rows", [1, 8], F32)
    facrow = pr.sbuf("ch_facrow", [1, NE], F32)
    maskrow = pr.sbuf("ch_mask", [1, NE], F32)
    srecv = pr.sbuf("ch_srecv", [128, 256], F32)
    ps_n = pr.psum("ch_ps_n", [1, 2], F32)
    ps_f = pr.psum("ch_ps_f", [128, 2], F32)
    stA = dr["ch_stentAll"].ap().rearrange("(k p t) -> k p t", k=NSPANS, p=128)
    fcA = dr["ch_facsAll"].ap().rearrange("(k o e) -> k o e", k=NSPANS, o=1)

    for k in range(NSPANS):
        if k == 0:
            pr.memset("V", srecv[:], 0.0)
        else:
            bo = dr[f"ch_bout_{k - 1}"].ap()

            def entry(rank, k=k, bo=bo):
                if rank == k:
                    return (srecv[:], bo[(k - 1) * 128:k * 128, 0:256])
                if rank == 7 - k:
                    return (srecv[:], bo[(8 - k) * 128:(9 - k) * 128, 256:512])
                return (srecv[:], bo[0:128, 0:256])
            pr.dma_arms("Y", entry)
        pr.dma("P", stA[k], srecv[:])
        pr.dma("Y", maskrow[:], io["maskAll"][k:k + 1, :])
        pr.copy("V", state[:], srecv[:])
        pr.memset("V", ps_f[:, 1:2], 1.0)
        for eb in range(NBK):
            e0 = eb * EB
            ne = min(EB, NE - e0)
            pb, qb = pblk[eb % 2], qblk[eb % 2]
            pb3 = pb[:].rearrange("p (pr e) -> p pr e", e=EB)
            qb3 = qb[:].rearrange("p (pr e) -> p pr e", e=EB)

            def pqload(widx, dst, k=k, e0=e0, ne=ne):
                def f(rank):
                    si = 1 if rank == 7 - k else 0
                    PQ = dr[f"PQ_{si}"].ap().rearrange("w (p pr e) -> w p pr e", p=128, pr=256)
                    return (dst, PQ[widx, :, :, e0:e0 + ne])
                return f
            pr.dma_arms("Y", pqload(0, pb3[:, :, 0:ne]))
            pr.dma_arms("Y", pqload(1, qb3[:, :, 0:ne]))
            for el in range(ne):
                e = e0 + el
                Pv = pb3[:, :, el]
                Qv = qb3[:, :, el]
                pr.tt("V", tprod[:], Pv, state[:], OP.mult)
                pr.stt("V", state[:], tprod[:], ps_f[:, (e + 1) % 2:(e + 1) % 2 + 1],
                       Qv, OP.mult, OP.add)
                pr.act("S", sqscr[:], state[:], AF.Square, accum_out=partials[:])
                pr.mm(ps_n[:1, e % 2:e % 2 + 1], onesf[:], partials[:], start=True, stop=True)
                pr.act("S", rows[:, 4:5], ps_n[:1, e % 2:e % 2 + 1], AF.Ln, bias=1e-30)
                pr.act("S", rows[:, 0:1], rows[:, 4:5], AF.Exp, scale=-0.5,
                       bias=float(math.log(20.0)))
                pr.ts("V", rows[:, 2:3], rows[:, 0:1], 1.0, None, OP.min)
                pr.tt("V", rows[:, 2:3], rows[:, 2:3], maskrow[:, e:e + 1], OP.max)
                pr.mm(ps_f[:, e % 2:e % 2 + 1], ones1x128f[:], rows[:, 2:3],
                      start=True, stop=True)
                pr.copy("V", facrow[:, e:e + 1], rows[:, 2:3])
        pr.ts("V", state[:], state[:], ps_f[:, (NE - 1) % 2:(NE - 1) % 2 + 1], None, OP.mult)
        pr.dma("P", fcA[k], facrow[:])
        cin = dr[f"ch_cin_{k}"].ap()
        pr.dma("P", cin[:, 0:256], state[:])
        pr.dma("P", cin[:, 256:512], state[:])
        pr.collective("AllGather", OP.bypass, [list(range(NCORES))],
                      [cin], [dr[f"ch_bout_{k}"].ap()])
    pr.dma_arms("Y", lambda r: (dr["stent_0"].ap(), stA[r]))
    pr.dma_arms("Y", lambda r: (dr["stent_1"].ap(), stA[7 - r]))
    pr.dma_arms("Y", lambda r: (dr["facs_0"].ap(), fcA[r]))
    pr.dma_arms("Y", lambda r: (dr["facs_1"].ap(), fcA[7 - r]))


def emit_stitch_fused(pr, io, dr):
    w = {}
    w["idenf"] = pr.sbuf("st_idenf", [128, 128], F32)
    pr.dma("Y", w["idenf"][:], io["idenf"])
    bi = dr["st_bin"]
    bo = dr["st_bout"]
    for si in range(2):
        pr.dma("P", bi.ap()[si * 32:(si + 1) * 32, :].rearrange("n (ri h) -> n ri h", ri=2),
               dr[f"send_{si}"].ap().rearrange("(ri n h) -> n ri h", ri=2, n=NS4))
    pr.collective("AllGather", OP.bypass, [list(range(NCORES))], [bi.ap()], [bo.ap()])
    lamn = emit_lamn(pr, w, io, 32, "st_lamn")
    l16 = pr.sbuf("st_l16", [32, 2 * HS4], F32)
    t1 = pr.sbuf("st_t1", [32, HS4], F32)
    t2 = pr.sbuf("st_t2", [32, HS4], F32)
    pr.copy("V", l16[:], lamn[:])
    re = l16[:][:, 0:HS4]
    im = l16[:][:, HS4:2 * HS4]
    for _ in range(4):
        pr.tt("V", t1[:], re, im, OP.mult)
        pr.tt("V", re, re, re, OP.mult)
        pr.tt("V", t2[:], im, im, OP.mult)
        pr.tt("V", re, re, t2[:], OP.subtract)
        pr.ts("V", im, t1[:], 2.0, None, OP.mult)
    Sre = pr.sbuf("st_Sre", [32, HS4], F32)
    Sim = pr.sbuf("st_Sim", [32, HS4], F32)
    sd = pr.sbuf("st_sd", [32, 2 * HS4], F32)
    SstAll = dr["SstAll"].ap().rearrange("(b s f) -> b s f", b=NB, s=NSPANS)
    for b in range(NB):
        pr.memset("V", Sre[:], 0.0)
        pr.memset("V", Sim[:], 0.0)
        for s in range(NSPANS):
            sv = SstAll[b, s].rearrange("(ri n h) -> ri n h", ri=2, n=NS4)
            pr.dma("P", sv[0], Sre[:])
            pr.dma("P", sv[1], Sim[:])
            rowbase = s * 64 if b == 0 else (7 - s) * 64 + 32
            pr.dma("Y", sd[:], bo.ap()[rowbase:rowbase + 32, :])
            pr.tt("V", t1[:], im, Sim[:], OP.mult)
            pr.tt("V", t2[:], im, Sre[:], OP.mult)
            pr.tt("V", Sre[:], Sre[:], re, OP.mult)
            pr.tt("V", Sre[:], Sre[:], t1[:], OP.subtract)
            pr.tt("V", Sre[:], Sre[:], sd[:][:, 0:HS4], OP.add)
            pr.tt("V", Sim[:], Sim[:], re, OP.mult)
            pr.tt("V", Sim[:], Sim[:], t2[:], OP.add)
            pr.tt("V", Sim[:], Sim[:], sd[:][:, HS4:2 * HS4], OP.add)
    for si in range(2):
        pr.dma_arms("Y", lambda r, si=si: (dr[f"Sst_{si}"].ap(),
                                           SstAll[_span_of_core(r)[si][0],
                                                  _span_of_core(r)[si][1]]))


def emit_halo(pr, dr, xekey, binm, boutm):
    bi = dr[binm]
    bo = dr[boutm]
    for si in range(2):
        xe = dr[xekey.format(si=si)].ap().rearrange("(g p) t -> p g t", p=128)
        bv = bi.ap().rearrange("p (s2 g e) -> p s2 g e", s2=2, g=8)
        pr.dma("P", bv[:, si, :, 0], xe[:, :, 1], allow_slow_non_contiguous=True)
        pr.dma("P", bv[:, si, :, 1], xe[:, :, XE - 2], allow_slow_non_contiguous=True)
    pr.collective("AllGather", OP.bypass, [list(range(NCORES))], [bi.ap()], [bo.ap()])
    bog = bo.ap().rearrange("(r p) (s2 g e) -> r p s2 g e", r=NCORES, p=128, s2=2, g=8)

    def pick(si, edge):
        def f(rank):
            b, s = _span_of_core(rank)[si]
            xe = dr[xekey.format(si=si)].ap().rearrange("(g p) t -> p g t", p=128)
            dst = xe[:, :, 0 if edge == 0 else XE - 1]
            ns = s - 1 if edge == 0 else s + 1
            if 0 <= ns < NSPANS:
                oc, osi = _owner(b, ns)
                src = bog[oc, :, osi, :, 1 if edge == 0 else 0]
            else:
                src = dr["zed"].ap()
            return (dst, src)
        return f
    for si in range(2):
        for edge in range(2):
            pr.dma_arms("P", pick(si, edge), allow_slow_non_contiguous=True)


def build_fused():
    nc = bass.Bass("TRN2", target_bir_lowering=False)
    pr = Prog(nc)
    hi = {}
    for nm, shp in [("WinT", [D, DH]), ("WdtT", [DI, DI]), ("WBT", [DI, NST]),
                    ("WCT", [DI, NST]), ("WoutT", [DI, D]), ("w1T", [D, DH]),
                    ("w2T", [DH, D]), ("normw", [128, 8]), ("cwa", [128, 64]),
                    ("cba", [128, 16]), ("bdt", [128, 16]), ("Alog", [128, 256]),
                    ("Dpar", [128, 16]), ("rscal", [1, 1]), ("cwf", [128, 96]),
                    ("cbf", [128, 32]), ("s4_logdt", [128, 8]), ("s4_Ar", [128, 256]),
                    ("s4_Ai", [128, 256]), ("s4_Cr", [128, 256]), ("s4_Ci", [128, 256]),
                    ("s4_D", [1, 1]), ("ones1x128f", [1, 128]), ("idenf", [128, 128]),
                    ("antif", [128, 128]), ("maskAll", [NSPANS, NE])]:
        hi[nm] = declare_in(nc, nm, shp)
    for si in range(2):
        hi[f"x_{si}"] = declare_in(nc, f"x_{si}", [D, SPAN])
        hi[f"xhalo_{si}"] = declare_in(nc, f"xhalo_{si}", [D, 128])
        hi[f"out_{si}"] = declare_out(nc, f"out_{si}", [D, SPAN])
    dr = {}

    def mkdr(name, shape, dtype=F32):
        dr[name] = nc.dram_tensor(name, shape, dtype)
        return dr[name]
    for si in range(2):
        mkdr(f"xcb_{si}", [DI, SPAN], BF16)
        mkdr(f"sg_{si}", [DI, SPAN], BF16)
        mkdr(f"stat_{si}", [2, SPAN])
        mkdr(f"dt_{si}", [DI, SPAN])
        mkdr(f"Brow_{si}", [NST, SPAN])
        mkdr(f"Crow_{si}", [NST, SPAN])
        mkdr(f"dtp_{si}", [DI, PW])
        mkdr(f"xcbp_{si}", [DI, PW], BF16)
        mkdr(f"sgp_{si}", [DI, PW], BF16)
        mkdr(f"Bp_{si}", [NST, PW])
        mkdr(f"Cp_{si}", [NST, PW])
        mkdr(f"PQ_{si}", [2, 128 * 256 * NE])
        mkdr(f"stent_{si}", [128, 256])
        mkdr(f"facs_{si}", [1, NE])
        mkdr(f"yac_{si}", [DI, PW])
        mkdr(f"ysgp_{si}", [DI, PW], BF16)
        mkdr(f"x2e_{si}", [D, XE])
        mkdr(f"rsc2_{si}", [1, XE])
        mkdr(f"h1_{si}", [DH, XE], BF16)
        mkdr(f"x3_{si}", [D, SPAN])
        mkdr(f"uT_{si}", [128 * NCHS * HS4], BF16)
        mkdr(f"hist_{si}", [NCHS * 2 * NS4 * HS4], BF16)
        mkdr(f"send_{si}", [2 * NS4 * HS4])
        mkdr(f"Sst_{si}", [2 * NS4 * HS4])
        mkdr(f"x4e_{si}", [D, XE])
        mkdr(f"rsc4_{si}", [1, XE])
        mkdr(f"h2_{si}", [DH, XE], BF16)
    mkdr("zDI", [DI, 16])
    mkdr("zDIb", [DI, 16], BF16)
    mkdr("zNST", [NST, 16])
    mkdr("zed", [128, 8])
    mkdr("powG", [128 * 2 * HS4 * NS4], BF16)
    mkdr("powE", [2 * HS4 * NS4 * 128], BF16)
    mkdr("kpad", [HS4 * 256], BF16)
    mkdr("kpadb", [HS4 * 256], BF16)
    mkdr("lamT", [2 * HS4 * NS4])
    mkdr("ch_stentAll", [NSPANS * 128 * 256])
    mkdr("ch_facsAll", [NSPANS * 1 * NE])
    for k in range(NSPANS):
        mkdr(f"ch_cin_{k}", [128, 512])
        dr[f"ch_bout_{k}"] = nc.dram_tensor(f"ch_bout_{k}", [128 * NCORES, 512], F32, addr_space="Shared")
    mkdr("SstAll", [NB * NSPANS * 2 * NS4 * HS4])
    mkdr("st_bin", [64, 2048])
    dr["st_bout"] = nc.dram_tensor("st_bout", [64 * NCORES, 2048], F32, addr_space="Shared")
    mkdr("hb2_in", [128, 32])
    dr["hb2_out"] = nc.dram_tensor("hb2_out", [128 * NCORES, 32], F32, addr_space="Shared")
    mkdr("hb4_in", [128, 32])
    dr["hb4_out"] = nc.dram_tensor("hb4_out", [128 * NCORES, 32], F32, addr_space="Shared")

    A = lambda n: dr[n].ap()

    pr.set_prefix("z_")
    zt = pr.sbuf("zt", [128, 16], F32)
    ztb = pr.sbuf("ztb", [128, 16], BF16)
    pr.memset("V", zt[:], 0.0)
    pr.memset("V", ztb[:], 0.0)
    for g in range(DI // 128):
        pr.dma("P", A("zDI").rearrange("(g p) c -> g p c", p=128)[g], zt[:])
        pr.dma("P", A("zDIb").rearrange("(g p) c -> g p c", p=128)[g], ztb[:])
    pr.dma("P", A("zNST"), zt[:][0:NST, :])
    pr.dma("P", A("zed"), zt[:][:, 0:8])

    pr.phase()
    pr.set_prefix("pp_")
    io = {k: hi[k] for k in ("s4_logdt", "s4_Ar", "s4_Ai", "s4_Cr", "s4_Ci", "s4_D")}
    io["powG"] = A("powG")
    io["powE"] = A("powE")
    io["kpad"] = A("kpad")
    io["lamT"] = A("lamT")
    emit_s4d_prep(pr, io)
    pr.dma("Y", A("kpadb"), A("kpad"))

    pr.phase()
    pr.set_prefix("sa_")
    io = {k: hi[k] for k in ("WinT", "normw", "cwa", "cba", "ones1x128f")}
    for si in range(2):
        io[f"x_{si}"] = hi[f"x_{si}"]
        io[f"xhalo_{si}"] = hi[f"xhalo_{si}"]
        io[f"xcb_{si}"] = A(f"xcb_{si}")
        io[f"sg_{si}"] = A(f"sg_{si}")
        io[f"stat_{si}"] = A(f"stat_{si}")
    w = emit_weights_1a(pr, io)
    for si in range(2):
        emit_stage1a_span(pr, w, io, si)

    pr.phase()
    pr.set_prefix("sb_")
    io = {k: hi[k] for k in ("WdtT", "WBT", "WCT", "bdt")}
    for si in range(2):
        io[f"xcb_{si}"] = A(f"xcb_{si}")
        io[f"dt_{si}"] = A(f"dt_{si}")
        io[f"Brow_{si}"] = A(f"Brow_{si}")
        io[f"Crow_{si}"] = A(f"Crow_{si}")
    w = emit_weights_1b(pr, io)
    for si in range(2):
        emit_stage1b_span(pr, w, io, si)

    pr.phase()
    pr.set_prefix("p1_")
    io = {"Alog": hi["Alog"], "ones1x128f": hi["ones1x128f"],
          "zDI": A("zDI"), "zDIb": A("zDIb"), "zNST": A("zNST")}
    for si in range(2):
        for k in ("dt", "xcb", "Brow", "dtp", "xcbp", "Bp", "PQ"):
            io[f"{k}_{si}"] = A(f"{k}_{si}")
    w = emit_scan_consts(pr, io)
    for si in range(2):
        emit_pass1_span(pr, w, io, si, None, fused=True)

    pr.phase()
    pr.set_prefix("ch_")
    io = {"ones1x128f": hi["ones1x128f"], "maskAll": hi["maskAll"]}
    emit_relay_chain(pr, io, dr)

    pr.phase()
    pr.set_prefix("p2_")
    io = {k: hi[k] for k in ("Alog", "ones1x128f", "Dpar", "WoutT", "rscal")}
    io.update({"zDI": A("zDI"), "zDIb": A("zDIb"), "zNST": A("zNST")})
    for si in range(2):
        for k in ("dt", "xcb", "sg", "Brow", "Crow", "stat", "stent", "facs",
                  "dtp", "xcbp", "sgp", "Bp", "Cp", "yac", "ysgp"):
            io[f"{k}_{si}"] = A(f"{k}_{si}")
        io[f"x_{si}"] = hi[f"x_{si}"]
        io[f"ysg_{si}"] = A(f"ysgp_{si}")
        io[f"x2_{si}"] = A(f"x2e_{si}")[:, 1:XE - 1]
    w = emit_pass2_weights(pr, io)
    for si in range(2):
        emit_pass2_span(pr, w, io, si, None, fused=True)

    pr.phase()
    pr.set_prefix("h2_")
    emit_halo(pr, dr, "x2e_{si}", "hb2_in", "hb2_out")

    for which, xk, hk, rk, outk in (("a", "x2e", "h1", "rsc2", None),
                                    ("b", "x2e", "h1", "rsc2", "x3")):
        pr.phase()
        pr.set_prefix(f"f1{which}_")
        io = {k: hi[k] for k in ("ones1x128f", "idenf", "cwf", "cbf")}
        io["w1T"] = hi["w1T"]
        io["w2T"] = hi["w2T"]
        for si in range(2):
            io[f"x2e_{si}"] = A(f"{xk}_{si}")
            io[f"h_{si}"] = A(f"{hk}_{si}")
            io[f"rsc_{si}"] = A(f"{rk}_{si}")
            if outk:
                io[f"x3_{si}"] = A(f"{outk}_{si}")
        w = emit_ffn_weights(pr, io, which)
        for si in range(2):
            if which == "a":
                emit_ffn_a_span(pr, w, io, si)
            else:
                emit_ffn_b_span(pr, w, io, si)

    pr.phase()
    pr.set_prefix("s6_")
    io = {"idenf": hi["idenf"], "ones1x128f": hi["ones1x128f"],
          "powG": A("powG"), "lamT": A("lamT")}
    for si in range(2):
        io[f"x3_{si}"] = A(f"x3_{si}")
        io[f"uT_{si}"] = A(f"uT_{si}")
        io[f"hist_{si}"] = A(f"hist_{si}")
        io[f"send_{si}"] = A(f"send_{si}")
    w = {}
    w["idenf"] = pr.sbuf("w_idenf", [128, 128], F32)
    w["idenb"] = pr.sbuf("w_idenb", [128, 128], BF16)
    w["onesb"] = pr.sbuf("w_onesb", [128, 1], BF16)
    w["ones1x128f"] = pr.sbuf("w_ones1f", [1, 128], F32)
    pr.dma("Y", w["idenf"][:], io["idenf"])
    pr.dma("Y", w["ones1x128f"][:], io["ones1x128f"])
    pr.copy("V", w["idenb"][:], w["idenf"][:])
    pr.memset("V", w["onesb"][:], 1.0)
    uT = pr.sbuf("s6_uT", [128, 2 * NCHS * HS4], BF16)
    lamn = emit_lamn(pr, w, io, 64, "s6_lamn")
    for si in range(2):
        emit_ln3_span(pr, w, io, si, uT)
    Gsb = emit_s4d_g(pr, w, io, uT)
    emit_s4d_scan(pr, io, lamn, Gsb)

    pr.phase()
    pr.set_prefix("st_")
    io = {"idenf": hi["idenf"], "lamT": A("lamT")}
    emit_stitch_fused(pr, io, dr)

    pr.phase()
    pr.set_prefix("s8_")
    io = {"idenf": hi["idenf"], "antif": hi["antif"], "lamT": A("lamT"),
          "powE": A("powE"), "kpad": A("kpad")}
    for si in range(2):
        io[f"x3_{si}"] = A(f"x3_{si}")
        io[f"uT_{si}"] = A(f"uT_{si}")
        io[f"hist_{si}"] = A(f"hist_{si}")
        io[f"Sst_{si}"] = A(f"Sst_{si}")
        io[f"x4_{si}"] = A(f"x4e_{si}")[:, 1:XE - 1]
    emit_s4d_final(pr, io, dr["kpadb"])

    pr.phase()
    pr.set_prefix("h4_")
    emit_halo(pr, dr, "x4e_{si}", "hb4_in", "hb4_out")

    for which in ("a", "b"):
        pr.phase()
        pr.set_prefix(f"f2{which}_")
        io = {k: hi[k] for k in ("ones1x128f", "idenf", "cwf", "cbf")}
        io["w1T"] = hi["w1T"]
        io["w2T"] = hi["w2T"]
        for si in range(2):
            io[f"x2e_{si}"] = A(f"x4e_{si}")
            io[f"h_{si}"] = A(f"h2_{si}")
            io[f"rsc_{si}"] = A(f"rsc4_{si}")
            io[f"x3_{si}"] = hi[f"out_{si}"]
        w = emit_ffn_weights(pr, io, which)
        for si in range(2):
            if which == "a":
                emit_ffn_a_span(pr, w, io, si)
            else:
                emit_ffn_b_span(pr, w, io, si)
    for e in Prog.ENGINES:
        pr.fence_all(e)
    pr.close()
    return nc


def kernel(**inputs):
    maps = host_prepare(inputs)
    mask = np.stack([span_mask(s) for s in range(NSPANS)])
    for c in range(NCORES):
        maps[c]["maskAll"] = mask
    nc = build_fused()
    res = run_bass_kernel_spmd(nc, maps, core_ids=list(range(NCORES)))
    out = np.empty((NB, L, D), np.float32)
    for c in range(NCORES):
        for si, (b, s) in enumerate(_span_of_core(c)):
            t0 = s * SPAN
            out[b, t0:t0 + SPAN] = np.asarray(res.results[c][f"out_{si}"]) \
                .reshape(8, 128, SPAN).transpose(2, 0, 1).reshape(SPAN, D)
    return out


def _mk_x_e(xg, c, si):
    b, s = _span_of_core(c)[si]
    t0 = s * SPAN
    xe = np.zeros((D, XE), np.float32)
    xe[:, 1:XE - 1] = xg[b, t0:t0 + SPAN].T
    if t0 > 0:
        xe[:, 0] = xg[b, t0 - 1]
    if t0 + SPAN < L:
        xe[:, XE - 1] = xg[b, t0 + SPAN]
    return xe


def _gather_fm(results, key):
    xg = np.zeros((NB, L, D), np.float32)
    for c in range(NCORES):
        for si, (b, s) in enumerate(_span_of_core(c)):
            t0 = s * SPAN
            xg[b, t0:t0 + SPAN] = np.asarray(results[c][key.format(si=si)]) \
                .reshape(8, 128, SPAN).transpose(2, 0, 1).reshape(SPAN, D)
    return xg


def _run_groups(build_fn, maps, key_of_core):
    out = [None] * NCORES
    groups = {}
    for c in range(NCORES):
        groups.setdefault(key_of_core(c), []).append(c)
    for key, cores in groups.items():
        nc = build_fn(cores[0])
        res = run_bass_kernel_spmd(nc, [maps[c] for c in cores], core_ids=cores)
        for i, c in enumerate(cores):
            out[c] = dict(res.results[i])
    return out


def _owner(b, s):
    for c in range(NCORES):
        for si, (bb, ss) in enumerate(_span_of_core(c)):
            if bb == b and ss == s:
                return c, si
    raise KeyError


def _ffn_pass(maps, xg):
    maps5 = []
    for c in range(NCORES):
        m = {k: maps[c][k] for k in ("ones1x128f", "idenf", "cwf", "cbf", "w1T", "w2T")}
        for si in range(2):
            m[f"x2e_{si}"] = _mk_x_e(xg, c, si)
        maps5.append(m)
    res = run_bass_kernel_spmd(build_launch5a(), maps5, core_ids=list(range(NCORES)))
    ra = [dict(r) for r in res.results]
    for c in range(NCORES):
        for si in range(2):
            maps5[c][f"h_{si}"] = ra[c][f"h_{si}"]
    res = run_bass_kernel_spmd(build_launch5b(), maps5, core_ids=list(range(NCORES)))
    rb = [dict(r) for r in res.results]
    return _gather_fm(rb, "x3_{si}")


def kernel_multi(**inputs):
    maps = host_prepare(inputs)
    cores = list(range(NCORES))
    r1 = [dict(r) for r in run_bass_kernel_spmd(build_launch1a(), maps, core_ids=cores).results]
    for c in cores:
        for si in range(2):
            maps[c][f"xcb_{si}"] = r1[c][f"xcb_{si}"]
    rb = [dict(r) for r in run_bass_kernel_spmd(build_launch1b(), maps, core_ids=cores).results]
    for c in cores:
        r1[c].update(rb[c])
        for si in range(2):
            for k in ("dt", "Brow"):
                maps[c][f"{k}_{si}"] = r1[c][f"{k}_{si}"]
    r2 = _run_groups(build_launch2, maps, lambda c: tuple(span_shift(s) for _, s in _span_of_core(c)))
    maps3 = []
    for b in range(NB):
        m = {"ones1x128f": maps[0]["ones1x128f"]}
        for s in range(NSPANS):
            oc, osi = _owner(b, s)
            m[f"PQc_{s}"] = r2[oc][f"PQ_{osi}"]
            m[f"mask_{s}"] = span_mask(s).reshape(1, NE)
        maps3.append(m)
    r3 = [dict(r) for r in run_bass_kernel_spmd(build_launch3(), maps3, core_ids=[0, 1]).results]
    for c in cores:
        for si, (b, s) in enumerate(_span_of_core(c)):
            for k in ("sg", "Crow", "stat"):
                maps[c][f"{k}_{si}"] = r1[c][f"{k}_{si}"]
            maps[c][f"stent_{si}"] = r3[b][f"stent_{s}"]
            maps[c][f"facs_{si}"] = r3[b][f"facs_{s}"]
    r4 = _run_groups(build_launch4, maps, lambda c: tuple(span_shift(s) for _, s in _span_of_core(c)))
    x2 = _gather_fm(r4, "x2_{si}")
    x3 = _ffn_pass(maps, x2)
    rp = [dict(r) for r in run_bass_kernel_spmd(build_s4d_prep_test(), maps, core_ids=cores).results]
    maps6 = []
    for c in cores:
        m = {"idenf": maps[c]["idenf"], "ones1x128f": maps[c]["ones1x128f"],
             "powG": rp[c]["powG"], "lamT": rp[c]["lamT"]}
        for si, (b, s) in enumerate(_span_of_core(c)):
            m[f"x3_{si}"] = np.ascontiguousarray(x3[b, s * SPAN:(s + 1) * SPAN].T)
        maps6.append(m)
    r6 = [dict(r) for r in run_bass_kernel_spmd(build_launch6(), maps6, core_ids=cores).results]
    maps7 = []
    for b in range(NB):
        m = {"idenf": maps[0]["idenf"], "lamT": rp[b]["lamT"]}
        for s in range(NSPANS):
            oc, osi = _owner(b, s)
            m[f"send_{s}"] = r6[oc][f"send_{osi}"]
        maps7.append(m)
    r7 = [dict(r) for r in run_bass_kernel_spmd(build_launch7(), maps7, core_ids=[0, 1]).results]
    maps8 = []
    for c in cores:
        m = {"idenf": maps[c]["idenf"], "antif": maps[c]["antif"],
             "lamT": rp[c]["lamT"], "powE": rp[c]["powE"], "kpad": rp[c]["kpad"]}
        for si, (b, s) in enumerate(_span_of_core(c)):
            m[f"x3_{si}"] = maps6[c][f"x3_{si}"]
            m[f"uT_{si}"] = r6[c][f"uT_{si}"]
            m[f"hist_{si}"] = r6[c][f"hist_{si}"]
            m[f"Sst_{si}"] = r7[b][f"Sst_{s}"]
        maps8.append(m)
    r8 = [dict(r) for r in run_bass_kernel_spmd(build_launch8(), maps8, core_ids=cores).results]
    x4 = _gather_fm(r8, "x4_{si}")
    x5 = _ffn_pass(maps, x4)
    return np.ascontiguousarray(x5)
